# version 05
# baseline (speedup 1.0000x reference)
import numpy as np
import ml_dtypes
import concourse.bass as bass
import concourse.mybir as mybir
from concourse.bass_utils import run_bass_kernel_spmd
from concourse.bass_types import AP

F32 = mybir.dt.float32
F32R = mybir.dt.float32r
BF16 = mybir.dt.bfloat16
AFT = mybir.ActivationFunctionType
ALU = mybir.AluOpType
AX = mybir.AxisListType
BF = ml_dtypes.bfloat16

B, S, D = 4, 4096, 1024
NCH = D // 128
DI, DIH = 2048, 1024
NST, DTR, KCONV = 16, 64, 4
NE, DFFP = 8, 512
NH, DK = 16, 64
EPS = 1e-5
TOK = S // 2

SEMS = ("t", "v", "s", "p", "dl", "ds")
BASS_KW = {}


class St:
    def __init__(self, nc, block, sems):
        self.nc, self.block, self.sem = nc, block, sems
        self.cnt = {k: 0 for k in sems}

    def tag(self, instr, name, amt=1):
        self.cnt[name] += amt
        instr.then_inc(self.sem[name], amt)

    def w(self, eng, name, val=None):
        v = self.cnt[name] if val is None else val
        if v > 0:
            eng.wait_ge(self.sem[name], v)

    def fence_eng(self, eng, skip=()):
        for n in self.sem:
            if n not in skip and self.cnt[n] > 0:
                eng.wait_ge(self.sem[n], self.cnt[n])


def fence(st, engines="tvsp"):
    blk = st.block
    m = {"t": blk.tensor, "v": blk.vector, "s": blk.scalar, "p": blk.gpsimd,
         "q": blk.sync}
    for e in engines:
        m[e](lambda eng, _st=st: _st.fence_eng(eng))


def fm_in_ap(dram_ap, nch, T, row0=0):
    return AP(dram_ap.tensor, dram_ap.offset + row0 * T,
              [[T, 128], [T * 128, nch], [1, T]])


CONSTVALS = (EPS, 1.0, -1.0, 0.5)


def declare_common(nc, ct):
    names = [("ident", [128, 128], F32), ("identb", [128, 128], BF16),
             ("ones_col", [128, 1], F32),
             ("ones_col_r", [128, 1], F32R), ("ones_row", [1, 128], F32),
             ("ones_row_b", [1, 128], BF16), ("cvals", [128, len(CONSTVALS)], F32)]
    ins = {}
    for nm, shp, dt in names:
        ct[nm] = nc.alloc_sbuf_tensor("c_" + nm, shp, dt)
        ins["i_" + nm] = (nc.dram_tensor("i_" + nm, shp, dt, kind="ExternalInput").ap(), nm)
    for i, v in enumerate(CONSTVALS):
        nc.const_aps.register(F32, v, ct["cvals"][:, i:i + 1])
    return ins


def common_inmap(consts):
    cv = np.broadcast_to(np.asarray(CONSTVALS, np.float32), (128, len(CONSTVALS))).copy()
    return dict(i_ident=consts["ident"], i_identb=consts["ident"].astype(BF),
                i_ones_col=consts["ones_col"],
                i_ones_col_r=consts["ones_col"], i_ones_row=consts["ones_row"],
                i_ones_row_b=consts["ones_row"].astype(BF), i_cvals=cv)


def host_consts():
    ident = np.eye(128, dtype=np.float32)
    ones_col = np.ones((128, 1), np.float32)
    ones_row = np.ones((1, 128), np.float32)
    inv = (np.float32(1.0) / (np.float32(10000.0) **
           (np.arange(0, DK, 2, dtype=np.float32) / np.float32(DK)))).astype(np.float32)
    pos = np.arange(S, dtype=np.float32)
    sc = (pos[:, None] * inv[None, :]).astype(np.float32)
    sin = np.repeat(np.sin(sc), 2, axis=1).T.astype(np.float32)
    cos = np.repeat(np.cos(sc), 2, axis=1).T.astype(np.float32)
    cos128 = np.concatenate([cos, cos], 0).copy()
    sin128 = np.concatenate([sin, sin], 0).copy()
    R = np.zeros((64, 64), np.float32)
    for i in range(32):
        R[2 * i, 2 * i + 1] = -1.0
        R[2 * i + 1, 2 * i] = 1.0
    R128 = np.zeros((128, 128), np.float32)
    R128[0:64, 0:64] = R.T
    R128[64:128, 64:128] = R.T
    bones = np.zeros((128, 16, 128), np.float32)
    for g in range(16):
        for dl in range(8):
            bones[dl * 16:(dl + 1) * 16, g, 8 * g + dl] = 1.0
    return dict(ident=ident, ones_col=ones_col, ones_row=ones_row,
                cos128=cos128, sin128=sin128, R128=R128,
                bones=bones.reshape(128, 16 * 128).copy())


def ln_stats_emit(st, ct, x_fm, T, nch=NCH):
    blk = st.block
    PS_A, PS_B = ct["PS_A"], ct["PS_B"]
    NT = T // 512
    t0s, t0t = st.cnt["s"], st.cnt["t"]

    single = ct.get("sq_single", False)

    def _sq(s):
        st.fence_eng(s, skip=("s",))
        for c in range(nch):
            if (c >= 1 and single) or c >= 2:
                st.w(s, "t", t0t + (c if single else c - 1))
            st.tag(s.activation(out=ct["sq%d" % (c % 2)][:, 0:T],
                                in_=x_fm[:, c * T:(c + 1) * T], func=AFT.Square), "s")
    blk.scalar(_sq)

    def _mm(t):
        st.fence_eng(t, skip=("t", "s"))
        for c in range(nch):
            for k in range(NT):
                t.matmul(out=PS_A[0:1, k * 512:(k + 1) * 512],
                         lhsT=ct["ones_col"][:, 0:1],
                         rhs=x_fm[:, c * T + k * 512:c * T + (k + 1) * 512],
                         start=(c == 0), stop=(c == nch - 1))
            st.w(t, "s", t0s + c + 1)
            for k in range(NT):
                ins = t.matmul(out=PS_B[0:1, k * 512:(k + 1) * 512],
                               lhsT=ct["ones_col"][:, 0:1],
                               rhs=ct["sq%d" % (c % 2)][:, k * 512:(k + 1) * 512],
                               start=(c == 0), stop=(c == nch - 1))
            st.tag(ins, "t")
    blk.tensor(_mm)

    def _rows_v(v):
        st.fence_eng(v, skip=("v",))
        v.tensor_scalar(out=ct["r_mean"][0:1, 0:T], in0=PS_A[0:1, 0:T],
                        scalar1=1.0 / D, scalar2=None, op0=ALU.mult)
        v.tensor_scalar(out=ct["r_tmp"][0:1, 0:T], in0=PS_B[0:1, 0:T],
                        scalar1=1.0 / D, scalar2=None, op0=ALU.mult)
        v.tensor_mul(out=ct["r_rstd"][0:1, 0:T], in0=ct["r_mean"][0:1, 0:T],
                     in1=ct["r_mean"][0:1, 0:T])
        st.tag(v.tensor_sub(out=ct["r_tmp"][0:1, 0:T], in0=ct["r_tmp"][0:1, 0:T],
                            in1=ct["r_rstd"][0:1, 0:T]), "v")
    blk.vector(_rows_v)

    def _rows_s(s):
        st.fence_eng(s, skip=("s",))
        st.tag(s.activation(out=ct["r_tmp"][0:1, 0:T], in_=ct["r_tmp"][0:1, 0:T],
                            func=AFT.Sqrt, bias=EPS), "s")
    blk.scalar(_rows_s)

    def _rows_v2(v):
        st.fence_eng(v, skip=("v",))
        st.tag(v.reciprocal(out=ct["r_rstd"][0:1, 0:T], in_=ct["r_tmp"][0:1, 0:T]), "v")
    blk.vector(_rows_v2)


def bcast_rows_emit(st, ct, row_a, row_b, T):
    NT = T // 512

    def _bcast(t):
        st.fence_eng(t, skip=("t",))
        for k in range(NT):
            t.matmul(out=ct["PS_A"][:, k * 512:(k + 1) * 512], lhsT=ct["ones_row"][0:1, :],
                     rhs=row_a[0:1, k * 512:(k + 1) * 512], start=True, stop=True)
        for k in range(NT):
            ins = t.matmul(out=ct["PS_B"][:, k * 512:(k + 1) * 512], lhsT=ct["ones_row"][0:1, :],
                           rhs=row_b[0:1, k * 512:(k + 1) * 512], start=True, stop=True)
        st.tag(ins, "t")
    st.block.tensor(_bcast)


def ln_emit(st, ct, x_fm, T, dst_fm, nch=NCH):
    ln_stats_emit(st, ct, x_fm, T, nch)
    fence(st)
    bcast_rows_emit(st, ct, ct["r_mean"], ct["r_rstd"], T)

    def _norm(v):
        st.fence_eng(v, skip=("v",))
        for c in range(nch):
            v.tensor_sub(out=ct["tmpa"][:, 0:T], in0=x_fm[:, c * T:(c + 1) * T],
                         in1=ct["PS_A"][:, 0:T])
            ins = v.tensor_mul(out=dst_fm[:, c * T:(c + 1) * T],
                               in0=ct["tmpa"][:, 0:T], in1=ct["PS_B"][:, 0:T])
        st.tag(ins, "v")
    st.block.vector(_norm)


def ln23_emit(st, ct, x_fm, T, g_sb, b_sb, per_chunk_out, chunk_pre_wait=None):
    ln_stats_emit(st, ct, x_fm, T)
    fence(st)

    def _rows_v(v):
        st.fence_eng(v, skip=("v",))
        st.tag(v.tensor_mul(out=ct["r_tmp"][0:1, 0:T], in0=ct["r_rstd"][0:1, 0:T],
                            in1=ct["r_rstd"][0:1, 0:T]), "v")
    st.block.vector(_rows_v)

    def _rows_v1b(v):
        st.fence_eng(v, skip=("v",))
        st.tag(v.tensor_scalar(out=ct["r_tmp"][0:1, 0:T], in0=ct["r_tmp"][0:1, 0:T],
                               scalar1=-EPS, scalar2=1.0, op0=ALU.mult,
                               op1=ALU.add), "v")
    st.block.vector(_rows_v1b)

    def _rows_s(s):
        st.fence_eng(s, skip=("s",))
        st.tag(s.activation(out=ct["r_tmp"][0:1, 0:T], in_=ct["r_tmp"][0:1, 0:T],
                            func=AFT.Sqrt, bias=EPS), "s")
    st.block.scalar(_rows_s)

    def _rows_v2(v):
        st.fence_eng(v, skip=("v",))
        v.reciprocal(out=ct["r_tmp"][0:1, 0:T], in_=ct["r_tmp"][0:1, 0:T])
        v.tensor_mul(out=ct["r_tmp"][0:1, 0:T], in0=ct["r_rstd"][0:1, 0:T],
                     in1=ct["r_tmp"][0:1, 0:T])
        st.tag(v.tensor_mul(out=ct["r_mean"][0:1, 0:T], in0=ct["r_mean"][0:1, 0:T],
                            in1=ct["r_tmp"][0:1, 0:T]), "v")
    st.block.vector(_rows_v2)
    fence(st)
    bcast_rows_emit(st, ct, ct["r_tmp"], ct["r_mean"], T)

    def _norm(v):
        st.fence_eng(v, skip=("v",))
        for c in range(NCH):
            if chunk_pre_wait is not None:
                chunk_pre_wait(v, c)
            v.tensor_mul(out=ct["tmpa"][:, 0:T], in0=x_fm[:, c * T:(c + 1) * T],
                         in1=ct["PS_A"][:, 0:T])
            v.tensor_sub(out=ct["tmpb"][:, 0:T], in0=ct["tmpa"][:, 0:T],
                         in1=ct["PS_B"][:, 0:T])
            st.tag(v.tensor_scalar(out=per_chunk_out(c), in0=ct["tmpb"][:, 0:T],
                                   scalar1=g_sb[:, c:c + 1], scalar2=b_sb[:, c:c + 1],
                                   op0=ALU.mult, op1=ALU.add), "v")
    st.block.vector(_norm)


def gates_emit(st, ct, x_fm, T, wg_f, rb_row, gt_dram):
    blk = st.block
    PS_A, PS_B = ct["PS_A"], ct["PS_B"]
    NT = T // 512
    NK = T // 128

    def _mm(t):
        st.fence_eng(t, skip=("t",))
        for k in range(NT):
            for c in range(NCH):
                ins = t.matmul(out=PS_A[0:8, k * 512:(k + 1) * 512],
                               lhsT=wg_f[:, c * 8:(c + 1) * 8],
                               rhs=x_fm[:, c * T + k * 512:c * T + (k + 1) * 512],
                               start=(c == 0), stop=(c == NCH - 1))
        for c in range(NCH):
            ins2 = t.matmul(out=PS_B[0:8, 512:513], lhsT=wg_f[:, c * 8:(c + 1) * 8],
                            rhs=ct["ones_col"][:, 0:1], start=(c == 0), stop=(c == NCH - 1))
        st.tag(ins, "t")
        st.tag(ins2, "t")
    blk.tensor(_mm)

    def _vneg(v):
        st.fence_eng(v, skip=("v",))
        st.tag(v.tensor_scalar(out=ct["r_tmp"][0:1, 0:T], in0=ct["r_mean"][0:1, 0:T],
                               scalar1=-1.0, scalar2=None, op0=ALU.mult), "v")
    blk.vector(_vneg)

    def _evac(s):
        st.fence_eng(s, skip=("s",))
        s.copy(out=ct["lsb"][0:8, 0:T], in_=PS_A[0:8, 0:T])
        st.tag(s.copy(out=ct["w1"][0:8, 0:1], in_=PS_B[0:8, 512:513]), "s")
    blk.scalar(_evac)
    fence(st, "t")

    def _tp(t):
        for k in range(NK):
            t.matmul(out=PS_B[:, k * 8:(k + 1) * 8],
                     lhsT=ct["lsb"][0:8, k * 128:(k + 1) * 128],
                     rhs=ct["ident"][0:8, 0:8], is_transpose=True,
                     start=(k == 0), stop=(k == NK - 1))
        t.matmul(out=PS_B[0:1, 520:528], lhsT=ct["w1"][0:8, 0:1],
                 rhs=ct["ident"][0:8, 0:8], is_transpose=True,
                 start=False, stop=True)
        for k in range(NK):
            t.matmul(out=PS_B[:, 1024 + k:1024 + k + 1],
                     lhsT=ct["r_tmp"][0:1, k * 128:(k + 1) * 128],
                     rhs=ct["ident"][0:1, 0:1], is_transpose=True,
                     start=(k == 0), stop=(k == NK - 1))
        for k in range(NK):
            ins = t.matmul(out=PS_B[:, 1536 + k:1536 + k + 1],
                           lhsT=ct["r_rstd"][0:1, k * 128:(k + 1) * 128],
                           rhs=ct["ident"][0:1, 0:1], is_transpose=True,
                           start=(k == 0), stop=(k == NK - 1))
        st.tag(ins, "t")
    blk.tensor(_tp)

    def _evac2(s):
        st.fence_eng(s, skip=("s",))
        s.copy(out=ct["tokg"][:, 0:NK * 8], in_=PS_B[:, 0:NK * 8])
        s.copy(out=ct["r_tmp"][0:1, 8:16], in_=PS_B[0:1, 520:528])
        s.copy(out=ct["mtok"][:, 0:NK], in_=PS_B[:, 1024:1024 + NK])
        st.tag(s.copy(out=ct["rtok"][:, 0:NK], in_=PS_B[:, 1536:1536 + NK]), "s")
    blk.scalar(_evac2)
    fence(st, "t")

    def _sbc(t):
        t.matmul(out=PS_B[:, 256:264], lhsT=ct["ones_row"][0:1, :],
                 rhs=ct["r_tmp"][0:1, 8:16], start=False, stop=True)
        if rb_row is not None:
            ins = t.matmul(out=PS_B[:, 272:280], lhsT=ct["ones_row"][0:1, :],
                           rhs=rb_row[0:1, 0:8], start=False, stop=True)
        else:
            ins = t.matmul(out=PS_B[:, 272:280], lhsT=ct["ones_row"][0:1, :],
                           rhs=ct["r_tmp"][0:1, 8:16], start=False, stop=True)
        st.tag(ins, "t")
    blk.tensor(_sbc)

    def _evac3(s):
        st.fence_eng(s, skip=("s",))
        s.copy(out=ct["sbc"][:, 0:8], in_=PS_B[:, 256:264])
        st.tag(s.copy(out=ct["rbc"][:, 0:8], in_=PS_B[:, 272:280]), "s")
    blk.scalar(_evac3)
    fence(st, "v")

    def _logits(v):
        def step(ins, _st=st, _v=v):
            _st.tag(ins, "v")
            _st.w(_v, "v")
        for k in range(NK):
            tg = ct["tokg"][:, k * 8:(k + 1) * 8]
            step(v.scalar_tensor_tensor(out=ct["w8a"][:, 0:8], in0=ct["sbc"][:, 0:8],
                                        scalar=ct["mtok"][:, k:k + 1], in1=tg,
                                        op0=ALU.mult, op1=ALU.add))
            if rb_row is not None:
                step(v.tensor_scalar(out=ct["w8a"][:, 0:8], in0=ct["w8a"][:, 0:8],
                                     scalar1=ct["rtok"][:, k:k + 1], scalar2=None,
                                     op0=ALU.mult))
                step(v.tensor_add(out=tg, in0=ct["w8a"][:, 0:8], in1=ct["rbc"][:, 0:8]))
            else:
                step(v.tensor_scalar(out=tg, in0=ct["w8a"][:, 0:8],
                                     scalar1=ct["rtok"][:, k:k + 1], scalar2=None,
                                     op0=ALU.mult))
    blk.vector(_logits)

    def _v1(v):
        st.fence_eng(v, skip=("v",))
        for k in range(NK):
            ins = v.reduce_max(out=ct["nm1s"][:, k:k + 1],
                               in_=ct["tokg"][:, k * 8:(k + 1) * 8],
                               axis=AX.X, negate=True)
        st.tag(ins, "v")
    blk.vector(_v1)

    def _s1(s):
        st.fence_eng(s, skip=("s",))
        for k in range(NK):
            ins = s.activation(out=ct["e8"][:, k * 8:(k + 1) * 8],
                               in_=ct["tokg"][:, k * 8:(k + 1) * 8],
                               func=AFT.Exp, bias=ct["nm1s"][:, k:k + 1])
        st.tag(ins, "s")
    blk.scalar(_s1)

    def _v2(v):
        st.fence_eng(v, skip=("v",))

        def step(ins, _st=st, _v=v):
            _st.tag(ins, "v")
            _st.w(_v, "v")
        for k in range(NK):
            g8 = ct["tokg"][:, k * 8:(k + 1) * 8]
            e8 = ct["e8"][:, k * 8:(k + 1) * 8]
            step(v.tensor_scalar(out=ct["w8a"][:, 0:8], in0=g8,
                                 scalar1=ct["nm1s"][:, k:k + 1], scalar2=None, op0=ALU.add))
            step(v.tensor_scalar(out=ct["w8b"][:, 0:8], in0=ct["w8a"][:, 0:8],
                                 scalar1=0.0, scalar2=None, op0=ALU.is_lt))
            step(v.tensor_mul(out=ct["w8a"][:, 0:8], in0=e8, in1=ct["w8b"][:, 0:8]))
            step(v.reduce_max(out=ct["w1"][:, 0:1], in_=ct["w8a"][:, 0:8], axis=AX.X))
            step(v.reduce_max(out=ct["w2"][:, 0:1], in_=e8, axis=AX.X))
            step(v.tensor_add(out=ct["w3"][:, 0:1], in0=ct["w1"][:, 0:1], in1=ct["w2"][:, 0:1]))
            step(v.reciprocal(out=ct["w3"][:, 0:1], in_=ct["w3"][:, 0:1]))
            step(v.tensor_scalar(out=ct["w8c"][:, 0:8], in0=ct["w8a"][:, 0:8],
                                 scalar1=ct["w1"][:, 0:1], scalar2=None, op0=ALU.is_ge))
            step(v.tensor_scalar(out=ct["w8b"][:, 0:8], in0=ct["w8b"][:, 0:8],
                                 scalar1=-1.0, scalar2=1.0, op0=ALU.mult, op1=ALU.add))
            step(v.tensor_add(out=ct["w8c"][:, 0:8], in0=ct["w8c"][:, 0:8], in1=ct["w8b"][:, 0:8]))
            step(v.tensor_mul(out=ct["w8c"][:, 0:8], in0=e8, in1=ct["w8c"][:, 0:8]))
            step(v.tensor_scalar(out=ct["tokg"][:, k * 8:(k + 1) * 8],
                                 in0=ct["w8c"][:, 0:8], scalar1=ct["w3"][:, 0:1],
                                 scalar2=0.5, op0=ALU.mult, op1=ALU.mult))
    blk.vector(_v2)
    fence(st, "t")

    def _tp2(t):
        for k in range(NK):
            ins = t.matmul(out=PS_B[0:8, k * 128:(k + 1) * 128],
                           lhsT=ct["tokg"][:, k * 8:(k + 1) * 8],
                           rhs=ct["ident"][:, :], is_transpose=True,
                           start=(k % 4 == 0), stop=(k % 4 == 3))
        st.tag(ins, "t")
    blk.tensor(_tp2)

    def _evac4(s):
        st.fence_eng(s, skip=("s",))
        st.tag(s.copy(out=ct["lsb_bf"][0:8, 0:T], in_=PS_B[0:8, 0:T]), "s")
    blk.scalar(_evac4)

    def _dma(q):
        st.fence_eng(q)
        st.tag(q.dma_start(out=gt_dram, in_=ct["lsb_bf"][0:8, 0:T]), "ds", 16)
    blk.sync(_dma)


def moe_l0_emit(st, ct, x_fm, xn_fm, T, we_dram, be_dram, gt_dram):
    blk = st.block
    PS_A, PS_B = ct["PS_A"], ct["PS_B"]
    NT = T // 512
    wbuf, wstage = ct["wbuf"], ct["wstage"]
    gtrb, beb, bebf = ct["gtrb"], ct["beb"], ct["bebf"]

    tA, vA, pA, sA = st.cnt["t"], st.cnt["v"], st.cnt["p"], st.cnt["s"]
    pA0 = pA
    gb_t, mm_t, xg_v, evac_v, cast_p, gbe_s = {}, {}, {}, {}, {}, {}
    becast_v = {}
    for e in range(NE):
        tA += 1
        gb_t[e] = tA
        sA += 1
        gbe_s[e] = sA
        vA += 1
        becast_v[e] = vA
        for dh in range(2):
            pA += 4
            cast_p[(e, dh)] = pA
            for tt in range(NT):
                vA += 1
                xg_v[(e, dh, tt)] = vA
                tA += 1
                mm_t[(e, dh, tt)] = tA
                vA += 1
                evac_v[(e, dh, tt)] = vA
    dl_w, dl_g = {}, {}

    def _loads(q):
        st.fence_eng(q, skip=("dl",))
        for e in range(NE):
            if e >= 1:
                st.w(q, "t", mm_t[(e - 1, 1, NT - 1)])
                st.w(q, "v", becast_v[e - 1])
            gs = "ge%d" % (e % 2)
            st.tag(q.dma_start(out=gtrb[e % 2][0:1, 0:T],
                               in_=AP(gt_dram.tensor, gt_dram.offset + e * T,
                                      [[1, 1], [1, T]])), gs, 16)
            st.tag(q.dma_start(out=beb[e % 2][0:1, 0:D],
                               in_=AP(be_dram.tensor, be_dram.offset + e * D,
                                      [[1, 1], [1, D]])), gs, 16)
            dl_g[e] = (gs, st.cnt[gs])
            for dh in range(2):
                for pc in range(4):
                    gidx = (2 * e + dh) * 4 + pc
                    if gidx >= 2:
                        st.w(q, "p", pA0 + gidx - 1)
                    ws = "wl%d" % (gidx % 2)
                    src = AP(we_dram.tensor,
                             we_dram.offset + e * D * D + dh * 512 + 2 * pc * 128 * D,
                             [[D, 128], [128 * D, 2], [128, 4], [1, 128]])
                    st.tag(q.dma_start(out=wstage[gidx % 2].ap().rearrange(
                        "p (kc dc j) -> p kc dc j", dc=4, j=128), in_=src), ws, 16)
                    dl_w[(e, dh, pc)] = (ws, st.cnt[ws])
    blk.sync(_loads)

    for e in range(NE):
        def _becast(v, e=e):
            st.w(v, *dl_g[e])
            if e >= 2:
                st.w(v, "t", mm_t[(e - 2, 1, NT - 1)])
            st.tag(v.tensor_copy(out=bebf[e % 2][0:1, 0:D], in_=beb[e % 2][0:1, 0:D]), "v")
            assert st.cnt["v"] == becast_v[e]
        blk.vector(_becast)

        def _gb(t, e=e):
            st.w(t, *dl_g[e])
            if e >= 1:
                st.w(t, "s", gbe_s[e - 1])
            for k in range(NT):
                ins = t.matmul(out=PS_B[:, k * 512:(k + 1) * 512],
                               lhsT=ct["ones_row_b"][0:1, :],
                               rhs=gtrb[e % 2][0:1, k * 512:(k + 1) * 512],
                               start=True, stop=True)
            st.tag(ins, "t")
            assert st.cnt["t"] == gb_t[e]
        blk.tensor(_gb)

        def _gbe(s, e=e):
            st.w(s, "t", gb_t[e])
            if e >= 1:
                st.w(s, "v", xg_v[(e - 1, 1, NT - 1)])
            st.tag(s.copy(out=ct["gbb"][:, 0:T], in_=PS_B[:, 0:T]), "s")
            assert st.cnt["s"] == gbe_s[e]
        blk.scalar(_gbe)

        for dh in range(2):
            def _cast(p, e=e, dh=dh):
                if e + dh >= 1:
                    pk = (e, 0) if dh == 1 else (e - 1, 1)
                    st.w(p, "t", mm_t[(pk[0], pk[1], NT - 1)])
                for pc in range(4):
                    st.w(p, *dl_w[(e, dh, pc)])
                    st.tag(p.tensor_copy(
                        out=wbuf[(2 * e + dh) % 2][:, pc * 1024:(pc + 1) * 1024],
                        in_=wstage[((2 * e + dh) * 4 + pc) % 2][:, 0:1024]), "p")
                assert st.cnt["p"] == cast_p[(e, dh)]
            blk.gpsimd(_cast)

            for tt in range(NT):
                def _xg(v, e=e, dh=dh, tt=tt):
                    st.w(v, "s", gbe_s[e])
                    key = (e, dh, tt)
                    prev = None
                    if tt > 0:
                        prev = (e, dh, tt - 1)
                    elif dh == 1:
                        prev = (e, 0, NT - 1)
                    elif e > 0:
                        prev = (e - 1, 1, NT - 1)
                    if prev is not None:
                        st.w(v, "t", mm_t[prev])
                    for kc in range(NCH):
                        ins = v.tensor_mul(out=ct["xg"][:, kc * 512:(kc + 1) * 512],
                                           in0=xn_fm[:, kc * T + tt * 512:kc * T + (tt + 1) * 512],
                                           in1=ct["gbb"][:, tt * 512:(tt + 1) * 512])
                    st.tag(ins, "v")
                    assert st.cnt["v"] == xg_v[(e, dh, tt)]
                blk.vector(_xg)

                def _mm(t, e=e, dh=dh, tt=tt):
                    st.w(t, "p", cast_p[(e, dh)])
                    st.w(t, "v", xg_v[(e, dh, tt)])
                    prev = None
                    if tt > 0:
                        prev = (e, dh, tt - 1)
                    elif dh == 1:
                        prev = (e, 0, NT - 1)
                    elif e > 0:
                        prev = (e - 1, 1, NT - 1)
                    if prev is not None:
                        st.w(t, "v", evac_v[prev])
                    for dc in range(4):
                        for kc in range(NCH):
                            t.matmul(out=PS_A[:, dc * 512:(dc + 1) * 512],
                                     lhsT=wbuf[(2 * e + dh) % 2][:, (kc * 4 + dc) * 128:(kc * 4 + dc + 1) * 128],
                                     rhs=ct["xg"][:, kc * 512:(kc + 1) * 512],
                                     start=(kc == 0), stop=False)
                        ins = t.matmul(out=PS_A[:, dc * 512:(dc + 1) * 512],
                                       lhsT=bebf[e % 2][0:1, (dh * 4 + dc) * 128:(dh * 4 + dc + 1) * 128],
                                       rhs=gtrb[e % 2][0:1, tt * 512:(tt + 1) * 512],
                                       start=False, stop=True)
                    st.tag(ins, "t")
                    assert st.cnt["t"] == mm_t[(e, dh, tt)]
                blk.tensor(_mm)

                def _evac(v, e=e, dh=dh, tt=tt):
                    st.w(v, "t", mm_t[(e, dh, tt)])
                    xv = x_fm.ap().rearrange("p (c u) -> p c u", u=T)[
                        :, dh * 4:(dh + 1) * 4, tt * 512:(tt + 1) * 512]
                    pv = PS_A.ap().rearrange("p (dc u) -> p dc u", u=512)
                    st.tag(v.tensor_add(out=xv, in0=pv, in1=xv), "v")
                    assert st.cnt["v"] == evac_v[(e, dh, tt)]
                blk.vector(_evac)


def build_launch_A(dbg=False):
    nc = bass.Bass("TRN2", target_bir_lowering=False, debug=False, **BASS_KW)
    T = TOK
    x_in = nc.dram_tensor("x_in", [D, T], F32, kind="ExternalInput").ap()
    wg_in = nc.dram_tensor("wg_in", [D, NE], F32, kind="ExternalInput").ap()
    we_in = nc.dram_tensor("we_in", [NE, D, D], F32, kind="ExternalInput").ap()
    be_in = nc.dram_tensor("be_in", [NE, D], F32, kind="ExternalInput").ap()
    g3_in = nc.dram_tensor("g3_in", [8, 128], F32, kind="ExternalInput").ap()
    b3_in = nc.dram_tensor("b3_in", [8, 128], F32, kind="ExternalInput").ap()
    x1_out = nc.dram_tensor("x1_out", [D, T], F32, kind="ExternalOutput").ap()
    if dbg:
        xn1_out = nc.dram_tensor("xn1_out", [D, T], BF16, kind="ExternalOutput").ap()
        tokg_out = nc.dram_tensor("tokg_out", [128, (T // 128) * 8], F32, kind="ExternalOutput").ap()
        e8_out = nc.dram_tensor("e8_out", [128, (T // 128) * 8], F32, kind="ExternalOutput").ap()
        gt_out = nc.dram_tensor("gt_out", [NE, T], BF16, kind="ExternalOutput").ap()
    xn2_out = nc.dram_tensor("xn2_out", [D, T], BF16, kind="ExternalOutput").ap()
    gt_dram = nc.dram_tensor("gt_scr", [NE, T], BF16, kind="Internal").ap()

    ct = {}
    x_fm = nc.alloc_sbuf_tensor("x_fm", [128, NCH * T], F32)
    xn1 = nc.alloc_sbuf_tensor("xn1", [128, NCH * T], BF16)
    ct["wstage"] = [nc.alloc_sbuf_tensor("wstage%d" % i, [128, 1024], F32) for i in range(2)]
    ct["wbuf"] = [nc.alloc_sbuf_tensor("wbuf%d" % i, [128, 4096], BF16) for i in range(2)]
    ct["xg"] = nc.alloc_sbuf_tensor("xg", [128, NCH * 512], BF16)
    ct["gbb"] = nc.alloc_sbuf_tensor("gbb", [128, T], BF16)
    gtrb1 = nc.alloc_sbuf_tensor("gtrb", [1, T], BF16)
    ct["gtrb"] = [gtrb1, gtrb1]
    beb1 = nc.alloc_sbuf_tensor("beb", [1, D], F32)
    ct["beb"] = [beb1, beb1]
    ct["bebf"] = [nc.alloc_sbuf_tensor("bebf%d" % i, [1, D], BF16) for i in range(2)]
    wg_f = nc.alloc_sbuf_tensor("wg_f", [128, NCH * 8], F32)
    g3_sb = nc.alloc_sbuf_tensor("g3_sb", [128, 8], F32)
    b3_sb = nc.alloc_sbuf_tensor("b3_sb", [128, 8], F32)
    ct["PS_A"] = nc.alloc_psum_tensor("PS_A", [128, 2048], F32)
    ct["PS_B"] = nc.alloc_psum_tensor("PS_B", [128, 2048], F32)
    ct["tmpa"] = nc.alloc_sbuf_tensor("tmpa", [128, T], F32)
    ct["tmpb"] = nc.alloc_sbuf_tensor("tmpb", [128, T], F32)
    ct["sq0"] = ct["tmpa"].ap()
    ct["sq1"] = ct["tmpb"].ap()
    ct["lsb"] = ct["tmpa"].ap()
    ct["lsb_bf"] = ct["tmpb"].ap().bitcast(BF16)
    for nm, shp, dt in [("tokg", [128, (T // 128) * 8], F32),
                        ("e8", [128, (T // 128) * 8], F32),
                        ("nm1s", [128, T // 128], F32),
                        ("w1", [128, 1], F32), ("w2", [128, 1], F32),
                        ("mtok", [128, 16], F32), ("rtok", [128, 16], F32),
                        ("sbc", [128, 8], F32), ("rbc", [128, 8], F32),
                        ("w3", [128, 1], F32), ("w8a", [128, 8], F32),
                        ("w8b", [128, 8], F32), ("w8c", [128, 8], F32),
                        ("r_mean", [1, T], F32), ("r_tmp", [1, T], F32),
                        ("r_rstd", [1, T], F32)]:
        ct[nm] = nc.alloc_sbuf_tensor(nm, shp, dt)

    extra = ["wl0", "wl1", "ge0", "ge1"]
    with (nc.semaphore("sem_t") as s_t, nc.semaphore("sem_v") as s_v,
          nc.semaphore("sem_s") as s_s, nc.semaphore("sem_p") as s_p,
          nc.semaphore("sem_dl") as s_dl, nc.semaphore("sem_ds") as s_ds,
          nc.Block() as blk):
        sems = dict(t=s_t, v=s_v, s=s_s, p=s_p, dl=s_dl, ds=s_ds)
        for nm in extra:
            sems[nm] = nc.alloc_semaphore(name="sem_" + nm)
        st = St(nc, blk, sems)
        cins = declare_common(nc, ct)

        def _l0(q):
            for k, (ap_, tname) in cins.items():
                st.tag(q.dma_start(out=ct[tname][:], in_=ap_), "dl", 16)
            st.tag(q.dma_start(out=x_fm.ap().rearrange("p (c u) -> p c u", u=T),
                               in_=fm_in_ap(x_in, NCH, T)), "dl", 16)
            src = AP(wg_in.tensor, 0, [[NE, 128], [NE * 128, NCH], [1, NE]])
            st.tag(q.dma_start(out=wg_f.ap().rearrange("p (c e) -> p c e", e=NE),
                               in_=src), "dl", 16)
            st.tag(q.dma_start(out=ct["lsb"][0:8, 0:128], in_=g3_in), "dl", 16)
            st.tag(q.dma_start(out=ct["lsb"][0:8, 128:256], in_=b3_in), "dl", 16)
        blk.sync(_l0)
        fence(st, "tv")

        def _tgb(t):
            t.transpose(out=ct["PS_A"][:, 0:8], in_=ct["lsb"][0:8, 0:128],
                        identity=ct["ident"][0:8, 0:8])
            st.tag(t.transpose(out=ct["PS_A"][:, 512:520], in_=ct["lsb"][0:8, 128:256],
                               identity=ct["ident"][0:8, 0:8]), "t")
        blk.tensor(_tgb)

        def _egb(s):
            st.fence_eng(s, skip=("s",))
            s.copy(out=g3_sb[:, 0:8], in_=ct["PS_A"][:, 0:8])
            st.tag(s.copy(out=b3_sb[:, 0:8], in_=ct["PS_A"][:, 512:520]), "s")
        blk.scalar(_egb)
        fence(st)

        ln_emit(st, ct, x_fm, T, xn1)
        fence(st)
        gates_emit(st, ct, x_fm, T, wg_f, None, gt_dram)
        fence(st)
        if dbg:
            def _dbg(q):
                st.fence_eng(q)
                st.tag(q.dma_start(out=fm_in_ap(xn1_out, NCH, T),
                                   in_=xn1.ap().rearrange("p (c u) -> p c u", u=T)), "ds", 16)
                st.tag(q.dma_start(out=gt_out, in_=ct["lsb_bf"][0:8, 0:T]), "ds", 16)
                st.tag(q.dma_start(out=tokg_out, in_=ct["tokg"][:, :]), "ds", 16)
                st.tag(q.dma_start(out=e8_out, in_=ct["e8"][:, :]), "ds", 16)
            blk.sync(_dbg)
            fence(st, "q")
        moe_l0_emit(st, ct, x_fm, xn1, T, we_in, be_in, gt_dram)
        fence(st)

        def _sx1(q):
            st.fence_eng(q)
            st.tag(q.dma_start(out=fm_in_ap(x1_out, NCH, T),
                               in_=x_fm.ap().rearrange("p (c u) -> p c u", u=T)), "ds", 16)
        blk.sync(_sx1)

        xgbf = ct["xg"]
        ds_base = st.cnt["ds"]

        def _pre(v, c, _st=st, _ds0=ds_base):
            if c >= 2:
                _st.w(v, "ds", _ds0 + 16 * (c - 1))
        ln23_emit(st, ct, x_fm, T, g3_sb, b3_sb,
                  lambda c: ct["gbb"][:, 0:T] if (c % 2 == 0) else xgbf.ap().bitcast(BF16)[:, 0:T],
                  chunk_pre_wait=_pre)
        v_norm0 = st.cnt["v"] - NCH

        def _sxn2(q):
            st.fence_eng(q, skip=("v", "ds"))
            for c in range(NCH):
                st.w(q, "v", v_norm0 + c + 1)
                buf = ct["gbb"][:, 0:T] if (c % 2 == 0) else xgbf.ap().bitcast(BF16)[:, 0:T]
                st.tag(q.dma_start(
                    out=AP(xn2_out.tensor, c * 128 * T, [[T, 128], [1, T]]),
                    in_=buf), "ds", 16)
            st.w(q, "ds")
        blk.sync(_sxn2)
    return nc


def inmaps_launch_A(inputs, consts):
    x = np.asarray(inputs["L0_attn_x"])
    wg = np.ascontiguousarray(inputs["L0_ffn_Wg"])
    we = np.ascontiguousarray(inputs["L0_ffn_We"])
    be = np.ascontiguousarray(inputs["L0_ffn_be"])
    g3 = np.ascontiguousarray(inputs["L0_attn_ln_gamma"].reshape(8, 128))
    b3 = np.ascontiguousarray(inputs["L0_attn_ln_beta"].reshape(8, 128))
    maps = []
    for c in range(8):
        b, h = c // 2, c % 2
        m = common_inmap(consts)
        m["x_in"] = np.ascontiguousarray(x[b, h * TOK:(h + 1) * TOK, :].T)
        m["wg_in"] = wg
        m["we_in"] = we
        m["be_in"] = be
        m["g3_in"] = g3
        m["b3_in"] = b3
        maps.append(m)
    return maps


def kernel(**inputs):
    consts = host_consts()
    ncA = build_launch_A()
    resA = run_bass_kernel_spmd(ncA, inmaps_launch_A(inputs, consts),
                                core_ids=list(range(8)))
    raise NotImplementedError("launches B-E pending")


def build_launch_B():
    nc = bass.Bass("TRN2", target_bir_lowering=False, debug=False, **BASS_KW)
    T = S
    NT = T // 512
    xn2_in = nc.dram_tensor("xn2_in", [D, T], BF16, kind="ExternalInput").ap()
    win_in = nc.dram_tensor("win_in", [D, DI], F32, kind="ExternalInput").ap()
    bin_in = nc.dram_tensor("bin_in", [16, 128], F32, kind="ExternalInput").ap()
    convw_in = nc.dram_tensor("convw_in", [4, DIH], F32, kind="ExternalInput").ap()
    convb_in = nc.dram_tensor("convb_in", [8, 128], F32, kind="ExternalInput").ap()
    wx_in = nc.dram_tensor("wx_in", [DIH, 96], F32, kind="ExternalInput").ap()
    u_out = nc.dram_tensor("u_out", [DIH, T], F32, kind="ExternalOutput").ap()
    res_out = nc.dram_tensor("res_out", [DIH, T], F32, kind="ExternalOutput").ap()
    xdbl_out = nc.dram_tensor("xdbl_out", [96, T], F32, kind="ExternalOutput").ap()

    ct = {}
    xn2 = nc.alloc_sbuf_tensor("xn2", [128, NCH * T], BF16)
    ubf = nc.alloc_sbuf_tensor("ubf", [128, 8 * T], BF16)
    wstage = [nc.alloc_sbuf_tensor("wstg%d" % i, [128, 1024], F32) for i in range(2)]
    wbuf = [nc.alloc_sbuf_tensor("wbufb%d" % i, [128, 1024], BF16) for i in range(2)]
    wxf = nc.alloc_sbuf_tensor("wxf", [128, 8 * 96], F32)
    wxb = nc.alloc_sbuf_tensor("wxb", [128, 8 * 96], BF16)
    upad = nc.alloc_sbuf_tensor("upad", [128, 515], F32)
    uconv = nc.alloc_sbuf_tensor("uconv", [128, 512], F32)
    ustore = [nc.alloc_sbuf_tensor("ustore%d" % i, [128, 512], F32) for i in range(2)]
    rstore = [nc.alloc_sbuf_tensor("rstore%d" % i, [128, 512], F32) for i in range(2)]
    xdst = [nc.alloc_sbuf_tensor("xdst%d" % i, [96, 512], F32) for i in range(2)]
    prev3 = nc.alloc_sbuf_tensor("prev3", [128, 3], F32)
    cw_sb = nc.alloc_sbuf_tensor("cw_sb", [128, 8 * 4], F32)
    cb_sb = nc.alloc_sbuf_tensor("cb_sb", [128, 8], F32)
    bin_sb = nc.alloc_sbuf_tensor("bin_sb", [128, 16], F32)
    ct["PS_A"] = nc.alloc_psum_tensor("PS_A", [128, 2048], F32)
    ct["PS_B"] = nc.alloc_psum_tensor("PS_B", [128, 2048], F32)

    extra = ["wl0", "wl1", "us0", "us1", "rs0", "rs1", "xs0", "xs1"]
    with (nc.semaphore("sem_t") as s_t, nc.semaphore("sem_v") as s_v,
          nc.semaphore("sem_s") as s_s, nc.semaphore("sem_p") as s_p,
          nc.semaphore("sem_dl") as s_dl, nc.semaphore("sem_ds") as s_ds,
          nc.Block() as blk):
        sems = dict(t=s_t, v=s_v, s=s_s, p=s_p, dl=s_dl, ds=s_ds)
        for nm in extra:
            sems[nm] = nc.alloc_semaphore(name="sem_" + nm)
        st = St(nc, blk, sems)
        cins = declare_common(nc, ct)

        def _l0(q):
            for k, (ap_, tname) in cins.items():
                st.tag(q.dma_start(out=ct[tname][:], in_=ap_), "dl", 16)
            st.tag(q.dma_start(out=xn2.ap().rearrange("p (c u) -> p c u", u=T),
                               in_=fm_in_ap(xn2_in, NCH, T)), "dl", 16)
            st.tag(q.dma_start(out=wxf.ap().rearrange("p (c u) -> p c u", u=96),
                               in_=AP(wx_in.tensor, 0, [[96, 128], [96 * 128, 8], [1, 96]])),
                   "dl", 16)
            for c in range(8):
                st.tag(q.dma_start(out=cw_sb[:, c * 4:(c + 1) * 4],
                                   in_=AP(convw_in.tensor, c * 128,
                                          [[1, 128], [DIH, 4]],),
                                   allow_slow_non_contiguous=True), "dl", 16)
            st.tag(q.dma_start(out=cb_sb[:, 0:8],
                               in_=AP(convb_in.tensor, 0, [[1, 128], [128, 8]]),
                               allow_slow_non_contiguous=True), "dl", 16)
            st.tag(q.dma_start(out=bin_sb[:, 0:16],
                               in_=AP(bin_in.tensor, 0, [[1, 128], [128, 16]]),
                               allow_slow_non_contiguous=True), "dl", 16)
        blk.sync(_l0)

        def _cwx(p):
            st.fence_eng(p, skip=("p", "ds"))
            st.tag(p.tensor_copy(out=wxb[:, :], in_=wxf[:, :]), "p")
        blk.gpsimd(_cwx)
        fence(st)

        tA, vA, sA, pA = st.cnt["t"], st.cnt["v"], st.cnt["s"], st.cnt["p"]
        dlA = st.cnt["dl"]
        cast_p, mm_t, evac_s, conv_v, silu_s, cast_v = {}, {}, {}, {}, {}, {}
        store_ds = {}
        dl_w = {}
        for oc in range(16):
            pA += 1
            cast_p[oc] = pA
            for tt in range(NT):
                tA += 1
                mm_t[(oc, tt)] = tA
                sA += 1
                evac_s[(oc, tt)] = sA
                if oc < 8:
                    vA += 3
                    conv_v[(oc, tt)] = vA
                    sA += 1
                    silu_s[(oc, tt)] = sA
                    vA += 1
                    cast_v[(oc, tt)] = vA

        def _wload0(q):
            st.fence_eng(q, skip=("dl", "ds"))
            for oc2 in range(2):
                ws = "wl%d" % (oc2 % 2)
                src = AP(win_in.tensor, oc2 * 128,
                         [[DI, 128], [128 * DI, NCH], [1, 128]])
                st.tag(q.dma_start(out=wstage[oc2 % 2].ap().rearrange(
                    "p (kc j) -> p kc j", j=128), in_=src), ws, 16)
                dl_w[oc2] = (ws, st.cnt[ws])
        blk.sync(_wload0)

        for oc in range(16):
            if oc + 2 < 16:
                def _wpre(q, oc2=oc + 2):
                    st.w(q, "p", cast_p[oc2 - 2])
                    ws = "wl%d" % (oc2 % 2)
                    src = AP(win_in.tensor, oc2 * 128,
                             [[DI, 128], [128 * DI, NCH], [1, 128]])
                    st.tag(q.dma_start(out=wstage[oc2 % 2].ap().rearrange(
                        "p (kc j) -> p kc j", j=128), in_=src), ws, 16)
                    dl_w[oc2] = (ws, st.cnt[ws])
                blk.sync(_wpre)
            def _cast(p, oc=oc):
                st.w(p, *dl_w[oc])
                if oc >= 2:
                    st.w(p, "t", mm_t[(oc - 2, NT - 1)])
                st.tag(p.tensor_copy(out=wbuf[oc % 2][:, :], in_=wstage[oc % 2][:, :]), "p")
                assert st.cnt["p"] == cast_p[oc]
            blk.gpsimd(_cast)
            if oc < 8:
                def _z(v, oc=oc):
                    if oc > 0:
                        st.w(v, "s", silu_s[(oc - 1, NT - 1)])
                    v.memset(prev3[:, :], 0.0)
                blk.vector(_z)
            for tt in range(NT):
                def _mm(t, oc=oc, tt=tt):
                    st.w(t, "p", cast_p[oc])
                    bank = (oc * NT + tt) % 4
                    prevkey = None
                    idx = oc * NT + tt
                    if idx >= 4:
                        pidx = idx - 4
                        prevkey = (pidx // NT, pidx % NT)
                    if prevkey is not None:
                        st.w(t, "s", evac_s[prevkey])
                    for kc in range(NCH):
                        ins = t.matmul(out=ct["PS_A"][:, bank * 512:(bank + 1) * 512],
                                       lhsT=wbuf[oc % 2][:, kc * 128:(kc + 1) * 128],
                                       rhs=xn2[:, kc * T + tt * 512:kc * T + (tt + 1) * 512],
                                       start=(kc == 0), stop=(kc == NCH - 1))
                    st.tag(ins, "t")
                    assert st.cnt["t"] == mm_t[(oc, tt)]
                blk.tensor(_mm)

                if oc < 8:
                    def _evac(s, oc=oc, tt=tt):
                        st.w(s, "t", mm_t[(oc, tt)])
                        if tt > 0:
                            st.w(s, "v", conv_v[(oc, tt - 1)])
                        elif oc > 0:
                            st.w(s, "v", conv_v[(oc - 1, NT - 1)])
                        bank = (oc * NT + tt) % 4
                        st.tag(s.activation(out=upad[:, 3:515],
                                            in_=ct["PS_A"][:, bank * 512:(bank + 1) * 512],
                                            func=AFT.Identity, bias=bin_sb[:, oc:oc + 1]), "s")
                        assert st.cnt["s"] == evac_s[(oc, tt)]
                    blk.scalar(_evac)

                    def _conv(v, oc=oc, tt=tt):
                        st.w(v, "s", evac_s[(oc, tt)])
                        st.tag(v.tensor_copy(out=upad[:, 0:3], in_=prev3[:, :]), "v")
                        st.w(v, "v")
                        v.tensor_scalar(out=uconv[:, :], in0=upad[:, 0:512],
                                        scalar1=cw_sb[:, oc * 4:oc * 4 + 1],
                                        scalar2=None, op0=ALU.mult)
                        for i in range(1, 4):
                            ins = v.scalar_tensor_tensor(out=uconv[:, :], in0=upad[:, i:i + 512],
                                                         scalar=cw_sb[:, oc * 4 + i:oc * 4 + i + 1],
                                                         in1=uconv[:, :], op0=ALU.mult, op1=ALU.add)
                        st.tag(ins, "v")
                        st.tag(v.tensor_copy(out=prev3[:, :], in_=upad[:, 512:515]), "v")
                        st.w(v, "v")
                        assert st.cnt["v"] == conv_v[(oc, tt)]
                    blk.vector(_conv)

                    def _silu(s, oc=oc, tt=tt):
                        st.w(s, "v", conv_v[(oc, tt)])
                        idx = oc * NT + tt
                        if idx >= 2:
                            pidx = idx - 2
                            st.w(s, *store_ds[(pidx // NT, pidx % NT, "u")])
                            st.w(s, "v", cast_v[(pidx // NT, pidx % NT)])
                        st.tag(s.activation(out=ustore[idx % 2][:, :], in_=uconv[:, :],
                                            func=AFT.Silu, bias=cb_sb[:, oc:oc + 1]), "s")
                        assert st.cnt["s"] == silu_s[(oc, tt)]
                    blk.scalar(_silu)

                    def _ucast(v, oc=oc, tt=tt):
                        st.w(v, "s", silu_s[(oc, tt)])
                        idx = oc * NT + tt
                        st.tag(v.tensor_copy(out=ubf[:, oc * T + tt * 512:oc * T + (tt + 1) * 512],
                                             in_=ustore[idx % 2][:, :]), "v")
                        assert st.cnt["v"] == cast_v[(oc, tt)]
                    blk.vector(_ucast)

                    def _ustore(q, oc=oc, tt=tt):
                        st.w(q, "s", silu_s[(oc, tt)])
                        idx = oc * NT + tt
                        us = "us%d" % (idx % 2)
                        st.tag(q.dma_start(
                            out=AP(u_out.tensor, oc * 128 * T + tt * 512, [[T, 128], [1, 512]]),
                            in_=ustore[idx % 2][:, :]), us, 16)
                        store_ds[(oc, tt, "u")] = (us, st.cnt[us])
                    blk.sync(_ustore)
                else:
                    def _evac(s, oc=oc, tt=tt):
                        st.w(s, "t", mm_t[(oc, tt)])
                        idx = (oc - 8) * NT + tt
                        if idx >= 2:
                            pidx = idx - 2
                            st.w(s, *store_ds[((pidx // NT) + 8, pidx % NT, "r")])
                        bank = (oc * NT + tt) % 4
                        st.tag(s.activation(out=rstore[idx % 2][:, :],
                                            in_=ct["PS_A"][:, bank * 512:(bank + 1) * 512],
                                            func=AFT.Identity, bias=bin_sb[:, oc:oc + 1]), "s")
                        assert st.cnt["s"] == evac_s[(oc, tt)]
                    blk.scalar(_evac)

                    def _rstore(q, oc=oc, tt=tt):
                        st.w(q, "s", evac_s[(oc, tt)])
                        idx = (oc - 8) * NT + tt
                        rs = "rs%d" % (idx % 2)
                        st.tag(q.dma_start(
                            out=AP(res_out.tensor, (oc - 8) * 128 * T + tt * 512,
                                   [[T, 128], [1, 512]]),
                            in_=rstore[idx % 2][:, :]), rs, 16)
                        store_ds[(oc, tt, "r")] = (rs, st.cnt[rs])
                    blk.sync(_rstore)
        fence(st)

        xd_t, xd_s, xd_ds = {}, {}, {}
        tA2, sA2 = st.cnt["t"], st.cnt["s"]
        for tt in range(NT):
            tA2 += 1
            xd_t[tt] = tA2
            sA2 += 1
            xd_s[tt] = sA2

        for tt in range(NT):
            def _xmm(t, tt=tt):
                if tt >= 2:
                    st.w(t, "s", xd_s[tt - 2])
                bank = tt % 2
                for kc in range(8):
                    ins = t.matmul(out=ct["PS_B"][0:96, bank * 512:bank * 512 + 512],
                                   lhsT=wxb[:, kc * 96:(kc + 1) * 96],
                                   rhs=ubf[:, kc * T + tt * 512:kc * T + (tt + 1) * 512],
                                   start=(kc == 0), stop=(kc == 7))
                st.tag(ins, "t")
            blk.tensor(_xmm)

            def _xevac(s, tt=tt):
                st.w(s, "t", xd_t[tt])
                if tt >= 2:
                    st.w(s, *xd_ds[tt - 2])
                bank = tt % 2
                st.tag(s.copy(out=xdst[tt % 2][0:96, :],
                              in_=ct["PS_B"][0:96, bank * 512:bank * 512 + 512]), "s")
            blk.scalar(_xevac)

            def _xstore(q, tt=tt):
                st.w(q, "s", xd_s[tt])
                xs = "xs%d" % (tt % 2)
                st.tag(q.dma_start(
                    out=AP(xdbl_out.tensor, tt * 512, [[T, 96], [1, 512]]),
                    in_=xdst[tt % 2][0:96, :]), xs, 16)
                xd_ds[tt] = (xs, st.cnt[xs])
            blk.sync(_xstore)

        def _fin(q):
            st.fence_eng(q)
        blk.sync(_fin)
    return nc


def inmaps_launch_B(inputs, consts, xn2_halves):
    win = np.asarray(inputs["L0_attn_W_in"])
    bin_ = np.asarray(inputs["L0_attn_b_in"])
    cw = np.asarray(inputs["L0_attn_conv_w"])
    cb = np.asarray(inputs["L0_attn_conv_b"])
    wx = np.asarray(inputs["L0_attn_W_x"])
    maps = []
    for c in range(8):
        b, dh = c // 2, c % 2
        m = common_inmap(consts)
        m["xn2_in"] = np.concatenate([np.asarray(xn2_halves[2 * b]),
                                      np.asarray(xn2_halves[2 * b + 1])], axis=1)
        ucols = slice(dh * DIH, (dh + 1) * DIH)
        rcols = slice(DI + dh * DIH, DI + (dh + 1) * DIH)
        m["win_in"] = np.ascontiguousarray(
            np.concatenate([win[:, ucols], win[:, rcols]], axis=1))
        m["bin_in"] = np.ascontiguousarray(
            np.concatenate([bin_[ucols], bin_[rcols]]).reshape(16, 128))
        m["convw_in"] = np.ascontiguousarray(cw[:, dh * DIH:(dh + 1) * DIH])
        m["convb_in"] = np.ascontiguousarray(cb[dh * DIH:(dh + 1) * DIH].reshape(8, 128))
        m["wx_in"] = np.ascontiguousarray(wx[dh * DIH:(dh + 1) * DIH, :])
        maps.append(m)
    return maps


def build_launch_C():
    nc = bass.Bass("TRN2", target_bir_lowering=False, debug=False, **BASS_KW)
    T = S
    NCK = T // 512
    u_in = nc.dram_tensor("u_in", [DIH, T], F32, kind="ExternalInput").ap()
    res_in = nc.dram_tensor("res_in", [DIH, T], F32, kind="ExternalInput").ap()
    xd0_in = nc.dram_tensor("xd0_in", [96, T], F32, kind="ExternalInput").ap()
    xd1_in = nc.dram_tensor("xd1_in", [96, T], F32, kind="ExternalInput").ap()
    wdt_in = nc.dram_tensor("wdt_in", [DTR, DIH], F32R, kind="ExternalInput").ap()
    bdt_in = nc.dram_tensor("bdt_in", [8, 128], F32, kind="ExternalInput").ap()
    alog_in = nc.dram_tensor("alog_in", [DIH, NST], F32, kind="ExternalInput").ap()
    dvec_in = nc.dram_tensor("dvec_in", [8, 128], F32, kind="ExternalInput").ap()
    wout_in = nc.dram_tensor("wout_in", [DIH, D], F32R, kind="ExternalInput").ap()
    bones_in = nc.dram_tensor("bones_in", [128, 16 * 128], F32R, kind="ExternalInput").ap()
    part_out = nc.dram_tensor("part_out", [D, T], F32, kind="ExternalOutput").ap()
    xdbl_d = nc.dram_tensor("xdbl_d", [96, T], F32R, kind="Internal").ap()
    delta_d = nc.dram_tensor("delta_d", [DIH, T], F32, kind="Internal").ap()
    dtu_d = nc.dram_tensor("dtu_d", [DIH, T], F32, kind="Internal").ap()
    a_d = nc.dram_tensor("a_d", [DIH, NST], F32, kind="Internal").ap()

    ct = {}
    u_ck = nc.alloc_sbuf_tensor("u_ck", [128, 8 * 512], F32)
    res_ck = nc.alloc_sbuf_tensor("res_ck", [128, 8 * 512], F32)
    delta = nc.alloc_sbuf_tensor("delta", [128, 8 * 512], F32)
    dtu = nc.alloc_sbuf_tensor("dtu", [128, 8 * 512], F32)
    xdsum = nc.alloc_sbuf_tensor("xdsum", [96, T], F32R)
    wout = nc.alloc_sbuf_tensor("wout", [128, 64 * 128], F32R)
    wdt = nc.alloc_sbuf_tensor("wdt", [64, 8 * 128], F32R)
    bones = nc.alloc_sbuf_tensor("bones", [128, 16 * 128], F32R)
    yg = nc.alloc_sbuf_tensor("yg", [128, 8 * 512], F32R)
    A_il = nc.alloc_sbuf_tensor("A_il", [128, 128], F32)
    asb = nc.alloc_sbuf_tensor("asb", [128, 8 * 16], F32)
    Dv = nc.alloc_sbuf_tensor("Dv", [128, 8], F32)
    bdt_sb = nc.alloc_sbuf_tensor("bdt_sb", [128, 8], F32)
    hstate = nc.alloc_sbuf_tensor("hstate", [128, 128], F32)
    dlt16 = [nc.alloc_sbuf_tensor("dlt16_%d" % i, [128, 512], F32) for i in range(2)]
    dtu16 = [nc.alloc_sbuf_tensor("dtu16_%d" % i, [128, 512], F32) for i in range(2)]
    B16 = nc.alloc_sbuf_tensor("B16", [128, 512], F32)
    C16 = nc.alloc_sbuf_tensor("C16", [128, 512], F32)
    dAt = [nc.alloc_sbuf_tensor("dAt%d" % i, [128, 512], F32) for i in range(2)]
    dtuB = [nc.alloc_sbuf_tensor("dtuB%d" % i, [128, 512], F32) for i in range(2)]
    hbuf = [nc.alloc_sbuf_tensor("hbuf%d" % i, [128, 512], F32) for i in range(2)]
    hCr = [nc.alloc_sbuf_tensor("hCr%d" % i, [128, 512], F32R) for i in range(2)]
    ydt = nc.alloc_sbuf_tensor("ydt", [128, 512], F32)
    sut = nc.alloc_sbuf_tensor("sut", [128, 512], F32)
    outst = [nc.alloc_sbuf_tensor("outst%d" % i, [128, 512], F32) for i in range(2)]
    PS_A = nc.alloc_psum_tensor("PS_A", [128, 2048], F32)
    PS_B = nc.alloc_psum_tensor("PS_B", [128, 2048], F32)
    ct["PS_A"], ct["PS_B"] = PS_A, PS_B

    extra = ["sc0", "sc1", "dd", "bc"]
    with (nc.semaphore("sem_t") as s_t, nc.semaphore("sem_v") as s_v,
          nc.semaphore("sem_s") as s_s, nc.semaphore("sem_p") as s_p,
          nc.semaphore("sem_dl") as s_dl, nc.semaphore("sem_ds") as s_ds,
          nc.Block() as blk):
        sems = dict(t=s_t, v=s_v, s=s_s, p=s_p, dl=s_dl, ds=s_ds)
        for nm in extra:
            sems[nm] = nc.alloc_semaphore(name="sem_" + nm)
        st = St(nc, blk, sems)
        cins = declare_common(nc, ct)

        def _l0(q):
            for k, (ap_, tname) in cins.items():
                st.tag(q.dma_start(out=ct[tname][:], in_=ap_), "dl", 16)
            st.tag(q.dma_start(out=wdt.ap().rearrange("p (c u) -> p c u", u=128),
                               in_=AP(wdt_in.tensor, 0, [[DIH, 64], [128, 8], [1, 128]])), "dl", 16)
            st.tag(q.dma_start(
                out=wout.ap().rearrange("p (kc oc u) -> p kc oc u", oc=8, u=128),
                in_=AP(wout_in.tensor, 0, [[D, 128], [128 * D, 8], [128, 8], [1, 128]])), "dl", 16)
            st.tag(q.dma_start(out=bones[:, :], in_=bones_in), "dl", 16)
            st.tag(q.dma_start(out=asb.ap().rearrange("p (c u) -> p c u", u=NST),
                               in_=AP(alog_in.tensor, 0, [[NST, 128], [NST * 128, 8], [1, NST]])), "dl", 16)
            st.tag(q.dma_start(out=Dv[:, 0:8],
                               in_=AP(dvec_in.tensor, 0, [[1, 128], [128, 8]]),
                               allow_slow_non_contiguous=True), "dl", 16)
            st.tag(q.dma_start(out=bdt_sb[:, 0:8],
                               in_=AP(bdt_in.tensor, 0, [[1, 128], [128, 8]]),
                               allow_slow_non_contiguous=True), "dl", 16)
            st.tag(q.dma_start(out=u_ck[0:96, 0:4096], in_=xd0_in), "dl", 16)
            st.tag(q.dma_start(out=res_ck[0:96, 0:4096], in_=xd1_in), "dl", 16)
        blk.sync(_l0)
        fence(st, "vs")

        def _xds(v):
            v.memset(hstate[:, :], 0.0)
            st.tag(v.tensor_add(out=xdsum[0:96, :], in0=u_ck[0:96, 0:4096],
                                in1=res_ck[0:96, 0:4096]), "v")
        blk.vector(_xds)

        def _aprep(s):
            st.fence_eng(s, skip=("s",))
            st.tag(s.activation(out=asb[:, :], in_=asb[:, :], func=AFT.Exp), "s")
        blk.scalar(_aprep)

        def _aneg(v):
            st.fence_eng(v, skip=("v",))
            st.tag(v.tensor_scalar(out=asb[:, :], in0=asb[:, :], scalar1=-1.0,
                                   scalar2=None, op0=ALU.mult), "v")
        blk.vector(_aneg)

        def _adma(q):
            st.fence_eng(q)
            st.tag(q.dma_start(out=xdbl_d, in_=xdsum[0:96, :]), "ds", 16)
            st.tag(q.dma_start(out=a_d, in_=asb.ap().rearrange("p (c u) -> p c u", u=NST)), "ds", 16)
            st.w(q, "ds")
            for j in range(128):
                st.tag(q.dma_start(out=A_il[:, j:j + 1],
                                   in_=AP(a_d.tensor, 8 * j * NST, [[NST, 8], [1, NST], [0, 1]])),
                       "dl", 16)
        blk.sync(_adma)
        fence(st)

        for ck in range(NCK):
            dlbase = st.cnt["dl"]

            def _cl(q, ck=ck):
                if ck > 0:
                    st.fence_eng(q, skip=("dl", "ds"))
                for src_d, dst in ((u_in, u_ck), (res_in, res_ck)):
                    st.tag(q.dma_start(
                        out=dst.ap().rearrange("p (c u) -> p c u", u=512),
                        in_=AP(src_d.tensor, ck * 512, [[T, 128], [128 * T, 8], [1, 512]])),
                        "dl", 16)
            blk.sync(_cl)

            dmm_t, dsp_s, dsp_v = {}, {}, {}
            tA, sA, vA = st.cnt["t"], st.cnt["s"], st.cnt["v"]
            for dc in range(8):
                tA += 1
                dmm_t[dc] = tA
                sA += 4
                dsp_s[dc] = sA
                vA += 1
                dsp_v[dc] = vA
            for dc in range(8):
                def _dmm(t, ck=ck, dc=dc):
                    if dc == 0:
                        st.fence_eng(t, skip=("t",))
                    if dc >= 2:
                        st.w(t, "s", dsp_s[dc - 2])
                    ins = t.matmul(out=PS_A[:, (dc % 2) * 512:(dc % 2 + 1) * 512],
                                   lhsT=wdt[:, dc * 128:(dc + 1) * 128],
                                   rhs=xdsum[0:64, ck * 512:(ck + 1) * 512],
                                   start=True, stop=True)
                    st.tag(ins, "t")
                    assert st.cnt["t"] == dmm_t[dc]
                blk.tensor(_dmm)

                def _dsp(s, ck=ck, dc=dc):
                    st.w(s, "t", dmm_t[dc])
                    if dc >= 1:
                        st.w(s, "v", dsp_v[dc - 1])
                    pv = PS_A[:, (dc % 2) * 512:(dc % 2 + 1) * 512]
                    s.activation(out=sut[:, :], in_=pv, func=AFT.Abs,
                                 bias=bdt_sb[:, dc:dc + 1])
                    s.activation(out=sut[:, :], in_=sut[:, :], func=AFT.Exp, scale=-1.0)
                    s.activation(out=sut[:, :], in_=sut[:, :], func=AFT.Ln, bias=1.0)
                    st.tag(s.activation(out=ydt[:, :], in_=pv, func=AFT.Relu,
                                        bias=bdt_sb[:, dc:dc + 1]), "s", 4)
                    assert st.cnt["s"] == dsp_s[dc]
                blk.scalar(_dsp)

                def _dv(v, ck=ck, dc=dc):
                    st.w(v, "s", dsp_s[dc])
                    st.tag(v.tensor_add(out=delta[:, dc * 512:(dc + 1) * 512],
                                        in0=sut[:, :], in1=ydt[:, :]), "v")
                    assert st.cnt["v"] == dsp_v[dc]
                blk.vector(_dv)

            def _dtu(v, ck=ck):
                st.fence_eng(v, skip=("v",))
                for kc in range(8):
                    ins = v.tensor_mul(out=dtu[:, kc * 512:(kc + 1) * 512],
                                       in0=delta[:, kc * 512:(kc + 1) * 512],
                                       in1=u_ck[:, kc * 512:(kc + 1) * 512])
                st.tag(ins, "v")
            blk.vector(_dtu)

            def _dst(q, ck=ck):
                st.fence_eng(q, skip=("dl", "ds", "sc0", "sc1", "bc", "dd"))
                st.tag(q.dma_start(
                    out=AP(delta_d.tensor, ck * 512, [[T, 128], [128 * T, 8], [1, 512]]),
                    in_=delta.ap().rearrange("p (c u) -> p c u", u=512)), "dd", 16)
                st.tag(q.dma_start(
                    out=AP(dtu_d.tensor, ck * 512, [[T, 128], [128 * T, 8], [1, 512]]),
                    in_=dtu.ap().rearrange("p (c u) -> p c u", u=512)), "dd", 16)
                st.w(q, "dd")
                st.tag(q.dma_start(out=B16[:, :],
                                   in_=AP(xdbl_d.tensor, 64 * T + ck * 512,
                                          [[0, 8], [T, 16], [1, 512]]).bitcast(F32)), "bc", 16)
                st.tag(q.dma_start(out=C16[:, :],
                                   in_=AP(xdbl_d.tensor, 80 * T + ck * 512,
                                          [[0, 8], [T, 16], [1, 512]]).bitcast(F32)), "bc", 16)
            blk.sync(_dst)

            base = dict(s=st.cnt["s"], p=st.cnt["p"], v=st.cnt["v"], t=st.cnt["t"])
            sdl, ss, sp, sv, stt = {}, {}, {}, {}, {}
            sA2, pA2, vA2, tA2 = base["s"], base["p"], base["v"], base["t"]
            scc = {0: st.cnt["sc0"], 1: st.cnt["sc1"]}
            for j in range(128):
                scc[j % 2] += 32
                sdl[j] = ("sc%d" % (j % 2), scc[j % 2])
                sA2 += 1
                ss[j] = sA2
                pA2 += 1
                sp[j] = pA2
                vA2 += 2
                sv[j] = vA2
                tA2 += 1
                stt[j] = tA2

            def _sdma(q, ck=ck):
                for j in range(128):
                    if j >= 2:
                        st.w(q, "s", ss[j - 2])
                        st.w(q, "p", sp[j - 2])
                    scs = "sc%d" % (j % 2)
                    st.tag(q.dma_start(out=dlt16[j % 2][:, :],
                                       in_=AP(delta_d.tensor, 8 * j * T + ck * 512,
                                              [[T, 8], [0, 16], [1, 512]])), scs, 16)
                    st.tag(q.dma_start(out=dtu16[j % 2][:, :],
                                       in_=AP(dtu_d.tensor, 8 * j * T + ck * 512,
                                              [[T, 8], [0, 16], [1, 512]])), scs, 16)
                    assert st.cnt[scs] == sdl[j][1]
            blk.sync(_sdma)

            def _sact(s, ck=ck):
                for j in range(128):
                    st.w(s, *sdl[j])
                    if j == 0:
                        st.w(s, "bc", st.cnt["bc"])
                    if j >= 2:
                        st.w(s, "v", sv[j - 2])
                    st.tag(s.activation(out=dAt[j % 2][:, :], in_=dlt16[j % 2][:, :],
                                        func=AFT.Exp, scale=A_il[:, j:j + 1]), "s")
            blk.scalar(_sact)

            def _sp_(p, ck=ck):
                for j in range(128):
                    st.w(p, *sdl[j])
                    if j == 0:
                        st.w(p, "bc", st.cnt["bc"])
                    if j >= 2:
                        st.w(p, "v", sv[j - 2])
                    st.tag(p.tensor_mul(out=dtuB[j % 2][:, :], in0=dtu16[j % 2][:, :],
                                        in1=B16[:, :]), "p")
            blk.gpsimd(_sp_)

            def _sv_(v, ck=ck):
                for j in range(128):
                    st.w(v, "s", ss[j])
                    st.w(v, "p", sp[j])
                    if j >= 2:
                        st.w(v, "t", stt[j - 2])
                    v.tensor_tensor_scan(out=hbuf[j % 2][:, :], data0=dAt[j % 2][:, :],
                                         data1=dtuB[j % 2][:, :],
                                         initial=hstate[:, j:j + 1],
                                         op0=ALU.mult, op1=ALU.add)
                    st.tag(v.tensor_copy(out=hstate[:, j:j + 1],
                                         in_=hbuf[j % 2][:, 511:512]), "v")
                    st.tag(v.tensor_mul(out=hCr[j % 2][:, :], in0=hbuf[j % 2][:, :],
                                        in1=C16[:, :]), "v")
            blk.vector(_sv_)

            def _st_(t, ck=ck):
                if ck > 0:
                    st.w(t, "s", ct["_prev_wev7"])
                for j in range(128):
                    st.w(t, "v", sv[j])
                    st.tag(t.matmul(out=PS_A.ap().bitcast(F32).rearrange(
                        "p (oc u) -> p oc u", u=512)[:, j // 16 if False else 0, 0:512]
                        if False else
                        (PS_A[:, (j // 16) * 512:(j // 16 + 1) * 512] if j < 64 else
                         PS_B[:, ((j // 16) - 4) * 512:((j // 16) - 3) * 512]),
                        lhsT=bones[:, (j % 16) * 128:(j % 16 + 1) * 128],
                        rhs=hCr[j % 2][:, :],
                        start=(j % 16 == 0), stop=(j % 16 == 15)), "t")
            blk.tensor(_st_)

            yg_v = {}
            for oc in range(8):
                def _yev(v, ck=ck, oc=oc):
                    if oc == 0:
                        st.w(v, "t", stt[127])
                        st.w(v, "s", st.cnt["s"])
                    pv = (PS_A[:, oc * 512:(oc + 1) * 512] if oc < 4 else
                          PS_B[:, (oc - 4) * 512:(oc - 3) * 512])
                    st.tag(v.scalar_tensor_tensor(out=ydt[:, :],
                                                  in0=u_ck[:, oc * 512:(oc + 1) * 512],
                                                  scalar=Dv[:, oc:oc + 1], in1=pv,
                                                  op0=ALU.mult, op1=ALU.add), "v")
                blk.vector(_yev)

                def _ysu(s, ck=ck, oc=oc):
                    st.w(s, "v", st.cnt["v"])
                    st.tag(s.activation(out=sut[:, :], in_=res_ck[:, oc * 512:(oc + 1) * 512],
                                        func=AFT.Silu), "s")
                blk.scalar(_ysu)

                def _ymu(v, ck=ck, oc=oc):
                    st.w(v, "s", st.cnt["s"])
                    st.tag(v.tensor_mul(out=yg[:, oc * 512:(oc + 1) * 512],
                                        in0=ydt[:, :], in1=sut[:, :]), "v")
                    yg_v[oc] = st.cnt["v"]
                blk.vector(_ymu)

            wev_s = {}
            for outc in range(8):
                def _wmm(t, ck=ck, outc=outc):
                    if outc == 0:
                        st.w(t, "v", yg_v[7])
                    if outc >= 4:
                        st.w(t, "s", wev_s[outc - 4])
                    for kc in range(8):
                        ins = t.matmul(out=PS_A[:, (outc % 4) * 512:(outc % 4 + 1) * 512],
                                       lhsT=wout[:, (kc * 8 + outc) * 128:(kc * 8 + outc + 1) * 128],
                                       rhs=yg[:, kc * 512:(kc + 1) * 512],
                                       start=(kc == 0), stop=(kc == 7))
                    st.tag(ins, "t")
                blk.tensor(_wmm)

                def _wev(s, ck=ck, outc=outc):
                    st.w(s, "t", st.cnt["t"])
                    if outc >= 2:
                        st.w(s, "ds", st.cnt["ds"])
                    st.tag(s.copy(out=outst[outc % 2][:, :],
                                  in_=PS_A[:, (outc % 4) * 512:(outc % 4 + 1) * 512]), "s")
                    wev_s[outc] = st.cnt["s"]
                blk.scalar(_wev)

                if outc == 7:
                    ct["_prev_wev7"] = None

                def _wst(q, ck=ck, outc=outc):
                    st.w(q, "s", wev_s[outc])
                    st.tag(q.dma_start(
                        out=AP(part_out.tensor, outc * 128 * T + ck * 512,
                               [[T, 128], [1, 512]]),
                        in_=outst[outc % 2][:, :]), "ds", 16)
                blk.sync(_wst)
            ct["_prev_wev7"] = wev_s[7]

        def _fin(q):
            st.fence_eng(q)
        blk.sync(_fin)
    return nc


def inmaps_launch_C(inputs, consts, u_arrs, res_arrs, xd_arrs):
    wdt = np.asarray(inputs["L0_attn_W_dt"])
    bdt = np.asarray(inputs["L0_attn_b_dt"])
    alog = np.asarray(inputs["L0_attn_A_log"])
    dvec = np.asarray(inputs["L0_attn_D"])
    wo = np.asarray(inputs["L0_attn_W_out"])
    maps = []
    for c in range(8):
        b, dh = c // 2, c % 2
        m = common_inmap(consts)
        m["u_in"] = np.asarray(u_arrs[c])
        m["res_in"] = np.asarray(res_arrs[c])
        m["xd0_in"] = np.asarray(xd_arrs[2 * b])
        m["xd1_in"] = np.asarray(xd_arrs[2 * b + 1])
        sl = slice(dh * DIH, (dh + 1) * DIH)
        m["wdt_in"] = np.ascontiguousarray(wdt[:, sl])
        m["bdt_in"] = np.ascontiguousarray(bdt[sl].reshape(8, 128))
        m["alog_in"] = np.ascontiguousarray(alog[sl])
        m["dvec_in"] = np.ascontiguousarray(dvec[sl].reshape(8, 128))
        m["wout_in"] = np.ascontiguousarray(wo[sl])
        m["bones_in"] = consts["bones"]
        maps.append(m)
    return maps


def moe_l1_emit(st, ct, x_fm, xn_fm, T, w1_dram, w2_dram, b1_dram, b2_dram,
                gt_dram):
    blk = st.block
    PS_A, PS_B = ct["PS_A"], ct["PS_B"]
    NT = T // 512
    w1b, w2b = ct["w1b"], ct["w2b"]
    wstage = ct["wstage"]
    gtrb, beb, bebf = ct["gtrb"], ct["beb"], ct["bebf"]

    tA, vA, pA, sA = st.cnt["t"], st.cnt["v"], st.cnt["p"], st.cnt["s"]
    pA0 = pA
    gb_t, gbe_s, becast_v, cast_p = {}, {}, {}, {}
    h1mm_t, h1ev_s, h1g_v, omm_t, oev_v = {}, {}, {}, {}, {}
    for e in range(NE):
        tA += 1
        gb_t[e] = tA
        sA += 1
        gbe_s[e] = sA
        vA += 1
        becast_v[e] = vA
        pA += 8
        cast_p[e] = pA
        for tt in range(NT):
            for fc in range(4):
                tA += 1
                h1mm_t[(e, tt, fc)] = tA
                sA += 1
                h1ev_s[(e, tt, fc)] = sA
                vA += 1
                h1g_v[(e, tt, fc)] = vA
            for dh in range(2):
                tA += 1
                omm_t[(e, tt, dh)] = tA
                vA += 1
                oev_v[(e, tt, dh)] = vA
    dl_g, dl_w = {}, {}

    def _loads(q):
        st.fence_eng(q, skip=("dl",))
        for e in range(NE):
            gs = "ge%d" % (e % 2)
            if e >= 1:
                st.w(q, "t", omm_t[(e - 1, NT - 1, 1)])
                st.w(q, "v", becast_v[e - 1])
            st.tag(q.dma_start(out=gtrb[e % 2][0:1, 0:T],
                               in_=AP(gt_dram.tensor, gt_dram.offset + e * T,
                                      [[1, 1], [1, T]])), gs, 16)
            st.tag(q.dma_start(out=beb[e % 2][0:1, 0:D],
                               in_=AP(b2_dram.tensor, b2_dram.offset + e * D,
                                      [[1, 1], [1, D]])), gs, 16)
            st.tag(q.dma_start(out=ct["b1sb"][e % 2][:, 0:4],
                               in_=AP(b1_dram.tensor, b1_dram.offset + e * DFFP,
                                      [[1, 128], [128, 4]]),
                               allow_slow_non_contiguous=True), gs, 16)
            dl_g[e] = (gs, st.cnt[gs])
            for pc in range(4):
                gidx = e * 8 + pc
                if gidx >= 2:
                    st.w(q, "p", pA0 + gidx - 1)
                ws = "wl%d" % (gidx % 2)
                src = AP(w1_dram.tensor, w1_dram.offset + e * D * DFFP + 2 * pc * 128 * DFFP,
                         [[DFFP, 128], [128 * DFFP, 2], [128, 4], [1, 128]])
                st.tag(q.dma_start(out=wstage[gidx % 2].ap().rearrange(
                    "p (kc fc j) -> p kc fc j", fc=4, j=128), in_=src), ws, 16)
                dl_w[(e, "w1", pc)] = (ws, st.cnt[ws])
            for pc in range(4):
                gidx = e * 8 + 4 + pc
                if gidx >= 2:
                    st.w(q, "p", pA0 + gidx - 1)
                ws = "wl%d" % (gidx % 2)
                src = AP(w2_dram.tensor, w2_dram.offset + e * DFFP * D + pc * 128 * D,
                         [[D, 128], [128 * D, 1], [128, 8], [1, 128]])
                st.tag(q.dma_start(out=wstage[gidx % 2].ap().rearrange(
                    "p (kc oc j) -> p kc oc j", oc=8, j=128), in_=src), ws, 16)
                dl_w[(e, "w2", pc)] = (ws, st.cnt[ws])
    blk.sync(_loads)

    for e in range(NE):
        def _becast(v, e=e):
            st.w(v, *dl_g[e])
            if e >= 2:
                st.w(v, "t", omm_t[(e - 2, NT - 1, 1)])
            st.tag(v.tensor_copy(out=bebf[e % 2][0:1, 0:D], in_=beb[e % 2][0:1, 0:D]), "v")
            assert st.cnt["v"] == becast_v[e]
        blk.vector(_becast)

        def _gb(t, e=e):
            st.w(t, *dl_g[e])
            if e >= 1:
                st.w(t, "s", gbe_s[e - 1])
                st.w(t, "v", oev_v[(e - 1, NT - 1, 1)])
            for k in range(NT):
                ins = t.matmul(out=PS_B[:, k * 512:(k + 1) * 512],
                               lhsT=ct["ones_row_b"][0:1, :],
                               rhs=gtrb[e % 2][0:1, k * 512:(k + 1) * 512],
                               start=True, stop=True)
            st.tag(ins, "t")
            assert st.cnt["t"] == gb_t[e]
        blk.tensor(_gb)

        def _gbe(s, e=e):
            st.w(s, "t", gb_t[e])
            if e >= 1:
                st.w(s, "v", h1g_v[(e - 1, NT - 1, 3)])
            st.tag(s.copy(out=ct["gbb"][:, 0:T], in_=PS_B[:, 0:T]), "s")
            assert st.cnt["s"] == gbe_s[e]
        blk.scalar(_gbe)

        def _cast(p, e=e):
            if e >= 2:
                st.w(p, "t", omm_t[(e - 2, NT - 1, 1)])
            for pc in range(4):
                st.w(p, *dl_w[(e, "w1", pc)])
                st.tag(p.tensor_copy(out=w1b[:, pc * 1024:(pc + 1) * 1024],
                                     in_=wstage[(e * 8 + pc) % 2][:, 0:1024]), "p")
            for pc in range(4):
                st.w(p, *dl_w[(e, "w2", pc)])
                st.tag(p.tensor_copy(out=w2b[:, pc * 1024:(pc + 1) * 1024],
                                     in_=wstage[(e * 8 + 4 + pc) % 2][:, 0:1024]), "p")
            assert st.cnt["p"] == cast_p[e]
        blk.gpsimd(_cast)

        for tt in range(NT):
            for fc in range(4):
                def _h1mm(t, e=e, tt=tt, fc=fc):
                    st.w(t, "p", cast_p[e])
                    prev = None
                    if tt > 0:
                        prev = h1ev_s[(e, tt - 1, fc)]
                    elif e > 0:
                        prev = h1ev_s[(e - 1, NT - 1, fc)]
                    if prev is not None:
                        st.w(t, "s", prev)
                    for kc in range(NCH):
                        ins = t.matmul(out=PS_A[:, fc * 512:(fc + 1) * 512],
                                       lhsT=w1b[:, (kc * 4 + fc) * 128:(kc * 4 + fc + 1) * 128],
                                       rhs=xn_fm[:, kc * T + tt * 512:kc * T + (tt + 1) * 512],
                                       start=(kc == 0), stop=(kc == NCH - 1))
                    st.tag(ins, "t")
                    assert st.cnt["t"] == h1mm_t[(e, tt, fc)]
                blk.tensor(_h1mm)

                def _h1ev(s, e=e, tt=tt, fc=fc):
                    st.w(s, "t", h1mm_t[(e, tt, fc)])
                    st.w(s, "v", st.cnt["v"])
                    st.tag(s.activation(out=ct["h1f"][:, :],
                                        in_=PS_A[:, fc * 512:(fc + 1) * 512],
                                        func=AFT.Gelu, bias=ct["b1sb"][e % 2][:, fc:fc + 1]), "s")
                    assert st.cnt["s"] == h1ev_s[(e, tt, fc)]
                blk.scalar(_h1ev)

                def _h1g(v, e=e, tt=tt, fc=fc):
                    st.w(v, "s", h1ev_s[(e, tt, fc)])
                    if (tt, fc) == (0, 0):
                        st.w(v, "s", gbe_s[e])
                    prev = None
                    if tt > 0 and fc == 0:
                        prev = omm_t[(e, tt - 1, 1)]
                    elif tt == 0 and fc == 0 and e > 0:
                        prev = omm_t[(e - 1, NT - 1, 1)]
                    if prev is not None:
                        st.w(v, "t", prev)
                    st.tag(v.tensor_mul(out=ct["h1g"][:, fc * 512:(fc + 1) * 512],
                                        in0=ct["h1f"][:, :],
                                        in1=ct["gbb"][:, tt * 512:(tt + 1) * 512]), "v")
                    assert st.cnt["v"] == h1g_v[(e, tt, fc)]
                blk.vector(_h1g)

            for dh in range(2):
                def _omm(t, e=e, tt=tt, dh=dh):
                    st.w(t, "v", h1g_v[(e, tt, 3)])
                    prev = None
                    if (e, tt, dh) != (0, 0, 0):
                        pkey = (e, tt, 0) if dh == 1 else \
                            ((e, tt - 1, 1) if tt > 0 else (e - 1, NT - 1, 1))
                        prev = oev_v[pkey]
                    if prev is not None:
                        st.w(t, "v", prev)
                    for dc in range(4):
                        for fc in range(4):
                            t.matmul(out=PS_B[:, dc * 512:(dc + 1) * 512],
                                     lhsT=w2b[:, (fc * 8 + dh * 4 + dc) * 128:(fc * 8 + dh * 4 + dc + 1) * 128],
                                     rhs=ct["h1g"][:, fc * 512:(fc + 1) * 512],
                                     start=(fc == 0), stop=False)
                        ins = t.matmul(out=PS_B[:, dc * 512:(dc + 1) * 512],
                                       lhsT=bebf[e % 2][0:1, (dh * 4 + dc) * 128:(dh * 4 + dc + 1) * 128],
                                       rhs=gtrb[e % 2][0:1, tt * 512:(tt + 1) * 512],
                                       start=False, stop=True)
                    st.tag(ins, "t")
                    assert st.cnt["t"] == omm_t[(e, tt, dh)]
                blk.tensor(_omm)

                def _oev(v, e=e, tt=tt, dh=dh):
                    st.w(v, "t", omm_t[(e, tt, dh)])
                    xv = x_fm.ap().rearrange("p (c u) -> p c u", u=T)[
                        :, dh * 4:(dh + 1) * 4, tt * 512:(tt + 1) * 512]
                    pv = PS_B.ap().rearrange("p (dc u) -> p dc u", u=512)
                    st.tag(v.tensor_add(out=xv, in0=pv, in1=xv), "v")
                    assert st.cnt["v"] == oev_v[(e, tt, dh)]
                blk.vector(_oev)


def build_launch_D(dbg=False, stages=frozenset({"s3", "s4", "qkv"}), rope_on=True,
                   qkv_n=3, qkv_depth=5, qkv_scratch=False):
    nc = bass.Bass("TRN2", target_bir_lowering=False, debug=False, **BASS_KW)
    T = TOK
    NT = T // 512
    x1_in = nc.dram_tensor("x1_in", [D, T], F32, kind="ExternalInput").ap()
    p0_in = nc.dram_tensor("p0_in", [D, T], F32, kind="ExternalInput").ap()
    p1_in = nc.dram_tensor("p1_in", [D, T], F32, kind="ExternalInput").ap()
    bout_in = nc.dram_tensor("bout_in", [8, 128], F32, kind="ExternalInput").ap()
    wg_in = nc.dram_tensor("wg_in", [D, NE], F32, kind="ExternalInput").ap()
    we_in = nc.dram_tensor("we_in", [NE, D, D], F32, kind="ExternalInput").ap()
    be_in = nc.dram_tensor("be_in", [NE, D], F32, kind="ExternalInput").ap()
    rw_in = nc.dram_tensor("rw_in", [D, NE], F32, kind="ExternalInput").ap()
    rb_in = nc.dram_tensor("rb_in", [1, NE], F32, kind="ExternalInput").ap()
    w1_in = nc.dram_tensor("w1_in", [NE, D, DFFP], F32, kind="ExternalInput").ap()
    w2_in = nc.dram_tensor("w2_in", [NE, DFFP, D], F32, kind="ExternalInput").ap()
    b1_in = nc.dram_tensor("b1_in", [NE, DFFP], F32, kind="ExternalInput").ap()
    b2_in = nc.dram_tensor("b2_in", [NE, D], F32, kind="ExternalInput").ap()
    wqT_in = nc.dram_tensor("wqT_in", [D, D], F32, kind="ExternalInput").ap()
    wkT_in = nc.dram_tensor("wkT_in", [D, D], F32, kind="ExternalInput").ap()
    wvT_in = nc.dram_tensor("wvT_in", [D, D], F32, kind="ExternalInput").ap()
    cos_in = nc.dram_tensor("cos_in", [128, T], F32, kind="ExternalInput").ap()
    sin_in = nc.dram_tensor("sin_in", [128, T], F32, kind="ExternalInput").ap()
    x4_out = nc.dram_tensor("x4_out", [D, T], F32, kind="ExternalOutput").ap()
    q_out = nc.dram_tensor("q_out", [D, T], BF16, kind="ExternalOutput").ap()
    k_out = nc.dram_tensor("k_out", [D, T], BF16, kind="ExternalOutput").ap()
    v_out = nc.dram_tensor("v_out", [D, T], BF16, kind="ExternalOutput").ap()
    if qkv_scratch:
        q_scr = nc.dram_tensor("q_scr", [D, T], BF16, kind="Internal").ap()
        q_out = k_out = v_out = q_scr
    gt_dram = nc.dram_tensor("gt_scr", [NE, T], BF16, kind="Internal").ap()
    if dbg:
        x2_out = nc.dram_tensor("x2_out", [D, T], F32, kind="ExternalOutput").ap()
        x3_out = nc.dram_tensor("x3_out", [D, T], F32, kind="ExternalOutput").ap()

    ct = {}
    x_fm = nc.alloc_sbuf_tensor("x_fm", [128, NCH * T], F32)
    xnb = nc.alloc_sbuf_tensor("xnb", [128, NCH * T], BF16)
    ct["wstage"] = [nc.alloc_sbuf_tensor("wstage%d" % i, [128, 1024], F32) for i in range(2)]
    ct["wbuf"] = [nc.alloc_sbuf_tensor("wbuf%d" % i, [128, 4096], BF16) for i in range(2)]
    ct["w1b"] = nc.alloc_sbuf_tensor("w1b", [128, 4096], BF16)
    ct["w2b"] = nc.alloc_sbuf_tensor("w2b", [128, 4096], BF16)
    ct["xg"] = nc.alloc_sbuf_tensor("xg", [128, NCH * 512], BF16)
    ct["h1g"] = ct["xg"].ap()[:, 0:2048].bitcast(BF16)
    ct["h1f"] = nc.alloc_sbuf_tensor("h1f", [128, 512], F32)
    ct["gbb"] = nc.alloc_sbuf_tensor("gbb", [128, T], BF16)
    gtrb1 = nc.alloc_sbuf_tensor("gtrb", [1, T], BF16)
    ct["gtrb"] = [gtrb1, gtrb1]
    beb1 = nc.alloc_sbuf_tensor("beb", [1, D], F32)
    ct["beb"] = [beb1, beb1]
    ct["bebf"] = [nc.alloc_sbuf_tensor("bebf%d" % i, [1, D], BF16) for i in range(2)]
    ct["b1sb"] = [nc.alloc_sbuf_tensor("b1sb%d" % i, [128, 4], F32) for i in range(2)]
    wg_f = nc.alloc_sbuf_tensor("wg_f", [128, NCH * 8], F32)
    rw_f = nc.alloc_sbuf_tensor("rw_f", [128, NCH * 8], F32)
    rb_sb = nc.alloc_sbuf_tensor("rb_sb", [1, 8], F32)
    bout_sb = nc.alloc_sbuf_tensor("bout_sb", [128, 8], F32)
    ct["PS_A"] = nc.alloc_psum_tensor("PS_A", [128, 2048], F32)
    ct["PS_B"] = nc.alloc_psum_tensor("PS_B", [128, 2048], F32)
    ct["tmpa"] = nc.alloc_sbuf_tensor("tmpa", [128, T], F32)
    ct["tmpb"] = nc.alloc_sbuf_tensor("tmpb", [128, T], F32)
    ct["sq0"] = ct["tmpa"].ap()
    ct["sq1"] = ct["tmpb"].ap()
    ct["lsb"] = ct["tmpa"].ap()
    ct["lsb_bf"] = ct["tmpb"].ap().bitcast(BF16)
    pb0 = ct["w1b"].ap().bitcast(F32)
    pb1 = ct["w2b"].ap().bitcast(F32)
    cos_sb = ct["wbuf"][0].ap().bitcast(F32)
    sin_sb = ct["wbuf"][1].ap().bitcast(F32)
    qt1 = None
    qt2 = None
    qst = [nc.alloc_sbuf_tensor("qst%d" % i, [128, 512], BF16) for i in range(2)]
    for nm, shp, dt in [("tokg", [128, (T // 128) * 8], F32),
                        ("e8", [128, (T // 128) * 8], F32),
                        ("nm1s", [128, T // 128], F32),
                        ("w1", [128, 1], F32), ("w2", [128, 1], F32),
                        ("w3", [128, 1], F32), ("w8a", [128, 8], F32),
                        ("w8b", [128, 8], F32), ("w8c", [128, 8], F32),
                        ("mtok", [128, 16], F32), ("rtok", [128, 16], F32),
                        ("sbc", [128, 8], F32), ("rbc", [128, 8], F32),
                        ("r_mean", [1, T], F32), ("r_tmp", [1, T], F32),
                        ("r_rstd", [1, T], F32)]:
        ct[nm] = nc.alloc_sbuf_tensor(nm, shp, dt)

    extra = ["wl0", "wl1", "ge0", "ge1", "pc0", "pc1"]
    with (nc.semaphore("sem_t") as s_t, nc.semaphore("sem_v") as s_v,
          nc.semaphore("sem_s") as s_s, nc.semaphore("sem_p") as s_p,
          nc.semaphore("sem_dl") as s_dl, nc.semaphore("sem_ds") as s_ds,
          nc.Block() as blk):
        sems = dict(t=s_t, v=s_v, s=s_s, p=s_p, dl=s_dl, ds=s_ds)
        for nm in extra:
            sems[nm] = nc.alloc_semaphore(name="sem_" + nm)
        st = St(nc, blk, sems)
        cins = declare_common(nc, ct)

        def _l0(q):
            for k, (ap_, tname) in cins.items():
                st.tag(q.dma_start(out=ct[tname][:], in_=ap_), "dl", 16)
            st.tag(q.dma_start(out=x_fm.ap().rearrange("p (c u) -> p c u", u=T),
                               in_=fm_in_ap(x1_in, NCH, T)), "dl", 16)
            for w_, d_ in ((wg_in, wg_f), (rw_in, rw_f)):
                src = AP(w_.tensor, 0, [[NE, 128], [NE * 128, NCH], [1, NE]])
                st.tag(q.dma_start(out=d_.ap().rearrange("p (c e) -> p c e", e=NE),
                                   in_=src), "dl", 16)
            st.tag(q.dma_start(out=rb_sb[0:1, :], in_=rb_in), "dl", 16)
            st.tag(q.dma_start(out=bout_sb[:, 0:8],
                               in_=AP(bout_in.tensor, 0, [[1, 128], [128, 8]]),
                               allow_slow_non_contiguous=True), "dl", 16)
        blk.sync(_l0)
        fence(st)

        ln_stats_emit(st, ct, x_fm, T)
        fence(st)
        bcast_rows_emit(st, ct, ct["r_mean"], ct["r_rstd"], T)
        fence(st, "q")
        pld = {}

        def _pl(q):
            for c in range(NCH):
                pc_ = "pc%d" % (c % 2)
                if c >= 2:
                    st.w(q, "v", cmb_v[c - 2])
                st.tag(q.dma_start(out=pb0[:, 0:T] if False else (pb0 if c % 2 == 0 else pb1)[:, 0:T],
                                   in_=AP(p0_in.tensor, c * 128 * T, [[T, 128], [1, T]])), pc_, 16)
                st.tag(q.dma_start(out=(pb0 if c % 2 == 0 else pb1)[:, T:2 * T] if False
                                   else (pb0 if c % 2 == 0 else pb1).tensor.ap()[:, 0:T] if False
                                   else (pb0 if c % 2 == 0 else pb1)[:, 0:T],
                                   in_=AP(p1_in.tensor, c * 128 * T, [[T, 128], [1, T]])), pc_, 16)
                pld[c] = (pc_, st.cnt[pc_])
        cmb_v = {}

        def _pl2(q):
            for c in range(NCH):
                if c >= 1:
                    st.w(q, "v", cmb_v[c - 1])
                st.tag(q.dma_start(out=pb0[:, 0:T],
                                   in_=AP(p0_in.tensor, c * 128 * T, [[T, 128], [1, T]])), "pc0", 16)
                st.tag(q.dma_start(out=pb1[:, 0:T],
                                   in_=AP(p1_in.tensor, c * 128 * T, [[T, 128], [1, T]])), "pc0", 16)
                pld[c] = ("pc0", st.cnt["pc0"])
        vA = st.cnt["v"]
        for c in range(NCH):
            vA += 1
            cmb_v[c] = vA
        blk.sync(_pl2)

        def _cmb(v):
            st.fence_eng(v, skip=("v", "pc0"))
            for c in range(NCH):
                st.w(v, *pld[c])
                xc = x_fm[:, c * T:(c + 1) * T]
                v.tensor_sub(out=ct["tmpa"][:, 0:T], in0=xc, in1=ct["PS_A"][:, 0:T])
                v.tensor_mul(out=ct["tmpb"][:, 0:T], in0=ct["tmpa"][:, 0:T],
                             in1=ct["PS_B"][:, 0:T])
                v.scalar_tensor_tensor(out=ct["tmpb"][:, 0:T], in0=ct["tmpb"][:, 0:T],
                                       scalar=bout_sb[:, c:c + 1], in1=xc,
                                       op0=ALU.add, op1=ALU.add)
                v.tensor_add(out=ct["tmpb"][:, 0:T], in0=ct["tmpb"][:, 0:T],
                             in1=pb0[:, 0:T])
                st.tag(v.tensor_add(out=xc, in0=ct["tmpb"][:, 0:T], in1=pb1[:, 0:T]), "v")
                assert st.cnt["v"] == cmb_v[c]
        blk.vector(_cmb)
        fence(st)
        if dbg:
            def _dx2(q):
                st.fence_eng(q)
                st.tag(q.dma_start(out=fm_in_ap(x2_out, NCH, T),
                                   in_=x_fm.ap().rearrange("p (c u) -> p c u", u=T)), "ds", 16)
            blk.sync(_dx2)
            fence(st, "q")

        if "s3" in stages:
            ln_emit(st, ct, x_fm, T, xnb)
            fence(st)
            gates_emit(st, ct, x_fm, T, wg_f, None, gt_dram)
            fence(st)
            moe_l0_emit(st, ct, x_fm, xnb, T, we_in, be_in, gt_dram)
            fence(st)
        if dbg:
            def _dx3(q):
                st.fence_eng(q)
                st.tag(q.dma_start(out=fm_in_ap(x3_out, NCH, T),
                                   in_=x_fm.ap().rearrange("p (c u) -> p c u", u=T)), "ds", 16)
            blk.sync(_dx3)
            fence(st, "q")

        if "s4" in stages:
            ln_emit(st, ct, x_fm, T, xnb)
            fence(st)
            gates_emit(st, ct, x_fm, T, rw_f, rb_sb, gt_dram)
            fence(st)
            moe_l1_emit(st, ct, x_fm, xnb, T, w1_in, w2_in, b1_in, b2_in, gt_dram)
            fence(st)

        def _sx4(q):
            st.fence_eng(q)
            st.tag(q.dma_start(out=fm_in_ap(x4_out, NCH, T),
                               in_=x_fm.ap().rearrange("p (c u) -> p c u", u=T)), "ds", 16)
        blk.sync(_sx4)

        if "qkv" in stages:
            ln_emit(st, ct, x_fm, T, xnb)
            fence(st)

            def _lrope(q):
                st.tag(q.dma_start(out=cos_sb[:, 0:T], in_=cos_in), "dl", 16)
                st.tag(q.dma_start(out=sin_sb[:, 0:T], in_=sin_in), "dl", 16)
            blk.sync(_lrope)
            fence(st)

        swapmask = []
        for i in range(16):
            swapmask += [2 * i + 1, 2 * i]
        qst_hist = []
        for pj, (wT, dst, dorope) in enumerate(
                ([(wqT_in, q_out, rope_on), (wkT_in, k_out, rope_on),
                  (wvT_in, v_out, False)][:qkv_n])
                if "qkv" in stages else []):
            tA, sA, vA, pA = st.cnt["t"], st.cnt["s"], st.cnt["v"], st.cnt["p"]
            qmm_t, qev_s, qrp_v, qst_ds = {}, {}, {}, {}
            wcast_p, wld = {}, {}
            sA_, vA_, tA_, pA_ = sA, vA, tA, pA
            for oc in range(NCH):
                if qkv_depth >= 2:
                    pA_ += 1
                    wcast_p[oc] = pA_
                for tt in range(NT):
                    if qkv_depth >= 3:
                        tA_ += 1
                        qmm_t[(oc, tt)] = tA_
                    if qkv_depth >= 4:
                        sA_ += 1
                        qev_s[(oc, tt)] = sA_
                    if dorope:
                        vA_ += 2
                        qrp_v[(oc, tt)] = vA_

            def _wl(q, wT=wT, pj=pj, wcast_p=wcast_p, wld=wld):
                st.fence_eng(q, skip=("dl", "ds", "wl0", "wl1", "pc0", "pc1",
                                      "ge0", "ge1"))
                for oc in range(NCH):
                    ws = "wl%d" % (oc % 2)
                    if oc >= 2 and qkv_depth >= 2:
                        st.w(q, "p", wcast_p[oc - 2])
                    src = AP(wT.tensor, oc * 128, [[D, 128], [128 * D, 8], [1, 128]])
                    st.tag(q.dma_start(out=ct["wstage"][oc % 2].ap().rearrange(
                        "p (kc j) -> p kc j", j=128), in_=src), ws, 16)
                    wld[oc] = (ws, st.cnt[ws])
            blk.sync(_wl)

            for oc in range(NCH):
                if qkv_depth < 2:
                    continue
                def _wc(p, oc=oc, wld=wld, wcast_p=wcast_p, qmm_t=qmm_t):
                    st.w(p, *wld[oc])
                    if oc >= 2 and qkv_depth >= 3:
                        st.w(p, "t", qmm_t[(oc - 2, NT - 1)])
                    st.tag(p.tensor_copy(out=ct["xg"][:, (oc % 2) * 1024:(oc % 2) * 1024 + 1024],
                                         in_=ct["wstage"][oc % 2][:, :]), "p")
                    assert st.cnt["p"] == wcast_p[oc]
                blk.gpsimd(_wc)
                for tt in range(NT):
                    if qkv_depth < 3:
                        continue
                    def _qmm(t, oc=oc, tt=tt, wcast_p=wcast_p, qmm_t=qmm_t, qev_s=qev_s):
                        st.w(t, "p", wcast_p[oc])
                        idx = oc * NT + tt
                        if idx >= 2 and qkv_depth >= 4:
                            st.w(t, "s", qev_s[((idx - 2) // NT, (idx - 2) % NT)])
                        for kc in range(NCH):
                            ins = t.matmul(out=ct["PS_A"][:, (idx % 2) * 512:(idx % 2) * 512 + 512],
                                           lhsT=ct["xg"].ap().bitcast(BF16)[:, (oc % 2) * 2048 + kc * 128:
                                                                            (oc % 2) * 2048 + (kc + 1) * 128] if False
                                           else ct["xg"][:, (oc % 2) * 1024 * 2 // 2 + kc * 128:
                                                         (oc % 2) * 1024 + (kc + 1) * 128] if False
                                           else ct["xg"][:, (oc % 2) * 1024 + kc * 128:(oc % 2) * 1024 + (kc + 1) * 128],
                                           rhs=xnb[:, kc * T + tt * 512:kc * T + (tt + 1) * 512],
                                           start=(kc == 0), stop=(kc == NCH - 1))
                        st.tag(ins, "t")
                        assert st.cnt["t"] == qmm_t[(oc, tt)]
                    blk.tensor(_qmm)

                    if dorope:
                        def _qev(s, oc=oc, tt=tt, qmm_t=qmm_t, qev_s=qev_s, qrp_v=qrp_v):
                            st.w(s, "t", qmm_t[(oc, tt)])
                            idx = oc * NT + tt
                            if idx >= 1:
                                st.w(s, "v", qrp_v[((idx - 1) // NT, (idx - 1) % NT)])
                            st.tag(s.copy(out=ct["tmpa"][:, 0:512],
                                          in_=ct["PS_A"][:, (idx % 2) * 512:(idx % 2) * 512 + 512]), "s")
                            assert st.cnt["s"] == qev_s[(oc, tt)], (st.cnt["s"], qev_s[(oc, tt)], oc, tt)
                        blk.scalar(_qev)

                        def _qrp(v, oc=oc, tt=tt, pj=pj, qev_s=qev_s, qrp_v=qrp_v, qst_ds=qst_ds, dst=dst):
                            st.w(v, "s", qev_s[(oc, tt)])
                            idx = oc * NT + tt
                            gidx = pj * NCH * NT + idx
                            if gidx >= 2:
                                st.w(v, *qst_hist[gidx - 2])
                            v.stream_shuffle(out=ct["tmpa"][:, 512:1024],
                                             in_=ct["tmpa"][:, 0:512], mask=swapmask)
                            st.tag(v.tensor_mul(out=ct["tmpa"][:, 512:1024],
                                                in0=ct["tmpa"][:, 512:1024],
                                                in1=sin_sb[:, tt * 512:(tt + 1) * 512]), "v")
                            st.w(v, "v")
                            v.tensor_mul(out=ct["tmpa"][:, 0:512], in0=ct["tmpa"][:, 0:512],
                                         in1=cos_sb[:, tt * 512:(tt + 1) * 512])
                            st.tag(v.tensor_add(out=qst[idx % 2][:, :],
                                                in0=ct["tmpa"][:, 0:512],
                                                in1=ct["tmpa"][:, 512:1024]), "v")
                            assert st.cnt["v"] == qrp_v[(oc, tt)]
                        blk.vector(_qrp)

                        def _qst(q, oc=oc, tt=tt, qrp_v=qrp_v, qst_ds=qst_ds, dst=dst, pj=pj):
                            st.w(q, "v", qrp_v[(oc, tt)])
                            idx = oc * NT + tt
                            pc_ = "pc%d" % (idx % 2)
                            st.tag(q.dma_start(
                                out=AP(dst.tensor, oc * 128 * T + tt * 512, [[T, 128], [1, 512]]),
                                in_=qst[idx % 2][:, :]), pc_, 16)
                            qst_ds[(oc, tt)] = (pc_, st.cnt[pc_])
                            qst_hist.append((pc_, st.cnt[pc_]))
                        blk.gpsimd(_qst)
                    else:
                        if qkv_depth < 4:
                            continue
                        def _qev(s, oc=oc, tt=tt, pj=pj, qmm_t=qmm_t, qev_s=qev_s, qst_ds=qst_ds):
                            st.w(s, "t", qmm_t[(oc, tt)])
                            idx = oc * NT + tt
                            gidx = pj * NCH * NT + idx
                            if gidx >= 2 and qkv_depth >= 5:
                                st.w(s, *qst_hist[gidx - 2])
                            st.tag(s.copy(out=qst[idx % 2][:, :],
                                          in_=ct["PS_A"][:, (idx % 2) * 512:(idx % 2) * 512 + 512]), "s")
                            assert st.cnt["s"] == qev_s[(oc, tt)]
                        blk.scalar(_qev)

                        if qkv_depth < 5:
                            continue
                        def _qst(q, oc=oc, tt=tt, qev_s=qev_s, qst_ds=qst_ds, dst=dst):
                            st.w(q, "s", qev_s[(oc, tt)])
                            idx = oc * NT + tt
                            pc_ = "pc%d" % (idx % 2)
                            st.tag(q.dma_start(
                                out=AP(dst.tensor, oc * 128 * T + tt * 512, [[T, 128], [1, 512]]),
                                in_=qst[idx % 2][:, :]), pc_, 16)
                            qst_ds[(oc, tt)] = (pc_, st.cnt[pc_])
                            qst_hist.append((pc_, st.cnt[pc_]))
                        blk.gpsimd(_qst)

        def _fin(q):
            st.fence_eng(q)
        blk.sync(_fin)
    return nc


def inmaps_launch_D(inputs, consts, x1_arrs, part_arrs):
    wg = np.ascontiguousarray(inputs["L0_ffn_Wg"])
    we = np.ascontiguousarray(inputs["L0_ffn_We"])
    be = np.ascontiguousarray(inputs["L0_ffn_be"])
    bout = np.ascontiguousarray(inputs["L0_attn_b_out"].reshape(8, 128))
    rw = np.ascontiguousarray(inputs["L1_ffn_router_w"])
    rb = np.ascontiguousarray(inputs["L1_ffn_router_b"].reshape(1, 8))
    w1 = np.ascontiguousarray(inputs["L1_ffn_W1"])
    w2 = np.ascontiguousarray(inputs["L1_ffn_W2"])
    b1 = np.ascontiguousarray(inputs["L1_ffn_b1"])
    b2 = np.ascontiguousarray(inputs["L1_ffn_b2"])
    wqT = np.ascontiguousarray(inputs["L1_attn_W_q"].T)
    wkT = np.ascontiguousarray(inputs["L1_attn_W_k"].T)
    wvT = np.ascontiguousarray(inputs["L1_attn_W_v"].T)
    sin_signed = consts["sin128"].copy()
    sin_signed[0::2, :] *= -1.0
    maps = []
    for c in range(8):
        b, h = c // 2, c % 2
        t0 = h * TOK
        m = common_inmap(consts)
        m["x1_in"] = np.asarray(x1_arrs[c])
        m["p0_in"] = np.ascontiguousarray(np.asarray(part_arrs[2 * b])[:, t0:t0 + TOK])
        m["p1_in"] = np.ascontiguousarray(np.asarray(part_arrs[2 * b + 1])[:, t0:t0 + TOK])
        m["bout_in"] = bout
        m["wg_in"] = wg
        m["we_in"] = we
        m["be_in"] = be
        m["rw_in"] = rw
        m["rb_in"] = rb
        m["w1_in"] = w1
        m["w2_in"] = w2
        m["b1_in"] = b1
        m["b2_in"] = b2
        m["wqT_in"] = wqT
        m["wkT_in"] = wkT
        m["wvT_in"] = wvT
        m["cos_in"] = np.ascontiguousarray(consts["cos128"][:, t0:t0 + TOK])
        m["sin_in"] = np.ascontiguousarray(sin_signed[:, t0:t0 + TOK])
        maps.append(m)
    return maps


def build_launch_E(dbg=False):
    nc = bass.Bass("TRN2", target_bir_lowering=False, debug=False, **BASS_KW)
    T = TOK
    NT = T // 512
    NQS = T // 512
    x4_in = nc.dram_tensor("x4_in", [D, T], F32, kind="ExternalInput").ap()
    q_in = nc.dram_tensor("q_in", [D, T], BF16, kind="ExternalInput").ap()
    k_in = nc.dram_tensor("k_in", [D, S], BF16, kind="ExternalInput").ap()
    v_in = nc.dram_tensor("v_in", [D, S], BF16, kind="ExternalInput").ap()
    woT_in = nc.dram_tensor("woT_in", [D, D], F32, kind="ExternalInput").ap()
    rw_in = nc.dram_tensor("rw_in", [D, NE], F32, kind="ExternalInput").ap()
    rb_in = nc.dram_tensor("rb_in", [1, NE], F32, kind="ExternalInput").ap()
    w1_in = nc.dram_tensor("w1_in", [NE, D, DFFP], F32, kind="ExternalInput").ap()
    w2_in = nc.dram_tensor("w2_in", [NE, DFFP, D], F32, kind="ExternalInput").ap()
    b1_in = nc.dram_tensor("b1_in", [NE, DFFP], F32, kind="ExternalInput").ap()
    b2_in = nc.dram_tensor("b2_in", [NE, D], F32, kind="ExternalInput").ap()
    x6_out = nc.dram_tensor("x6_out", [D, T], F32, kind="ExternalOutput").ap()
    gt_dram = nc.dram_tensor("gt_scr", [NE, T], BF16, kind="Internal").ap()
    if dbg:
        x5_out = nc.dram_tensor("x5_out", [D, T], F32, kind="ExternalOutput").ap()

    ct = {}
    x_fm = nc.alloc_sbuf_tensor("x_fm", [128, NCH * T], F32)
    att = nc.alloc_sbuf_tensor("att", [128, NCH * T], BF16)
    big1 = nc.alloc_sbuf_tensor("big1", [128, 16384], BF16)
    A1 = nc.alloc_sbuf_tensor("A1", [128, 4096], BF16)
    A3 = nc.alloc_sbuf_tensor("A3", [128, 4096], BF16)
    A4 = nc.alloc_sbuf_tensor("A4", [128, 2048], BF16)
    A5 = nc.alloc_sbuf_tensor("A5", [128, 2048], BF16)
    P_sb = nc.alloc_sbuf_tensor("P_sb", [128, 4096], BF16)
    ct["wstage"] = [nc.alloc_sbuf_tensor("wstage%d" % i, [128, 1024], F32) for i in range(2)]
    ct["w1b"] = A1.ap()
    ct["w2b"] = P_sb.ap()
    ct["h1g"] = A3.ap()[:, 0:2048]
    ct["gbb"] = A3.ap()[:, 2048:4096]
    ct["h1f"] = nc.alloc_sbuf_tensor("h1f", [128, 512], F32)
    gtrb1 = nc.alloc_sbuf_tensor("gtrb", [1, T], BF16)
    ct["gtrb"] = [gtrb1, gtrb1]
    beb1 = nc.alloc_sbuf_tensor("beb", [1, D], F32)
    ct["beb"] = [beb1, beb1]
    bebf1 = nc.alloc_sbuf_tensor("bebf", [1, D], BF16)
    ct["bebf"] = [bebf1, bebf1]
    ct["b1sb"] = [nc.alloc_sbuf_tensor("b1sb%d" % i, [128, 4], F32) for i in range(2)]
    rw_f = nc.alloc_sbuf_tensor("rw_f", [128, NCH * 8], F32)
    rb_sb = nc.alloc_sbuf_tensor("rb_sb", [1, 8], F32)
    ct["PS_A"] = nc.alloc_psum_tensor("PS_A", [128, 2048], F32)
    ct["PS_B"] = nc.alloc_psum_tensor("PS_B", [128, 2048], F32)
    ct["tmpa"] = nc.alloc_sbuf_tensor("tmpa", [128, T], F32)
    ct["sq0"] = ct["tmpa"].ap()
    ct["sq1"] = ct["tmpa"].ap()
    ct["sq_single"] = True
    ct["lsb"] = ct["tmpa"].ap()
    ct["lsb_bf"] = P_sb.ap()[:, 0:T]
    rsum = nc.alloc_sbuf_tensor("rsum", [128, 8], F32)
    nmx = nc.alloc_sbuf_tensor("nmx", [128, 4], F32)
    rla = nc.alloc_sbuf_tensor("rla", [128, 4], F32)
    rlrow = ct["tmpa"].ap()[0:1, 512:1024]
    accf = ct["tmpa"].ap()[0:64, 1024:1536]
    for nm, shp, dt in [("tokg", [128, (T // 128) * 8], F32),
                        ("e8", [128, (T // 128) * 8], F32),
                        ("nm1s", [128, T // 128], F32),
                        ("w1", [128, 1], F32), ("w2", [128, 1], F32),
                        ("w3", [128, 1], F32), ("w8a", [128, 8], F32),
                        ("w8b", [128, 8], F32), ("w8c", [128, 8], F32),
                        ("mtok", [128, 16], F32), ("rtok", [128, 16], F32),
                        ("sbc", [128, 8], F32), ("rbc", [128, 8], F32),
                        ("r_mean", [1, T], F32), ("r_tmp", [1, T], F32),
                        ("r_rstd", [1, T], F32)]:
        if nm in ("r_mean", "r_tmp", "r_rstd"):
            continue
        ct[nm] = nc.alloc_sbuf_tensor(nm, shp, dt)
    attf = att.ap().bitcast(F32)
    ct["r_mean"] = attf[0:1, 0:2048]
    ct["r_tmp"] = attf[0:1, 2048:4096]
    ct["r_rstd"] = attf[0:1, 4096:6144]
    PS_A, PS_B = ct["PS_A"], ct["PS_B"]
    PAb = PS_A.ap().bitcast(BF16)

    extra = ["wl0", "wl1", "ge0", "ge1", "kv0", "kv1", "pc0", "pc1"]
    with (nc.semaphore("sem_t") as s_t, nc.semaphore("sem_v") as s_v,
          nc.semaphore("sem_s") as s_s, nc.semaphore("sem_p") as s_p,
          nc.semaphore("sem_dl") as s_dl, nc.semaphore("sem_ds") as s_ds,
          nc.Block() as blk):
        sems = dict(t=s_t, v=s_v, s=s_s, p=s_p, dl=s_dl, ds=s_ds)
        for nm in extra:
            sems[nm] = nc.alloc_semaphore(name="sem_" + nm)
        st = St(nc, blk, sems)
        cins = declare_common(nc, ct)

        def _l0(q):
            for k, (ap_, tname) in cins.items():
                st.tag(q.dma_start(out=ct[tname][:], in_=ap_), "dl", 16)
            st.tag(q.dma_start(out=x_fm.ap().rearrange("p (c u) -> p c u", u=T),
                               in_=fm_in_ap(x4_in, NCH, T)), "dl", 16)
            src = AP(rw_in.tensor, 0, [[NE, 128], [NE * 128, NCH], [1, NE]])
            st.tag(q.dma_start(out=rw_f.ap().rearrange("p (c e) -> p c e", e=NE),
                               in_=src), "dl", 16)
            st.tag(q.dma_start(out=rb_sb[0:1, :], in_=rb_in), "dl", 16)
        blk.sync(_l0)
        fence(st)

        hload, hdone_t, hdone_v, hdone_s = {}, {}, {}, {}
        for h in range(NH):
            kbuf = A1.ap()[0:64, 0:S]
            vbuf = A3.ap()[0:64, 0:S]
            qbuf = A4.ap()[0:64, 0:T]

            def _hl(q, h=h, kbuf=kbuf, vbuf=vbuf, qbuf=qbuf):
                kv = "kv%d" % (h % 2)
                if h >= 1:
                    st.w(q, "t", hdone_t[h - 1])
                    st.w(q, "v", hdone_v[h - 1])
                    st.w(q, "s", hdone_s[h - 1])
                st.tag(q.dma_start(out=kbuf,
                                   in_=AP(k_in.tensor, h * 64 * S, [[S, 64], [1, S]])), kv, 16)
                st.tag(q.dma_start(out=vbuf,
                                   in_=AP(v_in.tensor, h * 64 * S, [[S, 64], [1, S]])), kv, 16)
                st.tag(q.dma_start(out=qbuf,
                                   in_=AP(q_in.tensor, h * 64 * T, [[T, 64], [1, T]])), kv, 16)
                hload[h] = (kv, st.cnt[kv])
            blk.sync(_hl)

            def _vt(t, h=h, vbuf=vbuf):
                st.w(t, *hload[h])
                if h >= 1:
                    st.w(t, "s", hdone_s[h - 1])
                for kt in range(32):
                    ins = t.matmul(out=PAb[:, kt * 64:(kt + 1) * 64],
                                   lhsT=vbuf[:, kt * 128:(kt + 1) * 128],
                                   rhs=ct["identb"][0:64, 0:64],
                                   is_transpose=True,
                                   start=(kt % 8 == 0), stop=(kt % 8 == 7))
                st.tag(ins, "t")
            blk.tensor(_vt)

            def _vte(s, h=h):
                st.fence_eng(s, skip=("s", "dl", "ds", "kv0", "kv1", "wl0", "wl1",
                                      "ge0", "ge1", "pc0", "pc1", "p"))
                st.tag(s.copy(out=A5[:, 0:2048], in_=PAb[:, 0:2048]), "s")
            blk.scalar(_vte)

            for qs in range(NQS):
                qmax_v, qexp_s, qtp_t, qte_s = {}, {}, {}, {}
                for qt4 in range(4):
                    qti = qs * 4 + qt4

                    def _qk(t, h=h, qs=qs, qt4=qt4, qti=qti):
                        st.fence_eng(t, skip=("t", "dl", "ds", "kv0", "kv1",
                                              "wl0", "wl1", "ge0", "ge1",
                                              "pc0", "pc1", "p"))
                        st.w(t, *hload[h])
                        kbuf = A1.ap()[0:64, 0:S]
                        qbuf = A4.ap()[0:64, 0:T]
                        for c in range(8):
                            psd = PS_A if c < 4 else PS_B
                            ins = t.matmul(out=psd[:, (c % 4) * 512:(c % 4 + 1) * 512],
                                           lhsT=qbuf[:, qti * 128:(qti + 1) * 128],
                                           rhs=kbuf[:, c * 512:(c + 1) * 512],
                                           start=True, stop=True)
                        st.tag(ins, "t")
                        qtp_t[(qt4, "qk")] = st.cnt["t"]
                    blk.tensor(_qk)

                    def _mx(v, h=h, qt4=qt4):
                        st.w(v, "t", qtp_t[(qt4, "qk")])
                        v.reduce_max(out=nmx[:, 0:1], in_=PS_A[:, 0:2048], axis=AX.X,
                                     negate=True)
                        st.tag(v.reduce_max(out=nmx[:, 1:2], in_=PS_B[:, 0:2048],
                                            axis=AX.X, negate=True), "v")
                        st.w(v, "v")
                        st.tag(v.tensor_tensor(out=nmx[:, 2:3], in0=nmx[:, 0:1],
                                               in1=nmx[:, 1:2], op=ALU.min), "v")
                        st.w(v, "v")
                        st.tag(v.tensor_scalar(out=nmx[:, 2:3], in0=nmx[:, 2:3],
                                               scalar1=0.125, scalar2=None,
                                               op0=ALU.mult), "v")
                        qmax_v[qt4] = st.cnt["v"]
                    blk.vector(_mx)

                    def _ex(s, h=h, qt4=qt4):
                        st.w(s, "v", qmax_v[qt4])
                        for c in range(8):
                            psd = PS_A if c < 4 else PS_B
                            ins = s.activation(out=P_sb[:, c * 512:(c + 1) * 512],
                                               in_=psd[:, (c % 4) * 512:(c % 4 + 1) * 512],
                                               func=AFT.Exp, bias=nmx[:, 2:3],
                                               scale=0.125,
                                               accum_out=rsum[:, c:c + 1])
                        st.tag(ins, "s")
                        qexp_s[qt4] = st.cnt["s"]
                    blk.scalar(_ex)

                    def _l(v, h=h, qt4=qt4):
                        st.w(v, "s", qexp_s[qt4])
                        st.tag(v.reduce_sum(out=rla[:, qt4:qt4 + 1], in_=rsum[:, 0:8],
                                            axis=AX.X), "v")
                        st.w(v, "v")
                        st.tag(v.reciprocal(out=rla[:, qt4:qt4 + 1],
                                            in_=rla[:, qt4:qt4 + 1]), "v")
                    blk.vector(_l)

                    def _tp(t, h=h, qt4=qt4):
                        st.w(t, "s", qexp_s[qt4])
                        for kt in range(32):
                            ins = t.matmul(out=PAb[:, 2048 + (kt % 16) * 128:2048 + (kt % 16 + 1) * 128]
                                           if kt < 16 else
                                           PS_B.ap().bitcast(BF16)[:, 2048 + (kt % 16) * 128:2048 + (kt % 16 + 1) * 128],
                                           lhsT=P_sb[:, kt * 128:(kt + 1) * 128],
                                           rhs=ct["identb"][:, :], is_transpose=True,
                                           start=(kt % 8 == 0), stop=(kt % 8 == 7))
                        st.tag(ins, "t")
                        qtp_t[qt4] = st.cnt["t"]
                    blk.tensor(_tp)

                    def _te(s, h=h, qs=qs, qt4=qt4):
                        st.w(s, "t", qtp_t[qt4])
                        for g in range(2):
                            src = (PAb if g == 0 else PS_B.ap().bitcast(BF16))[:, 2048:4096]
                            dst = big1.ap().rearrange("p (kt u) -> p kt u", u=512)[
                                :, g * 16:(g + 1) * 16, qt4 * 128:(qt4 + 1) * 128]
                            ins = s.copy(out=dst, in_=src.rearrange("p (kt u) -> p kt u", u=128))
                        st.tag(ins, "s")
                        qte_s[qt4] = st.cnt["s"]
                    blk.scalar(_te)

                def _pv(t, h=h, qs=qs):
                    for qt4 in range(4):
                        st.w(t, "s", qte_s[qt4])
                    for kt in range(32):
                        ins = t.matmul(out=PS_A[0:64, 1536:2048],
                                       lhsT=A5[:, kt * 64:(kt + 1) * 64],
                                       rhs=big1[:, kt * 512:(kt + 1) * 512],
                                       start=(kt == 0), stop=(kt == 31))
                    st.tag(ins, "t")
                    hdone_t[h] = st.cnt["t"]
                blk.tensor(_pv)

                def _rlt(t, h=h, qs=qs):
                    for qt4 in range(4):
                        ins = t.matmul(out=PS_B[0:1, 512 + qt4 * 128:512 + (qt4 + 1) * 128],
                                       lhsT=rla[:, qt4:qt4 + 1],
                                       rhs=ct["ident"][:, :], is_transpose=True,
                                       start=(qt4 == 0), stop=(qt4 == 3))
                    st.tag(ins, "t")
                blk.tensor(_rlt)

                def _rle(s, h=h, qs=qs):
                    st.fence_eng(s, skip=("s", "dl", "ds", "kv0", "kv1", "wl0", "wl1",
                                          "ge0", "ge1", "pc0", "pc1", "p"))
                    s.copy(out=rlrow[0:1, 0:512], in_=PS_B[0:1, 512:1024])
                    st.tag(s.copy(out=accf[0:64, :], in_=PS_A[0:64, 1536:2048]), "s")
                    hdone_s[h] = st.cnt["s"]
                blk.scalar(_rle)

                def _rlb(t, h=h, qs=qs):
                    st.fence_eng(t, skip=("t", "dl", "ds", "kv0", "kv1", "wl0", "wl1",
                                          "ge0", "ge1", "pc0", "pc1", "p"))
                    st.tag(t.matmul(out=PS_B[0:64, 0:512], lhsT=ct["ones_row"][0:1, 0:64],
                                    rhs=rlrow[0:1, 0:512], start=True, stop=True), "t")
                blk.tensor(_rlb)

                def _nrm(v, h=h, qs=qs):
                    st.fence_eng(v, skip=("v", "dl", "ds", "kv0", "kv1", "wl0", "wl1",
                                          "ge0", "ge1", "pc0", "pc1", "p"))
                    dst = att[:, :].ap() if False else att.ap()
                    row0 = (h % 2) * 64
                    dview = att.ap().rearrange("p (c u) -> p c u", u=T)[
                        row0:row0 + 64, h // 2, qs * 512:(qs + 1) * 512]
                    st.tag(v.tensor_mul(out=dview, in0=accf[0:64, :],
                                        in1=PS_B[0:64, 0:512]), "v")
                    hdone_v[h] = st.cnt["v"]
                blk.vector(_nrm)
        fence(st)

        wo_t, wo_s, wo_p, wld = {}, {}, {}, {}
        tA, sA, pA = st.cnt["t"], st.cnt["s"], st.cnt["p"]
        pA0w = pA
        for oc in range(NCH):
            pA += 1
            wo_p[oc] = pA
            for tt in range(NT):
                tA += 1
                wo_t[(oc, tt)] = tA

        def _wol(q):
            st.fence_eng(q, skip=("dl", "ds", "wl0", "wl1"))
            for oc in range(NCH):
                ws = "wl%d" % (oc % 2)
                if oc >= 2:
                    st.w(q, "p", wo_p[oc - 2])
                src = AP(woT_in.tensor, oc * 128, [[D, 128], [128 * D, 8], [1, 128]])
                st.tag(q.dma_start(out=ct["wstage"][oc % 2].ap().rearrange(
                    "p (kc j) -> p kc j", j=128), in_=src), ws, 16)
                wld[oc] = (ws, st.cnt[ws])
        blk.sync(_wol)

        for oc in range(NCH):
            def _woc(p, oc=oc):
                st.w(p, *wld[oc])
                if oc >= 2:
                    st.w(p, "t", wo_t[(oc - 2, NT - 1)])
                st.tag(p.tensor_copy(out=A3.ap()[:, (oc % 2) * 1024:(oc % 2 + 1) * 1024],
                                     in_=ct["wstage"][oc % 2][:, :]), "p")
                assert st.cnt["p"] == wo_p[oc]
            blk.gpsimd(_woc)
            for tt in range(NT):
                def _wom(t, oc=oc, tt=tt):
                    st.w(t, "p", wo_p[oc])
                    idx = oc * NT + tt
                    if idx >= 2:
                        st.w(t, "v", st.cnt["v"])
                    for kc in range(NCH):
                        ins = t.matmul(out=PS_A[:, (idx % 2) * 512:(idx % 2) * 512 + 512],
                                       lhsT=A3[:, (oc % 2) * 1024 + kc * 128:(oc % 2) * 1024 + (kc + 1) * 128],
                                       rhs=att[:, kc * T + tt * 512:kc * T + (tt + 1) * 512],
                                       start=(kc == 0), stop=(kc == NCH - 1))
                    st.tag(ins, "t")
                    assert st.cnt["t"] == wo_t[(oc, tt)]
                blk.tensor(_wom)

                def _woe(v, oc=oc, tt=tt):
                    st.w(v, "t", wo_t[(oc, tt)])
                    idx = oc * NT + tt
                    xc = x_fm[:, oc * T + tt * 512:oc * T + (tt + 1) * 512]
                    st.tag(v.tensor_add(out=xc,
                                        in0=PS_A[:, (idx % 2) * 512:(idx % 2) * 512 + 512],
                                        in1=xc), "v")
                blk.vector(_woe)
        fence(st)
        if dbg:
            def _dx5(q):
                st.fence_eng(q)
                st.tag(q.dma_start(out=fm_in_ap(x5_out, NCH, T),
                                   in_=x_fm.ap().rearrange("p (c u) -> p c u", u=T)), "ds", 16)
            blk.sync(_dx5)
            fence(st, "q")

        xnb = big1.ap()[:, 0:NCH * T]
        ln_emit(st, ct, x_fm, T, xnb)
        fence(st)
        gates_emit(st, ct, x_fm, T, rw_f, rb_sb, gt_dram)
        fence(st)
        moe_l1_emit(st, ct, x_fm, xnb, T, w1_in, w2_in, b1_in, b2_in, gt_dram)
        fence(st)

        def _sx6(q):
            st.fence_eng(q)
            st.tag(q.dma_start(out=fm_in_ap(x6_out, NCH, T),
                               in_=x_fm.ap().rearrange("p (c u) -> p c u", u=T)), "ds", 16)
            st.w(q, "ds")
        blk.sync(_sx6)
    return nc


def inmaps_launch_E(inputs, consts, x4_arrs, q_arrs, k_arrs, v_arrs):
    woT = np.ascontiguousarray(inputs["L1_attn_W_o"].T)
    rw = np.ascontiguousarray(inputs["L1_ffn_router_w"])
    rb = np.ascontiguousarray(inputs["L1_ffn_router_b"].reshape(1, 8))
    w1 = np.ascontiguousarray(inputs["L1_ffn_W1"])
    w2 = np.ascontiguousarray(inputs["L1_ffn_W2"])
    b1 = np.ascontiguousarray(inputs["L1_ffn_b1"])
    b2 = np.ascontiguousarray(inputs["L1_ffn_b2"])
    maps = []
    for c in range(8):
        b, h = c // 2, c % 2
        m = common_inmap(consts)
        m["x4_in"] = np.asarray(x4_arrs[c])
        m["q_in"] = np.asarray(q_arrs[c])
        m["k_in"] = np.concatenate([np.asarray(k_arrs[2 * b]),
                                    np.asarray(k_arrs[2 * b + 1])], axis=1)
        m["v_in"] = np.concatenate([np.asarray(v_arrs[2 * b]),
                                    np.asarray(v_arrs[2 * b + 1])], axis=1)
        m["woT_in"] = woT
        m["rw_in"] = rw
        m["rb_in"] = rb
        m["w1_in"] = w1
        m["w2_in"] = w2
        m["b1_in"] = b1
        m["b2_in"] = b2
        maps.append(m)
    return maps


def kernel_multilaunch(**inputs):
    consts = host_consts()
    cores = list(range(8))

    ncA = build_launch_A()
    rA = run_bass_kernel_spmd(ncA, inmaps_launch_A(inputs, consts), core_ids=cores)
    x1 = [rA.results[c]["x1_out"] for c in range(8)]
    xn2 = [rA.results[c]["xn2_out"] for c in range(8)]

    ncB = build_launch_B()
    rB = run_bass_kernel_spmd(ncB, inmaps_launch_B(inputs, consts, xn2), core_ids=cores)
    u = [rB.results[c]["u_out"] for c in range(8)]
    res = [rB.results[c]["res_out"] for c in range(8)]
    xd = [rB.results[c]["xdbl_out"] for c in range(8)]

    ncC = build_launch_C()
    rC = run_bass_kernel_spmd(ncC, inmaps_launch_C(inputs, consts, u, res, xd), core_ids=cores)
    part = [rC.results[c]["part_out"] for c in range(8)]

    ncD = build_launch_D()
    rD = run_bass_kernel_spmd(ncD, inmaps_launch_D(inputs, consts, x1, part), core_ids=cores)
    x4 = [rD.results[c]["x4_out"] for c in range(8)]
    qq = [rD.results[c]["q_out"] for c in range(8)]
    kk = [rD.results[c]["k_out"] for c in range(8)]
    vv = [rD.results[c]["v_out"] for c in range(8)]

    ncE = build_launch_E()
    rE = run_bass_kernel_spmd(ncE, inmaps_launch_E(inputs, consts, x4, qq, kk, vv),
                              core_ids=cores)
    out = np.empty((B, S, D), np.float32)
    for c in range(8):
        b, h = c // 2, c % 2
        out[b, h * TOK:(h + 1) * TOK, :] = np.asarray(rE.results[c]["x6_out"]).T
    return out


GROUPS2 = [[0, 1], [2, 3], [4, 5], [6, 7]]
CC_SEMS = ["cc0", "cc1", "cc2", "cc3", "cc4"]


def emit_cc(st, blk, kind, op, in_ap, out_ap, sem_name):
    def _cc(p):
        st.fence_eng(p)
        ins_ = p.collective_compute(kind, op, GROUPS2,
                                    ins=[in_ap.opt()], outs=[out_ap.opt()])
        ins_.then_inc(st.sem[sem_name])
        st.cnt[sem_name] += 1
    blk.gpsimd(_cc)
    fence(st)


def emit_phase_A(nc, st, blk, ct, io):
    T = TOK
    x_fm, xn1 = ct["x_fm"], ct["xn1"]
    wg_f, g3_sb, b3_sb = ct["wg_f"], ct["g3_sb"], ct["b3_sb"]

    def _l0(q):
        st.fence_eng(q)
        st.tag(q.dma_start(out=x_fm.ap().rearrange("p (c u) -> p c u", u=T),
                           in_=fm_in_ap(io["x_in"], NCH, T)), "dl", 16)
        src = AP(io["wg_in"].tensor, 0, [[NE, 128], [NE * 128, NCH], [1, NE]])
        st.tag(q.dma_start(out=wg_f.ap().rearrange("p (c e) -> p c e", e=NE),
                           in_=src), "dl", 16)
        st.tag(q.dma_start(out=ct["lsb"][0:8, 0:128], in_=io["g3_in"]), "dl", 16)
        st.tag(q.dma_start(out=ct["lsb"][0:8, 128:256], in_=io["b3_in"]), "dl", 16)
    blk.sync(_l0)
    fence(st, "tv")

    def _tgb(t):
        st.fence_eng(t, skip=("t",))
        t.transpose(out=ct["PS_A"][:, 0:8], in_=ct["lsb"][0:8, 0:128],
                    identity=ct["ident"][0:8, 0:8])
        st.tag(t.transpose(out=ct["PS_A"][:, 512:520], in_=ct["lsb"][0:8, 128:256],
                           identity=ct["ident"][0:8, 0:8]), "t")
    blk.tensor(_tgb)

    def _egb(s):
        st.fence_eng(s, skip=("s",))
        s.copy(out=g3_sb[:, 0:8], in_=ct["PS_A"][:, 0:8])
        st.tag(s.copy(out=b3_sb[:, 0:8], in_=ct["PS_A"][:, 512:520]), "s")
    blk.scalar(_egb)
    fence(st)

    ln_emit(st, ct, x_fm, T, xn1)
    fence(st)
    gates_emit(st, ct, x_fm, T, wg_f, None, io["gt_dram"])
    fence(st)
    moe_l0_emit(st, ct, x_fm, xn1, T, io["we_in"], io["be_in"], io["gt_dram"])
    fence(st)

    def _sx1(q):
        st.fence_eng(q)
        st.tag(q.dma_start(out=fm_in_ap(io["x1_spill"], NCH, T),
                           in_=x_fm.ap().rearrange("p (c u) -> p c u", u=T)), "ds", 16)
    blk.sync(_sx1)

    xgbf = ct["xg"]
    ds_base = st.cnt["ds"]

    def _pre(v, c, _st=st, _ds0=ds_base):
        if c >= 2:
            _st.w(v, "ds", _ds0 + 16 * (c - 1))
    ln23_emit(st, ct, x_fm, T, g3_sb, b3_sb,
              lambda c: ct["gbb"][:, 0:T] if (c % 2 == 0) else xgbf.ap().bitcast(BF16)[:, 0:T],
              chunk_pre_wait=_pre)
    v_norm0 = st.cnt["v"] - NCH

    def _sxn2(q):
        st.fence_eng(q, skip=("v", "ds"))
        for c in range(NCH):
            st.w(q, "v", v_norm0 + c + 1)
            buf = ct["gbb"][:, 0:T] if (c % 2 == 0) else xgbf.ap().bitcast(BF16)[:, 0:T]
            st.tag(q.dma_start(
                out=AP(io["cc_xn2_in"].tensor, c * 128 * T, [[T, 128], [1, T]]),
                in_=buf), "ds", 16)
        st.w(q, "ds")
    blk.sync(_sxn2)
    fence(st)


def emit_phase_B(nc, st, blk, ct, io):
    T = S
    NT = T // 512
    xn2 = ct["xn2"]
    ubf = ct["ubf"]
    wstage = ct["wstage"]
    wbuf = ct["wbufB"]
    wxf, wxb = ct["wxf"], ct["wxb"]
    upad, uconv = ct["upad"], ct["uconv"]
    ustore, rstore, xdst = ct["ustore"], ct["rstore"], ct["xdst"]
    prev3, cw_sb, cb_sb, bin_sb = ct["prev3"], ct["cw_sb"], ct["cb_sb"], ct["bin_sb"]
    u_out, res_out, xdbl_out = io["u_d"], io["res_d"], io["cc_xd_in"]
    win_in = io["win_in"]

    def _l0(q):
        st.fence_eng(q)
        for hh in range(2):
            st.tag(q.dma_start(
                out=xn2.ap().rearrange("p (c hh t) -> p c hh t", hh=2, t=TOK)[:, :, hh, :],
                in_=AP(io["cc_xn2_out"].tensor, hh * D * TOK,
                       [[TOK, 128], [128 * TOK, NCH], [1, TOK]])), "dl", 16)
        st.tag(q.dma_start(out=wxf.ap().rearrange("p (c u) -> p c u", u=96),
                           in_=AP(io["wx_in"].tensor, 0, [[96, 128], [96 * 128, 8], [1, 96]])),
               "dl", 16)
        for c in range(8):
            st.tag(q.dma_start(out=cw_sb[:, c * 4:(c + 1) * 4],
                               in_=AP(io["convw_in"].tensor, c * 128,
                                      [[1, 128], [DIH, 4]],),
                               allow_slow_non_contiguous=True), "dl", 16)
        st.tag(q.dma_start(out=cb_sb[:, 0:8],
                           in_=AP(io["convb_in"].tensor, 0, [[1, 128], [128, 8]]),
                           allow_slow_non_contiguous=True), "dl", 16)
        st.tag(q.dma_start(out=bin_sb[:, 0:16],
                           in_=AP(io["bin_in"].tensor, 0, [[1, 128], [128, 16]]),
                           allow_slow_non_contiguous=True), "dl", 16)
    blk.sync(_l0)

    def _cwx(p):
        st.fence_eng(p, skip=("p", "ds"))
        st.tag(p.tensor_copy(out=wxb[:, :], in_=wxf[:, :]), "p")
    blk.gpsimd(_cwx)
    fence(st)

    tA, vA, sA, pA = st.cnt["t"], st.cnt["v"], st.cnt["s"], st.cnt["p"]
    cast_p, mm_t, evac_s, conv_v, silu_s, cast_v = {}, {}, {}, {}, {}, {}
    store_ds = {}
    dl_w = {}
    for oc in range(16):
        pA += 1
        cast_p[oc] = pA
        for tt in range(NT):
            tA += 1
            mm_t[(oc, tt)] = tA
            sA += 1
            evac_s[(oc, tt)] = sA
            if oc < 8:
                vA += 3
                conv_v[(oc, tt)] = vA
                sA += 1
                silu_s[(oc, tt)] = sA
                vA += 1
                cast_v[(oc, tt)] = vA

    def _wload0(q):
        st.fence_eng(q, skip=("dl", "ds"))
        for oc2 in range(2):
            ws = "wl%d" % (oc2 % 2)
            src = AP(win_in.tensor, oc2 * 128,
                     [[DI, 128], [128 * DI, NCH], [1, 128]])
            st.tag(q.dma_start(out=wstage[oc2 % 2].ap().rearrange(
                "p (kc j) -> p kc j", j=128), in_=src), ws, 16)
            dl_w[oc2] = (ws, st.cnt[ws])
    blk.sync(_wload0)

    for oc in range(16):
        if oc + 2 < 16:
            def _wpre(q, oc2=oc + 2):
                st.w(q, "p", cast_p[oc2 - 2])
                ws = "wl%d" % (oc2 % 2)
                src = AP(win_in.tensor, oc2 * 128,
                         [[DI, 128], [128 * DI, NCH], [1, 128]])
                st.tag(q.dma_start(out=wstage[oc2 % 2].ap().rearrange(
                    "p (kc j) -> p kc j", j=128), in_=src), ws, 16)
                dl_w[oc2] = (ws, st.cnt[ws])
            blk.sync(_wpre)

        def _cast(p, oc=oc):
            st.w(p, *dl_w[oc])
            if oc >= 2:
                st.w(p, "t", mm_t[(oc - 2, NT - 1)])
            st.tag(p.tensor_copy(out=wbuf[oc % 2][:, :], in_=wstage[oc % 2][:, :]), "p")
            assert st.cnt["p"] == cast_p[oc]
        blk.gpsimd(_cast)
        if oc < 8:
            def _z(v, oc=oc):
                if oc > 0:
                    st.w(v, "s", silu_s[(oc - 1, NT - 1)])
                v.memset(prev3[:, :], 0.0)
            blk.vector(_z)
        for tt in range(NT):
            def _mm(t, oc=oc, tt=tt):
                st.w(t, "p", cast_p[oc])
                bank = (oc * NT + tt) % 4
                prevkey = None
                idx = oc * NT + tt
                if idx >= 4:
                    pidx = idx - 4
                    prevkey = (pidx // NT, pidx % NT)
                if prevkey is not None:
                    st.w(t, "s", evac_s[prevkey])
                for kc in range(NCH):
                    ins = t.matmul(out=ct["PS_A"][:, bank * 512:(bank + 1) * 512],
                                   lhsT=wbuf[oc % 2][:, kc * 128:(kc + 1) * 128],
                                   rhs=xn2[:, kc * T + tt * 512:kc * T + (tt + 1) * 512],
                                   start=(kc == 0), stop=(kc == NCH - 1))
                st.tag(ins, "t")
                assert st.cnt["t"] == mm_t[(oc, tt)]
            blk.tensor(_mm)

            if oc < 8:
                def _evac(s, oc=oc, tt=tt):
                    st.w(s, "t", mm_t[(oc, tt)])
                    if tt > 0:
                        st.w(s, "v", conv_v[(oc, tt - 1)])
                    elif oc > 0:
                        st.w(s, "v", conv_v[(oc - 1, NT - 1)])
                    bank = (oc * NT + tt) % 4
                    st.tag(s.activation(out=upad[:, 3:515],
                                        in_=ct["PS_A"][:, bank * 512:(bank + 1) * 512],
                                        func=AFT.Identity, bias=bin_sb[:, oc:oc + 1]), "s")
                    assert st.cnt["s"] == evac_s[(oc, tt)]
                blk.scalar(_evac)

                def _conv(v, oc=oc, tt=tt):
                    st.w(v, "s", evac_s[(oc, tt)])
                    st.tag(v.tensor_copy(out=upad[:, 0:3], in_=prev3[:, :]), "v")
                    st.w(v, "v")
                    v.tensor_scalar(out=uconv[:, :], in0=upad[:, 0:512],
                                    scalar1=cw_sb[:, oc * 4:oc * 4 + 1],
                                    scalar2=None, op0=ALU.mult)
                    for i in range(1, 4):
                        ins = v.scalar_tensor_tensor(out=uconv[:, :], in0=upad[:, i:i + 512],
                                                     scalar=cw_sb[:, oc * 4 + i:oc * 4 + i + 1],
                                                     in1=uconv[:, :], op0=ALU.mult, op1=ALU.add)
                    st.tag(ins, "v")
                    st.tag(v.tensor_copy(out=prev3[:, :], in_=upad[:, 512:515]), "v")
                    st.w(v, "v")
                    assert st.cnt["v"] == conv_v[(oc, tt)]
                blk.vector(_conv)

                def _silu(s, oc=oc, tt=tt):
                    st.w(s, "v", conv_v[(oc, tt)])
                    idx = oc * NT + tt
                    if idx >= 2:
                        pidx = idx - 2
                        st.w(s, *store_ds[(pidx // NT, pidx % NT, "u")])
                        st.w(s, "v", cast_v[(pidx // NT, pidx % NT)])
                    st.tag(s.activation(out=ustore[idx % 2][:, :], in_=uconv[:, :],
                                        func=AFT.Silu, bias=cb_sb[:, oc:oc + 1]), "s")
                    assert st.cnt["s"] == silu_s[(oc, tt)]
                blk.scalar(_silu)

                def _ucast(v, oc=oc, tt=tt):
                    st.w(v, "s", silu_s[(oc, tt)])
                    idx = oc * NT + tt
                    st.tag(v.tensor_copy(out=ubf[:, oc * T + tt * 512:oc * T + (tt + 1) * 512],
                                         in_=ustore[idx % 2][:, :]), "v")
                    assert st.cnt["v"] == cast_v[(oc, tt)]
                blk.vector(_ucast)

                def _ustore(q, oc=oc, tt=tt):
                    st.w(q, "s", silu_s[(oc, tt)])
                    idx = oc * NT + tt
                    us = "us%d" % (idx % 2)
                    st.tag(q.dma_start(
                        out=AP(u_out.tensor, oc * 128 * T + tt * 512, [[T, 128], [1, 512]]),
                        in_=ustore[idx % 2][:, :]), us, 16)
                    store_ds[(oc, tt, "u")] = (us, st.cnt[us])
                blk.sync(_ustore)
            else:
                def _evac(s, oc=oc, tt=tt):
                    st.w(s, "t", mm_t[(oc, tt)])
                    idx = (oc - 8) * NT + tt
                    if idx >= 2:
                        pidx = idx - 2
                        st.w(s, *store_ds[((pidx // NT) + 8, pidx % NT, "r")])
                    bank = (oc * NT + tt) % 4
                    st.tag(s.activation(out=rstore[idx % 2][:, :],
                                        in_=ct["PS_A"][:, bank * 512:(bank + 1) * 512],
                                        func=AFT.Identity, bias=bin_sb[:, oc:oc + 1]), "s")
                    assert st.cnt["s"] == evac_s[(oc, tt)]
                blk.scalar(_evac)

                def _rstore(q, oc=oc, tt=tt):
                    st.w(q, "s", evac_s[(oc, tt)])
                    idx = (oc - 8) * NT + tt
                    rs = "rs%d" % (idx % 2)
                    st.tag(q.dma_start(
                        out=AP(res_out.tensor, (oc - 8) * 128 * T + tt * 512,
                               [[T, 128], [1, 512]]),
                        in_=rstore[idx % 2][:, :]), rs, 16)
                    store_ds[(oc, tt, "r")] = (rs, st.cnt[rs])
                blk.sync(_rstore)
    fence(st)

    xd_t, xd_s, xd_ds = {}, {}, {}
    tA2, sA2 = st.cnt["t"], st.cnt["s"]
    for tt in range(NT):
        tA2 += 1
        xd_t[tt] = tA2
        sA2 += 1
        xd_s[tt] = sA2

    for tt in range(NT):
        def _xmm(t, tt=tt):
            if tt >= 2:
                st.w(t, "s", xd_s[tt - 2])
            bank = tt % 2
            for kc in range(8):
                ins = t.matmul(out=ct["PS_B"][0:96, bank * 512:bank * 512 + 512],
                               lhsT=wxb[:, kc * 96:(kc + 1) * 96],
                               rhs=ubf[:, kc * T + tt * 512:kc * T + (tt + 1) * 512],
                               start=(kc == 0), stop=(kc == 7))
            st.tag(ins, "t")
        blk.tensor(_xmm)

        def _xevac(s, tt=tt):
            st.w(s, "t", xd_t[tt])
            if tt >= 2:
                st.w(s, *xd_ds[tt - 2])
            bank = tt % 2
            st.tag(s.copy(out=xdst[tt % 2][0:96, :],
                          in_=ct["PS_B"][0:96, bank * 512:bank * 512 + 512]), "s")
        blk.scalar(_xevac)

        def _xstore(q, tt=tt):
            st.w(q, "s", xd_s[tt])
            xs = "xs%d" % (tt % 2)
            st.tag(q.dma_start(
                out=AP(xdbl_out.tensor, tt * 512, [[T, 96], [1, 512]]),
                in_=xdst[tt % 2][0:96, :]), xs, 16)
            xd_ds[tt] = (xs, st.cnt[xs])
        blk.sync(_xstore)
    fence(st)


def emit_phase_C(nc, st, blk, ct, io):
    T = S
    NCK = T // 512
    PS_A, PS_B = ct["PS_A"], ct["PS_B"]
    u_in, res_in = io["u_d"], io["res_d"]
    xdbl_d = io["cc_xd_out"]
    delta_d, dtu_d, a_d = io["delta_d"], io["dtu_d"], io["a_d"]
    part_out = io["cc_part_in"]
    u_ck, res_ck = ct["u_ck"], ct["res_ck"]
    delta, dtu = ct["delta"], ct["dtu"]
    xdsum, wout, wdt, bones, yg = ct["xdsum"], ct["wout"], ct["wdt"], ct["bones"], ct["yg"]
    A_il, asb, Dv, bdt_sb, hstate = ct["A_il"], ct["asb"], ct["Dv"], ct["bdt_sb"], ct["hstate"]
    dlt16, dtu16, B16, C16 = ct["dlt16"], ct["dtu16"], ct["B16"], ct["C16"]
    dAt, dtuB, hbuf, hCr = ct["dAt"], ct["dtuB"], ct["hbuf"], ct["hCr"]
    ydt, sut, outst = ct["ydt"], ct["sut"], ct["outst"]

    def _l0(q):
        st.fence_eng(q)
        st.tag(q.dma_start(out=wdt.ap().rearrange("p (c u) -> p c u", u=128),
                           in_=AP(io["wdt_in"].tensor, 0, [[DIH, 64], [128, 8], [1, 128]])), "dl", 16)
        st.tag(q.dma_start(
            out=wout.ap().rearrange("p (kc oc u) -> p kc oc u", oc=8, u=128),
            in_=AP(io["wout_in"].tensor, 0, [[D, 128], [128 * D, 8], [128, 8], [1, 128]])), "dl", 16)
        st.tag(q.dma_start(out=bones[:, :], in_=io["bones_in"]), "dl", 16)
        st.tag(q.dma_start(out=asb.ap().rearrange("p (c u) -> p c u", u=NST),
                           in_=AP(io["alog_in"].tensor, 0, [[NST, 128], [NST * 128, 8], [1, NST]])), "dl", 16)
        st.tag(q.dma_start(out=Dv[:, 0:8],
                           in_=AP(io["dvec_in"].tensor, 0, [[1, 128], [128, 8]]),
                           allow_slow_non_contiguous=True), "dl", 16)
        st.tag(q.dma_start(out=bdt_sb[:, 0:8],
                           in_=AP(io["bdt_in"].tensor, 0, [[1, 128], [128, 8]]),
                           allow_slow_non_contiguous=True), "dl", 16)
        st.tag(q.dma_start(out=xdsum.ap().bitcast(F32), in_=xdbl_d), "dl", 16)
    blk.sync(_l0)
    fence(st, "vs")

    def _xds(v):
        st.tag(v.memset(hstate[:, :], 0.0), "v")
    blk.vector(_xds)

    def _aprep(s):
        st.fence_eng(s, skip=("s",))
        st.tag(s.activation(out=asb[:, :], in_=asb[:, :], func=AFT.Exp), "s")
    blk.scalar(_aprep)

    def _aneg(v):
        st.fence_eng(v, skip=("v",))
        st.tag(v.tensor_scalar(out=asb[:, :], in0=asb[:, :], scalar1=-1.0,
                               scalar2=None, op0=ALU.mult), "v")
    blk.vector(_aneg)

    def _adma(q):
        st.fence_eng(q)
        st.tag(q.dma_start(out=a_d, in_=asb.ap().rearrange("p (c u) -> p c u", u=NST)), "ds", 16)
        st.w(q, "ds")
        for j in range(128):
            st.tag(q.dma_start(out=A_il[:, j:j + 1],
                               in_=AP(a_d.tensor, 8 * j * NST, [[NST, 8], [1, NST], [0, 1]])),
                   "dl", 16)
    blk.sync(_adma)
    fence(st)

    for ck in range(NCK):
        def _cl(q, ck=ck):
            if ck > 0:
                st.fence_eng(q, skip=("dl", "ds"))
            for src_d, dst in ((u_in, u_ck), (res_in, res_ck)):
                st.tag(q.dma_start(
                    out=dst.ap().rearrange("p (c u) -> p c u", u=512),
                    in_=AP(src_d.tensor, ck * 512, [[T, 128], [128 * T, 8], [1, 512]])),
                    "dl", 16)
        blk.sync(_cl)

        dmm_t, dsp_s, dsp_v = {}, {}, {}
        tA, sA, vA = st.cnt["t"], st.cnt["s"], st.cnt["v"]
        for dc in range(8):
            tA += 1
            dmm_t[dc] = tA
            sA += 4
            dsp_s[dc] = sA
            vA += 1
            dsp_v[dc] = vA
        for dc in range(8):
            def _dmm(t, ck=ck, dc=dc):
                if dc == 0:
                    st.fence_eng(t, skip=("t",))
                if dc >= 2:
                    st.w(t, "s", dsp_s[dc - 2])
                ins = t.matmul(out=PS_A[:, (dc % 2) * 512:(dc % 2 + 1) * 512],
                               lhsT=wdt[:, dc * 128:(dc + 1) * 128],
                               rhs=xdsum[0:64, ck * 512:(ck + 1) * 512],
                               start=True, stop=True)
                st.tag(ins, "t")
                assert st.cnt["t"] == dmm_t[dc]
            blk.tensor(_dmm)

            def _dsp(s, ck=ck, dc=dc):
                st.w(s, "t", dmm_t[dc])
                if dc >= 1:
                    st.w(s, "v", dsp_v[dc - 1])
                pv = PS_A[:, (dc % 2) * 512:(dc % 2 + 1) * 512]
                s.activation(out=sut[:, :], in_=pv, func=AFT.Abs,
                             bias=bdt_sb[:, dc:dc + 1])
                s.activation(out=sut[:, :], in_=sut[:, :], func=AFT.Exp, scale=-1.0)
                s.activation(out=sut[:, :], in_=sut[:, :], func=AFT.Ln, bias=1.0)
                st.tag(s.activation(out=ydt[:, :], in_=pv, func=AFT.Relu,
                                    bias=bdt_sb[:, dc:dc + 1]), "s", 4)
                assert st.cnt["s"] == dsp_s[dc]
            blk.scalar(_dsp)

            def _dv(v, ck=ck, dc=dc):
                st.w(v, "s", dsp_s[dc])
                st.tag(v.tensor_add(out=delta[:, dc * 512:(dc + 1) * 512],
                                    in0=sut[:, :], in1=ydt[:, :]), "v")
                assert st.cnt["v"] == dsp_v[dc]
            blk.vector(_dv)

        def _dtu(v, ck=ck):
            st.fence_eng(v, skip=("v",))
            for kc in range(8):
                ins = v.tensor_mul(out=dtu[:, kc * 512:(kc + 1) * 512],
                                   in0=delta[:, kc * 512:(kc + 1) * 512],
                                   in1=u_ck[:, kc * 512:(kc + 1) * 512])
            st.tag(ins, "v")
        blk.vector(_dtu)

        def _dst(q, ck=ck):
            st.fence_eng(q, skip=("dl", "ds", "sc0", "sc1", "bc", "dd"))
            st.tag(q.dma_start(
                out=AP(delta_d.tensor, ck * 512, [[T, 128], [128 * T, 8], [1, 512]]),
                in_=delta.ap().rearrange("p (c u) -> p c u", u=512)), "dd", 16)
            st.tag(q.dma_start(
                out=AP(dtu_d.tensor, ck * 512, [[T, 128], [128 * T, 8], [1, 512]]),
                in_=dtu.ap().rearrange("p (c u) -> p c u", u=512)), "dd", 16)
            st.w(q, "dd")
            st.tag(q.dma_start(out=B16[:, :],
                               in_=AP(xdbl_d.tensor, 64 * T + ck * 512,
                                      [[0, 8], [T, 16], [1, 512]])), "bc", 16)
            st.tag(q.dma_start(out=C16[:, :],
                               in_=AP(xdbl_d.tensor, 80 * T + ck * 512,
                                      [[0, 8], [T, 16], [1, 512]])), "bc", 16)
        blk.sync(_dst)

        base = dict(s=st.cnt["s"], v=st.cnt["v"], t=st.cnt["t"])
        sdl, ss, sv, stt = {}, {}, {}, {}
        sA2, vA2, tA2 = base["s"], base["v"], base["t"]
        scc = {0: st.cnt["sc0"], 1: st.cnt["sc1"]}
        for j in range(128):
            scc[j % 2] += 32
            sdl[j] = ("sc%d" % (j % 2), scc[j % 2])
            sA2 += 1
            ss[j] = sA2
            vA2 += 2
            sv[j] = vA2
            tA2 += 1
            stt[j] = tA2

        def _sdma(q, ck=ck):
            for j in range(128):
                if j >= 2:
                    st.w(q, "s", ss[j - 2])
                    st.w(q, "v", sv[j - 2])
                scs = "sc%d" % (j % 2)
                st.tag(q.dma_start(out=dlt16[j % 2][:, :],
                                   in_=AP(delta_d.tensor, 8 * j * T + ck * 512,
                                          [[T, 8], [0, 16], [1, 512]])), scs, 16)
                st.tag(q.dma_start(out=dtu16[j % 2][:, :],
                                   in_=AP(dtu_d.tensor, 8 * j * T + ck * 512,
                                          [[T, 8], [0, 16], [1, 512]])), scs, 16)
                assert st.cnt[scs] == sdl[j][1]
        blk.sync(_sdma)

        def _sact(s, ck=ck):
            for j in range(128):
                st.w(s, *sdl[j])
                if j == 0:
                    st.w(s, "bc", st.cnt["bc"])
                if j >= 2:
                    st.w(s, "v", sv[j - 2])
                st.tag(s.activation(out=dAt[j % 2][:, :], in_=dlt16[j % 2][:, :],
                                    func=AFT.Exp, scale=A_il[:, j:j + 1]), "s")
        blk.scalar(_sact)

        def _sv_(v, ck=ck):
            for j in range(128):
                st.w(v, "s", ss[j])
                st.w(v, *sdl[j])
                if j == 0:
                    st.w(v, "bc", st.cnt["bc"])
                if j >= 2:
                    st.w(v, "t", stt[j - 2])
                v.tensor_mul(out=dtuB[j % 2][:, :], in0=dtu16[j % 2][:, :],
                             in1=B16[:, :])
                v.tensor_tensor_scan(out=hbuf[j % 2][:, :], data0=dAt[j % 2][:, :],
                                     data1=dtuB[j % 2][:, :],
                                     initial=hstate[:, j:j + 1],
                                     op0=ALU.mult, op1=ALU.add)
                st.tag(v.tensor_copy(out=hstate[:, j:j + 1],
                                     in_=hbuf[j % 2][:, 511:512]), "v")
                st.tag(v.tensor_mul(out=hCr[j % 2][:, :], in0=hbuf[j % 2][:, :],
                                    in1=C16[:, :]), "v")
        blk.vector(_sv_)

        def _st_(t, ck=ck):
            if ck > 0:
                st.w(t, "s", ct["_prev_wev7"])
            for j in range(128):
                st.w(t, "v", sv[j])
                st.tag(t.matmul(out=(PS_A[:, (j // 16) * 512:(j // 16 + 1) * 512] if j < 64 else
                                     PS_B[:, ((j // 16) - 4) * 512:((j // 16) - 3) * 512]),
                                lhsT=bones[:, (j % 16) * 128:(j % 16 + 1) * 128],
                                rhs=hCr[j % 2][:, :],
                                start=(j % 16 == 0), stop=(j % 16 == 15)), "t")
        blk.tensor(_st_)

        yg_v = {}
        for oc in range(8):
            def _yev(v, ck=ck, oc=oc):
                if oc == 0:
                    st.w(v, "t", stt[127])
                    st.w(v, "s", st.cnt["s"])
                pv = (PS_A[:, oc * 512:(oc + 1) * 512] if oc < 4 else
                      PS_B[:, (oc - 4) * 512:(oc - 3) * 512])
                st.tag(v.scalar_tensor_tensor(out=ydt[:, :],
                                              in0=u_ck[:, oc * 512:(oc + 1) * 512],
                                              scalar=Dv[:, oc:oc + 1], in1=pv,
                                              op0=ALU.mult, op1=ALU.add), "v")
            blk.vector(_yev)

            def _ysu(s, ck=ck, oc=oc):
                st.w(s, "v", st.cnt["v"])
                st.tag(s.activation(out=sut[:, :], in_=res_ck[:, oc * 512:(oc + 1) * 512],
                                    func=AFT.Silu), "s")
            blk.scalar(_ysu)

            def _ymu(v, ck=ck, oc=oc):
                st.w(v, "s", st.cnt["s"])
                st.tag(v.tensor_mul(out=yg[:, oc * 512:(oc + 1) * 512],
                                    in0=ydt[:, :], in1=sut[:, :]), "v")
                yg_v[oc] = st.cnt["v"]
            blk.vector(_ymu)

        wev_s = {}
        for outc in range(8):
            def _wmm(t, ck=ck, outc=outc):
                if outc == 0:
                    st.w(t, "v", yg_v[7])
                if outc >= 4:
                    st.w(t, "s", wev_s[outc - 4])
                for kc in range(8):
                    ins = t.matmul(out=PS_A[:, (outc % 4) * 512:(outc % 4 + 1) * 512],
                                   lhsT=wout[:, (kc * 8 + outc) * 128:(kc * 8 + outc + 1) * 128],
                                   rhs=yg[:, kc * 512:(kc + 1) * 512],
                                   start=(kc == 0), stop=(kc == 7))
                st.tag(ins, "t")
            blk.tensor(_wmm)

            def _wev(s, ck=ck, outc=outc):
                st.w(s, "t", st.cnt["t"])
                if outc >= 2:
                    st.w(s, "ds", st.cnt["ds"])
                st.tag(s.copy(out=outst[outc % 2][:, :],
                              in_=PS_A[:, (outc % 4) * 512:(outc % 4 + 1) * 512]), "s")
                wev_s[outc] = st.cnt["s"]
            blk.scalar(_wev)

            def _wst(q, ck=ck, outc=outc):
                st.w(q, "s", wev_s[outc])
                st.tag(q.dma_start(
                    out=AP(part_out.tensor,
                           (ck // 4) * D * TOK + outc * 128 * TOK + (ck % 4) * 512,
                           [[TOK, 128], [1, 512]]),
                    in_=outst[outc % 2][:, :]), "ds", 16)
            blk.sync(_wst)
        ct["_prev_wev7"] = wev_s[7]
    fence(st)


def emit_phase_D(nc, st, blk, ct, io):
    T = TOK
    NT = T // 512
    PS_A, PS_B = ct["PS_A"], ct["PS_B"]
    x_fm, xnb = ct["x_fm"], ct["xnb"]
    wg_f, rw_f, rb_sb, bout_sb = ct["wg_f"], ct["rw_f"], ct["rb_sb"], ct["bout_sb"]
    pb0 = ct["w1b"].bitcast(F32)
    cos_sb = ct["wbuf"][0].ap().bitcast(F32)
    sin_sb = ct["wbuf"][1].ap().bitcast(F32)
    qst = ct["qst"]
    gt_dram = io["gt_dram"]

    def _l0(q):
        st.fence_eng(q)
        st.tag(q.dma_start(out=x_fm.ap().rearrange("p (c u) -> p c u", u=T),
                           in_=fm_in_ap(io["x1_spill"], NCH, T)), "dl", 16)
        for w_, d_ in ((io["wg_in"], wg_f), (io["rw_in"], rw_f)):
            src = AP(w_.tensor, 0, [[NE, 128], [NE * 128, NCH], [1, NE]])
            st.tag(q.dma_start(out=d_.ap().rearrange("p (c e) -> p c e", e=NE),
                               in_=src), "dl", 16)
        st.tag(q.dma_start(out=rb_sb[0:1, :], in_=io["rb_in"]), "dl", 16)
        st.tag(q.dma_start(out=bout_sb[:, 0:8],
                           in_=AP(io["bout_in"].tensor, 0, [[1, 128], [128, 8]]),
                           allow_slow_non_contiguous=True), "dl", 16)
    blk.sync(_l0)
    fence(st)

    ln_stats_emit(st, ct, x_fm, T)
    fence(st)
    bcast_rows_emit(st, ct, ct["r_mean"], ct["r_rstd"], T)
    fence(st, "q")
    pld, cmb_v = {}, {}

    def _pl2(q):
        for c in range(NCH):
            if c >= 1:
                st.w(q, "v", cmb_v[c - 1])
            st.tag(q.dma_start(out=pb0[:, 0:T],
                               in_=AP(io["cc_part_out"].tensor, c * 128 * T,
                                      [[T, 128], [1, T]])), "pc0", 16)
            pld[c] = ("pc0", st.cnt["pc0"])
    vA = st.cnt["v"]
    for c in range(NCH):
        vA += 1
        cmb_v[c] = vA
    blk.sync(_pl2)

    def _cmb(v):
        st.fence_eng(v, skip=("v", "pc0"))
        for c in range(NCH):
            st.w(v, *pld[c])
            xc = x_fm[:, c * T:(c + 1) * T]
            v.tensor_sub(out=ct["tmpa"][:, 0:T], in0=xc, in1=ct["PS_A"][:, 0:T])
            v.tensor_mul(out=ct["tmpb"][:, 0:T], in0=ct["tmpa"][:, 0:T],
                         in1=ct["PS_B"][:, 0:T])
            v.scalar_tensor_tensor(out=ct["tmpb"][:, 0:T], in0=ct["tmpb"][:, 0:T],
                                   scalar=bout_sb[:, c:c + 1], in1=xc,
                                   op0=ALU.add, op1=ALU.add)
            st.tag(v.tensor_add(out=xc, in0=ct["tmpb"][:, 0:T], in1=pb0[:, 0:T]), "v")
            assert st.cnt["v"] == cmb_v[c]
    blk.vector(_cmb)
    fence(st)
    if "dbg_x2" in io:
        def _dx2(q):
            st.fence_eng(q)
            st.tag(q.dma_start(out=fm_in_ap(io["dbg_x2"], NCH, T),
                               in_=x_fm.ap().rearrange("p (c u) -> p c u", u=T)), "ds", 16)
        blk.sync(_dx2)
        fence(st, "q")

    ln_emit(st, ct, x_fm, T, xnb)
    fence(st)
    gates_emit(st, ct, x_fm, T, wg_f, None, gt_dram)
    fence(st)
    moe_l0_emit(st, ct, x_fm, xnb, T, io["we_in"], io["be_in"], gt_dram)
    fence(st)
    if "dbg_x3" in io:
        def _dx3(q):
            st.fence_eng(q)
            st.tag(q.dma_start(out=fm_in_ap(io["dbg_x3"], NCH, T),
                               in_=x_fm.ap().rearrange("p (c u) -> p c u", u=T)), "ds", 16)
        blk.sync(_dx3)
        fence(st, "q")

    ln_emit(st, ct, x_fm, T, xnb)
    fence(st)
    gates_emit(st, ct, x_fm, T, rw_f, rb_sb, gt_dram)
    fence(st)
    moe_l1_emit(st, ct, x_fm, xnb, T, io["w1_in"], io["w2_in"], io["b1_in"],
                io["b2_in"], gt_dram)
    fence(st)
    if "dbg_x4" in io:
        def _dx4(q):
            st.fence_eng(q)
            st.tag(q.dma_start(out=fm_in_ap(io["dbg_x4"], NCH, T),
                               in_=x_fm.ap().rearrange("p (c u) -> p c u", u=T)), "ds", 16)
        blk.sync(_dx4)
        fence(st, "q")

    ln_emit(st, ct, x_fm, T, xnb)
    fence(st)
    if "dbg_xn5" in io:
        def _dxn5(q):
            st.fence_eng(q)
            st.tag(q.dma_start(out=fm_in_ap(io["dbg_xn5"], NCH, T),
                               in_=xnb.ap().rearrange("p (c u) -> p c u", u=T)), "ds", 16)
        blk.sync(_dxn5)
        fence(st, "q")

    def _lrope(q):
        st.fence_eng(q)
        st.tag(q.dma_start(out=cos_sb[:, 0:T], in_=io["cos_in"]), "dl", 16)
        st.tag(q.dma_start(out=sin_sb[:, 0:T], in_=io["sin_in"]), "dl", 16)
    blk.sync(_lrope)
    fence(st)

    swapmask = []
    for i in range(16):
        swapmask += [2 * i + 1, 2 * i]
    qst_hist = []
    for pj, (wT, dst, dorope) in enumerate(
            [(io["wqT_in"], io["q_d"], True), (io["wkT_in"], io["cc_k_in"], True),
             (io["wvT_in"], io["cc_v_in"], False)]):
        tA, sA, vA, pA = st.cnt["t"], st.cnt["s"], st.cnt["v"], st.cnt["p"]
        qmm_t, qev_s, qrp_v, qst_ds = {}, {}, {}, {}
        wcast_p, wld = {}, {}
        sA_, vA_, tA_, pA_ = sA, vA, tA, pA
        for oc in range(NCH):
            pA_ += 1
            wcast_p[oc] = pA_
            for tt in range(NT):
                tA_ += 1
                qmm_t[(oc, tt)] = tA_
                sA_ += 1
                qev_s[(oc, tt)] = sA_
                if dorope:
                    vA_ += 2
                    qrp_v[(oc, tt)] = vA_

        def _wl(q, wT=wT, pj=pj, wcast_p=wcast_p, wld=wld):
            st.fence_eng(q, skip=("dl", "ds", "wl0", "wl1", "pc0", "pc1",
                                  "ge0", "ge1"))
            for oc in range(NCH):
                ws = "wl%d" % (oc % 2)
                if oc >= 2:
                    st.w(q, "p", wcast_p[oc - 2])
                src = AP(wT.tensor, oc * 128, [[D, 128], [128 * D, 8], [1, 128]])
                st.tag(q.dma_start(out=ct["wstage"][oc % 2].ap().rearrange(
                    "p (kc j) -> p kc j", j=128), in_=src), ws, 16)
                wld[oc] = (ws, st.cnt[ws])
        blk.sync(_wl)

        for oc in range(NCH):
            def _wc(p, oc=oc, wld=wld, wcast_p=wcast_p, qmm_t=qmm_t):
                st.w(p, *wld[oc])
                if oc >= 2:
                    st.w(p, "t", qmm_t[(oc - 2, NT - 1)])
                st.tag(p.tensor_copy(out=ct["xg"][:, (oc % 2) * 1024:(oc % 2) * 1024 + 1024],
                                     in_=ct["wstage"][oc % 2][:, :]), "p")
                assert st.cnt["p"] == wcast_p[oc]
            blk.gpsimd(_wc)
            for tt in range(NT):
                def _qmm(t, oc=oc, tt=tt, wcast_p=wcast_p, qmm_t=qmm_t, qev_s=qev_s):
                    st.w(t, "p", wcast_p[oc])
                    idx = oc * NT + tt
                    if idx >= 2:
                        st.w(t, "s", qev_s[((idx - 2) // NT, (idx - 2) % NT)])
                    for kc in range(NCH):
                        ins = t.matmul(out=ct["PS_A"][:, (idx % 2) * 512:(idx % 2) * 512 + 512],
                                       lhsT=ct["xg"][:, (oc % 2) * 1024 + kc * 128:(oc % 2) * 1024 + (kc + 1) * 128],
                                       rhs=xnb[:, kc * T + tt * 512:kc * T + (tt + 1) * 512],
                                       start=(kc == 0), stop=(kc == NCH - 1))
                    st.tag(ins, "t")
                    assert st.cnt["t"] == qmm_t[(oc, tt)]
                blk.tensor(_qmm)

                if dorope:
                    def _qev(s, oc=oc, tt=tt, qmm_t=qmm_t, qev_s=qev_s, qrp_v=qrp_v):
                        st.w(s, "t", qmm_t[(oc, tt)])
                        idx = oc * NT + tt
                        if idx >= 1:
                            st.w(s, "v", qrp_v[((idx - 1) // NT, (idx - 1) % NT)])
                        st.tag(s.copy(out=ct["tmpa"][:, 0:512],
                                      in_=ct["PS_A"][:, (idx % 2) * 512:(idx % 2) * 512 + 512]), "s")
                        assert st.cnt["s"] == qev_s[(oc, tt)]
                    blk.scalar(_qev)

                    def _qrp(v, oc=oc, tt=tt, pj=pj, qev_s=qev_s, qrp_v=qrp_v, dst=dst):
                        st.w(v, "s", qev_s[(oc, tt)])
                        idx = oc * NT + tt
                        gidx = pj * NCH * NT + idx
                        if gidx >= 2:
                            st.w(v, *qst_hist[gidx - 2])
                        v.stream_shuffle(out=ct["tmpa"][:, 512:1024],
                                         in_=ct["tmpa"][:, 0:512], mask=swapmask)
                        st.tag(v.tensor_mul(out=ct["tmpa"][:, 512:1024],
                                            in0=ct["tmpa"][:, 512:1024],
                                            in1=sin_sb[:, tt * 512:(tt + 1) * 512]), "v")
                        st.w(v, "v")
                        v.tensor_mul(out=ct["tmpa"][:, 0:512], in0=ct["tmpa"][:, 0:512],
                                     in1=cos_sb[:, tt * 512:(tt + 1) * 512])
                        st.tag(v.tensor_add(out=qst[idx % 2][:, :],
                                            in0=ct["tmpa"][:, 0:512],
                                            in1=ct["tmpa"][:, 512:1024]), "v")
                        assert st.cnt["v"] == qrp_v[(oc, tt)]
                    blk.vector(_qrp)

                    def _qst(q, oc=oc, tt=tt, qrp_v=qrp_v, dst=dst):
                        st.w(q, "v", qrp_v[(oc, tt)])
                        idx = oc * NT + tt
                        pc_ = "pc%d" % (idx % 2)
                        st.tag(q.dma_start(
                            out=AP(dst.tensor, oc * 128 * T + tt * 512, [[T, 128], [1, 512]]),
                            in_=qst[idx % 2][:, :]), pc_, 16)
                        qst_hist.append((pc_, st.cnt[pc_]))
                    blk.gpsimd(_qst)
                else:
                    def _qev(s, oc=oc, tt=tt, pj=pj, qmm_t=qmm_t, qev_s=qev_s):
                        st.w(s, "t", qmm_t[(oc, tt)])
                        idx = oc * NT + tt
                        gidx = pj * NCH * NT + idx
                        if gidx >= 2:
                            st.w(s, *qst_hist[gidx - 2])
                        st.tag(s.copy(out=qst[idx % 2][:, :],
                                      in_=ct["PS_A"][:, (idx % 2) * 512:(idx % 2) * 512 + 512]), "s")
                        assert st.cnt["s"] == qev_s[(oc, tt)]
                    blk.scalar(_qev)

                    def _qst(q, oc=oc, tt=tt, qev_s=qev_s, dst=dst):
                        st.w(q, "s", qev_s[(oc, tt)])
                        idx = oc * NT + tt
                        pc_ = "pc%d" % (idx % 2)
                        st.tag(q.dma_start(
                            out=AP(dst.tensor, oc * 128 * T + tt * 512, [[T, 128], [1, 512]]),
                            in_=qst[idx % 2][:, :]), pc_, 16)
                        qst_hist.append((pc_, st.cnt[pc_]))
                    blk.gpsimd(_qst)
    fence(st)


def emit_phase_E(nc, st, blk, ct, io):
    T = TOK
    NT = T // 512
    NQS = T // 512
    PS_A, PS_B = ct["PS_A"], ct["PS_B"]
    x_fm = ct["x_fm"]
    att, big1 = ct["att"], ct["big1"]
    A1, A3, A4, A5, P_sb = ct["A1"], ct["A3"], ct["A4"], ct["A5"], ct["P_sb"]
    rw_f, rb_sb = ct["rw_f"], ct["rb_sb"]
    rsum, nmx, rla = ct["rsum"], ct["nmx"], ct["rla"]
    rlrow, accf = ct["rlrow"], ct["accf"]
    PAb = PS_A.ap().bitcast(BF16)
    gt_dram = io["gt_dram"]

    def _l0(q):
        st.fence_eng(q)
        src = AP(io["rw_in"].tensor, 0, [[NE, 128], [NE * 128, NCH], [1, NE]])
        st.tag(q.dma_start(out=rw_f.ap().rearrange("p (c e) -> p c e", e=NE),
                           in_=src), "dl", 16)
        st.tag(q.dma_start(out=rb_sb[0:1, :], in_=io["rb_in"]), "dl", 16)
    blk.sync(_l0)
    fence(st)

    hload, hdone_t, hdone_v, hdone_s = {}, {}, {}, {}
    for h in range(NH):
        kbuf = A1.ap()[0:64, 0:S]
        vbuf = A3.ap()[0:64, 0:S]
        qbuf = A4.ap()[0:64, 0:T]

        def _hl(q, h=h, kbuf=kbuf, vbuf=vbuf, qbuf=qbuf):
            kv = "kv%d" % (h % 2)
            if h >= 1:
                st.w(q, "t", hdone_t[h - 1])
                st.w(q, "v", hdone_v[h - 1])
                st.w(q, "s", hdone_s[h - 1])
            st.tag(q.dma_start(
                out=kbuf.rearrange("p (hh t) -> p hh t", hh=2, t=TOK),
                in_=AP(io["cc_kgath"].tensor, h * 64 * TOK,
                       [[TOK, 64], [D * TOK, 2], [1, TOK]])), kv, 16)
            st.tag(q.dma_start(
                out=vbuf.rearrange("p (hh t) -> p hh t", hh=2, t=TOK),
                in_=AP(io["cc_vgath"].tensor, h * 64 * TOK,
                       [[TOK, 64], [D * TOK, 2], [1, TOK]])), kv, 16)
            st.tag(q.dma_start(out=qbuf,
                               in_=AP(io["q_d"].tensor, h * 64 * T, [[T, 64], [1, T]])), kv, 16)
            hload[h] = (kv, st.cnt[kv])
        blk.sync(_hl)

        def _vt(t, h=h, vbuf=vbuf):
            st.w(t, *hload[h])
            if h >= 1:
                st.w(t, "s", hdone_s[h - 1])
            for kt in range(32):
                ins = t.matmul(out=PAb[:, kt * 64:(kt + 1) * 64],
                               lhsT=vbuf[:, kt * 128:(kt + 1) * 128],
                               rhs=ct["identb"][0:64, 0:64],
                               is_transpose=True,
                               start=(kt % 8 == 0), stop=(kt % 8 == 7))
            st.tag(ins, "t")
        blk.tensor(_vt)

        def _vte(s, h=h):
            st.fence_eng(s, skip=("s", "dl", "ds", "kv0", "kv1", "wl0", "wl1",
                                  "ge0", "ge1", "pc0", "pc1", "p"))
            st.tag(s.copy(out=A5[:, 0:2048], in_=PAb[:, 0:2048]), "s")
        blk.scalar(_vte)

        for qs in range(NQS):
            qmax_v, qexp_s, qtp_t, qte_s = {}, {}, {}, {}
            for qt4 in range(4):
                qti = qs * 4 + qt4

                def _qk(t, h=h, qs=qs, qt4=qt4, qti=qti):
                    st.fence_eng(t, skip=("t", "dl", "ds", "kv0", "kv1",
                                          "wl0", "wl1", "ge0", "ge1",
                                          "pc0", "pc1", "p"))
                    st.w(t, *hload[h])
                    kbuf = A1.ap()[0:64, 0:S]
                    qbuf = A4.ap()[0:64, 0:T]
                    for c in range(8):
                        psd = PS_A if c < 4 else PS_B
                        ins = t.matmul(out=psd[:, (c % 4) * 512:(c % 4 + 1) * 512],
                                       lhsT=qbuf[:, qti * 128:(qti + 1) * 128],
                                       rhs=kbuf[:, c * 512:(c + 1) * 512],
                                       start=True, stop=True)
                    st.tag(ins, "t")
                    qtp_t[(qt4, "qk")] = st.cnt["t"]
                blk.tensor(_qk)

                def _mx(v, h=h, qt4=qt4):
                    st.w(v, "t", qtp_t[(qt4, "qk")])
                    v.reduce_max(out=nmx[:, 0:1], in_=PS_A[:, 0:2048], axis=AX.X,
                                 negate=True)
                    st.tag(v.reduce_max(out=nmx[:, 1:2], in_=PS_B[:, 0:2048],
                                        axis=AX.X, negate=True), "v")
                    st.w(v, "v")
                    st.tag(v.tensor_tensor(out=nmx[:, 2:3], in0=nmx[:, 0:1],
                                           in1=nmx[:, 1:2], op=ALU.min), "v")
                    st.w(v, "v")
                    st.tag(v.tensor_scalar(out=nmx[:, 2:3], in0=nmx[:, 2:3],
                                           scalar1=0.125, scalar2=None,
                                           op0=ALU.mult), "v")
                    qmax_v[qt4] = st.cnt["v"]
                blk.vector(_mx)

                def _ex(s, h=h, qt4=qt4):
                    st.w(s, "v", qmax_v[qt4])
                    for c in range(8):
                        psd = PS_A if c < 4 else PS_B
                        ins = s.activation(out=P_sb[:, c * 512:(c + 1) * 512],
                                           in_=psd[:, (c % 4) * 512:(c % 4 + 1) * 512],
                                           func=AFT.Exp, bias=nmx[:, 2:3],
                                           scale=0.125,
                                           accum_out=rsum[:, c:c + 1])
                    st.tag(ins, "s")
                    qexp_s[qt4] = st.cnt["s"]
                blk.scalar(_ex)

                def _l(v, h=h, qt4=qt4):
                    st.w(v, "s", qexp_s[qt4])
                    st.tag(v.reduce_sum(out=rla[:, qt4:qt4 + 1], in_=rsum[:, 0:8],
                                        axis=AX.X), "v")
                    st.w(v, "v")
                    st.tag(v.reciprocal(out=rla[:, qt4:qt4 + 1],
                                        in_=rla[:, qt4:qt4 + 1]), "v")
                blk.vector(_l)

                def _tp(t, h=h, qt4=qt4):
                    st.w(t, "s", qexp_s[qt4])
                    for kt in range(32):
                        ins = t.matmul(out=PAb[:, 2048 + (kt % 16) * 128:2048 + (kt % 16 + 1) * 128]
                                       if kt < 16 else
                                       PS_B.ap().bitcast(BF16)[:, 2048 + (kt % 16) * 128:2048 + (kt % 16 + 1) * 128],
                                       lhsT=P_sb[:, kt * 128:(kt + 1) * 128],
                                       rhs=ct["identb"][:, :], is_transpose=True,
                                       start=(kt % 8 == 0), stop=(kt % 8 == 7))
                    st.tag(ins, "t")
                    qtp_t[qt4] = st.cnt["t"]
                blk.tensor(_tp)

                def _te(s, h=h, qs=qs, qt4=qt4):
                    st.w(s, "t", qtp_t[qt4])
                    for g in range(2):
                        src = (PAb if g == 0 else PS_B.ap().bitcast(BF16))[:, 2048:4096]
                        dst = big1.ap().rearrange("p (kt u) -> p kt u", u=512)[
                            :, g * 16:(g + 1) * 16, qt4 * 128:(qt4 + 1) * 128]
                        ins = s.copy(out=dst, in_=src.rearrange("p (kt u) -> p kt u", u=128))
                    st.tag(ins, "s")
                    qte_s[qt4] = st.cnt["s"]
                blk.scalar(_te)

            def _pv(t, h=h, qs=qs):
                for qt4 in range(4):
                    st.w(t, "s", qte_s[qt4])
                for kt in range(32):
                    ins = t.matmul(out=PS_A[0:64, 1536:2048],
                                   lhsT=A5[:, kt * 64:(kt + 1) * 64],
                                   rhs=big1[:, kt * 512:(kt + 1) * 512],
                                   start=(kt == 0), stop=(kt == 31))
                st.tag(ins, "t")
                hdone_t[h] = st.cnt["t"]
            blk.tensor(_pv)

            def _rlt(t, h=h, qs=qs):
                for qt4 in range(4):
                    ins = t.matmul(out=PS_B[0:1, 512 + qt4 * 128:512 + (qt4 + 1) * 128],
                                   lhsT=rla[:, qt4:qt4 + 1],
                                   rhs=ct["ident"][:, :], is_transpose=True,
                                   start=(qt4 == 0), stop=(qt4 == 3))
                st.tag(ins, "t")
            blk.tensor(_rlt)

            def _rle(s, h=h, qs=qs):
                st.fence_eng(s, skip=("s", "dl", "ds", "kv0", "kv1", "wl0", "wl1",
                                      "ge0", "ge1", "pc0", "pc1", "p"))
                s.copy(out=rlrow[0:1, 0:512], in_=PS_B[0:1, 512:1024])
                st.tag(s.copy(out=accf[0:64, :], in_=PS_A[0:64, 1536:2048]), "s")
                hdone_s[h] = st.cnt["s"]
            blk.scalar(_rle)

            def _rlb(t, h=h, qs=qs):
                st.fence_eng(t, skip=("t", "dl", "ds", "kv0", "kv1", "wl0", "wl1",
                                      "ge0", "ge1", "pc0", "pc1", "p"))
                st.tag(t.matmul(out=PS_B[0:64, 0:512], lhsT=ct["ones_row"][0:1, 0:64],
                                rhs=rlrow[0:1, 0:512], start=True, stop=True), "t")
            blk.tensor(_rlb)

            def _nrm(v, h=h, qs=qs):
                st.fence_eng(v, skip=("v", "dl", "ds", "kv0", "kv1", "wl0", "wl1",
                                      "ge0", "ge1", "pc0", "pc1", "p"))
                row0 = (h % 2) * 64
                dview = att.ap().rearrange("p (c u) -> p c u", u=T)[
                    row0:row0 + 64, h // 2, qs * 512:(qs + 1) * 512]
                st.tag(v.tensor_mul(out=dview, in0=accf[0:64, :],
                                    in1=PS_B[0:64, 0:512]), "v")
                hdone_v[h] = st.cnt["v"]
            blk.vector(_nrm)
    fence(st)

    wo_t, wo_s, wo_p, wld = {}, {}, {}, {}
    tA, sA, pA = st.cnt["t"], st.cnt["s"], st.cnt["p"]
    for oc in range(NCH):
        pA += 1
        wo_p[oc] = pA
        for tt in range(NT):
            tA += 1
            wo_t[(oc, tt)] = tA

    def _wol(q):
        st.fence_eng(q, skip=("dl", "ds", "wl0", "wl1"))
        for oc in range(NCH):
            ws = "wl%d" % (oc % 2)
            if oc >= 2:
                st.w(q, "p", wo_p[oc - 2])
            src = AP(io["woT_in"].tensor, oc * 128, [[D, 128], [128 * D, 8], [1, 128]])
            st.tag(q.dma_start(out=ct["wstage"][oc % 2].ap().rearrange(
                "p (kc j) -> p kc j", j=128), in_=src), ws, 16)
            wld[oc] = (ws, st.cnt[ws])
    blk.sync(_wol)

    for oc in range(NCH):
        def _woc(p, oc=oc):
            st.w(p, *wld[oc])
            if oc >= 2:
                st.w(p, "t", wo_t[(oc - 2, NT - 1)])
            st.tag(p.tensor_copy(out=A3.ap()[:, (oc % 2) * 1024:(oc % 2 + 1) * 1024],
                                 in_=ct["wstage"][oc % 2][:, :]), "p")
            assert st.cnt["p"] == wo_p[oc]
        blk.gpsimd(_woc)
        for tt in range(NT):
            def _wom(t, oc=oc, tt=tt):
                st.w(t, "p", wo_p[oc])
                idx = oc * NT + tt
                if idx >= 2:
                    st.w(t, "v", st.cnt["v"])
                for kc in range(NCH):
                    ins = t.matmul(out=PS_A[:, (idx % 2) * 512:(idx % 2) * 512 + 512],
                                   lhsT=A3[:, (oc % 2) * 1024 + kc * 128:(oc % 2) * 1024 + (kc + 1) * 128],
                                   rhs=att[:, kc * T + tt * 512:kc * T + (tt + 1) * 512],
                                   start=(kc == 0), stop=(kc == NCH - 1))
                st.tag(ins, "t")
                assert st.cnt["t"] == wo_t[(oc, tt)]
            blk.tensor(_wom)

            def _woe(v, oc=oc, tt=tt):
                st.w(v, "t", wo_t[(oc, tt)])
                idx = oc * NT + tt
                xc = x_fm[:, oc * T + tt * 512:oc * T + (tt + 1) * 512]
                st.tag(v.tensor_add(out=xc,
                                    in0=PS_A[:, (idx % 2) * 512:(idx % 2) * 512 + 512],
                                    in1=xc), "v")
            blk.vector(_woe)
    fence(st)

    xnb = big1.ap()[:, 0:NCH * T]
    ln_emit(st, ct, x_fm, T, xnb)
    fence(st)
    gates_emit(st, ct, x_fm, T, rw_f, rb_sb, gt_dram)
    fence(st)
    moe_l1_emit(st, ct, x_fm, xnb, T, io["w1_in"], io["w2_in"], io["b1_in"],
                io["b2_in"], gt_dram)
    fence(st)

    def _sx6(q):
        st.fence_eng(q)
        st.tag(q.dma_start(out=fm_in_ap(io["x6_out"], NCH, T),
                           in_=x_fm.ap().rearrange("p (c u) -> p c u", u=T)), "ds", 16)
        st.w(q, "ds")
    blk.sync(_sx6)


def build_fused(phases=5, dbg=False):
    from contextlib import ExitStack
    nc = bass.Bass("TRN2", target_bir_lowering=False, debug=False, **BASS_KW)
    T = TOK
    dri = lambda nm, shp, dt: nc.dram_tensor(nm, shp, dt, kind="ExternalInput").ap()
    drs = lambda nm, shp, dt: nc.dram_tensor(nm, shp, dt, kind="Internal").ap()
    io = {}
    io["x_in"] = dri("x_in", [D, T], F32)
    io["wg_in"] = dri("wg_in", [D, NE], F32)
    io["we_in"] = dri("we_in", [NE, D, D], F32)
    io["be_in"] = dri("be_in", [NE, D], F32)
    io["g3_in"] = dri("g3_in", [8, 128], F32)
    io["b3_in"] = dri("b3_in", [8, 128], F32)
    io["win_in"] = dri("win_in", [D, DI], F32)
    io["bin_in"] = dri("bin_in", [16, 128], F32)
    io["convw_in"] = dri("convw_in", [4, DIH], F32)
    io["convb_in"] = dri("convb_in", [8, 128], F32)
    io["wx_in"] = dri("wx_in", [DIH, 96], F32)
    io["wdt_in"] = dri("wdt_in", [DTR, DIH], F32R)
    io["bdt_in"] = dri("bdt_in", [8, 128], F32)
    io["alog_in"] = dri("alog_in", [DIH, NST], F32)
    io["dvec_in"] = dri("dvec_in", [8, 128], F32)
    io["wout_in"] = dri("wout_in", [DIH, D], F32R)
    io["bones_in"] = dri("bones_in", [128, 16 * 128], F32R)
    io["bout_in"] = dri("bout_in", [8, 128], F32)
    io["rw_in"] = dri("rw_in", [D, NE], F32)
    io["rb_in"] = dri("rb_in", [1, NE], F32)
    io["w1_in"] = dri("w1_in", [NE, D, DFFP], F32)
    io["w2_in"] = dri("w2_in", [NE, DFFP, D], F32)
    io["b1_in"] = dri("b1_in", [NE, DFFP], F32)
    io["b2_in"] = dri("b2_in", [NE, D], F32)
    io["wqT_in"] = dri("wqT_in", [D, D], F32)
    io["wkT_in"] = dri("wkT_in", [D, D], F32)
    io["wvT_in"] = dri("wvT_in", [D, D], F32)
    io["woT_in"] = dri("woT_in", [D, D], F32)
    io["cos_in"] = dri("cos_in", [128, T], F32)
    io["sin_in"] = dri("sin_in", [128, T], F32)
    io["x6_out"] = nc.dram_tensor("x6_out", [D, T], F32, kind="ExternalOutput").ap()
    io["x1_spill"] = drs("x1_spill", [D, T], F32)
    io["cc_xn2_in"] = drs("cc_xn2_in", [D, T], BF16)
    io["cc_xn2_out"] = drs("cc_xn2_out", [2 * D, T], BF16)
    io["u_d"] = drs("u_d", [DIH, S], F32)
    io["res_d"] = drs("res_d", [DIH, S], F32)
    io["cc_xd_in"] = drs("cc_xd_in", [96, S], F32)
    io["cc_xd_out"] = drs("cc_xd_out", [96, S], F32)
    io["delta_d"] = drs("delta_d", [DIH, S], F32)
    io["dtu_d"] = drs("dtu_d", [DIH, S], F32)
    io["a_d"] = drs("a_d", [DIH, NST], F32)
    io["cc_part_in"] = drs("cc_part_in", [2 * D, T], F32)
    io["cc_part_out"] = drs("cc_part_out", [D, T], F32)
    io["q_d"] = drs("q_d", [D, T], BF16)
    io["cc_k_in"] = drs("cc_k_in", [D, T], BF16)
    io["cc_kgath"] = drs("cc_kgath", [2 * D, T], BF16)
    io["cc_v_in"] = drs("cc_v_in", [D, T], BF16)
    io["cc_vgath"] = drs("cc_vgath", [2 * D, T], BF16)
    io["gt_dram"] = drs("gt_scr", [NE, T], BF16)
    dumps = []
    if dbg:
        def dbgt(nm, src_ap, shp, dt):
            d_ = nc.dram_tensor("dbg_" + nm, shp, dt, kind="ExternalOutput").ap()
            dumps.append((d_, src_ap))
        if phases >= 1:
            dbgt("x1", io["x1_spill"], [D, T], F32)
            dbgt("xn2g", io["cc_xn2_out"], [2 * D, T], BF16)
        if phases >= 2:
            dbgt("u", io["u_d"], [DIH, S], F32)
            dbgt("res", io["res_d"], [DIH, S], F32)
            dbgt("xdg", io["cc_xd_out"], [96, S], F32)
        if phases >= 3:
            dbgt("partg", io["cc_part_out"], [D, T], F32)
        if phases >= 4:
            io["dbg_x2"] = nc.dram_tensor("dbg_x2", [D, T], F32, kind="ExternalOutput").ap()
            io["dbg_x3"] = nc.dram_tensor("dbg_x3", [D, T], F32, kind="ExternalOutput").ap()
            io["dbg_x4"] = nc.dram_tensor("dbg_x4", [D, T], F32, kind="ExternalOutput").ap()
            io["dbg_xn5"] = nc.dram_tensor("dbg_xn5", [D, T], BF16, kind="ExternalOutput").ap()
            dbgt("q", io["q_d"], [D, T], BF16)
            dbgt("kg", io["cc_kgath"], [2 * D, T], BF16)
            dbgt("vg", io["cc_vgath"], [2 * D, T], BF16)

    ct = {}
    ct["PS_A"] = nc.alloc_psum_tensor("PS_A", [128, 2048], F32)
    ct["PS_B"] = nc.alloc_psum_tensor("PS_B", [128, 2048], F32)
    for nm, shp, dt in [("tokg", [128, (T // 128) * 8], F32),
                        ("e8", [128, (T // 128) * 8], F32),
                        ("nm1s", [128, T // 128], F32),
                        ("w1", [128, 1], F32), ("w2", [128, 1], F32),
                        ("mtok", [128, 16], F32), ("rtok", [128, 16], F32),
                        ("sbc", [128, 8], F32), ("rbc", [128, 8], F32),
                        ("w3", [128, 1], F32), ("w8a", [128, 8], F32),
                        ("w8b", [128, 8], F32), ("w8c", [128, 8], F32),
                        ("wg_f", [128, NCH * 8], F32), ("rw_f", [128, NCH * 8], F32),
                        ("g3_sb", [128, 8], F32), ("b3_sb", [128, 8], F32),
                        ("bout_sb", [128, 8], F32),
                        ("rsum", [128, 8], F32), ("nmx", [128, 4], F32),
                        ("rla", [128, 4], F32)]:
        ct[nm] = nc.alloc_sbuf_tensor(nm, shp, dt)
    rb_sb = nc.alloc_sbuf_tensor("rb_sb", [1, 8], F32)
    ct["rb_sb"] = rb_sb
    b1sb = [nc.alloc_sbuf_tensor("b1sb%d" % i, [128, 4], F32) for i in range(2)]
    ct["b1sb"] = b1sb

    extra = ["wl0", "wl1", "ge0", "ge1", "pc0", "pc1", "kv0", "kv1",
             "us0", "us1", "rs0", "rs1", "xs0", "xs1", "sc0", "sc1",
             "dd", "bc"] + CC_SEMS
    with (nc.semaphore("sem_t") as s_t, nc.semaphore("sem_v") as s_v,
          nc.semaphore("sem_s") as s_s, nc.semaphore("sem_p") as s_p,
          nc.semaphore("sem_dl") as s_dl, nc.semaphore("sem_ds") as s_ds,
          nc.Block() as blk):
        sems = dict(t=s_t, v=s_v, s=s_s, p=s_p, dl=s_dl, ds=s_ds)
        for nm in extra:
            sems[nm] = nc.alloc_semaphore(name="sem_" + nm)
        st = St(nc, blk, sems)
        cins = declare_common(nc, ct)

        def _lconsts(q):
            for k, (ap_, tname) in cins.items():
                st.tag(q.dma_start(out=ct[tname][:], in_=ap_), "dl", 16)
        blk.sync(_lconsts)

        def small_rows(es, pfx, nbebf=2):
            for nm in ("r_mean", "r_tmp", "r_rstd"):
                ct[nm] = es.enter_context(nc.sbuf_tensor(pfx + nm, [1, T], F32))
            gtrb = es.enter_context(nc.sbuf_tensor(pfx + "gtrb", [1, T], BF16))
            ct["gtrb"] = [gtrb, gtrb]
            beb = es.enter_context(nc.sbuf_tensor(pfx + "beb", [1, D], F32))
            ct["beb"] = [beb, beb]
            if nbebf == 2:
                ct["bebf"] = [es.enter_context(nc.sbuf_tensor(pfx + "bebf%d" % i, [1, D], BF16))
                              for i in range(2)]
            else:
                bb = es.enter_context(nc.sbuf_tensor(pfx + "bebf", [1, D], BF16))
                ct["bebf"] = [bb, bb]

        if phases >= 1:
         with ExitStack() as es:
            x_fm = es.enter_context(nc.sbuf_tensor("a_x_fm", [128, NCH * T], F32))
            xn1 = es.enter_context(nc.sbuf_tensor("a_xn1", [128, NCH * T], BF16))
            ct["x_fm"], ct["xn1"] = x_fm, xn1
            ct["wstage"] = [es.enter_context(nc.sbuf_tensor("a_wstage%d" % i, [128, 1024], F32)) for i in range(2)]
            ct["wbuf"] = [es.enter_context(nc.sbuf_tensor("a_wbuf%d" % i, [128, 4096], BF16)) for i in range(2)]
            ct["xg"] = es.enter_context(nc.sbuf_tensor("a_xg", [128, NCH * 512], BF16))
            ct["gbb"] = es.enter_context(nc.sbuf_tensor("a_gbb", [128, T], BF16))
            tmpa = es.enter_context(nc.sbuf_tensor("a_tmpa", [128, T], F32))
            tmpb = es.enter_context(nc.sbuf_tensor("a_tmpb", [128, T], F32))
            ct["tmpa"], ct["tmpb"] = tmpa, tmpb
            ct["sq0"], ct["sq1"] = tmpa.ap(), tmpb.ap()
            ct["lsb"] = tmpa.ap()
            ct["lsb_bf"] = tmpb.ap().bitcast(BF16)
            ct["sq_single"] = False
            small_rows(es, "a_")
            emit_phase_A(nc, st, blk, ct, io)
        if phases >= 1:
         emit_cc(st, blk, "AllGather", ALU.bypass,
                 io["cc_xn2_in"], io["cc_xn2_out"], "cc0")

        if phases >= 2:
         with ExitStack() as es:
            ct["xn2"] = es.enter_context(nc.sbuf_tensor("b_xn2", [128, NCH * S], BF16))
            ct["ubf"] = es.enter_context(nc.sbuf_tensor("b_ubf", [128, 8 * S], BF16))
            ct["wstage"] = [es.enter_context(nc.sbuf_tensor("b_wstg%d" % i, [128, 1024], F32)) for i in range(2)]
            ct["wbufB"] = [es.enter_context(nc.sbuf_tensor("b_wbufb%d" % i, [128, 1024], BF16)) for i in range(2)]
            ct["wxf"] = es.enter_context(nc.sbuf_tensor("b_wxf", [128, 8 * 96], F32))
            ct["wxb"] = es.enter_context(nc.sbuf_tensor("b_wxb", [128, 8 * 96], BF16))
            ct["upad"] = es.enter_context(nc.sbuf_tensor("b_upad", [128, 515], F32))
            ct["uconv"] = es.enter_context(nc.sbuf_tensor("b_uconv", [128, 512], F32))
            ct["ustore"] = [es.enter_context(nc.sbuf_tensor("b_ustore%d" % i, [128, 512], F32)) for i in range(2)]
            ct["rstore"] = [es.enter_context(nc.sbuf_tensor("b_rstore%d" % i, [128, 512], F32)) for i in range(2)]
            ct["xdst"] = [es.enter_context(nc.sbuf_tensor("b_xdst%d" % i, [96, 512], F32)) for i in range(2)]
            ct["prev3"] = es.enter_context(nc.sbuf_tensor("b_prev3", [128, 3], F32))
            ct["cw_sb"] = es.enter_context(nc.sbuf_tensor("b_cw_sb", [128, 8 * 4], F32))
            ct["cb_sb"] = es.enter_context(nc.sbuf_tensor("b_cb_sb", [128, 8], F32))
            ct["bin_sb"] = es.enter_context(nc.sbuf_tensor("b_bin_sb", [128, 16], F32))
            emit_phase_B(nc, st, blk, ct, io)
        if phases >= 2:
         emit_cc(st, blk, "AllReduce", ALU.add,
                 io["cc_xd_in"], io["cc_xd_out"], "cc1")

        if phases >= 3:
         with ExitStack() as es:
            for nm, shp, dt in [("u_ck", [128, 8 * 512], F32), ("res_ck", [128, 8 * 512], F32),
                                ("delta", [128, 8 * 512], F32), ("dtu", [128, 8 * 512], F32),
                                ("xdsum", [96, S], F32R), ("wout", [128, 64 * 128], F32R),
                                ("wdt", [64, 8 * 128], F32R), ("bones", [128, 16 * 128], F32R),
                                ("yg", [128, 8 * 512], F32R), ("A_il", [128, 128], F32),
                                ("asb", [128, 8 * 16], F32), ("Dv", [128, 8], F32),
                                ("bdt_sb", [128, 8], F32), ("hstate", [128, 128], F32),
                                ("B16", [128, 512], F32), ("C16", [128, 512], F32),
                                ("ydt", [128, 512], F32), ("sut", [128, 512], F32)]:
                ct[nm] = es.enter_context(nc.sbuf_tensor("c_" + nm, shp, dt))
            for nm, dt in [("dlt16", F32), ("dtu16", F32), ("dAt", F32),
                           ("dtuB", F32), ("hbuf", F32), ("hCr", F32R),
                           ("outst", F32)]:
                ct[nm] = [es.enter_context(nc.sbuf_tensor("c_%s%d" % (nm, i), [128, 512], dt))
                          for i in range(2)]
            emit_phase_C(nc, st, blk, ct, io)
        if phases >= 3:
         emit_cc(st, blk, "ReduceScatter", ALU.add,
                 io["cc_part_in"], io["cc_part_out"], "cc2")

        if phases >= 4:
         with ExitStack() as esx:
            x_fm = esx.enter_context(nc.sbuf_tensor("de_x_fm", [128, NCH * T], F32))
            ct["x_fm"] = x_fm
            with ExitStack() as es:
                ct["xnb"] = es.enter_context(nc.sbuf_tensor("d_xnb", [128, NCH * T], BF16))
                ct["wstage"] = [es.enter_context(nc.sbuf_tensor("d_wstage%d" % i, [128, 1024], F32)) for i in range(2)]
                ct["wbuf"] = [es.enter_context(nc.sbuf_tensor("d_wbuf%d" % i, [128, 4096], BF16)) for i in range(2)]
                ct["w1b"] = ct["wbuf"][0].ap()
                ct["w2b"] = ct["wbuf"][1].ap()
                ct["xg"] = es.enter_context(nc.sbuf_tensor("d_xg", [128, NCH * 512], BF16))
                ct["h1g"] = ct["xg"].ap()[:, 0:2048].bitcast(BF16)
                ct["gbb"] = es.enter_context(nc.sbuf_tensor("d_gbb", [128, T], BF16))
                ct["h1f"] = es.enter_context(nc.sbuf_tensor("d_h1f", [128, 512], F32))
                ct["qst"] = [es.enter_context(nc.sbuf_tensor("d_qst%d" % i, [128, 512], BF16)) for i in range(2)]
                tmpa = es.enter_context(nc.sbuf_tensor("d_tmpa", [128, T], F32))
                ct["tmpa"] = tmpa
                ct["tmpb"] = tmpa
                ct["sq0"] = tmpa.ap()
                ct["sq1"] = tmpa.ap()
                ct["lsb"] = tmpa.ap()
                ct["lsb_bf"] = ct["gbb"].ap()
                ct["sq_single"] = True
                small_rows(es, "d_")
                emit_phase_D(nc, st, blk, ct, io)
            emit_cc(st, blk, "AllGather", ALU.bypass,
                    io["cc_k_in"], io["cc_kgath"], "cc3")
            emit_cc(st, blk, "AllGather", ALU.bypass,
                    io["cc_v_in"], io["cc_vgath"], "cc4")
            if phases >= 5:
             with ExitStack() as es:
                att = es.enter_context(nc.sbuf_tensor("e_att", [128, NCH * T], BF16))
                big1 = es.enter_context(nc.sbuf_tensor("e_big1", [128, 16384], BF16))
                A1 = es.enter_context(nc.sbuf_tensor("e_A1", [128, 4096], BF16))
                A3 = es.enter_context(nc.sbuf_tensor("e_A3", [128, 4096], BF16))
                A4 = es.enter_context(nc.sbuf_tensor("e_A4", [128, 2048], BF16))
                A5 = es.enter_context(nc.sbuf_tensor("e_A5", [128, 2048], BF16))
                P_sb = es.enter_context(nc.sbuf_tensor("e_P_sb", [128, 4096], BF16))
                ct["att"], ct["big1"] = att, big1
                ct["A1"], ct["A3"], ct["A4"], ct["A5"], ct["P_sb"] = A1, A3, A4, A5, P_sb
                ct["wstage"] = [es.enter_context(nc.sbuf_tensor("e_wstage%d" % i, [128, 1024], F32)) for i in range(2)]
                ct["w1b"] = A1.ap()
                ct["w2b"] = P_sb.ap()
                ct["h1g"] = A3.ap()[:, 0:2048]
                ct["gbb"] = A3.ap()[:, 2048:4096]
                ct["h1f"] = es.enter_context(nc.sbuf_tensor("e_h1f", [128, 512], F32))
                tmpa = es.enter_context(nc.sbuf_tensor("e_tmpa", [128, T], F32))
                ct["tmpa"] = tmpa
                ct["sq0"] = tmpa.ap()
                ct["sq1"] = tmpa.ap()
                ct["sq_single"] = True
                ct["lsb"] = tmpa.ap()
                ct["lsb_bf"] = P_sb.ap()[:, 0:T]
                ct["rlrow"] = tmpa.ap()[0:1, 512:1024]
                ct["accf"] = tmpa.ap()[0:64, 1024:1536]
                attf = att.ap().bitcast(F32)
                ct["r_mean"] = attf[0:1, 0:2048]
                ct["r_tmp"] = attf[0:1, 2048:4096]
                ct["r_rstd"] = attf[0:1, 4096:6144]
                gtrb = es.enter_context(nc.sbuf_tensor("e_gtrb", [1, T], BF16))
                ct["gtrb"] = [gtrb, gtrb]
                beb = es.enter_context(nc.sbuf_tensor("e_beb", [1, D], F32))
                ct["beb"] = [beb, beb]
                bebf = es.enter_context(nc.sbuf_tensor("e_bebf", [1, D], BF16))
                ct["bebf"] = [bebf, bebf]
                emit_phase_E(nc, st, blk, ct, io)

        if dumps:
            def _dump(q):
                st.fence_eng(q)
                for d_, s_ in dumps:
                    st.tag(q.dma_start(out=d_, in_=s_), "ds", 16)
                st.w(q, "ds")
            blk.sync(_dump)
    return nc


def inmaps_fused(inputs, consts):
    x = np.asarray(inputs["L0_attn_x"])
    wg = np.ascontiguousarray(inputs["L0_ffn_Wg"])
    we = np.ascontiguousarray(inputs["L0_ffn_We"])
    be = np.ascontiguousarray(inputs["L0_ffn_be"])
    g3 = np.ascontiguousarray(inputs["L0_attn_ln_gamma"].reshape(8, 128))
    b3 = np.ascontiguousarray(inputs["L0_attn_ln_beta"].reshape(8, 128))
    win = np.asarray(inputs["L0_attn_W_in"])
    bin_ = np.asarray(inputs["L0_attn_b_in"])
    cw = np.asarray(inputs["L0_attn_conv_w"])
    cb = np.asarray(inputs["L0_attn_conv_b"])
    wx = np.asarray(inputs["L0_attn_W_x"])
    wdt = np.asarray(inputs["L0_attn_W_dt"])
    bdt = np.asarray(inputs["L0_attn_b_dt"])
    alog = np.asarray(inputs["L0_attn_A_log"])
    dvec = np.asarray(inputs["L0_attn_D"])
    wo = np.asarray(inputs["L0_attn_W_out"])
    bout = np.ascontiguousarray(inputs["L0_attn_b_out"].reshape(8, 128))
    rw = np.ascontiguousarray(inputs["L1_ffn_router_w"])
    rb = np.ascontiguousarray(inputs["L1_ffn_router_b"].reshape(1, 8))
    w1 = np.ascontiguousarray(inputs["L1_ffn_W1"])
    w2 = np.ascontiguousarray(inputs["L1_ffn_W2"])
    b1 = np.ascontiguousarray(inputs["L1_ffn_b1"])
    b2 = np.ascontiguousarray(inputs["L1_ffn_b2"])
    wqT = np.ascontiguousarray(inputs["L1_attn_W_q"].T)
    wkT = np.ascontiguousarray(inputs["L1_attn_W_k"].T)
    wvT = np.ascontiguousarray(inputs["L1_attn_W_v"].T)
    woT = np.ascontiguousarray(inputs["L1_attn_W_o"].T)
    sin_signed = consts["sin128"].copy()
    sin_signed[0::2, :] *= -1.0
    maps = []
    for c in range(8):
        b, h = c // 2, c % 2
        dh = h
        t0 = h * TOK
        m = common_inmap(consts)
        m["x_in"] = np.ascontiguousarray(x[b, t0:t0 + TOK, :].T)
        m["wg_in"] = wg
        m["we_in"] = we
        m["be_in"] = be
        m["g3_in"] = g3
        m["b3_in"] = b3
        ucols = slice(dh * DIH, (dh + 1) * DIH)
        rcols = slice(DI + dh * DIH, DI + (dh + 1) * DIH)
        m["win_in"] = np.ascontiguousarray(
            np.concatenate([win[:, ucols], win[:, rcols]], axis=1))
        m["bin_in"] = np.ascontiguousarray(
            np.concatenate([bin_[ucols], bin_[rcols]]).reshape(16, 128))
        m["convw_in"] = np.ascontiguousarray(cw[:, dh * DIH:(dh + 1) * DIH])
        m["convb_in"] = np.ascontiguousarray(cb[dh * DIH:(dh + 1) * DIH].reshape(8, 128))
        m["wx_in"] = np.ascontiguousarray(wx[dh * DIH:(dh + 1) * DIH, :])
        sl = slice(dh * DIH, (dh + 1) * DIH)
        m["wdt_in"] = np.ascontiguousarray(wdt[:, sl])
        m["bdt_in"] = np.ascontiguousarray(bdt[sl].reshape(8, 128))
        m["alog_in"] = np.ascontiguousarray(alog[sl])
        m["dvec_in"] = np.ascontiguousarray(dvec[sl].reshape(8, 128))
        m["wout_in"] = np.ascontiguousarray(wo[sl])
        m["bones_in"] = consts["bones"]
        m["bout_in"] = bout
        m["rw_in"] = rw
        m["rb_in"] = rb
        m["w1_in"] = w1
        m["w2_in"] = w2
        m["b1_in"] = b1
        m["b2_in"] = b2
        m["wqT_in"] = wqT
        m["wkT_in"] = wkT
        m["wvT_in"] = wvT
        m["woT_in"] = woT
        m["cos_in"] = np.ascontiguousarray(consts["cos128"][:, t0:t0 + TOK])
        m["sin_in"] = np.ascontiguousarray(sin_signed[:, t0:t0 + TOK])
        maps.append(m)
    return maps


def kernel(**inputs):
    consts = host_consts()
    nc = build_fused()
    r = run_bass_kernel_spmd(nc, inmaps_fused(inputs, consts),
                             core_ids=list(range(8)))
    out = np.empty((B, S, D), np.float32)
    for c in range(8):
        b, h = c // 2, c % 2
        out[b, h * TOK:(h + 1) * TOK, :] = np.asarray(r.results[c]["x6_out"]).T
    return out


# version 06
# speedup vs baseline: 1.2861x; 1.2861x over previous
import numpy as np
import ml_dtypes
import concourse.bass as bass
import concourse.mybir as mybir
from concourse.bass_utils import run_bass_kernel_spmd
from concourse.bass_types import AP

F32 = mybir.dt.float32
F32R = mybir.dt.float32r
BF16 = mybir.dt.bfloat16
AFT = mybir.ActivationFunctionType
ALU = mybir.AluOpType
AX = mybir.AxisListType
BF = ml_dtypes.bfloat16

B, S, D = 4, 4096, 1024
NCH = D // 128
DI, DIH = 2048, 1024
NST, DTR, KCONV = 16, 64, 4
NE, DFFP = 8, 512
NH, DK = 16, 64
EPS = 1e-5
TOK = S // 2

SEMS = ("t", "v", "s", "p", "dl", "ds")
BASS_KW = {}


class St:
    def __init__(self, nc, block, sems):
        self.nc, self.block, self.sem = nc, block, sems
        self.cnt = {k: 0 for k in sems}

    def tag(self, instr, name, amt=1):
        self.cnt[name] += amt
        instr.then_inc(self.sem[name], amt)

    def w(self, eng, name, val=None):
        v = self.cnt[name] if val is None else val
        if v > 0:
            eng.wait_ge(self.sem[name], v)

    def fence_eng(self, eng, skip=()):
        for n in self.sem:
            if n not in skip and self.cnt[n] > 0:
                eng.wait_ge(self.sem[n], self.cnt[n])


def fence(st, engines="tvsp"):
    blk = st.block
    m = {"t": blk.tensor, "v": blk.vector, "s": blk.scalar, "p": blk.gpsimd,
         "q": blk.sync}
    for e in engines:
        m[e](lambda eng, _st=st: _st.fence_eng(eng))


def fm_in_ap(dram_ap, nch, T, row0=0):
    return AP(dram_ap.tensor, dram_ap.offset + row0 * T,
              [[T, 128], [T * 128, nch], [1, T]])


CONSTVALS = (EPS, 1.0, -1.0, 0.5)


def declare_common(nc, ct):
    names = [("ident", [128, 128], F32), ("identb", [128, 128], BF16),
             ("ones_col", [128, 1], F32),
             ("ones_col_r", [128, 1], F32R), ("ones_row", [1, 128], F32),
             ("ones_row_b", [1, 128], BF16), ("cvals", [128, len(CONSTVALS)], F32)]
    ins = {}
    for nm, shp, dt in names:
        ct[nm] = nc.alloc_sbuf_tensor("c_" + nm, shp, dt)
        ins["i_" + nm] = (nc.dram_tensor("i_" + nm, shp, dt, kind="ExternalInput").ap(), nm)
    for i, v in enumerate(CONSTVALS):
        nc.const_aps.register(F32, v, ct["cvals"][:, i:i + 1])
    return ins


def common_inmap(consts):
    cv = np.broadcast_to(np.asarray(CONSTVALS, np.float32), (128, len(CONSTVALS))).copy()
    return dict(i_ident=consts["ident"], i_identb=consts["ident"].astype(BF),
                i_ones_col=consts["ones_col"],
                i_ones_col_r=consts["ones_col"], i_ones_row=consts["ones_row"],
                i_ones_row_b=consts["ones_row"].astype(BF), i_cvals=cv)


def host_consts():
    ident = np.eye(128, dtype=np.float32)
    ones_col = np.ones((128, 1), np.float32)
    ones_row = np.ones((1, 128), np.float32)
    inv = (np.float32(1.0) / (np.float32(10000.0) **
           (np.arange(0, DK, 2, dtype=np.float32) / np.float32(DK)))).astype(np.float32)
    pos = np.arange(S, dtype=np.float32)
    sc = (pos[:, None] * inv[None, :]).astype(np.float32)
    sin = np.repeat(np.sin(sc), 2, axis=1).T.astype(np.float32)
    cos = np.repeat(np.cos(sc), 2, axis=1).T.astype(np.float32)
    cos128 = np.concatenate([cos, cos], 0).copy()
    sin128 = np.concatenate([sin, sin], 0).copy()
    R = np.zeros((64, 64), np.float32)
    for i in range(32):
        R[2 * i, 2 * i + 1] = -1.0
        R[2 * i + 1, 2 * i] = 1.0
    R128 = np.zeros((128, 128), np.float32)
    R128[0:64, 0:64] = R.T
    R128[64:128, 64:128] = R.T
    bones = np.zeros((128, 16, 128), np.float32)
    for g in range(16):
        for dl in range(8):
            bones[dl * 16:(dl + 1) * 16, g, 8 * g + dl] = 1.0
    return dict(ident=ident, ones_col=ones_col, ones_row=ones_row,
                cos128=cos128, sin128=sin128, R128=R128,
                bones=bones.reshape(128, 16 * 128).copy())


def ln_stats_emit(st, ct, x_fm, T, nch=NCH):
    blk = st.block
    PS_A, PS_B = ct["PS_A"], ct["PS_B"]
    NT = T // 512
    t0s, t0t = st.cnt["s"], st.cnt["t"]

    single = ct.get("sq_single", False)

    def _sq(s):
        st.fence_eng(s, skip=("s",))
        for c in range(nch):
            if (c >= 1 and single) or c >= 2:
                st.w(s, "t", t0t + (c if single else c - 1))
            st.tag(s.activation(out=ct["sq%d" % (c % 2)][:, 0:T],
                                in_=x_fm[:, c * T:(c + 1) * T], func=AFT.Square), "s")
    blk.scalar(_sq)

    def _mm(t):
        st.fence_eng(t, skip=("t", "s"))
        for c in range(nch):
            for k in range(NT):
                t.matmul(out=PS_A[0:1, k * 512:(k + 1) * 512],
                         lhsT=ct["ones_col"][:, 0:1],
                         rhs=x_fm[:, c * T + k * 512:c * T + (k + 1) * 512],
                         start=(c == 0), stop=(c == nch - 1))
            st.w(t, "s", t0s + c + 1)
            for k in range(NT):
                ins = t.matmul(out=PS_B[0:1, k * 512:(k + 1) * 512],
                               lhsT=ct["ones_col"][:, 0:1],
                               rhs=ct["sq%d" % (c % 2)][:, k * 512:(k + 1) * 512],
                               start=(c == 0), stop=(c == nch - 1))
            st.tag(ins, "t")
    blk.tensor(_mm)

    def _rows_v(v):
        st.fence_eng(v, skip=("v",))
        v.tensor_scalar(out=ct["r_mean"][0:1, 0:T], in0=PS_A[0:1, 0:T],
                        scalar1=1.0 / D, scalar2=None, op0=ALU.mult)
        v.tensor_scalar(out=ct["r_tmp"][0:1, 0:T], in0=PS_B[0:1, 0:T],
                        scalar1=1.0 / D, scalar2=None, op0=ALU.mult)
        v.tensor_mul(out=ct["r_rstd"][0:1, 0:T], in0=ct["r_mean"][0:1, 0:T],
                     in1=ct["r_mean"][0:1, 0:T])
        st.tag(v.tensor_sub(out=ct["r_tmp"][0:1, 0:T], in0=ct["r_tmp"][0:1, 0:T],
                            in1=ct["r_rstd"][0:1, 0:T]), "v")
    blk.vector(_rows_v)

    def _rows_s(s):
        st.fence_eng(s, skip=("s",))
        st.tag(s.activation(out=ct["r_tmp"][0:1, 0:T], in_=ct["r_tmp"][0:1, 0:T],
                            func=AFT.Sqrt, bias=EPS), "s")
    blk.scalar(_rows_s)

    def _rows_v2(v):
        st.fence_eng(v, skip=("v",))
        st.tag(v.reciprocal(out=ct["r_rstd"][0:1, 0:T], in_=ct["r_tmp"][0:1, 0:T]), "v")
    blk.vector(_rows_v2)


def bcast_rows_emit(st, ct, row_a, row_b, T):
    NT = T // 512

    def _bcast(t):
        st.fence_eng(t, skip=("t",))
        for k in range(NT):
            t.matmul(out=ct["PS_A"][:, k * 512:(k + 1) * 512], lhsT=ct["ones_row"][0:1, :],
                     rhs=row_a[0:1, k * 512:(k + 1) * 512], start=True, stop=True)
        for k in range(NT):
            ins = t.matmul(out=ct["PS_B"][:, k * 512:(k + 1) * 512], lhsT=ct["ones_row"][0:1, :],
                           rhs=row_b[0:1, k * 512:(k + 1) * 512], start=True, stop=True)
        st.tag(ins, "t")
    st.block.tensor(_bcast)


def ln_emit(st, ct, x_fm, T, dst_fm, nch=NCH):
    ln_stats_emit(st, ct, x_fm, T, nch)
    fence(st)
    bcast_rows_emit(st, ct, ct["r_mean"], ct["r_rstd"], T)

    def _norm(v):
        st.fence_eng(v, skip=("v",))
        for c in range(nch):
            v.tensor_sub(out=ct["tmpa"][:, 0:T], in0=x_fm[:, c * T:(c + 1) * T],
                         in1=ct["PS_A"][:, 0:T])
            ins = v.tensor_mul(out=dst_fm[:, c * T:(c + 1) * T],
                               in0=ct["tmpa"][:, 0:T], in1=ct["PS_B"][:, 0:T])
        st.tag(ins, "v")
    st.block.vector(_norm)


def ln23_emit(st, ct, x_fm, T, g_sb, b_sb, per_chunk_out, chunk_pre_wait=None):
    ln_stats_emit(st, ct, x_fm, T)
    fence(st)

    def _rows_v(v):
        st.fence_eng(v, skip=("v",))
        st.tag(v.tensor_mul(out=ct["r_tmp"][0:1, 0:T], in0=ct["r_rstd"][0:1, 0:T],
                            in1=ct["r_rstd"][0:1, 0:T]), "v")
    st.block.vector(_rows_v)

    def _rows_v1b(v):
        st.fence_eng(v, skip=("v",))
        st.tag(v.tensor_scalar(out=ct["r_tmp"][0:1, 0:T], in0=ct["r_tmp"][0:1, 0:T],
                               scalar1=-EPS, scalar2=1.0, op0=ALU.mult,
                               op1=ALU.add), "v")
    st.block.vector(_rows_v1b)

    def _rows_s(s):
        st.fence_eng(s, skip=("s",))
        st.tag(s.activation(out=ct["r_tmp"][0:1, 0:T], in_=ct["r_tmp"][0:1, 0:T],
                            func=AFT.Sqrt, bias=EPS), "s")
    st.block.scalar(_rows_s)

    def _rows_v2(v):
        st.fence_eng(v, skip=("v",))
        v.reciprocal(out=ct["r_tmp"][0:1, 0:T], in_=ct["r_tmp"][0:1, 0:T])
        v.tensor_mul(out=ct["r_tmp"][0:1, 0:T], in0=ct["r_rstd"][0:1, 0:T],
                     in1=ct["r_tmp"][0:1, 0:T])
        st.tag(v.tensor_mul(out=ct["r_mean"][0:1, 0:T], in0=ct["r_mean"][0:1, 0:T],
                            in1=ct["r_tmp"][0:1, 0:T]), "v")
    st.block.vector(_rows_v2)
    fence(st)
    bcast_rows_emit(st, ct, ct["r_tmp"], ct["r_mean"], T)

    def _norm(v):
        st.fence_eng(v, skip=("v",))
        for c in range(NCH):
            if chunk_pre_wait is not None:
                chunk_pre_wait(v, c)
            v.tensor_mul(out=ct["tmpa"][:, 0:T], in0=x_fm[:, c * T:(c + 1) * T],
                         in1=ct["PS_A"][:, 0:T])
            v.tensor_sub(out=ct["tmpb"][:, 0:T], in0=ct["tmpa"][:, 0:T],
                         in1=ct["PS_B"][:, 0:T])
            st.tag(v.tensor_scalar(out=per_chunk_out(c), in0=ct["tmpb"][:, 0:T],
                                   scalar1=g_sb[:, c:c + 1], scalar2=b_sb[:, c:c + 1],
                                   op0=ALU.mult, op1=ALU.add), "v")
    st.block.vector(_norm)


def gates_emit(st, ct, x_fm, T, wg_f, rb_row, gt_dram):
    blk = st.block
    PS_A, PS_B = ct["PS_A"], ct["PS_B"]
    NT = T // 512
    NK = T // 128

    def _mm(t):
        st.fence_eng(t, skip=("t",))
        for k in range(NT):
            for c in range(NCH):
                ins = t.matmul(out=PS_A[0:8, k * 512:(k + 1) * 512],
                               lhsT=wg_f[:, c * 8:(c + 1) * 8],
                               rhs=x_fm[:, c * T + k * 512:c * T + (k + 1) * 512],
                               start=(c == 0), stop=(c == NCH - 1))
        for c in range(NCH):
            ins2 = t.matmul(out=PS_B[0:8, 512:513], lhsT=wg_f[:, c * 8:(c + 1) * 8],
                            rhs=ct["ones_col"][:, 0:1], start=(c == 0), stop=(c == NCH - 1))
        st.tag(ins, "t")
        st.tag(ins2, "t")
    blk.tensor(_mm)

    def _vneg(v):
        st.fence_eng(v, skip=("v",))
        st.tag(v.tensor_scalar(out=ct["r_tmp"][0:1, 0:T], in0=ct["r_mean"][0:1, 0:T],
                               scalar1=-1.0, scalar2=None, op0=ALU.mult), "v")
    blk.vector(_vneg)

    def _evac(s):
        st.fence_eng(s, skip=("s",))
        s.copy(out=ct["lsb"][0:8, 0:T], in_=PS_A[0:8, 0:T])
        st.tag(s.copy(out=ct["w1"][0:8, 0:1], in_=PS_B[0:8, 512:513]), "s")
    blk.scalar(_evac)
    fence(st, "t")

    def _tp(t):
        for k in range(NK):
            t.matmul(out=PS_B[:, k * 8:(k + 1) * 8],
                     lhsT=ct["lsb"][0:8, k * 128:(k + 1) * 128],
                     rhs=ct["ident"][0:8, 0:8], is_transpose=True,
                     start=(k == 0), stop=(k == NK - 1))
        t.matmul(out=PS_B[0:1, 520:528], lhsT=ct["w1"][0:8, 0:1],
                 rhs=ct["ident"][0:8, 0:8], is_transpose=True,
                 start=False, stop=True)
        for k in range(NK):
            t.matmul(out=PS_B[:, 1024 + k:1024 + k + 1],
                     lhsT=ct["r_tmp"][0:1, k * 128:(k + 1) * 128],
                     rhs=ct["ident"][0:1, 0:1], is_transpose=True,
                     start=(k == 0), stop=(k == NK - 1))
        for k in range(NK):
            ins = t.matmul(out=PS_B[:, 1536 + k:1536 + k + 1],
                           lhsT=ct["r_rstd"][0:1, k * 128:(k + 1) * 128],
                           rhs=ct["ident"][0:1, 0:1], is_transpose=True,
                           start=(k == 0), stop=(k == NK - 1))
        st.tag(ins, "t")
    blk.tensor(_tp)

    def _evac2(s):
        st.fence_eng(s, skip=("s",))
        s.copy(out=ct["tokg"][:, 0:NK * 8], in_=PS_B[:, 0:NK * 8])
        s.copy(out=ct["r_tmp"][0:1, 8:16], in_=PS_B[0:1, 520:528])
        s.copy(out=ct["mtok"][:, 0:NK], in_=PS_B[:, 1024:1024 + NK])
        st.tag(s.copy(out=ct["rtok"][:, 0:NK], in_=PS_B[:, 1536:1536 + NK]), "s")
    blk.scalar(_evac2)
    fence(st, "t")

    def _sbc(t):
        t.matmul(out=PS_B[:, 256:264], lhsT=ct["ones_row"][0:1, :],
                 rhs=ct["r_tmp"][0:1, 8:16], start=False, stop=True)
        if rb_row is not None:
            ins = t.matmul(out=PS_B[:, 272:280], lhsT=ct["ones_row"][0:1, :],
                           rhs=rb_row[0:1, 0:8], start=False, stop=True)
        else:
            ins = t.matmul(out=PS_B[:, 272:280], lhsT=ct["ones_row"][0:1, :],
                           rhs=ct["r_tmp"][0:1, 8:16], start=False, stop=True)
        st.tag(ins, "t")
    blk.tensor(_sbc)

    def _evac3(s):
        st.fence_eng(s, skip=("s",))
        s.copy(out=ct["sbc"][:, 0:8], in_=PS_B[:, 256:264])
        st.tag(s.copy(out=ct["rbc"][:, 0:8], in_=PS_B[:, 272:280]), "s")
    blk.scalar(_evac3)
    fence(st, "v")

    def _logits(v):
        def step(ins, _st=st, _v=v):
            _st.tag(ins, "v")
            _st.w(_v, "v")
        for k in range(NK):
            tg = ct["tokg"][:, k * 8:(k + 1) * 8]
            step(v.scalar_tensor_tensor(out=ct["w8a"][:, 0:8], in0=ct["sbc"][:, 0:8],
                                        scalar=ct["mtok"][:, k:k + 1], in1=tg,
                                        op0=ALU.mult, op1=ALU.add))
            if rb_row is not None:
                step(v.tensor_scalar(out=ct["w8a"][:, 0:8], in0=ct["w8a"][:, 0:8],
                                     scalar1=ct["rtok"][:, k:k + 1], scalar2=None,
                                     op0=ALU.mult))
                step(v.tensor_add(out=tg, in0=ct["w8a"][:, 0:8], in1=ct["rbc"][:, 0:8]))
            else:
                step(v.tensor_scalar(out=tg, in0=ct["w8a"][:, 0:8],
                                     scalar1=ct["rtok"][:, k:k + 1], scalar2=None,
                                     op0=ALU.mult))
    blk.vector(_logits)

    def _v1(v):
        st.fence_eng(v, skip=("v",))
        for k in range(NK):
            ins = v.reduce_max(out=ct["nm1s"][:, k:k + 1],
                               in_=ct["tokg"][:, k * 8:(k + 1) * 8],
                               axis=AX.X, negate=True)
        st.tag(ins, "v")
    blk.vector(_v1)

    def _s1(s):
        st.fence_eng(s, skip=("s",))
        for k in range(NK):
            ins = s.activation(out=ct["e8"][:, k * 8:(k + 1) * 8],
                               in_=ct["tokg"][:, k * 8:(k + 1) * 8],
                               func=AFT.Exp, bias=ct["nm1s"][:, k:k + 1])
        st.tag(ins, "s")
    blk.scalar(_s1)

    def _v2(v):
        st.fence_eng(v, skip=("v",))

        def step(ins, _st=st, _v=v):
            _st.tag(ins, "v")
            _st.w(_v, "v")
        for k in range(NK):
            g8 = ct["tokg"][:, k * 8:(k + 1) * 8]
            e8 = ct["e8"][:, k * 8:(k + 1) * 8]
            step(v.tensor_scalar(out=ct["w8a"][:, 0:8], in0=g8,
                                 scalar1=ct["nm1s"][:, k:k + 1], scalar2=None, op0=ALU.add))
            step(v.tensor_scalar(out=ct["w8b"][:, 0:8], in0=ct["w8a"][:, 0:8],
                                 scalar1=0.0, scalar2=None, op0=ALU.is_lt))
            step(v.tensor_mul(out=ct["w8a"][:, 0:8], in0=e8, in1=ct["w8b"][:, 0:8]))
            step(v.reduce_max(out=ct["w1"][:, 0:1], in_=ct["w8a"][:, 0:8], axis=AX.X))
            step(v.reduce_max(out=ct["w2"][:, 0:1], in_=e8, axis=AX.X))
            step(v.tensor_add(out=ct["w3"][:, 0:1], in0=ct["w1"][:, 0:1], in1=ct["w2"][:, 0:1]))
            step(v.reciprocal(out=ct["w3"][:, 0:1], in_=ct["w3"][:, 0:1]))
            step(v.tensor_scalar(out=ct["w8c"][:, 0:8], in0=ct["w8a"][:, 0:8],
                                 scalar1=ct["w1"][:, 0:1], scalar2=None, op0=ALU.is_ge))
            step(v.tensor_scalar(out=ct["w8b"][:, 0:8], in0=ct["w8b"][:, 0:8],
                                 scalar1=-1.0, scalar2=1.0, op0=ALU.mult, op1=ALU.add))
            step(v.tensor_add(out=ct["w8c"][:, 0:8], in0=ct["w8c"][:, 0:8], in1=ct["w8b"][:, 0:8]))
            step(v.tensor_mul(out=ct["w8c"][:, 0:8], in0=e8, in1=ct["w8c"][:, 0:8]))
            step(v.tensor_scalar(out=ct["tokg"][:, k * 8:(k + 1) * 8],
                                 in0=ct["w8c"][:, 0:8], scalar1=ct["w3"][:, 0:1],
                                 scalar2=0.5, op0=ALU.mult, op1=ALU.mult))
    blk.vector(_v2)
    fence(st, "t")

    def _tp2(t):
        for k in range(NK):
            ins = t.matmul(out=PS_B[0:8, k * 128:(k + 1) * 128],
                           lhsT=ct["tokg"][:, k * 8:(k + 1) * 8],
                           rhs=ct["ident"][:, :], is_transpose=True,
                           start=(k % 4 == 0), stop=(k % 4 == 3))
        st.tag(ins, "t")
    blk.tensor(_tp2)

    def _evac4(s):
        st.fence_eng(s, skip=("s",))
        st.tag(s.copy(out=ct["lsb_bf"][0:8, 0:T], in_=PS_B[0:8, 0:T]), "s")
    blk.scalar(_evac4)

    def _dma(q):
        st.fence_eng(q)
        st.tag(q.dma_start(out=gt_dram, in_=ct["lsb_bf"][0:8, 0:T]), "ds", 16)
    blk.sync(_dma)


def moe_l0_emit(st, ct, x_fm, xn_fm, T, we_dram, be_dram, gt_dram):
    blk = st.block
    PS_A, PS_B = ct["PS_A"], ct["PS_B"]
    NT = T // 512
    wbuf, wstage = ct["wbuf"], ct["wstage"]
    gtrb, beb, bebf = ct["gtrb"], ct["beb"], ct["bebf"]

    tA, vA, pA, sA = st.cnt["t"], st.cnt["v"], st.cnt["p"], st.cnt["s"]
    pA0 = pA
    gb_t, mm_t, xg_v, evac_v, cast_p, gbe_s = {}, {}, {}, {}, {}, {}
    becast_v = {}
    for e in range(NE):
        tA += 1
        gb_t[e] = tA
        sA += 1
        gbe_s[e] = sA
        vA += 1
        becast_v[e] = vA
        for dh in range(2):
            pA += 4
            cast_p[(e, dh)] = pA
            for tt in range(NT):
                vA += 1
                xg_v[(e, dh, tt)] = vA
                tA += 1
                mm_t[(e, dh, tt)] = tA
                vA += 1
                evac_v[(e, dh, tt)] = vA
    dl_w, dl_g = {}, {}

    def _loads(q):
        st.fence_eng(q, skip=("dl",))
        for e in range(NE):
            if e >= 1:
                st.w(q, "t", mm_t[(e - 1, 1, NT - 1)])
                st.w(q, "v", becast_v[e - 1])
            gs = "ge%d" % (e % 2)
            st.tag(q.dma_start(out=gtrb[e % 2][0:1, 0:T],
                               in_=AP(gt_dram.tensor, gt_dram.offset + e * T,
                                      [[1, 1], [1, T]])), gs, 16)
            st.tag(q.dma_start(out=beb[e % 2][0:1, 0:D],
                               in_=AP(be_dram.tensor, be_dram.offset + e * D,
                                      [[1, 1], [1, D]])), gs, 16)
            dl_g[e] = (gs, st.cnt[gs])
            for dh in range(2):
                for pc in range(4):
                    gidx = (2 * e + dh) * 4 + pc
                    if gidx >= 2:
                        st.w(q, "p", pA0 + gidx - 1)
                    ws = "wl%d" % (gidx % 2)
                    src = AP(we_dram.tensor,
                             we_dram.offset + e * D * D + dh * 512 + 2 * pc * 128 * D,
                             [[D, 128], [128 * D, 2], [128, 4], [1, 128]])
                    st.tag(q.dma_start(out=wstage[gidx % 2].ap().rearrange(
                        "p (kc dc j) -> p kc dc j", dc=4, j=128), in_=src), ws, 16)
                    dl_w[(e, dh, pc)] = (ws, st.cnt[ws])
    blk.sync(_loads)

    for e in range(NE):
        def _becast(v, e=e):
            st.w(v, *dl_g[e])
            if e >= 2:
                st.w(v, "t", mm_t[(e - 2, 1, NT - 1)])
            st.tag(v.tensor_copy(out=bebf[e % 2][0:1, 0:D], in_=beb[e % 2][0:1, 0:D]), "v")
            assert st.cnt["v"] == becast_v[e]
        blk.vector(_becast)

        def _gb(t, e=e):
            st.w(t, *dl_g[e])
            if e >= 1:
                st.w(t, "s", gbe_s[e - 1])
            for k in range(NT):
                ins = t.matmul(out=PS_B[:, k * 512:(k + 1) * 512],
                               lhsT=ct["ones_row_b"][0:1, :],
                               rhs=gtrb[e % 2][0:1, k * 512:(k + 1) * 512],
                               start=True, stop=True)
            st.tag(ins, "t")
            assert st.cnt["t"] == gb_t[e]
        blk.tensor(_gb)

        def _gbe(s, e=e):
            st.w(s, "t", gb_t[e])
            if e >= 1:
                st.w(s, "v", xg_v[(e - 1, 1, NT - 1)])
            st.tag(s.copy(out=ct["gbb"][:, 0:T], in_=PS_B[:, 0:T]), "s")
            assert st.cnt["s"] == gbe_s[e]
        blk.scalar(_gbe)

        for dh in range(2):
            def _cast(p, e=e, dh=dh):
                if e + dh >= 1:
                    pk = (e, 0) if dh == 1 else (e - 1, 1)
                    st.w(p, "t", mm_t[(pk[0], pk[1], NT - 1)])
                for pc in range(4):
                    st.w(p, *dl_w[(e, dh, pc)])
                    st.tag(p.tensor_copy(
                        out=wbuf[(2 * e + dh) % 2][:, pc * 1024:(pc + 1) * 1024],
                        in_=wstage[((2 * e + dh) * 4 + pc) % 2][:, 0:1024]), "p")
                assert st.cnt["p"] == cast_p[(e, dh)]
            blk.gpsimd(_cast)

            for tt in range(NT):
                def _xg(v, e=e, dh=dh, tt=tt):
                    st.w(v, "s", gbe_s[e])
                    key = (e, dh, tt)
                    prev = None
                    if tt > 0:
                        prev = (e, dh, tt - 1)
                    elif dh == 1:
                        prev = (e, 0, NT - 1)
                    elif e > 0:
                        prev = (e - 1, 1, NT - 1)
                    if prev is not None:
                        st.w(v, "t", mm_t[prev])
                    for kc in range(NCH):
                        ins = v.tensor_mul(out=ct["xg"][:, kc * 512:(kc + 1) * 512],
                                           in0=xn_fm[:, kc * T + tt * 512:kc * T + (tt + 1) * 512],
                                           in1=ct["gbb"][:, tt * 512:(tt + 1) * 512])
                    st.tag(ins, "v")
                    assert st.cnt["v"] == xg_v[(e, dh, tt)]
                blk.vector(_xg)

                def _mm(t, e=e, dh=dh, tt=tt):
                    st.w(t, "p", cast_p[(e, dh)])
                    st.w(t, "v", xg_v[(e, dh, tt)])
                    prev = None
                    if tt > 0:
                        prev = (e, dh, tt - 1)
                    elif dh == 1:
                        prev = (e, 0, NT - 1)
                    elif e > 0:
                        prev = (e - 1, 1, NT - 1)
                    if prev is not None:
                        st.w(t, "v", evac_v[prev])
                    for dc in range(4):
                        for kc in range(NCH):
                            t.matmul(out=PS_A[:, dc * 512:(dc + 1) * 512],
                                     lhsT=wbuf[(2 * e + dh) % 2][:, (kc * 4 + dc) * 128:(kc * 4 + dc + 1) * 128],
                                     rhs=ct["xg"][:, kc * 512:(kc + 1) * 512],
                                     start=(kc == 0), stop=False)
                        ins = t.matmul(out=PS_A[:, dc * 512:(dc + 1) * 512],
                                       lhsT=bebf[e % 2][0:1, (dh * 4 + dc) * 128:(dh * 4 + dc + 1) * 128],
                                       rhs=gtrb[e % 2][0:1, tt * 512:(tt + 1) * 512],
                                       start=False, stop=True)
                    st.tag(ins, "t")
                    assert st.cnt["t"] == mm_t[(e, dh, tt)]
                blk.tensor(_mm)

                def _evac(v, e=e, dh=dh, tt=tt):
                    st.w(v, "t", mm_t[(e, dh, tt)])
                    xv = x_fm.ap().rearrange("p (c u) -> p c u", u=T)[
                        :, dh * 4:(dh + 1) * 4, tt * 512:(tt + 1) * 512]
                    pv = PS_A.ap().rearrange("p (dc u) -> p dc u", u=512)
                    st.tag(v.tensor_add(out=xv, in0=pv, in1=xv), "v")
                    assert st.cnt["v"] == evac_v[(e, dh, tt)]
                blk.vector(_evac)


def build_launch_A(dbg=False):
    nc = bass.Bass("TRN2", target_bir_lowering=False, debug=False, **BASS_KW)
    T = TOK
    x_in = nc.dram_tensor("x_in", [D, T], F32, kind="ExternalInput").ap()
    wg_in = nc.dram_tensor("wg_in", [D, NE], F32, kind="ExternalInput").ap()
    we_in = nc.dram_tensor("we_in", [NE, D, D], F32, kind="ExternalInput").ap()
    be_in = nc.dram_tensor("be_in", [NE, D], F32, kind="ExternalInput").ap()
    g3_in = nc.dram_tensor("g3_in", [8, 128], F32, kind="ExternalInput").ap()
    b3_in = nc.dram_tensor("b3_in", [8, 128], F32, kind="ExternalInput").ap()
    x1_out = nc.dram_tensor("x1_out", [D, T], F32, kind="ExternalOutput").ap()
    if dbg:
        xn1_out = nc.dram_tensor("xn1_out", [D, T], BF16, kind="ExternalOutput").ap()
        tokg_out = nc.dram_tensor("tokg_out", [128, (T // 128) * 8], F32, kind="ExternalOutput").ap()
        e8_out = nc.dram_tensor("e8_out", [128, (T // 128) * 8], F32, kind="ExternalOutput").ap()
        gt_out = nc.dram_tensor("gt_out", [NE, T], BF16, kind="ExternalOutput").ap()
    xn2_out = nc.dram_tensor("xn2_out", [D, T], BF16, kind="ExternalOutput").ap()
    gt_dram = nc.dram_tensor("gt_scr", [NE, T], BF16, kind="Internal").ap()

    ct = {}
    x_fm = nc.alloc_sbuf_tensor("x_fm", [128, NCH * T], F32)
    xn1 = nc.alloc_sbuf_tensor("xn1", [128, NCH * T], BF16)
    ct["wstage"] = [nc.alloc_sbuf_tensor("wstage%d" % i, [128, 1024], F32) for i in range(2)]
    ct["wbuf"] = [nc.alloc_sbuf_tensor("wbuf%d" % i, [128, 4096], BF16) for i in range(2)]
    ct["xg"] = nc.alloc_sbuf_tensor("xg", [128, NCH * 512], BF16)
    ct["gbb"] = nc.alloc_sbuf_tensor("gbb", [128, T], BF16)
    gtrb1 = nc.alloc_sbuf_tensor("gtrb", [1, T], BF16)
    ct["gtrb"] = [gtrb1, gtrb1]
    beb1 = nc.alloc_sbuf_tensor("beb", [1, D], F32)
    ct["beb"] = [beb1, beb1]
    ct["bebf"] = [nc.alloc_sbuf_tensor("bebf%d" % i, [1, D], BF16) for i in range(2)]
    wg_f = nc.alloc_sbuf_tensor("wg_f", [128, NCH * 8], F32)
    g3_sb = nc.alloc_sbuf_tensor("g3_sb", [128, 8], F32)
    b3_sb = nc.alloc_sbuf_tensor("b3_sb", [128, 8], F32)
    ct["PS_A"] = nc.alloc_psum_tensor("PS_A", [128, 2048], F32)
    ct["PS_B"] = nc.alloc_psum_tensor("PS_B", [128, 2048], F32)
    ct["tmpa"] = nc.alloc_sbuf_tensor("tmpa", [128, T], F32)
    ct["tmpb"] = nc.alloc_sbuf_tensor("tmpb", [128, T], F32)
    ct["sq0"] = ct["tmpa"].ap()
    ct["sq1"] = ct["tmpb"].ap()
    ct["lsb"] = ct["tmpa"].ap()
    ct["lsb_bf"] = ct["tmpb"].ap().bitcast(BF16)
    for nm, shp, dt in [("tokg", [128, (T // 128) * 8], F32),
                        ("e8", [128, (T // 128) * 8], F32),
                        ("nm1s", [128, T // 128], F32),
                        ("w1", [128, 1], F32), ("w2", [128, 1], F32),
                        ("mtok", [128, 16], F32), ("rtok", [128, 16], F32),
                        ("sbc", [128, 8], F32), ("rbc", [128, 8], F32),
                        ("w3", [128, 1], F32), ("w8a", [128, 8], F32),
                        ("w8b", [128, 8], F32), ("w8c", [128, 8], F32),
                        ("r_mean", [1, T], F32), ("r_tmp", [1, T], F32),
                        ("r_rstd", [1, T], F32)]:
        ct[nm] = nc.alloc_sbuf_tensor(nm, shp, dt)

    extra = ["wl0", "wl1", "ge0", "ge1"]
    with (nc.semaphore("sem_t") as s_t, nc.semaphore("sem_v") as s_v,
          nc.semaphore("sem_s") as s_s, nc.semaphore("sem_p") as s_p,
          nc.semaphore("sem_dl") as s_dl, nc.semaphore("sem_ds") as s_ds,
          nc.Block() as blk):
        sems = dict(t=s_t, v=s_v, s=s_s, p=s_p, dl=s_dl, ds=s_ds)
        for nm in extra:
            sems[nm] = nc.alloc_semaphore(name="sem_" + nm)
        st = St(nc, blk, sems)
        cins = declare_common(nc, ct)

        def _l0(q):
            for k, (ap_, tname) in cins.items():
                st.tag(q.dma_start(out=ct[tname][:], in_=ap_), "dl", 16)
            st.tag(q.dma_start(out=x_fm.ap().rearrange("p (c u) -> p c u", u=T),
                               in_=fm_in_ap(x_in, NCH, T)), "dl", 16)
            src = AP(wg_in.tensor, 0, [[NE, 128], [NE * 128, NCH], [1, NE]])
            st.tag(q.dma_start(out=wg_f.ap().rearrange("p (c e) -> p c e", e=NE),
                               in_=src), "dl", 16)
            st.tag(q.dma_start(out=ct["lsb"][0:8, 0:128], in_=g3_in), "dl", 16)
            st.tag(q.dma_start(out=ct["lsb"][0:8, 128:256], in_=b3_in), "dl", 16)
        blk.sync(_l0)
        fence(st, "tv")

        def _tgb(t):
            t.transpose(out=ct["PS_A"][:, 0:8], in_=ct["lsb"][0:8, 0:128],
                        identity=ct["ident"][0:8, 0:8])
            st.tag(t.transpose(out=ct["PS_A"][:, 512:520], in_=ct["lsb"][0:8, 128:256],
                               identity=ct["ident"][0:8, 0:8]), "t")
        blk.tensor(_tgb)

        def _egb(s):
            st.fence_eng(s, skip=("s",))
            s.copy(out=g3_sb[:, 0:8], in_=ct["PS_A"][:, 0:8])
            st.tag(s.copy(out=b3_sb[:, 0:8], in_=ct["PS_A"][:, 512:520]), "s")
        blk.scalar(_egb)
        fence(st)

        ln_emit(st, ct, x_fm, T, xn1)
        fence(st)
        gates_emit(st, ct, x_fm, T, wg_f, None, gt_dram)
        fence(st)
        if dbg:
            def _dbg(q):
                st.fence_eng(q)
                st.tag(q.dma_start(out=fm_in_ap(xn1_out, NCH, T),
                                   in_=xn1.ap().rearrange("p (c u) -> p c u", u=T)), "ds", 16)
                st.tag(q.dma_start(out=gt_out, in_=ct["lsb_bf"][0:8, 0:T]), "ds", 16)
                st.tag(q.dma_start(out=tokg_out, in_=ct["tokg"][:, :]), "ds", 16)
                st.tag(q.dma_start(out=e8_out, in_=ct["e8"][:, :]), "ds", 16)
            blk.sync(_dbg)
            fence(st, "q")
        moe_l0_emit(st, ct, x_fm, xn1, T, we_in, be_in, gt_dram)
        fence(st)

        def _sx1(q):
            st.fence_eng(q)
            st.tag(q.dma_start(out=fm_in_ap(x1_out, NCH, T),
                               in_=x_fm.ap().rearrange("p (c u) -> p c u", u=T)), "ds", 16)
        blk.sync(_sx1)

        xgbf = ct["xg"]
        ds_base = st.cnt["ds"]

        def _pre(v, c, _st=st, _ds0=ds_base):
            if c >= 2:
                _st.w(v, "ds", _ds0 + 16 * (c - 1))
        ln23_emit(st, ct, x_fm, T, g3_sb, b3_sb,
                  lambda c: ct["gbb"][:, 0:T] if (c % 2 == 0) else xgbf.ap().bitcast(BF16)[:, 0:T],
                  chunk_pre_wait=_pre)
        v_norm0 = st.cnt["v"] - NCH

        def _sxn2(q):
            st.fence_eng(q, skip=("v", "ds"))
            for c in range(NCH):
                st.w(q, "v", v_norm0 + c + 1)
                buf = ct["gbb"][:, 0:T] if (c % 2 == 0) else xgbf.ap().bitcast(BF16)[:, 0:T]
                st.tag(q.dma_start(
                    out=AP(xn2_out.tensor, c * 128 * T, [[T, 128], [1, T]]),
                    in_=buf), "ds", 16)
            st.w(q, "ds")
        blk.sync(_sxn2)
    return nc


def inmaps_launch_A(inputs, consts):
    x = np.asarray(inputs["L0_attn_x"])
    wg = np.ascontiguousarray(inputs["L0_ffn_Wg"])
    we = np.ascontiguousarray(inputs["L0_ffn_We"])
    be = np.ascontiguousarray(inputs["L0_ffn_be"])
    g3 = np.ascontiguousarray(inputs["L0_attn_ln_gamma"].reshape(8, 128))
    b3 = np.ascontiguousarray(inputs["L0_attn_ln_beta"].reshape(8, 128))
    maps = []
    for c in range(8):
        b, h = c // 2, c % 2
        m = common_inmap(consts)
        m["x_in"] = np.ascontiguousarray(x[b, h * TOK:(h + 1) * TOK, :].T)
        m["wg_in"] = wg
        m["we_in"] = we
        m["be_in"] = be
        m["g3_in"] = g3
        m["b3_in"] = b3
        maps.append(m)
    return maps


def kernel(**inputs):
    consts = host_consts()
    ncA = build_launch_A()
    resA = run_bass_kernel_spmd(ncA, inmaps_launch_A(inputs, consts),
                                core_ids=list(range(8)))
    raise NotImplementedError("launches B-E pending")


def build_launch_B():
    nc = bass.Bass("TRN2", target_bir_lowering=False, debug=False, **BASS_KW)
    T = S
    NT = T // 512
    xn2_in = nc.dram_tensor("xn2_in", [D, T], BF16, kind="ExternalInput").ap()
    win_in = nc.dram_tensor("win_in", [D, DI], F32, kind="ExternalInput").ap()
    bin_in = nc.dram_tensor("bin_in", [16, 128], F32, kind="ExternalInput").ap()
    convw_in = nc.dram_tensor("convw_in", [4, DIH], F32, kind="ExternalInput").ap()
    convb_in = nc.dram_tensor("convb_in", [8, 128], F32, kind="ExternalInput").ap()
    wx_in = nc.dram_tensor("wx_in", [DIH, 96], F32, kind="ExternalInput").ap()
    u_out = nc.dram_tensor("u_out", [DIH, T], F32, kind="ExternalOutput").ap()
    res_out = nc.dram_tensor("res_out", [DIH, T], F32, kind="ExternalOutput").ap()
    xdbl_out = nc.dram_tensor("xdbl_out", [96, T], F32, kind="ExternalOutput").ap()

    ct = {}
    xn2 = nc.alloc_sbuf_tensor("xn2", [128, NCH * T], BF16)
    ubf = nc.alloc_sbuf_tensor("ubf", [128, 8 * T], BF16)
    wstage = [nc.alloc_sbuf_tensor("wstg%d" % i, [128, 1024], F32) for i in range(2)]
    wbuf = [nc.alloc_sbuf_tensor("wbufb%d" % i, [128, 1024], BF16) for i in range(2)]
    wxf = nc.alloc_sbuf_tensor("wxf", [128, 8 * 96], F32)
    wxb = nc.alloc_sbuf_tensor("wxb", [128, 8 * 96], BF16)
    upad = nc.alloc_sbuf_tensor("upad", [128, 515], F32)
    uconv = nc.alloc_sbuf_tensor("uconv", [128, 512], F32)
    ustore = [nc.alloc_sbuf_tensor("ustore%d" % i, [128, 512], F32) for i in range(2)]
    rstore = [nc.alloc_sbuf_tensor("rstore%d" % i, [128, 512], F32) for i in range(2)]
    xdst = [nc.alloc_sbuf_tensor("xdst%d" % i, [96, 512], F32) for i in range(2)]
    prev3 = nc.alloc_sbuf_tensor("prev3", [128, 3], F32)
    cw_sb = nc.alloc_sbuf_tensor("cw_sb", [128, 8 * 4], F32)
    cb_sb = nc.alloc_sbuf_tensor("cb_sb", [128, 8], F32)
    bin_sb = nc.alloc_sbuf_tensor("bin_sb", [128, 16], F32)
    ct["PS_A"] = nc.alloc_psum_tensor("PS_A", [128, 2048], F32)
    ct["PS_B"] = nc.alloc_psum_tensor("PS_B", [128, 2048], F32)

    extra = ["wl0", "wl1", "us0", "us1", "rs0", "rs1", "xs0", "xs1"]
    with (nc.semaphore("sem_t") as s_t, nc.semaphore("sem_v") as s_v,
          nc.semaphore("sem_s") as s_s, nc.semaphore("sem_p") as s_p,
          nc.semaphore("sem_dl") as s_dl, nc.semaphore("sem_ds") as s_ds,
          nc.Block() as blk):
        sems = dict(t=s_t, v=s_v, s=s_s, p=s_p, dl=s_dl, ds=s_ds)
        for nm in extra:
            sems[nm] = nc.alloc_semaphore(name="sem_" + nm)
        st = St(nc, blk, sems)
        cins = declare_common(nc, ct)

        def _l0(q):
            for k, (ap_, tname) in cins.items():
                st.tag(q.dma_start(out=ct[tname][:], in_=ap_), "dl", 16)
            st.tag(q.dma_start(out=xn2.ap().rearrange("p (c u) -> p c u", u=T),
                               in_=fm_in_ap(xn2_in, NCH, T)), "dl", 16)
            st.tag(q.dma_start(out=wxf.ap().rearrange("p (c u) -> p c u", u=96),
                               in_=AP(wx_in.tensor, 0, [[96, 128], [96 * 128, 8], [1, 96]])),
                   "dl", 16)
            for c in range(8):
                st.tag(q.dma_start(out=cw_sb[:, c * 4:(c + 1) * 4],
                                   in_=AP(convw_in.tensor, c * 128,
                                          [[1, 128], [DIH, 4]],),
                                   allow_slow_non_contiguous=True), "dl", 16)
            st.tag(q.dma_start(out=cb_sb[:, 0:8],
                               in_=AP(convb_in.tensor, 0, [[1, 128], [128, 8]]),
                               allow_slow_non_contiguous=True), "dl", 16)
            st.tag(q.dma_start(out=bin_sb[:, 0:16],
                               in_=AP(bin_in.tensor, 0, [[1, 128], [128, 16]]),
                               allow_slow_non_contiguous=True), "dl", 16)
        blk.sync(_l0)

        def _cwx(p):
            st.fence_eng(p, skip=("p", "ds"))
            st.tag(p.tensor_copy(out=wxb[:, :], in_=wxf[:, :]), "p")
        blk.gpsimd(_cwx)
        fence(st)

        tA, vA, sA, pA = st.cnt["t"], st.cnt["v"], st.cnt["s"], st.cnt["p"]
        dlA = st.cnt["dl"]
        cast_p, mm_t, evac_s, conv_v, silu_s, cast_v = {}, {}, {}, {}, {}, {}
        store_ds = {}
        dl_w = {}
        for oc in range(16):
            pA += 1
            cast_p[oc] = pA
            for tt in range(NT):
                tA += 1
                mm_t[(oc, tt)] = tA
                sA += 1
                evac_s[(oc, tt)] = sA
                if oc < 8:
                    vA += 3
                    conv_v[(oc, tt)] = vA
                    sA += 1
                    silu_s[(oc, tt)] = sA
                    vA += 1
                    cast_v[(oc, tt)] = vA

        def _wload0(q):
            st.fence_eng(q, skip=("dl", "ds"))
            for oc2 in range(2):
                ws = "wl%d" % (oc2 % 2)
                src = AP(win_in.tensor, oc2 * 128,
                         [[DI, 128], [128 * DI, NCH], [1, 128]])
                st.tag(q.dma_start(out=wstage[oc2 % 2].ap().rearrange(
                    "p (kc j) -> p kc j", j=128), in_=src), ws, 16)
                dl_w[oc2] = (ws, st.cnt[ws])
        blk.sync(_wload0)

        for oc in range(16):
            if oc + 2 < 16:
                def _wpre(q, oc2=oc + 2):
                    st.w(q, "p", cast_p[oc2 - 2])
                    ws = "wl%d" % (oc2 % 2)
                    src = AP(win_in.tensor, oc2 * 128,
                             [[DI, 128], [128 * DI, NCH], [1, 128]])
                    st.tag(q.dma_start(out=wstage[oc2 % 2].ap().rearrange(
                        "p (kc j) -> p kc j", j=128), in_=src), ws, 16)
                    dl_w[oc2] = (ws, st.cnt[ws])
                blk.sync(_wpre)
            def _cast(p, oc=oc):
                st.w(p, *dl_w[oc])
                if oc >= 2:
                    st.w(p, "t", mm_t[(oc - 2, NT - 1)])
                st.tag(p.tensor_copy(out=wbuf[oc % 2][:, :], in_=wstage[oc % 2][:, :]), "p")
                assert st.cnt["p"] == cast_p[oc]
            blk.gpsimd(_cast)
            if oc < 8:
                def _z(v, oc=oc):
                    if oc > 0:
                        st.w(v, "s", silu_s[(oc - 1, NT - 1)])
                    v.memset(prev3[:, :], 0.0)
                blk.vector(_z)
            for tt in range(NT):
                def _mm(t, oc=oc, tt=tt):
                    st.w(t, "p", cast_p[oc])
                    bank = (oc * NT + tt) % 4
                    prevkey = None
                    idx = oc * NT + tt
                    if idx >= 4:
                        pidx = idx - 4
                        prevkey = (pidx // NT, pidx % NT)
                    if prevkey is not None:
                        st.w(t, "s", evac_s[prevkey])
                    for kc in range(NCH):
                        ins = t.matmul(out=ct["PS_A"][:, bank * 512:(bank + 1) * 512],
                                       lhsT=wbuf[oc % 2][:, kc * 128:(kc + 1) * 128],
                                       rhs=xn2[:, kc * T + tt * 512:kc * T + (tt + 1) * 512],
                                       start=(kc == 0), stop=(kc == NCH - 1))
                    st.tag(ins, "t")
                    assert st.cnt["t"] == mm_t[(oc, tt)]
                blk.tensor(_mm)

                if oc < 8:
                    def _evac(s, oc=oc, tt=tt):
                        st.w(s, "t", mm_t[(oc, tt)])
                        if tt > 0:
                            st.w(s, "v", conv_v[(oc, tt - 1)])
                        elif oc > 0:
                            st.w(s, "v", conv_v[(oc - 1, NT - 1)])
                        bank = (oc * NT + tt) % 4
                        st.tag(s.activation(out=upad[:, 3:515],
                                            in_=ct["PS_A"][:, bank * 512:(bank + 1) * 512],
                                            func=AFT.Identity, bias=bin_sb[:, oc:oc + 1]), "s")
                        assert st.cnt["s"] == evac_s[(oc, tt)]
                    blk.scalar(_evac)

                    def _conv(v, oc=oc, tt=tt):
                        st.w(v, "s", evac_s[(oc, tt)])
                        st.tag(v.tensor_copy(out=upad[:, 0:3], in_=prev3[:, :]), "v")
                        st.w(v, "v")
                        v.tensor_scalar(out=uconv[:, :], in0=upad[:, 0:512],
                                        scalar1=cw_sb[:, oc * 4:oc * 4 + 1],
                                        scalar2=None, op0=ALU.mult)
                        for i in range(1, 4):
                            ins = v.scalar_tensor_tensor(out=uconv[:, :], in0=upad[:, i:i + 512],
                                                         scalar=cw_sb[:, oc * 4 + i:oc * 4 + i + 1],
                                                         in1=uconv[:, :], op0=ALU.mult, op1=ALU.add)
                        st.tag(ins, "v")
                        st.tag(v.tensor_copy(out=prev3[:, :], in_=upad[:, 512:515]), "v")
                        st.w(v, "v")
                        assert st.cnt["v"] == conv_v[(oc, tt)]
                    blk.vector(_conv)

                    def _silu(s, oc=oc, tt=tt):
                        st.w(s, "v", conv_v[(oc, tt)])
                        idx = oc * NT + tt
                        if idx >= 2:
                            pidx = idx - 2
                            st.w(s, *store_ds[(pidx // NT, pidx % NT, "u")])
                            st.w(s, "v", cast_v[(pidx // NT, pidx % NT)])
                        st.tag(s.activation(out=ustore[idx % 2][:, :], in_=uconv[:, :],
                                            func=AFT.Silu, bias=cb_sb[:, oc:oc + 1]), "s")
                        assert st.cnt["s"] == silu_s[(oc, tt)]
                    blk.scalar(_silu)

                    def _ucast(v, oc=oc, tt=tt):
                        st.w(v, "s", silu_s[(oc, tt)])
                        idx = oc * NT + tt
                        st.tag(v.tensor_copy(out=ubf[:, oc * T + tt * 512:oc * T + (tt + 1) * 512],
                                             in_=ustore[idx % 2][:, :]), "v")
                        assert st.cnt["v"] == cast_v[(oc, tt)]
                    blk.vector(_ucast)

                    def _ustore(q, oc=oc, tt=tt):
                        st.w(q, "s", silu_s[(oc, tt)])
                        idx = oc * NT + tt
                        us = "us%d" % (idx % 2)
                        st.tag(q.dma_start(
                            out=AP(u_out.tensor, oc * 128 * T + tt * 512, [[T, 128], [1, 512]]),
                            in_=ustore[idx % 2][:, :]), us, 16)
                        store_ds[(oc, tt, "u")] = (us, st.cnt[us])
                    blk.sync(_ustore)
                else:
                    def _evac(s, oc=oc, tt=tt):
                        st.w(s, "t", mm_t[(oc, tt)])
                        idx = (oc - 8) * NT + tt
                        if idx >= 2:
                            pidx = idx - 2
                            st.w(s, *store_ds[((pidx // NT) + 8, pidx % NT, "r")])
                        bank = (oc * NT + tt) % 4
                        st.tag(s.activation(out=rstore[idx % 2][:, :],
                                            in_=ct["PS_A"][:, bank * 512:(bank + 1) * 512],
                                            func=AFT.Identity, bias=bin_sb[:, oc:oc + 1]), "s")
                        assert st.cnt["s"] == evac_s[(oc, tt)]
                    blk.scalar(_evac)

                    def _rstore(q, oc=oc, tt=tt):
                        st.w(q, "s", evac_s[(oc, tt)])
                        idx = (oc - 8) * NT + tt
                        rs = "rs%d" % (idx % 2)
                        st.tag(q.dma_start(
                            out=AP(res_out.tensor, (oc - 8) * 128 * T + tt * 512,
                                   [[T, 128], [1, 512]]),
                            in_=rstore[idx % 2][:, :]), rs, 16)
                        store_ds[(oc, tt, "r")] = (rs, st.cnt[rs])
                    blk.sync(_rstore)
        fence(st)

        xd_t, xd_s, xd_ds = {}, {}, {}
        tA2, sA2 = st.cnt["t"], st.cnt["s"]
        for tt in range(NT):
            tA2 += 1
            xd_t[tt] = tA2
            sA2 += 1
            xd_s[tt] = sA2

        for tt in range(NT):
            def _xmm(t, tt=tt):
                if tt >= 2:
                    st.w(t, "s", xd_s[tt - 2])
                bank = tt % 2
                for kc in range(8):
                    ins = t.matmul(out=ct["PS_B"][0:96, bank * 512:bank * 512 + 512],
                                   lhsT=wxb[:, kc * 96:(kc + 1) * 96],
                                   rhs=ubf[:, kc * T + tt * 512:kc * T + (tt + 1) * 512],
                                   start=(kc == 0), stop=(kc == 7))
                st.tag(ins, "t")
            blk.tensor(_xmm)

            def _xevac(s, tt=tt):
                st.w(s, "t", xd_t[tt])
                if tt >= 2:
                    st.w(s, *xd_ds[tt - 2])
                bank = tt % 2
                st.tag(s.copy(out=xdst[tt % 2][0:96, :],
                              in_=ct["PS_B"][0:96, bank * 512:bank * 512 + 512]), "s")
            blk.scalar(_xevac)

            def _xstore(q, tt=tt):
                st.w(q, "s", xd_s[tt])
                xs = "xs%d" % (tt % 2)
                st.tag(q.dma_start(
                    out=AP(xdbl_out.tensor, tt * 512, [[T, 96], [1, 512]]),
                    in_=xdst[tt % 2][0:96, :]), xs, 16)
                xd_ds[tt] = (xs, st.cnt[xs])
            blk.sync(_xstore)

        def _fin(q):
            st.fence_eng(q)
        blk.sync(_fin)
    return nc


def inmaps_launch_B(inputs, consts, xn2_halves):
    win = np.asarray(inputs["L0_attn_W_in"])
    bin_ = np.asarray(inputs["L0_attn_b_in"])
    cw = np.asarray(inputs["L0_attn_conv_w"])
    cb = np.asarray(inputs["L0_attn_conv_b"])
    wx = np.asarray(inputs["L0_attn_W_x"])
    maps = []
    for c in range(8):
        b, dh = c // 2, c % 2
        m = common_inmap(consts)
        m["xn2_in"] = np.concatenate([np.asarray(xn2_halves[2 * b]),
                                      np.asarray(xn2_halves[2 * b + 1])], axis=1)
        ucols = slice(dh * DIH, (dh + 1) * DIH)
        rcols = slice(DI + dh * DIH, DI + (dh + 1) * DIH)
        m["win_in"] = np.ascontiguousarray(
            np.concatenate([win[:, ucols], win[:, rcols]], axis=1))
        m["bin_in"] = np.ascontiguousarray(
            np.concatenate([bin_[ucols], bin_[rcols]]).reshape(16, 128))
        m["convw_in"] = np.ascontiguousarray(cw[:, dh * DIH:(dh + 1) * DIH])
        m["convb_in"] = np.ascontiguousarray(cb[dh * DIH:(dh + 1) * DIH].reshape(8, 128))
        m["wx_in"] = np.ascontiguousarray(wx[dh * DIH:(dh + 1) * DIH, :])
        maps.append(m)
    return maps


def build_launch_C():
    nc = bass.Bass("TRN2", target_bir_lowering=False, debug=False, **BASS_KW)
    T = S
    NCK = T // 512
    u_in = nc.dram_tensor("u_in", [DIH, T], F32, kind="ExternalInput").ap()
    res_in = nc.dram_tensor("res_in", [DIH, T], F32, kind="ExternalInput").ap()
    xd0_in = nc.dram_tensor("xd0_in", [96, T], F32, kind="ExternalInput").ap()
    xd1_in = nc.dram_tensor("xd1_in", [96, T], F32, kind="ExternalInput").ap()
    wdt_in = nc.dram_tensor("wdt_in", [DTR, DIH], F32R, kind="ExternalInput").ap()
    bdt_in = nc.dram_tensor("bdt_in", [8, 128], F32, kind="ExternalInput").ap()
    alog_in = nc.dram_tensor("alog_in", [DIH, NST], F32, kind="ExternalInput").ap()
    dvec_in = nc.dram_tensor("dvec_in", [8, 128], F32, kind="ExternalInput").ap()
    wout_in = nc.dram_tensor("wout_in", [DIH, D], F32R, kind="ExternalInput").ap()
    bones_in = nc.dram_tensor("bones_in", [128, 16 * 128], F32R, kind="ExternalInput").ap()
    part_out = nc.dram_tensor("part_out", [D, T], F32, kind="ExternalOutput").ap()
    xdbl_d = nc.dram_tensor("xdbl_d", [96, T], F32R, kind="Internal").ap()
    delta_d = nc.dram_tensor("delta_d", [DIH, T], F32, kind="Internal").ap()
    dtu_d = nc.dram_tensor("dtu_d", [DIH, T], F32, kind="Internal").ap()
    a_d = nc.dram_tensor("a_d", [DIH, NST], F32, kind="Internal").ap()

    ct = {}
    u_ck = nc.alloc_sbuf_tensor("u_ck", [128, 8 * 512], F32)
    res_ck = nc.alloc_sbuf_tensor("res_ck", [128, 8 * 512], F32)
    delta = nc.alloc_sbuf_tensor("delta", [128, 8 * 512], F32)
    dtu = nc.alloc_sbuf_tensor("dtu", [128, 8 * 512], F32)
    xdsum = nc.alloc_sbuf_tensor("xdsum", [96, T], F32R)
    wout = nc.alloc_sbuf_tensor("wout", [128, 64 * 128], F32R)
    wdt = nc.alloc_sbuf_tensor("wdt", [64, 8 * 128], F32R)
    bones = nc.alloc_sbuf_tensor("bones", [128, 16 * 128], F32R)
    yg = nc.alloc_sbuf_tensor("yg", [128, 8 * 512], F32R)
    A_il = nc.alloc_sbuf_tensor("A_il", [128, 128], F32)
    asb = nc.alloc_sbuf_tensor("asb", [128, 8 * 16], F32)
    Dv = nc.alloc_sbuf_tensor("Dv", [128, 8], F32)
    bdt_sb = nc.alloc_sbuf_tensor("bdt_sb", [128, 8], F32)
    hstate = nc.alloc_sbuf_tensor("hstate", [128, 128], F32)
    dlt16 = [nc.alloc_sbuf_tensor("dlt16_%d" % i, [128, 512], F32) for i in range(2)]
    dtu16 = [nc.alloc_sbuf_tensor("dtu16_%d" % i, [128, 512], F32) for i in range(2)]
    B16 = nc.alloc_sbuf_tensor("B16", [128, 512], F32)
    C16 = nc.alloc_sbuf_tensor("C16", [128, 512], F32)
    dAt = [nc.alloc_sbuf_tensor("dAt%d" % i, [128, 512], F32) for i in range(2)]
    dtuB = [nc.alloc_sbuf_tensor("dtuB%d" % i, [128, 512], F32) for i in range(2)]
    hbuf = [nc.alloc_sbuf_tensor("hbuf%d" % i, [128, 512], F32) for i in range(2)]
    hCr = [nc.alloc_sbuf_tensor("hCr%d" % i, [128, 512], F32R) for i in range(2)]
    ydt = nc.alloc_sbuf_tensor("ydt", [128, 512], F32)
    sut = nc.alloc_sbuf_tensor("sut", [128, 512], F32)
    outst = [nc.alloc_sbuf_tensor("outst%d" % i, [128, 512], F32) for i in range(2)]
    PS_A = nc.alloc_psum_tensor("PS_A", [128, 2048], F32)
    PS_B = nc.alloc_psum_tensor("PS_B", [128, 2048], F32)
    ct["PS_A"], ct["PS_B"] = PS_A, PS_B

    extra = ["sc0", "sc1", "dd", "bc"]
    with (nc.semaphore("sem_t") as s_t, nc.semaphore("sem_v") as s_v,
          nc.semaphore("sem_s") as s_s, nc.semaphore("sem_p") as s_p,
          nc.semaphore("sem_dl") as s_dl, nc.semaphore("sem_ds") as s_ds,
          nc.Block() as blk):
        sems = dict(t=s_t, v=s_v, s=s_s, p=s_p, dl=s_dl, ds=s_ds)
        for nm in extra:
            sems[nm] = nc.alloc_semaphore(name="sem_" + nm)
        st = St(nc, blk, sems)
        cins = declare_common(nc, ct)

        def _l0(q):
            for k, (ap_, tname) in cins.items():
                st.tag(q.dma_start(out=ct[tname][:], in_=ap_), "dl", 16)
            st.tag(q.dma_start(out=wdt.ap().rearrange("p (c u) -> p c u", u=128),
                               in_=AP(wdt_in.tensor, 0, [[DIH, 64], [128, 8], [1, 128]])), "dl", 16)
            st.tag(q.dma_start(
                out=wout.ap().rearrange("p (kc oc u) -> p kc oc u", oc=8, u=128),
                in_=AP(wout_in.tensor, 0, [[D, 128], [128 * D, 8], [128, 8], [1, 128]])), "dl", 16)
            st.tag(q.dma_start(out=bones[:, :], in_=bones_in), "dl", 16)
            st.tag(q.dma_start(out=asb.ap().rearrange("p (c u) -> p c u", u=NST),
                               in_=AP(alog_in.tensor, 0, [[NST, 128], [NST * 128, 8], [1, NST]])), "dl", 16)
            st.tag(q.dma_start(out=Dv[:, 0:8],
                               in_=AP(dvec_in.tensor, 0, [[1, 128], [128, 8]]),
                               allow_slow_non_contiguous=True), "dl", 16)
            st.tag(q.dma_start(out=bdt_sb[:, 0:8],
                               in_=AP(bdt_in.tensor, 0, [[1, 128], [128, 8]]),
                               allow_slow_non_contiguous=True), "dl", 16)
            st.tag(q.dma_start(out=u_ck[0:96, 0:4096], in_=xd0_in), "dl", 16)
            st.tag(q.dma_start(out=res_ck[0:96, 0:4096], in_=xd1_in), "dl", 16)
        blk.sync(_l0)
        fence(st, "vs")

        def _xds(v):
            v.memset(hstate[:, :], 0.0)
            st.tag(v.tensor_add(out=xdsum[0:96, :], in0=u_ck[0:96, 0:4096],
                                in1=res_ck[0:96, 0:4096]), "v")
        blk.vector(_xds)

        def _aprep(s):
            st.fence_eng(s, skip=("s",))
            st.tag(s.activation(out=asb[:, :], in_=asb[:, :], func=AFT.Exp), "s")
        blk.scalar(_aprep)

        def _aneg(v):
            st.fence_eng(v, skip=("v",))
            st.tag(v.tensor_scalar(out=asb[:, :], in0=asb[:, :], scalar1=-1.0,
                                   scalar2=None, op0=ALU.mult), "v")
        blk.vector(_aneg)

        def _adma(q):
            st.fence_eng(q)
            st.tag(q.dma_start(out=xdbl_d, in_=xdsum[0:96, :]), "ds", 16)
            st.tag(q.dma_start(out=a_d, in_=asb.ap().rearrange("p (c u) -> p c u", u=NST)), "ds", 16)
            st.w(q, "ds")
            for j in range(128):
                st.tag(q.dma_start(out=A_il[:, j:j + 1],
                                   in_=AP(a_d.tensor, 8 * j * NST, [[NST, 8], [1, NST], [0, 1]])),
                       "dl", 16)
        blk.sync(_adma)
        fence(st)

        for ck in range(NCK):
            dlbase = st.cnt["dl"]

            def _cl(q, ck=ck):
                if ck > 0:
                    st.fence_eng(q, skip=("dl", "ds"))
                for src_d, dst in ((u_in, u_ck), (res_in, res_ck)):
                    st.tag(q.dma_start(
                        out=dst.ap().rearrange("p (c u) -> p c u", u=512),
                        in_=AP(src_d.tensor, ck * 512, [[T, 128], [128 * T, 8], [1, 512]])),
                        "dl", 16)
            blk.sync(_cl)

            dmm_t, dsp_s, dsp_v = {}, {}, {}
            tA, sA, vA = st.cnt["t"], st.cnt["s"], st.cnt["v"]
            for dc in range(8):
                tA += 1
                dmm_t[dc] = tA
                sA += 4
                dsp_s[dc] = sA
                vA += 1
                dsp_v[dc] = vA
            for dc in range(8):
                def _dmm(t, ck=ck, dc=dc):
                    if dc == 0:
                        st.fence_eng(t, skip=("t",))
                    if dc >= 2:
                        st.w(t, "s", dsp_s[dc - 2])
                    ins = t.matmul(out=PS_A[:, (dc % 2) * 512:(dc % 2 + 1) * 512],
                                   lhsT=wdt[:, dc * 128:(dc + 1) * 128],
                                   rhs=xdsum[0:64, ck * 512:(ck + 1) * 512],
                                   start=True, stop=True)
                    st.tag(ins, "t")
                    assert st.cnt["t"] == dmm_t[dc]
                blk.tensor(_dmm)

                def _dsp(s, ck=ck, dc=dc):
                    st.w(s, "t", dmm_t[dc])
                    if dc >= 1:
                        st.w(s, "v", dsp_v[dc - 1])
                    pv = PS_A[:, (dc % 2) * 512:(dc % 2 + 1) * 512]
                    s.activation(out=sut[:, :], in_=pv, func=AFT.Abs,
                                 bias=bdt_sb[:, dc:dc + 1])
                    s.activation(out=sut[:, :], in_=sut[:, :], func=AFT.Exp, scale=-1.0)
                    s.activation(out=sut[:, :], in_=sut[:, :], func=AFT.Ln, bias=1.0)
                    st.tag(s.activation(out=ydt[:, :], in_=pv, func=AFT.Relu,
                                        bias=bdt_sb[:, dc:dc + 1]), "s", 4)
                    assert st.cnt["s"] == dsp_s[dc]
                blk.scalar(_dsp)

                def _dv(v, ck=ck, dc=dc):
                    st.w(v, "s", dsp_s[dc])
                    st.tag(v.tensor_add(out=delta[:, dc * 512:(dc + 1) * 512],
                                        in0=sut[:, :], in1=ydt[:, :]), "v")
                    assert st.cnt["v"] == dsp_v[dc]
                blk.vector(_dv)

            def _dtu(v, ck=ck):
                st.fence_eng(v, skip=("v",))
                for kc in range(8):
                    ins = v.tensor_mul(out=dtu[:, kc * 512:(kc + 1) * 512],
                                       in0=delta[:, kc * 512:(kc + 1) * 512],
                                       in1=u_ck[:, kc * 512:(kc + 1) * 512])
                st.tag(ins, "v")
            blk.vector(_dtu)

            def _dst(q, ck=ck):
                st.fence_eng(q, skip=("dl", "ds", "sc0", "sc1", "bc", "dd"))
                st.tag(q.dma_start(
                    out=AP(delta_d.tensor, ck * 512, [[T, 128], [128 * T, 8], [1, 512]]),
                    in_=delta.ap().rearrange("p (c u) -> p c u", u=512)), "dd", 16)
                st.tag(q.dma_start(
                    out=AP(dtu_d.tensor, ck * 512, [[T, 128], [128 * T, 8], [1, 512]]),
                    in_=dtu.ap().rearrange("p (c u) -> p c u", u=512)), "dd", 16)
                st.w(q, "dd")
                st.tag(q.dma_start(out=B16[:, :],
                                   in_=AP(xdbl_d.tensor, 64 * T + ck * 512,
                                          [[0, 8], [T, 16], [1, 512]]).bitcast(F32)), "bc", 16)
                st.tag(q.dma_start(out=C16[:, :],
                                   in_=AP(xdbl_d.tensor, 80 * T + ck * 512,
                                          [[0, 8], [T, 16], [1, 512]]).bitcast(F32)), "bc", 16)
            blk.sync(_dst)

            base = dict(s=st.cnt["s"], p=st.cnt["p"], v=st.cnt["v"], t=st.cnt["t"])
            sdl, ss, sp, sv, stt = {}, {}, {}, {}, {}
            sA2, pA2, vA2, tA2 = base["s"], base["p"], base["v"], base["t"]
            scc = {0: st.cnt["sc0"], 1: st.cnt["sc1"]}
            for j in range(128):
                scc[j % 2] += 32
                sdl[j] = ("sc%d" % (j % 2), scc[j % 2])
                sA2 += 1
                ss[j] = sA2
                pA2 += 1
                sp[j] = pA2
                vA2 += 2
                sv[j] = vA2
                tA2 += 1
                stt[j] = tA2

            def _sdma(q, ck=ck):
                for j in range(128):
                    if j >= 2:
                        st.w(q, "s", ss[j - 2])
                        st.w(q, "p", sp[j - 2])
                    scs = "sc%d" % (j % 2)
                    st.tag(q.dma_start(out=dlt16[j % 2][:, :],
                                       in_=AP(delta_d.tensor, 8 * j * T + ck * 512,
                                              [[T, 8], [0, 16], [1, 512]])), scs, 16)
                    st.tag(q.dma_start(out=dtu16[j % 2][:, :],
                                       in_=AP(dtu_d.tensor, 8 * j * T + ck * 512,
                                              [[T, 8], [0, 16], [1, 512]])), scs, 16)
                    assert st.cnt[scs] == sdl[j][1]
            blk.sync(_sdma)

            def _sact(s, ck=ck):
                for j in range(128):
                    st.w(s, *sdl[j])
                    if j == 0:
                        st.w(s, "bc", st.cnt["bc"])
                    if j >= 2:
                        st.w(s, "v", sv[j - 2])
                    st.tag(s.activation(out=dAt[j % 2][:, :], in_=dlt16[j % 2][:, :],
                                        func=AFT.Exp, scale=A_il[:, j:j + 1]), "s")
            blk.scalar(_sact)

            def _sp_(p, ck=ck):
                for j in range(128):
                    st.w(p, *sdl[j])
                    if j == 0:
                        st.w(p, "bc", st.cnt["bc"])
                    if j >= 2:
                        st.w(p, "v", sv[j - 2])
                    st.tag(p.tensor_mul(out=dtuB[j % 2][:, :], in0=dtu16[j % 2][:, :],
                                        in1=B16[:, :]), "p")
            blk.gpsimd(_sp_)

            def _sv_(v, ck=ck):
                for j in range(128):
                    st.w(v, "s", ss[j])
                    st.w(v, "p", sp[j])
                    if j >= 2:
                        st.w(v, "t", stt[j - 2])
                    v.tensor_tensor_scan(out=hbuf[j % 2][:, :], data0=dAt[j % 2][:, :],
                                         data1=dtuB[j % 2][:, :],
                                         initial=hstate[:, j:j + 1],
                                         op0=ALU.mult, op1=ALU.add)
                    st.tag(v.tensor_copy(out=hstate[:, j:j + 1],
                                         in_=hbuf[j % 2][:, 511:512]), "v")
                    st.tag(v.tensor_mul(out=hCr[j % 2][:, :], in0=hbuf[j % 2][:, :],
                                        in1=C16[:, :]), "v")
            blk.vector(_sv_)

            def _st_(t, ck=ck):
                if ck > 0:
                    st.w(t, "s", ct["_prev_wev7"])
                for j in range(128):
                    st.w(t, "v", sv[j])
                    st.tag(t.matmul(out=PS_A.ap().bitcast(F32).rearrange(
                        "p (oc u) -> p oc u", u=512)[:, j // 16 if False else 0, 0:512]
                        if False else
                        (PS_A[:, (j // 16) * 512:(j // 16 + 1) * 512] if j < 64 else
                         PS_B[:, ((j // 16) - 4) * 512:((j // 16) - 3) * 512]),
                        lhsT=bones[:, (j % 16) * 128:(j % 16 + 1) * 128],
                        rhs=hCr[j % 2][:, :],
                        start=(j % 16 == 0), stop=(j % 16 == 15)), "t")
            blk.tensor(_st_)

            yg_v = {}
            for oc in range(8):
                def _yev(v, ck=ck, oc=oc):
                    if oc == 0:
                        st.w(v, "t", stt[127])
                        st.w(v, "s", st.cnt["s"])
                    pv = (PS_A[:, oc * 512:(oc + 1) * 512] if oc < 4 else
                          PS_B[:, (oc - 4) * 512:(oc - 3) * 512])
                    st.tag(v.scalar_tensor_tensor(out=ydt[:, :],
                                                  in0=u_ck[:, oc * 512:(oc + 1) * 512],
                                                  scalar=Dv[:, oc:oc + 1], in1=pv,
                                                  op0=ALU.mult, op1=ALU.add), "v")
                blk.vector(_yev)

                def _ysu(s, ck=ck, oc=oc):
                    st.w(s, "v", st.cnt["v"])
                    st.tag(s.activation(out=sut[:, :], in_=res_ck[:, oc * 512:(oc + 1) * 512],
                                        func=AFT.Silu), "s")
                blk.scalar(_ysu)

                def _ymu(v, ck=ck, oc=oc):
                    st.w(v, "s", st.cnt["s"])
                    st.tag(v.tensor_mul(out=yg[:, oc * 512:(oc + 1) * 512],
                                        in0=ydt[:, :], in1=sut[:, :]), "v")
                    yg_v[oc] = st.cnt["v"]
                blk.vector(_ymu)

            wev_s = {}
            for outc in range(8):
                def _wmm(t, ck=ck, outc=outc):
                    if outc == 0:
                        st.w(t, "v", yg_v[7])
                    if outc >= 4:
                        st.w(t, "s", wev_s[outc - 4])
                    for kc in range(8):
                        ins = t.matmul(out=PS_A[:, (outc % 4) * 512:(outc % 4 + 1) * 512],
                                       lhsT=wout[:, (kc * 8 + outc) * 128:(kc * 8 + outc + 1) * 128],
                                       rhs=yg[:, kc * 512:(kc + 1) * 512],
                                       start=(kc == 0), stop=(kc == 7))
                    st.tag(ins, "t")
                blk.tensor(_wmm)

                def _wev(s, ck=ck, outc=outc):
                    st.w(s, "t", st.cnt["t"])
                    if outc >= 2:
                        st.w(s, "ds", st.cnt["ds"])
                    st.tag(s.copy(out=outst[outc % 2][:, :],
                                  in_=PS_A[:, (outc % 4) * 512:(outc % 4 + 1) * 512]), "s")
                    wev_s[outc] = st.cnt["s"]
                blk.scalar(_wev)

                if outc == 7:
                    ct["_prev_wev7"] = None

                def _wst(q, ck=ck, outc=outc):
                    st.w(q, "s", wev_s[outc])
                    st.tag(q.dma_start(
                        out=AP(part_out.tensor, outc * 128 * T + ck * 512,
                               [[T, 128], [1, 512]]),
                        in_=outst[outc % 2][:, :]), "ds", 16)
                blk.sync(_wst)
            ct["_prev_wev7"] = wev_s[7]

        def _fin(q):
            st.fence_eng(q)
        blk.sync(_fin)
    return nc


def inmaps_launch_C(inputs, consts, u_arrs, res_arrs, xd_arrs):
    wdt = np.asarray(inputs["L0_attn_W_dt"])
    bdt = np.asarray(inputs["L0_attn_b_dt"])
    alog = np.asarray(inputs["L0_attn_A_log"])
    dvec = np.asarray(inputs["L0_attn_D"])
    wo = np.asarray(inputs["L0_attn_W_out"])
    maps = []
    for c in range(8):
        b, dh = c // 2, c % 2
        m = common_inmap(consts)
        m["u_in"] = np.asarray(u_arrs[c])
        m["res_in"] = np.asarray(res_arrs[c])
        m["xd0_in"] = np.asarray(xd_arrs[2 * b])
        m["xd1_in"] = np.asarray(xd_arrs[2 * b + 1])
        sl = slice(dh * DIH, (dh + 1) * DIH)
        m["wdt_in"] = np.ascontiguousarray(wdt[:, sl])
        m["bdt_in"] = np.ascontiguousarray(bdt[sl].reshape(8, 128))
        m["alog_in"] = np.ascontiguousarray(alog[sl])
        m["dvec_in"] = np.ascontiguousarray(dvec[sl].reshape(8, 128))
        m["wout_in"] = np.ascontiguousarray(wo[sl])
        m["bones_in"] = consts["bones"]
        maps.append(m)
    return maps


def moe_l1_emit(st, ct, x_fm, xn_fm, T, w1_dram, w2_dram, b1_dram, b2_dram,
                gt_dram):
    blk = st.block
    PS_A, PS_B = ct["PS_A"], ct["PS_B"]
    NT = T // 512
    w1b, w2b = ct["w1b"], ct["w2b"]
    wstage = ct["wstage"]
    gtrb, beb, bebf = ct["gtrb"], ct["beb"], ct["bebf"]

    tA, vA, pA, sA = st.cnt["t"], st.cnt["v"], st.cnt["p"], st.cnt["s"]
    pA0 = pA
    gb_t, gbe_s, becast_v, cast_p = {}, {}, {}, {}
    h1mm_t, h1ev_s, h1g_v, omm_t, oev_v = {}, {}, {}, {}, {}
    for e in range(NE):
        tA += 1
        gb_t[e] = tA
        sA += 1
        gbe_s[e] = sA
        vA += 1
        becast_v[e] = vA
        pA += 8
        cast_p[e] = pA
        for tt in range(NT):
            for fc in range(4):
                tA += 1
                h1mm_t[(e, tt, fc)] = tA
                sA += 1
                h1ev_s[(e, tt, fc)] = sA
                vA += 1
                h1g_v[(e, tt, fc)] = vA
            for dh in range(2):
                tA += 1
                omm_t[(e, tt, dh)] = tA
                vA += 1
                oev_v[(e, tt, dh)] = vA
    dl_g, dl_w = {}, {}

    def _loads(q):
        st.fence_eng(q, skip=("dl",))
        for e in range(NE):
            gs = "ge%d" % (e % 2)
            if e >= 1:
                st.w(q, "t", omm_t[(e - 1, NT - 1, 1)])
                st.w(q, "v", becast_v[e - 1])
            st.tag(q.dma_start(out=gtrb[e % 2][0:1, 0:T],
                               in_=AP(gt_dram.tensor, gt_dram.offset + e * T,
                                      [[1, 1], [1, T]])), gs, 16)
            st.tag(q.dma_start(out=beb[e % 2][0:1, 0:D],
                               in_=AP(b2_dram.tensor, b2_dram.offset + e * D,
                                      [[1, 1], [1, D]])), gs, 16)
            st.tag(q.dma_start(out=ct["b1sb"][e % 2][:, 0:4],
                               in_=AP(b1_dram.tensor, b1_dram.offset + e * DFFP,
                                      [[1, 128], [128, 4]]),
                               allow_slow_non_contiguous=True), gs, 16)
            dl_g[e] = (gs, st.cnt[gs])
            for pc in range(4):
                gidx = e * 8 + pc
                if gidx >= 2:
                    st.w(q, "p", pA0 + gidx - 1)
                ws = "wl%d" % (gidx % 2)
                src = AP(w1_dram.tensor, w1_dram.offset + e * D * DFFP + 2 * pc * 128 * DFFP,
                         [[DFFP, 128], [128 * DFFP, 2], [128, 4], [1, 128]])
                st.tag(q.dma_start(out=wstage[gidx % 2].ap().rearrange(
                    "p (kc fc j) -> p kc fc j", fc=4, j=128), in_=src), ws, 16)
                dl_w[(e, "w1", pc)] = (ws, st.cnt[ws])
            for pc in range(4):
                gidx = e * 8 + 4 + pc
                if gidx >= 2:
                    st.w(q, "p", pA0 + gidx - 1)
                ws = "wl%d" % (gidx % 2)
                src = AP(w2_dram.tensor, w2_dram.offset + e * DFFP * D + pc * 128 * D,
                         [[D, 128], [128 * D, 1], [128, 8], [1, 128]])
                st.tag(q.dma_start(out=wstage[gidx % 2].ap().rearrange(
                    "p (kc oc j) -> p kc oc j", oc=8, j=128), in_=src), ws, 16)
                dl_w[(e, "w2", pc)] = (ws, st.cnt[ws])
    blk.sync(_loads)

    for e in range(NE):
        def _becast(v, e=e):
            st.w(v, *dl_g[e])
            if e >= 2:
                st.w(v, "t", omm_t[(e - 2, NT - 1, 1)])
            st.tag(v.tensor_copy(out=bebf[e % 2][0:1, 0:D], in_=beb[e % 2][0:1, 0:D]), "v")
            assert st.cnt["v"] == becast_v[e]
        blk.vector(_becast)

        def _gb(t, e=e):
            st.w(t, *dl_g[e])
            if e >= 1:
                st.w(t, "s", gbe_s[e - 1])
                st.w(t, "v", oev_v[(e - 1, NT - 1, 1)])
            for k in range(NT):
                ins = t.matmul(out=PS_B[:, k * 512:(k + 1) * 512],
                               lhsT=ct["ones_row_b"][0:1, :],
                               rhs=gtrb[e % 2][0:1, k * 512:(k + 1) * 512],
                               start=True, stop=True)
            st.tag(ins, "t")
            assert st.cnt["t"] == gb_t[e]
        blk.tensor(_gb)

        def _gbe(s, e=e):
            st.w(s, "t", gb_t[e])
            if e >= 1:
                st.w(s, "v", h1g_v[(e - 1, NT - 1, 3)])
            st.tag(s.copy(out=ct["gbb"][:, 0:T], in_=PS_B[:, 0:T]), "s")
            assert st.cnt["s"] == gbe_s[e]
        blk.scalar(_gbe)

        def _cast(p, e=e):
            if e >= 2:
                st.w(p, "t", omm_t[(e - 2, NT - 1, 1)])
            for pc in range(4):
                st.w(p, *dl_w[(e, "w1", pc)])
                st.tag(p.tensor_copy(out=w1b[:, pc * 1024:(pc + 1) * 1024],
                                     in_=wstage[(e * 8 + pc) % 2][:, 0:1024]), "p")
            for pc in range(4):
                st.w(p, *dl_w[(e, "w2", pc)])
                st.tag(p.tensor_copy(out=w2b[:, pc * 1024:(pc + 1) * 1024],
                                     in_=wstage[(e * 8 + 4 + pc) % 2][:, 0:1024]), "p")
            assert st.cnt["p"] == cast_p[e]
        blk.gpsimd(_cast)

        for tt in range(NT):
            for fc in range(4):
                def _h1mm(t, e=e, tt=tt, fc=fc):
                    st.w(t, "p", cast_p[e])
                    prev = None
                    if tt > 0:
                        prev = h1ev_s[(e, tt - 1, fc)]
                    elif e > 0:
                        prev = h1ev_s[(e - 1, NT - 1, fc)]
                    if prev is not None:
                        st.w(t, "s", prev)
                    for kc in range(NCH):
                        ins = t.matmul(out=PS_A[:, fc * 512:(fc + 1) * 512],
                                       lhsT=w1b[:, (kc * 4 + fc) * 128:(kc * 4 + fc + 1) * 128],
                                       rhs=xn_fm[:, kc * T + tt * 512:kc * T + (tt + 1) * 512],
                                       start=(kc == 0), stop=(kc == NCH - 1))
                    st.tag(ins, "t")
                    assert st.cnt["t"] == h1mm_t[(e, tt, fc)]
                blk.tensor(_h1mm)

                def _h1ev(s, e=e, tt=tt, fc=fc):
                    st.w(s, "t", h1mm_t[(e, tt, fc)])
                    st.w(s, "v", st.cnt["v"])
                    st.tag(s.activation(out=ct["h1f"][:, :],
                                        in_=PS_A[:, fc * 512:(fc + 1) * 512],
                                        func=AFT.Gelu, bias=ct["b1sb"][e % 2][:, fc:fc + 1]), "s")
                    assert st.cnt["s"] == h1ev_s[(e, tt, fc)]
                blk.scalar(_h1ev)

                def _h1g(v, e=e, tt=tt, fc=fc):
                    st.w(v, "s", h1ev_s[(e, tt, fc)])
                    if (tt, fc) == (0, 0):
                        st.w(v, "s", gbe_s[e])
                    prev = None
                    if tt > 0 and fc == 0:
                        prev = omm_t[(e, tt - 1, 1)]
                    elif tt == 0 and fc == 0 and e > 0:
                        prev = omm_t[(e - 1, NT - 1, 1)]
                    if prev is not None:
                        st.w(v, "t", prev)
                    st.tag(v.tensor_mul(out=ct["h1g"][:, fc * 512:(fc + 1) * 512],
                                        in0=ct["h1f"][:, :],
                                        in1=ct["gbb"][:, tt * 512:(tt + 1) * 512]), "v")
                    assert st.cnt["v"] == h1g_v[(e, tt, fc)]
                blk.vector(_h1g)

            for dh in range(2):
                def _omm(t, e=e, tt=tt, dh=dh):
                    st.w(t, "v", h1g_v[(e, tt, 3)])
                    prev = None
                    if (e, tt, dh) != (0, 0, 0):
                        pkey = (e, tt, 0) if dh == 1 else \
                            ((e, tt - 1, 1) if tt > 0 else (e - 1, NT - 1, 1))
                        prev = oev_v[pkey]
                    if prev is not None:
                        st.w(t, "v", prev)
                    for dc in range(4):
                        for fc in range(4):
                            t.matmul(out=PS_B[:, dc * 512:(dc + 1) * 512],
                                     lhsT=w2b[:, (fc * 8 + dh * 4 + dc) * 128:(fc * 8 + dh * 4 + dc + 1) * 128],
                                     rhs=ct["h1g"][:, fc * 512:(fc + 1) * 512],
                                     start=(fc == 0), stop=False)
                        ins = t.matmul(out=PS_B[:, dc * 512:(dc + 1) * 512],
                                       lhsT=bebf[e % 2][0:1, (dh * 4 + dc) * 128:(dh * 4 + dc + 1) * 128],
                                       rhs=gtrb[e % 2][0:1, tt * 512:(tt + 1) * 512],
                                       start=False, stop=True)
                    st.tag(ins, "t")
                    assert st.cnt["t"] == omm_t[(e, tt, dh)]
                blk.tensor(_omm)

                def _oev(v, e=e, tt=tt, dh=dh):
                    st.w(v, "t", omm_t[(e, tt, dh)])
                    xv = x_fm.ap().rearrange("p (c u) -> p c u", u=T)[
                        :, dh * 4:(dh + 1) * 4, tt * 512:(tt + 1) * 512]
                    pv = PS_B.ap().rearrange("p (dc u) -> p dc u", u=512)
                    st.tag(v.tensor_add(out=xv, in0=pv, in1=xv), "v")
                    assert st.cnt["v"] == oev_v[(e, tt, dh)]
                blk.vector(_oev)


def build_launch_D(dbg=False, stages=frozenset({"s3", "s4", "qkv"}), rope_on=True,
                   qkv_n=3, qkv_depth=5, qkv_scratch=False):
    nc = bass.Bass("TRN2", target_bir_lowering=False, debug=False, **BASS_KW)
    T = TOK
    NT = T // 512
    x1_in = nc.dram_tensor("x1_in", [D, T], F32, kind="ExternalInput").ap()
    p0_in = nc.dram_tensor("p0_in", [D, T], F32, kind="ExternalInput").ap()
    p1_in = nc.dram_tensor("p1_in", [D, T], F32, kind="ExternalInput").ap()
    bout_in = nc.dram_tensor("bout_in", [8, 128], F32, kind="ExternalInput").ap()
    wg_in = nc.dram_tensor("wg_in", [D, NE], F32, kind="ExternalInput").ap()
    we_in = nc.dram_tensor("we_in", [NE, D, D], F32, kind="ExternalInput").ap()
    be_in = nc.dram_tensor("be_in", [NE, D], F32, kind="ExternalInput").ap()
    rw_in = nc.dram_tensor("rw_in", [D, NE], F32, kind="ExternalInput").ap()
    rb_in = nc.dram_tensor("rb_in", [1, NE], F32, kind="ExternalInput").ap()
    w1_in = nc.dram_tensor("w1_in", [NE, D, DFFP], F32, kind="ExternalInput").ap()
    w2_in = nc.dram_tensor("w2_in", [NE, DFFP, D], F32, kind="ExternalInput").ap()
    b1_in = nc.dram_tensor("b1_in", [NE, DFFP], F32, kind="ExternalInput").ap()
    b2_in = nc.dram_tensor("b2_in", [NE, D], F32, kind="ExternalInput").ap()
    wqT_in = nc.dram_tensor("wqT_in", [D, D], F32, kind="ExternalInput").ap()
    wkT_in = nc.dram_tensor("wkT_in", [D, D], F32, kind="ExternalInput").ap()
    wvT_in = nc.dram_tensor("wvT_in", [D, D], F32, kind="ExternalInput").ap()
    cos_in = nc.dram_tensor("cos_in", [128, T], F32, kind="ExternalInput").ap()
    sin_in = nc.dram_tensor("sin_in", [128, T], F32, kind="ExternalInput").ap()
    x4_out = nc.dram_tensor("x4_out", [D, T], F32, kind="ExternalOutput").ap()
    q_out = nc.dram_tensor("q_out", [D, T], BF16, kind="ExternalOutput").ap()
    k_out = nc.dram_tensor("k_out", [D, T], BF16, kind="ExternalOutput").ap()
    v_out = nc.dram_tensor("v_out", [D, T], BF16, kind="ExternalOutput").ap()
    if qkv_scratch:
        q_scr = nc.dram_tensor("q_scr", [D, T], BF16, kind="Internal").ap()
        q_out = k_out = v_out = q_scr
    gt_dram = nc.dram_tensor("gt_scr", [NE, T], BF16, kind="Internal").ap()
    if dbg:
        x2_out = nc.dram_tensor("x2_out", [D, T], F32, kind="ExternalOutput").ap()
        x3_out = nc.dram_tensor("x3_out", [D, T], F32, kind="ExternalOutput").ap()

    ct = {}
    x_fm = nc.alloc_sbuf_tensor("x_fm", [128, NCH * T], F32)
    xnb = nc.alloc_sbuf_tensor("xnb", [128, NCH * T], BF16)
    ct["wstage"] = [nc.alloc_sbuf_tensor("wstage%d" % i, [128, 1024], F32) for i in range(2)]
    ct["wbuf"] = [nc.alloc_sbuf_tensor("wbuf%d" % i, [128, 4096], BF16) for i in range(2)]
    ct["w1b"] = nc.alloc_sbuf_tensor("w1b", [128, 4096], BF16)
    ct["w2b"] = nc.alloc_sbuf_tensor("w2b", [128, 4096], BF16)
    ct["xg"] = nc.alloc_sbuf_tensor("xg", [128, NCH * 512], BF16)
    ct["h1g"] = ct["xg"].ap()[:, 0:2048].bitcast(BF16)
    ct["h1f"] = nc.alloc_sbuf_tensor("h1f", [128, 512], F32)
    ct["gbb"] = nc.alloc_sbuf_tensor("gbb", [128, T], BF16)
    gtrb1 = nc.alloc_sbuf_tensor("gtrb", [1, T], BF16)
    ct["gtrb"] = [gtrb1, gtrb1]
    beb1 = nc.alloc_sbuf_tensor("beb", [1, D], F32)
    ct["beb"] = [beb1, beb1]
    ct["bebf"] = [nc.alloc_sbuf_tensor("bebf%d" % i, [1, D], BF16) for i in range(2)]
    ct["b1sb"] = [nc.alloc_sbuf_tensor("b1sb%d" % i, [128, 4], F32) for i in range(2)]
    wg_f = nc.alloc_sbuf_tensor("wg_f", [128, NCH * 8], F32)
    rw_f = nc.alloc_sbuf_tensor("rw_f", [128, NCH * 8], F32)
    rb_sb = nc.alloc_sbuf_tensor("rb_sb", [1, 8], F32)
    bout_sb = nc.alloc_sbuf_tensor("bout_sb", [128, 8], F32)
    ct["PS_A"] = nc.alloc_psum_tensor("PS_A", [128, 2048], F32)
    ct["PS_B"] = nc.alloc_psum_tensor("PS_B", [128, 2048], F32)
    ct["tmpa"] = nc.alloc_sbuf_tensor("tmpa", [128, T], F32)
    ct["tmpb"] = nc.alloc_sbuf_tensor("tmpb", [128, T], F32)
    ct["sq0"] = ct["tmpa"].ap()
    ct["sq1"] = ct["tmpb"].ap()
    ct["lsb"] = ct["tmpa"].ap()
    ct["lsb_bf"] = ct["tmpb"].ap().bitcast(BF16)
    pb0 = ct["w1b"].ap().bitcast(F32)
    pb1 = ct["w2b"].ap().bitcast(F32)
    cos_sb = ct["wbuf"][0].ap().bitcast(F32)
    sin_sb = ct["wbuf"][1].ap().bitcast(F32)
    qt1 = None
    qt2 = None
    qst = [nc.alloc_sbuf_tensor("qst%d" % i, [128, 512], BF16) for i in range(2)]
    for nm, shp, dt in [("tokg", [128, (T // 128) * 8], F32),
                        ("e8", [128, (T // 128) * 8], F32),
                        ("nm1s", [128, T // 128], F32),
                        ("w1", [128, 1], F32), ("w2", [128, 1], F32),
                        ("w3", [128, 1], F32), ("w8a", [128, 8], F32),
                        ("w8b", [128, 8], F32), ("w8c", [128, 8], F32),
                        ("mtok", [128, 16], F32), ("rtok", [128, 16], F32),
                        ("sbc", [128, 8], F32), ("rbc", [128, 8], F32),
                        ("r_mean", [1, T], F32), ("r_tmp", [1, T], F32),
                        ("r_rstd", [1, T], F32)]:
        ct[nm] = nc.alloc_sbuf_tensor(nm, shp, dt)

    extra = ["wl0", "wl1", "ge0", "ge1", "pc0", "pc1"]
    with (nc.semaphore("sem_t") as s_t, nc.semaphore("sem_v") as s_v,
          nc.semaphore("sem_s") as s_s, nc.semaphore("sem_p") as s_p,
          nc.semaphore("sem_dl") as s_dl, nc.semaphore("sem_ds") as s_ds,
          nc.Block() as blk):
        sems = dict(t=s_t, v=s_v, s=s_s, p=s_p, dl=s_dl, ds=s_ds)
        for nm in extra:
            sems[nm] = nc.alloc_semaphore(name="sem_" + nm)
        st = St(nc, blk, sems)
        cins = declare_common(nc, ct)

        def _l0(q):
            for k, (ap_, tname) in cins.items():
                st.tag(q.dma_start(out=ct[tname][:], in_=ap_), "dl", 16)
            st.tag(q.dma_start(out=x_fm.ap().rearrange("p (c u) -> p c u", u=T),
                               in_=fm_in_ap(x1_in, NCH, T)), "dl", 16)
            for w_, d_ in ((wg_in, wg_f), (rw_in, rw_f)):
                src = AP(w_.tensor, 0, [[NE, 128], [NE * 128, NCH], [1, NE]])
                st.tag(q.dma_start(out=d_.ap().rearrange("p (c e) -> p c e", e=NE),
                                   in_=src), "dl", 16)
            st.tag(q.dma_start(out=rb_sb[0:1, :], in_=rb_in), "dl", 16)
            st.tag(q.dma_start(out=bout_sb[:, 0:8],
                               in_=AP(bout_in.tensor, 0, [[1, 128], [128, 8]]),
                               allow_slow_non_contiguous=True), "dl", 16)
        blk.sync(_l0)
        fence(st)

        ln_stats_emit(st, ct, x_fm, T)
        fence(st)
        bcast_rows_emit(st, ct, ct["r_mean"], ct["r_rstd"], T)
        fence(st, "q")
        pld = {}

        def _pl(q):
            for c in range(NCH):
                pc_ = "pc%d" % (c % 2)
                if c >= 2:
                    st.w(q, "v", cmb_v[c - 2])
                st.tag(q.dma_start(out=pb0[:, 0:T] if False else (pb0 if c % 2 == 0 else pb1)[:, 0:T],
                                   in_=AP(p0_in.tensor, c * 128 * T, [[T, 128], [1, T]])), pc_, 16)
                st.tag(q.dma_start(out=(pb0 if c % 2 == 0 else pb1)[:, T:2 * T] if False
                                   else (pb0 if c % 2 == 0 else pb1).tensor.ap()[:, 0:T] if False
                                   else (pb0 if c % 2 == 0 else pb1)[:, 0:T],
                                   in_=AP(p1_in.tensor, c * 128 * T, [[T, 128], [1, T]])), pc_, 16)
                pld[c] = (pc_, st.cnt[pc_])
        cmb_v = {}

        def _pl2(q):
            for c in range(NCH):
                if c >= 1:
                    st.w(q, "v", cmb_v[c - 1])
                st.tag(q.dma_start(out=pb0[:, 0:T],
                                   in_=AP(p0_in.tensor, c * 128 * T, [[T, 128], [1, T]])), "pc0", 16)
                st.tag(q.dma_start(out=pb1[:, 0:T],
                                   in_=AP(p1_in.tensor, c * 128 * T, [[T, 128], [1, T]])), "pc0", 16)
                pld[c] = ("pc0", st.cnt["pc0"])
        vA = st.cnt["v"]
        for c in range(NCH):
            vA += 1
            cmb_v[c] = vA
        blk.sync(_pl2)

        def _cmb(v):
            st.fence_eng(v, skip=("v", "pc0"))
            for c in range(NCH):
                st.w(v, *pld[c])
                xc = x_fm[:, c * T:(c + 1) * T]
                v.tensor_sub(out=ct["tmpa"][:, 0:T], in0=xc, in1=ct["PS_A"][:, 0:T])
                v.tensor_mul(out=ct["tmpb"][:, 0:T], in0=ct["tmpa"][:, 0:T],
                             in1=ct["PS_B"][:, 0:T])
                v.scalar_tensor_tensor(out=ct["tmpb"][:, 0:T], in0=ct["tmpb"][:, 0:T],
                                       scalar=bout_sb[:, c:c + 1], in1=xc,
                                       op0=ALU.add, op1=ALU.add)
                v.tensor_add(out=ct["tmpb"][:, 0:T], in0=ct["tmpb"][:, 0:T],
                             in1=pb0[:, 0:T])
                st.tag(v.tensor_add(out=xc, in0=ct["tmpb"][:, 0:T], in1=pb1[:, 0:T]), "v")
                assert st.cnt["v"] == cmb_v[c]
        blk.vector(_cmb)
        fence(st)
        if dbg:
            def _dx2(q):
                st.fence_eng(q)
                st.tag(q.dma_start(out=fm_in_ap(x2_out, NCH, T),
                                   in_=x_fm.ap().rearrange("p (c u) -> p c u", u=T)), "ds", 16)
            blk.sync(_dx2)
            fence(st, "q")

        if "s3" in stages:
            ln_emit(st, ct, x_fm, T, xnb)
            fence(st)
            gates_emit(st, ct, x_fm, T, wg_f, None, gt_dram)
            fence(st)
            moe_l0_emit(st, ct, x_fm, xnb, T, we_in, be_in, gt_dram)
            fence(st)
        if dbg:
            def _dx3(q):
                st.fence_eng(q)
                st.tag(q.dma_start(out=fm_in_ap(x3_out, NCH, T),
                                   in_=x_fm.ap().rearrange("p (c u) -> p c u", u=T)), "ds", 16)
            blk.sync(_dx3)
            fence(st, "q")

        if "s4" in stages:
            ln_emit(st, ct, x_fm, T, xnb)
            fence(st)
            gates_emit(st, ct, x_fm, T, rw_f, rb_sb, gt_dram)
            fence(st)
            moe_l1_emit(st, ct, x_fm, xnb, T, w1_in, w2_in, b1_in, b2_in, gt_dram)
            fence(st)

        def _sx4(q):
            st.fence_eng(q)
            st.tag(q.dma_start(out=fm_in_ap(x4_out, NCH, T),
                               in_=x_fm.ap().rearrange("p (c u) -> p c u", u=T)), "ds", 16)
        blk.sync(_sx4)

        if "qkv" in stages:
            ln_emit(st, ct, x_fm, T, xnb)
            fence(st)

            def _lrope(q):
                st.tag(q.dma_start(out=cos_sb[:, 0:T], in_=cos_in), "dl", 16)
                st.tag(q.dma_start(out=sin_sb[:, 0:T], in_=sin_in), "dl", 16)
            blk.sync(_lrope)
            fence(st)

        swapmask = []
        for i in range(16):
            swapmask += [2 * i + 1, 2 * i]
        qst_hist = []
        for pj, (wT, dst, dorope) in enumerate(
                ([(wqT_in, q_out, rope_on), (wkT_in, k_out, rope_on),
                  (wvT_in, v_out, False)][:qkv_n])
                if "qkv" in stages else []):
            tA, sA, vA, pA = st.cnt["t"], st.cnt["s"], st.cnt["v"], st.cnt["p"]
            qmm_t, qev_s, qrp_v, qst_ds = {}, {}, {}, {}
            wcast_p, wld = {}, {}
            sA_, vA_, tA_, pA_ = sA, vA, tA, pA
            for oc in range(NCH):
                if qkv_depth >= 2:
                    pA_ += 1
                    wcast_p[oc] = pA_
                for tt in range(NT):
                    if qkv_depth >= 3:
                        tA_ += 1
                        qmm_t[(oc, tt)] = tA_
                    if qkv_depth >= 4:
                        sA_ += 1
                        qev_s[(oc, tt)] = sA_
                    if dorope:
                        vA_ += 2
                        qrp_v[(oc, tt)] = vA_

            def _wl(q, wT=wT, pj=pj, wcast_p=wcast_p, wld=wld):
                st.fence_eng(q, skip=("dl", "ds", "wl0", "wl1", "pc0", "pc1",
                                      "ge0", "ge1"))
                for oc in range(NCH):
                    ws = "wl%d" % (oc % 2)
                    if oc >= 2 and qkv_depth >= 2:
                        st.w(q, "p", wcast_p[oc - 2])
                    src = AP(wT.tensor, oc * 128, [[D, 128], [128 * D, 8], [1, 128]])
                    st.tag(q.dma_start(out=ct["wstage"][oc % 2].ap().rearrange(
                        "p (kc j) -> p kc j", j=128), in_=src), ws, 16)
                    wld[oc] = (ws, st.cnt[ws])
            blk.sync(_wl)

            for oc in range(NCH):
                if qkv_depth < 2:
                    continue
                def _wc(p, oc=oc, wld=wld, wcast_p=wcast_p, qmm_t=qmm_t):
                    st.w(p, *wld[oc])
                    if oc >= 2 and qkv_depth >= 3:
                        st.w(p, "t", qmm_t[(oc - 2, NT - 1)])
                    st.tag(p.tensor_copy(out=ct["xg"][:, (oc % 2) * 1024:(oc % 2) * 1024 + 1024],
                                         in_=ct["wstage"][oc % 2][:, :]), "p")
                    assert st.cnt["p"] == wcast_p[oc]
                blk.gpsimd(_wc)
                for tt in range(NT):
                    if qkv_depth < 3:
                        continue
                    def _qmm(t, oc=oc, tt=tt, wcast_p=wcast_p, qmm_t=qmm_t, qev_s=qev_s):
                        st.w(t, "p", wcast_p[oc])
                        idx = oc * NT + tt
                        if idx >= 2 and qkv_depth >= 4:
                            st.w(t, "s", qev_s[((idx - 2) // NT, (idx - 2) % NT)])
                        for kc in range(NCH):
                            ins = t.matmul(out=ct["PS_A"][:, (idx % 2) * 512:(idx % 2) * 512 + 512],
                                           lhsT=ct["xg"].ap().bitcast(BF16)[:, (oc % 2) * 2048 + kc * 128:
                                                                            (oc % 2) * 2048 + (kc + 1) * 128] if False
                                           else ct["xg"][:, (oc % 2) * 1024 * 2 // 2 + kc * 128:
                                                         (oc % 2) * 1024 + (kc + 1) * 128] if False
                                           else ct["xg"][:, (oc % 2) * 1024 + kc * 128:(oc % 2) * 1024 + (kc + 1) * 128],
                                           rhs=xnb[:, kc * T + tt * 512:kc * T + (tt + 1) * 512],
                                           start=(kc == 0), stop=(kc == NCH - 1))
                        st.tag(ins, "t")
                        assert st.cnt["t"] == qmm_t[(oc, tt)]
                    blk.tensor(_qmm)

                    if dorope:
                        def _qev(s, oc=oc, tt=tt, qmm_t=qmm_t, qev_s=qev_s, qrp_v=qrp_v):
                            st.w(s, "t", qmm_t[(oc, tt)])
                            idx = oc * NT + tt
                            if idx >= 1:
                                st.w(s, "v", qrp_v[((idx - 1) // NT, (idx - 1) % NT)])
                            st.tag(s.copy(out=ct["tmpa"][:, 0:512],
                                          in_=ct["PS_A"][:, (idx % 2) * 512:(idx % 2) * 512 + 512]), "s")
                            assert st.cnt["s"] == qev_s[(oc, tt)], (st.cnt["s"], qev_s[(oc, tt)], oc, tt)
                        blk.scalar(_qev)

                        def _qrp(v, oc=oc, tt=tt, pj=pj, qev_s=qev_s, qrp_v=qrp_v, qst_ds=qst_ds, dst=dst):
                            st.w(v, "s", qev_s[(oc, tt)])
                            idx = oc * NT + tt
                            gidx = pj * NCH * NT + idx
                            if gidx >= 2:
                                st.w(v, *qst_hist[gidx - 2])
                            v.stream_shuffle(out=ct["tmpa"][:, 512:1024],
                                             in_=ct["tmpa"][:, 0:512], mask=swapmask)
                            st.tag(v.tensor_mul(out=ct["tmpa"][:, 512:1024],
                                                in0=ct["tmpa"][:, 512:1024],
                                                in1=sin_sb[:, tt * 512:(tt + 1) * 512]), "v")
                            st.w(v, "v")
                            v.tensor_mul(out=ct["tmpa"][:, 0:512], in0=ct["tmpa"][:, 0:512],
                                         in1=cos_sb[:, tt * 512:(tt + 1) * 512])
                            st.tag(v.tensor_add(out=qst[idx % 2][:, :],
                                                in0=ct["tmpa"][:, 0:512],
                                                in1=ct["tmpa"][:, 512:1024]), "v")
                            assert st.cnt["v"] == qrp_v[(oc, tt)]
                        blk.vector(_qrp)

                        def _qst(q, oc=oc, tt=tt, qrp_v=qrp_v, qst_ds=qst_ds, dst=dst, pj=pj):
                            st.w(q, "v", qrp_v[(oc, tt)])
                            idx = oc * NT + tt
                            pc_ = "pc%d" % (idx % 2)
                            st.tag(q.dma_start(
                                out=AP(dst.tensor, oc * 128 * T + tt * 512, [[T, 128], [1, 512]]),
                                in_=qst[idx % 2][:, :]), pc_, 16)
                            qst_ds[(oc, tt)] = (pc_, st.cnt[pc_])
                            qst_hist.append((pc_, st.cnt[pc_]))
                        blk.gpsimd(_qst)
                    else:
                        if qkv_depth < 4:
                            continue
                        def _qev(s, oc=oc, tt=tt, pj=pj, qmm_t=qmm_t, qev_s=qev_s, qst_ds=qst_ds):
                            st.w(s, "t", qmm_t[(oc, tt)])
                            idx = oc * NT + tt
                            gidx = pj * NCH * NT + idx
                            if gidx >= 2 and qkv_depth >= 5:
                                st.w(s, *qst_hist[gidx - 2])
                            st.tag(s.copy(out=qst[idx % 2][:, :],
                                          in_=ct["PS_A"][:, (idx % 2) * 512:(idx % 2) * 512 + 512]), "s")
                            assert st.cnt["s"] == qev_s[(oc, tt)]
                        blk.scalar(_qev)

                        if qkv_depth < 5:
                            continue
                        def _qst(q, oc=oc, tt=tt, qev_s=qev_s, qst_ds=qst_ds, dst=dst):
                            st.w(q, "s", qev_s[(oc, tt)])
                            idx = oc * NT + tt
                            pc_ = "pc%d" % (idx % 2)
                            st.tag(q.dma_start(
                                out=AP(dst.tensor, oc * 128 * T + tt * 512, [[T, 128], [1, 512]]),
                                in_=qst[idx % 2][:, :]), pc_, 16)
                            qst_ds[(oc, tt)] = (pc_, st.cnt[pc_])
                            qst_hist.append((pc_, st.cnt[pc_]))
                        blk.gpsimd(_qst)

        def _fin(q):
            st.fence_eng(q)
        blk.sync(_fin)
    return nc


def inmaps_launch_D(inputs, consts, x1_arrs, part_arrs):
    wg = np.ascontiguousarray(inputs["L0_ffn_Wg"])
    we = np.ascontiguousarray(inputs["L0_ffn_We"])
    be = np.ascontiguousarray(inputs["L0_ffn_be"])
    bout = np.ascontiguousarray(inputs["L0_attn_b_out"].reshape(8, 128))
    rw = np.ascontiguousarray(inputs["L1_ffn_router_w"])
    rb = np.ascontiguousarray(inputs["L1_ffn_router_b"].reshape(1, 8))
    w1 = np.ascontiguousarray(inputs["L1_ffn_W1"])
    w2 = np.ascontiguousarray(inputs["L1_ffn_W2"])
    b1 = np.ascontiguousarray(inputs["L1_ffn_b1"])
    b2 = np.ascontiguousarray(inputs["L1_ffn_b2"])
    wqT = np.ascontiguousarray(inputs["L1_attn_W_q"].T)
    wkT = np.ascontiguousarray(inputs["L1_attn_W_k"].T)
    wvT = np.ascontiguousarray(inputs["L1_attn_W_v"].T)
    sin_signed = consts["sin128"].copy()
    sin_signed[0::2, :] *= -1.0
    maps = []
    for c in range(8):
        b, h = c // 2, c % 2
        t0 = h * TOK
        m = common_inmap(consts)
        m["x1_in"] = np.asarray(x1_arrs[c])
        m["p0_in"] = np.ascontiguousarray(np.asarray(part_arrs[2 * b])[:, t0:t0 + TOK])
        m["p1_in"] = np.ascontiguousarray(np.asarray(part_arrs[2 * b + 1])[:, t0:t0 + TOK])
        m["bout_in"] = bout
        m["wg_in"] = wg
        m["we_in"] = we
        m["be_in"] = be
        m["rw_in"] = rw
        m["rb_in"] = rb
        m["w1_in"] = w1
        m["w2_in"] = w2
        m["b1_in"] = b1
        m["b2_in"] = b2
        m["wqT_in"] = wqT
        m["wkT_in"] = wkT
        m["wvT_in"] = wvT
        m["cos_in"] = np.ascontiguousarray(consts["cos128"][:, t0:t0 + TOK])
        m["sin_in"] = np.ascontiguousarray(sin_signed[:, t0:t0 + TOK])
        maps.append(m)
    return maps


def build_launch_E(dbg=False):
    nc = bass.Bass("TRN2", target_bir_lowering=False, debug=False, **BASS_KW)
    T = TOK
    NT = T // 512
    NQS = T // 512
    x4_in = nc.dram_tensor("x4_in", [D, T], F32, kind="ExternalInput").ap()
    q_in = nc.dram_tensor("q_in", [D, T], BF16, kind="ExternalInput").ap()
    k_in = nc.dram_tensor("k_in", [D, S], BF16, kind="ExternalInput").ap()
    v_in = nc.dram_tensor("v_in", [D, S], BF16, kind="ExternalInput").ap()
    woT_in = nc.dram_tensor("woT_in", [D, D], F32, kind="ExternalInput").ap()
    rw_in = nc.dram_tensor("rw_in", [D, NE], F32, kind="ExternalInput").ap()
    rb_in = nc.dram_tensor("rb_in", [1, NE], F32, kind="ExternalInput").ap()
    w1_in = nc.dram_tensor("w1_in", [NE, D, DFFP], F32, kind="ExternalInput").ap()
    w2_in = nc.dram_tensor("w2_in", [NE, DFFP, D], F32, kind="ExternalInput").ap()
    b1_in = nc.dram_tensor("b1_in", [NE, DFFP], F32, kind="ExternalInput").ap()
    b2_in = nc.dram_tensor("b2_in", [NE, D], F32, kind="ExternalInput").ap()
    x6_out = nc.dram_tensor("x6_out", [D, T], F32, kind="ExternalOutput").ap()
    gt_dram = nc.dram_tensor("gt_scr", [NE, T], BF16, kind="Internal").ap()
    if dbg:
        x5_out = nc.dram_tensor("x5_out", [D, T], F32, kind="ExternalOutput").ap()

    ct = {}
    x_fm = nc.alloc_sbuf_tensor("x_fm", [128, NCH * T], F32)
    att = nc.alloc_sbuf_tensor("att", [128, NCH * T], BF16)
    big1 = nc.alloc_sbuf_tensor("big1", [128, 16384], BF16)
    A1 = nc.alloc_sbuf_tensor("A1", [128, 4096], BF16)
    A3 = nc.alloc_sbuf_tensor("A3", [128, 4096], BF16)
    A4 = nc.alloc_sbuf_tensor("A4", [128, 2048], BF16)
    A5 = nc.alloc_sbuf_tensor("A5", [128, 2048], BF16)
    P_sb = nc.alloc_sbuf_tensor("P_sb", [128, 4096], BF16)
    ct["wstage"] = [nc.alloc_sbuf_tensor("wstage%d" % i, [128, 1024], F32) for i in range(2)]
    ct["w1b"] = A1.ap()
    ct["w2b"] = P_sb.ap()
    ct["h1g"] = A3.ap()[:, 0:2048]
    ct["gbb"] = A3.ap()[:, 2048:4096]
    ct["h1f"] = nc.alloc_sbuf_tensor("h1f", [128, 512], F32)
    gtrb1 = nc.alloc_sbuf_tensor("gtrb", [1, T], BF16)
    ct["gtrb"] = [gtrb1, gtrb1]
    beb1 = nc.alloc_sbuf_tensor("beb", [1, D], F32)
    ct["beb"] = [beb1, beb1]
    bebf1 = nc.alloc_sbuf_tensor("bebf", [1, D], BF16)
    ct["bebf"] = [bebf1, bebf1]
    ct["b1sb"] = [nc.alloc_sbuf_tensor("b1sb%d" % i, [128, 4], F32) for i in range(2)]
    rw_f = nc.alloc_sbuf_tensor("rw_f", [128, NCH * 8], F32)
    rb_sb = nc.alloc_sbuf_tensor("rb_sb", [1, 8], F32)
    ct["PS_A"] = nc.alloc_psum_tensor("PS_A", [128, 2048], F32)
    ct["PS_B"] = nc.alloc_psum_tensor("PS_B", [128, 2048], F32)
    ct["tmpa"] = nc.alloc_sbuf_tensor("tmpa", [128, T], F32)
    ct["sq0"] = ct["tmpa"].ap()
    ct["sq1"] = ct["tmpa"].ap()
    ct["sq_single"] = True
    ct["lsb"] = ct["tmpa"].ap()
    ct["lsb_bf"] = P_sb.ap()[:, 0:T]
    rsum = nc.alloc_sbuf_tensor("rsum", [128, 8], F32)
    nmx = nc.alloc_sbuf_tensor("nmx", [128, 4], F32)
    rla = nc.alloc_sbuf_tensor("rla", [128, 4], F32)
    rlrow = ct["tmpa"].ap()[0:1, 512:1024]
    accf = ct["tmpa"].ap()[0:64, 1024:1536]
    for nm, shp, dt in [("tokg", [128, (T // 128) * 8], F32),
                        ("e8", [128, (T // 128) * 8], F32),
                        ("nm1s", [128, T // 128], F32),
                        ("w1", [128, 1], F32), ("w2", [128, 1], F32),
                        ("w3", [128, 1], F32), ("w8a", [128, 8], F32),
                        ("w8b", [128, 8], F32), ("w8c", [128, 8], F32),
                        ("mtok", [128, 16], F32), ("rtok", [128, 16], F32),
                        ("sbc", [128, 8], F32), ("rbc", [128, 8], F32),
                        ("r_mean", [1, T], F32), ("r_tmp", [1, T], F32),
                        ("r_rstd", [1, T], F32)]:
        if nm in ("r_mean", "r_tmp", "r_rstd"):
            continue
        ct[nm] = nc.alloc_sbuf_tensor(nm, shp, dt)
    attf = att.ap().bitcast(F32)
    ct["r_mean"] = attf[0:1, 0:2048]
    ct["r_tmp"] = attf[0:1, 2048:4096]
    ct["r_rstd"] = attf[0:1, 4096:6144]
    PS_A, PS_B = ct["PS_A"], ct["PS_B"]
    PAb = PS_A.ap().bitcast(BF16)

    extra = ["wl0", "wl1", "ge0", "ge1", "kv0", "kv1", "pc0", "pc1"]
    with (nc.semaphore("sem_t") as s_t, nc.semaphore("sem_v") as s_v,
          nc.semaphore("sem_s") as s_s, nc.semaphore("sem_p") as s_p,
          nc.semaphore("sem_dl") as s_dl, nc.semaphore("sem_ds") as s_ds,
          nc.Block() as blk):
        sems = dict(t=s_t, v=s_v, s=s_s, p=s_p, dl=s_dl, ds=s_ds)
        for nm in extra:
            sems[nm] = nc.alloc_semaphore(name="sem_" + nm)
        st = St(nc, blk, sems)
        cins = declare_common(nc, ct)

        def _l0(q):
            for k, (ap_, tname) in cins.items():
                st.tag(q.dma_start(out=ct[tname][:], in_=ap_), "dl", 16)
            st.tag(q.dma_start(out=x_fm.ap().rearrange("p (c u) -> p c u", u=T),
                               in_=fm_in_ap(x4_in, NCH, T)), "dl", 16)
            src = AP(rw_in.tensor, 0, [[NE, 128], [NE * 128, NCH], [1, NE]])
            st.tag(q.dma_start(out=rw_f.ap().rearrange("p (c e) -> p c e", e=NE),
                               in_=src), "dl", 16)
            st.tag(q.dma_start(out=rb_sb[0:1, :], in_=rb_in), "dl", 16)
        blk.sync(_l0)
        fence(st)

        hload, hdone_t, hdone_v, hdone_s = {}, {}, {}, {}
        for h in range(NH):
            kbuf = A1.ap()[0:64, 0:S]
            vbuf = A3.ap()[0:64, 0:S]
            qbuf = A4.ap()[0:64, 0:T]

            def _hl(q, h=h, kbuf=kbuf, vbuf=vbuf, qbuf=qbuf):
                kv = "kv%d" % (h % 2)
                if h >= 1:
                    st.w(q, "t", hdone_t[h - 1])
                    st.w(q, "v", hdone_v[h - 1])
                    st.w(q, "s", hdone_s[h - 1])
                st.tag(q.dma_start(out=kbuf,
                                   in_=AP(k_in.tensor, h * 64 * S, [[S, 64], [1, S]])), kv, 16)
                st.tag(q.dma_start(out=vbuf,
                                   in_=AP(v_in.tensor, h * 64 * S, [[S, 64], [1, S]])), kv, 16)
                st.tag(q.dma_start(out=qbuf,
                                   in_=AP(q_in.tensor, h * 64 * T, [[T, 64], [1, T]])), kv, 16)
                hload[h] = (kv, st.cnt[kv])
            blk.sync(_hl)

            def _vt(t, h=h, vbuf=vbuf):
                st.w(t, *hload[h])
                if h >= 1:
                    st.w(t, "s", hdone_s[h - 1])
                for kt in range(32):
                    ins = t.matmul(out=PAb[:, kt * 64:(kt + 1) * 64],
                                   lhsT=vbuf[:, kt * 128:(kt + 1) * 128],
                                   rhs=ct["identb"][0:64, 0:64],
                                   is_transpose=True,
                                   start=(kt % 8 == 0), stop=(kt % 8 == 7))
                st.tag(ins, "t")
            blk.tensor(_vt)

            def _vte(s, h=h):
                st.fence_eng(s, skip=("s", "dl", "ds", "kv0", "kv1", "wl0", "wl1",
                                      "ge0", "ge1", "pc0", "pc1", "p"))
                st.tag(s.copy(out=A5[:, 0:2048], in_=PAb[:, 0:2048]), "s")
            blk.scalar(_vte)

            for qs in range(NQS):
                qmax_v, qexp_s, qtp_t, qte_s = {}, {}, {}, {}
                for qt4 in range(4):
                    qti = qs * 4 + qt4

                    def _qk(t, h=h, qs=qs, qt4=qt4, qti=qti):
                        st.fence_eng(t, skip=("t", "dl", "ds", "kv0", "kv1",
                                              "wl0", "wl1", "ge0", "ge1",
                                              "pc0", "pc1", "p"))
                        st.w(t, *hload[h])
                        kbuf = A1.ap()[0:64, 0:S]
                        qbuf = A4.ap()[0:64, 0:T]
                        for c in range(8):
                            psd = PS_A if c < 4 else PS_B
                            ins = t.matmul(out=psd[:, (c % 4) * 512:(c % 4 + 1) * 512],
                                           lhsT=qbuf[:, qti * 128:(qti + 1) * 128],
                                           rhs=kbuf[:, c * 512:(c + 1) * 512],
                                           start=True, stop=True)
                        st.tag(ins, "t")
                        qtp_t[(qt4, "qk")] = st.cnt["t"]
                    blk.tensor(_qk)

                    def _mx(v, h=h, qt4=qt4):
                        st.w(v, "t", qtp_t[(qt4, "qk")])
                        v.reduce_max(out=nmx[:, 0:1], in_=PS_A[:, 0:2048], axis=AX.X,
                                     negate=True)
                        st.tag(v.reduce_max(out=nmx[:, 1:2], in_=PS_B[:, 0:2048],
                                            axis=AX.X, negate=True), "v")
                        st.w(v, "v")
                        st.tag(v.tensor_tensor(out=nmx[:, 2:3], in0=nmx[:, 0:1],
                                               in1=nmx[:, 1:2], op=ALU.min), "v")
                        st.w(v, "v")
                        st.tag(v.tensor_scalar(out=nmx[:, 2:3], in0=nmx[:, 2:3],
                                               scalar1=0.125, scalar2=None,
                                               op0=ALU.mult), "v")
                        qmax_v[qt4] = st.cnt["v"]
                    blk.vector(_mx)

                    def _ex(s, h=h, qt4=qt4):
                        st.w(s, "v", qmax_v[qt4])
                        for c in range(8):
                            psd = PS_A if c < 4 else PS_B
                            ins = s.activation(out=P_sb[:, c * 512:(c + 1) * 512],
                                               in_=psd[:, (c % 4) * 512:(c % 4 + 1) * 512],
                                               func=AFT.Exp, bias=nmx[:, 2:3],
                                               scale=0.125,
                                               accum_out=rsum[:, c:c + 1])
                        st.tag(ins, "s")
                        qexp_s[qt4] = st.cnt["s"]
                    blk.scalar(_ex)

                    def _l(v, h=h, qt4=qt4):
                        st.w(v, "s", qexp_s[qt4])
                        st.tag(v.reduce_sum(out=rla[:, qt4:qt4 + 1], in_=rsum[:, 0:8],
                                            axis=AX.X), "v")
                        st.w(v, "v")
                        st.tag(v.reciprocal(out=rla[:, qt4:qt4 + 1],
                                            in_=rla[:, qt4:qt4 + 1]), "v")
                    blk.vector(_l)

                    def _tp(t, h=h, qt4=qt4):
                        st.w(t, "s", qexp_s[qt4])
                        for kt in range(32):
                            ins = t.matmul(out=PAb[:, 2048 + (kt % 16) * 128:2048 + (kt % 16 + 1) * 128]
                                           if kt < 16 else
                                           PS_B.ap().bitcast(BF16)[:, 2048 + (kt % 16) * 128:2048 + (kt % 16 + 1) * 128],
                                           lhsT=P_sb[:, kt * 128:(kt + 1) * 128],
                                           rhs=ct["identb"][:, :], is_transpose=True,
                                           start=(kt % 8 == 0), stop=(kt % 8 == 7))
                        st.tag(ins, "t")
                        qtp_t[qt4] = st.cnt["t"]
                    blk.tensor(_tp)

                    def _te(s, h=h, qs=qs, qt4=qt4):
                        st.w(s, "t", qtp_t[qt4])
                        for g in range(2):
                            src = (PAb if g == 0 else PS_B.ap().bitcast(BF16))[:, 2048:4096]
                            dst = big1.ap().rearrange("p (kt u) -> p kt u", u=512)[
                                :, g * 16:(g + 1) * 16, qt4 * 128:(qt4 + 1) * 128]
                            ins = s.copy(out=dst, in_=src.rearrange("p (kt u) -> p kt u", u=128))
                        st.tag(ins, "s")
                        qte_s[qt4] = st.cnt["s"]
                    blk.scalar(_te)

                def _pv(t, h=h, qs=qs):
                    for qt4 in range(4):
                        st.w(t, "s", qte_s[qt4])
                    for kt in range(32):
                        ins = t.matmul(out=PS_A[0:64, 1536:2048],
                                       lhsT=A5[:, kt * 64:(kt + 1) * 64],
                                       rhs=big1[:, kt * 512:(kt + 1) * 512],
                                       start=(kt == 0), stop=(kt == 31))
                    st.tag(ins, "t")
                    hdone_t[h] = st.cnt["t"]
                blk.tensor(_pv)

                def _rlt(t, h=h, qs=qs):
                    for qt4 in range(4):
                        ins = t.matmul(out=PS_B[0:1, 512 + qt4 * 128:512 + (qt4 + 1) * 128],
                                       lhsT=rla[:, qt4:qt4 + 1],
                                       rhs=ct["ident"][:, :], is_transpose=True,
                                       start=(qt4 == 0), stop=(qt4 == 3))
                    st.tag(ins, "t")
                blk.tensor(_rlt)

                def _rle(s, h=h, qs=qs):
                    st.fence_eng(s, skip=("s", "dl", "ds", "kv0", "kv1", "wl0", "wl1",
                                          "ge0", "ge1", "pc0", "pc1", "p"))
                    s.copy(out=rlrow[0:1, 0:512], in_=PS_B[0:1, 512:1024])
                    st.tag(s.copy(out=accf[0:64, :], in_=PS_A[0:64, 1536:2048]), "s")
                    hdone_s[h] = st.cnt["s"]
                blk.scalar(_rle)

                def _rlb(t, h=h, qs=qs):
                    st.fence_eng(t, skip=("t", "dl", "ds", "kv0", "kv1", "wl0", "wl1",
                                          "ge0", "ge1", "pc0", "pc1", "p"))
                    st.tag(t.matmul(out=PS_B[0:64, 0:512], lhsT=ct["ones_row"][0:1, 0:64],
                                    rhs=rlrow[0:1, 0:512], start=True, stop=True), "t")
                blk.tensor(_rlb)

                def _nrm(v, h=h, qs=qs):
                    st.fence_eng(v, skip=("v", "dl", "ds", "kv0", "kv1", "wl0", "wl1",
                                          "ge0", "ge1", "pc0", "pc1", "p"))
                    dst = att[:, :].ap() if False else att.ap()
                    row0 = (h % 2) * 64
                    dview = att.ap().rearrange("p (c u) -> p c u", u=T)[
                        row0:row0 + 64, h // 2, qs * 512:(qs + 1) * 512]
                    st.tag(v.tensor_mul(out=dview, in0=accf[0:64, :],
                                        in1=PS_B[0:64, 0:512]), "v")
                    hdone_v[h] = st.cnt["v"]
                blk.vector(_nrm)
        fence(st)

        wo_t, wo_s, wo_p, wld = {}, {}, {}, {}
        tA, sA, pA = st.cnt["t"], st.cnt["s"], st.cnt["p"]
        pA0w = pA
        for oc in range(NCH):
            pA += 1
            wo_p[oc] = pA
            for tt in range(NT):
                tA += 1
                wo_t[(oc, tt)] = tA

        def _wol(q):
            st.fence_eng(q, skip=("dl", "ds", "wl0", "wl1"))
            for oc in range(NCH):
                ws = "wl%d" % (oc % 2)
                if oc >= 2:
                    st.w(q, "p", wo_p[oc - 2])
                src = AP(woT_in.tensor, oc * 128, [[D, 128], [128 * D, 8], [1, 128]])
                st.tag(q.dma_start(out=ct["wstage"][oc % 2].ap().rearrange(
                    "p (kc j) -> p kc j", j=128), in_=src), ws, 16)
                wld[oc] = (ws, st.cnt[ws])
        blk.sync(_wol)

        for oc in range(NCH):
            def _woc(p, oc=oc):
                st.w(p, *wld[oc])
                if oc >= 2:
                    st.w(p, "t", wo_t[(oc - 2, NT - 1)])
                st.tag(p.tensor_copy(out=A3.ap()[:, (oc % 2) * 1024:(oc % 2 + 1) * 1024],
                                     in_=ct["wstage"][oc % 2][:, :]), "p")
                assert st.cnt["p"] == wo_p[oc]
            blk.gpsimd(_woc)
            for tt in range(NT):
                def _wom(t, oc=oc, tt=tt):
                    st.w(t, "p", wo_p[oc])
                    idx = oc * NT + tt
                    if idx >= 2:
                        st.w(t, "v", st.cnt["v"])
                    for kc in range(NCH):
                        ins = t.matmul(out=PS_A[:, (idx % 2) * 512:(idx % 2) * 512 + 512],
                                       lhsT=A3[:, (oc % 2) * 1024 + kc * 128:(oc % 2) * 1024 + (kc + 1) * 128],
                                       rhs=att[:, kc * T + tt * 512:kc * T + (tt + 1) * 512],
                                       start=(kc == 0), stop=(kc == NCH - 1))
                    st.tag(ins, "t")
                    assert st.cnt["t"] == wo_t[(oc, tt)]
                blk.tensor(_wom)

                def _woe(v, oc=oc, tt=tt):
                    st.w(v, "t", wo_t[(oc, tt)])
                    idx = oc * NT + tt
                    xc = x_fm[:, oc * T + tt * 512:oc * T + (tt + 1) * 512]
                    st.tag(v.tensor_add(out=xc,
                                        in0=PS_A[:, (idx % 2) * 512:(idx % 2) * 512 + 512],
                                        in1=xc), "v")
                blk.vector(_woe)
        fence(st)
        if dbg:
            def _dx5(q):
                st.fence_eng(q)
                st.tag(q.dma_start(out=fm_in_ap(x5_out, NCH, T),
                                   in_=x_fm.ap().rearrange("p (c u) -> p c u", u=T)), "ds", 16)
            blk.sync(_dx5)
            fence(st, "q")

        xnb = big1.ap()[:, 0:NCH * T]
        ln_emit(st, ct, x_fm, T, xnb)
        fence(st)
        gates_emit(st, ct, x_fm, T, rw_f, rb_sb, gt_dram)
        fence(st)
        moe_l1_emit(st, ct, x_fm, xnb, T, w1_in, w2_in, b1_in, b2_in, gt_dram)
        fence(st)

        def _sx6(q):
            st.fence_eng(q)
            st.tag(q.dma_start(out=fm_in_ap(x6_out, NCH, T),
                               in_=x_fm.ap().rearrange("p (c u) -> p c u", u=T)), "ds", 16)
            st.w(q, "ds")
        blk.sync(_sx6)
    return nc


def inmaps_launch_E(inputs, consts, x4_arrs, q_arrs, k_arrs, v_arrs):
    woT = np.ascontiguousarray(inputs["L1_attn_W_o"].T)
    rw = np.ascontiguousarray(inputs["L1_ffn_router_w"])
    rb = np.ascontiguousarray(inputs["L1_ffn_router_b"].reshape(1, 8))
    w1 = np.ascontiguousarray(inputs["L1_ffn_W1"])
    w2 = np.ascontiguousarray(inputs["L1_ffn_W2"])
    b1 = np.ascontiguousarray(inputs["L1_ffn_b1"])
    b2 = np.ascontiguousarray(inputs["L1_ffn_b2"])
    maps = []
    for c in range(8):
        b, h = c // 2, c % 2
        m = common_inmap(consts)
        m["x4_in"] = np.asarray(x4_arrs[c])
        m["q_in"] = np.asarray(q_arrs[c])
        m["k_in"] = np.concatenate([np.asarray(k_arrs[2 * b]),
                                    np.asarray(k_arrs[2 * b + 1])], axis=1)
        m["v_in"] = np.concatenate([np.asarray(v_arrs[2 * b]),
                                    np.asarray(v_arrs[2 * b + 1])], axis=1)
        m["woT_in"] = woT
        m["rw_in"] = rw
        m["rb_in"] = rb
        m["w1_in"] = w1
        m["w2_in"] = w2
        m["b1_in"] = b1
        m["b2_in"] = b2
        maps.append(m)
    return maps


def kernel_multilaunch(**inputs):
    consts = host_consts()
    cores = list(range(8))

    ncA = build_launch_A()
    rA = run_bass_kernel_spmd(ncA, inmaps_launch_A(inputs, consts), core_ids=cores)
    x1 = [rA.results[c]["x1_out"] for c in range(8)]
    xn2 = [rA.results[c]["xn2_out"] for c in range(8)]

    ncB = build_launch_B()
    rB = run_bass_kernel_spmd(ncB, inmaps_launch_B(inputs, consts, xn2), core_ids=cores)
    u = [rB.results[c]["u_out"] for c in range(8)]
    res = [rB.results[c]["res_out"] for c in range(8)]
    xd = [rB.results[c]["xdbl_out"] for c in range(8)]

    ncC = build_launch_C()
    rC = run_bass_kernel_spmd(ncC, inmaps_launch_C(inputs, consts, u, res, xd), core_ids=cores)
    part = [rC.results[c]["part_out"] for c in range(8)]

    ncD = build_launch_D()
    rD = run_bass_kernel_spmd(ncD, inmaps_launch_D(inputs, consts, x1, part), core_ids=cores)
    x4 = [rD.results[c]["x4_out"] for c in range(8)]
    qq = [rD.results[c]["q_out"] for c in range(8)]
    kk = [rD.results[c]["k_out"] for c in range(8)]
    vv = [rD.results[c]["v_out"] for c in range(8)]

    ncE = build_launch_E()
    rE = run_bass_kernel_spmd(ncE, inmaps_launch_E(inputs, consts, x4, qq, kk, vv),
                              core_ids=cores)
    out = np.empty((B, S, D), np.float32)
    for c in range(8):
        b, h = c // 2, c % 2
        out[b, h * TOK:(h + 1) * TOK, :] = np.asarray(rE.results[c]["x6_out"]).T
    return out


GROUPS2 = [[0, 1], [2, 3], [4, 5], [6, 7]]
CC_SEMS = ["cc0", "cc1", "cc2", "cc3", "cc4"]


def emit_cc(st, blk, kind, op, in_ap, out_ap, sem_name):
    def _cc(p):
        st.fence_eng(p)
        ins_ = p.collective_compute(kind, op, GROUPS2,
                                    ins=[in_ap.opt()], outs=[out_ap.opt()])
        ins_.then_inc(st.sem[sem_name])
        st.cnt[sem_name] += 1
    blk.gpsimd(_cc)
    fence(st)


def emit_phase_A(nc, st, blk, ct, io):
    T = TOK
    x_fm, xn1 = ct["x_fm"], ct["xn1"]
    wg_f, g3_sb, b3_sb = ct["wg_f"], ct["g3_sb"], ct["b3_sb"]

    def _l0(q):
        st.fence_eng(q)
        st.tag(q.dma_start(out=x_fm.ap().rearrange("p (c u) -> p c u", u=T),
                           in_=fm_in_ap(io["x_in"], NCH, T)), "dl", 16)
        src = AP(io["wg_in"].tensor, 0, [[NE, 128], [NE * 128, NCH], [1, NE]])
        st.tag(q.dma_start(out=wg_f.ap().rearrange("p (c e) -> p c e", e=NE),
                           in_=src), "dl", 16)
        st.tag(q.dma_start(out=ct["lsb"][0:8, 0:128], in_=io["g3_in"]), "dl", 16)
        st.tag(q.dma_start(out=ct["lsb"][0:8, 128:256], in_=io["b3_in"]), "dl", 16)
    blk.sync(_l0)
    fence(st, "tv")

    def _tgb(t):
        st.fence_eng(t, skip=("t",))
        t.transpose(out=ct["PS_A"][:, 0:8], in_=ct["lsb"][0:8, 0:128],
                    identity=ct["ident"][0:8, 0:8])
        st.tag(t.transpose(out=ct["PS_A"][:, 512:520], in_=ct["lsb"][0:8, 128:256],
                           identity=ct["ident"][0:8, 0:8]), "t")
    blk.tensor(_tgb)

    def _egb(s):
        st.fence_eng(s, skip=("s",))
        s.copy(out=g3_sb[:, 0:8], in_=ct["PS_A"][:, 0:8])
        st.tag(s.copy(out=b3_sb[:, 0:8], in_=ct["PS_A"][:, 512:520]), "s")
    blk.scalar(_egb)
    fence(st)

    ln_emit(st, ct, x_fm, T, xn1)
    fence(st)
    gates_emit(st, ct, x_fm, T, wg_f, None, io["gt_dram"])
    fence(st)
    moe_l0_emit(st, ct, x_fm, xn1, T, io["we_in"], io["be_in"], io["gt_dram"])
    fence(st)

    def _sx1(q):
        st.fence_eng(q)
        st.tag(q.dma_start(out=fm_in_ap(io["x1_spill"], NCH, T),
                           in_=x_fm.ap().rearrange("p (c u) -> p c u", u=T)), "ds", 16)
    blk.sync(_sx1)

    xgbf = ct["xg"]
    ds_base = st.cnt["ds"]

    def _pre(v, c, _st=st, _ds0=ds_base):
        if c >= 2:
            _st.w(v, "ds", _ds0 + 16 * (c - 1))
    ln23_emit(st, ct, x_fm, T, g3_sb, b3_sb,
              lambda c: ct["gbb"][:, 0:T] if (c % 2 == 0) else xgbf.ap().bitcast(BF16)[:, 0:T],
              chunk_pre_wait=_pre)
    v_norm0 = st.cnt["v"] - NCH

    def _sxn2(q):
        st.fence_eng(q, skip=("v", "ds"))
        for c in range(NCH):
            st.w(q, "v", v_norm0 + c + 1)
            buf = ct["gbb"][:, 0:T] if (c % 2 == 0) else xgbf.ap().bitcast(BF16)[:, 0:T]
            st.tag(q.dma_start(
                out=AP(io["cc_xn2_in"].tensor, c * 128 * T, [[T, 128], [1, T]]),
                in_=buf), "ds", 16)
        st.w(q, "ds")
    blk.sync(_sxn2)
    fence(st)


def emit_phase_B(nc, st, blk, ct, io):
    T = S
    NT = T // 512
    xn2 = ct["xn2"]
    ubf = ct["ubf"]
    wstage = ct["wstage"]
    wbuf = ct["wbufB"]
    wxf, wxb = ct["wxf"], ct["wxb"]
    upad, uconv = ct["upad"], ct["uconv"]
    ustore, rstore, xdst = ct["ustore"], ct["rstore"], ct["xdst"]
    prev3, cw_sb, cb_sb, bin_sb = ct["prev3"], ct["cw_sb"], ct["cb_sb"], ct["bin_sb"]
    u_out, res_out, xdbl_out = io["u_d"], io["res_d"], io["cc_xd_in"]
    win_in = io["win_in"]

    def _l0(q):
        st.fence_eng(q)
        for hh in range(2):
            st.tag(q.dma_start(
                out=xn2.ap().rearrange("p (c hh t) -> p c hh t", hh=2, t=TOK)[:, :, hh, :],
                in_=AP(io["cc_xn2_out"].tensor, hh * D * TOK,
                       [[TOK, 128], [128 * TOK, NCH], [1, TOK]])), "dl", 16)
        st.tag(q.dma_start(out=wxf.ap().rearrange("p (c u) -> p c u", u=96),
                           in_=AP(io["wx_in"].tensor, 0, [[96, 128], [96 * 128, 8], [1, 96]])),
               "dl", 16)
        for c in range(8):
            st.tag(q.dma_start(out=cw_sb[:, c * 4:(c + 1) * 4],
                               in_=AP(io["convw_in"].tensor, c * 128,
                                      [[1, 128], [DIH, 4]],),
                               allow_slow_non_contiguous=True), "dl", 16)
        st.tag(q.dma_start(out=cb_sb[:, 0:8],
                           in_=AP(io["convb_in"].tensor, 0, [[1, 128], [128, 8]]),
                           allow_slow_non_contiguous=True), "dl", 16)
        st.tag(q.dma_start(out=bin_sb[:, 0:16],
                           in_=AP(io["bin_in"].tensor, 0, [[1, 128], [128, 16]]),
                           allow_slow_non_contiguous=True), "dl", 16)
    blk.sync(_l0)

    def _cwx(p):
        st.fence_eng(p, skip=("p", "ds"))
        st.tag(p.tensor_copy(out=wxb[:, :], in_=wxf[:, :]), "p")
    blk.gpsimd(_cwx)
    fence(st)

    tA, vA, sA, pA = st.cnt["t"], st.cnt["v"], st.cnt["s"], st.cnt["p"]
    cast_p, mm_t, evac_s, conv_v, silu_s, cast_v = {}, {}, {}, {}, {}, {}
    store_ds = {}
    dl_w = {}
    for oc in range(16):
        pA += 1
        cast_p[oc] = pA
        for tt in range(NT):
            tA += 1
            mm_t[(oc, tt)] = tA
            sA += 1
            evac_s[(oc, tt)] = sA
            if oc < 8:
                vA += 3
                conv_v[(oc, tt)] = vA
                sA += 1
                silu_s[(oc, tt)] = sA
                vA += 1
                cast_v[(oc, tt)] = vA

    def _wload0(q):
        st.fence_eng(q, skip=("dl", "ds"))
        for oc2 in range(2):
            ws = "wl%d" % (oc2 % 2)
            src = AP(win_in.tensor, oc2 * 128,
                     [[DI, 128], [128 * DI, NCH], [1, 128]])
            st.tag(q.dma_start(out=wstage[oc2 % 2].ap().rearrange(
                "p (kc j) -> p kc j", j=128), in_=src), ws, 16)
            dl_w[oc2] = (ws, st.cnt[ws])
    blk.sync(_wload0)

    for oc in range(16):
        if oc + 2 < 16:
            def _wpre(q, oc2=oc + 2):
                st.w(q, "p", cast_p[oc2 - 2])
                ws = "wl%d" % (oc2 % 2)
                src = AP(win_in.tensor, oc2 * 128,
                         [[DI, 128], [128 * DI, NCH], [1, 128]])
                st.tag(q.dma_start(out=wstage[oc2 % 2].ap().rearrange(
                    "p (kc j) -> p kc j", j=128), in_=src), ws, 16)
                dl_w[oc2] = (ws, st.cnt[ws])
            blk.sync(_wpre)

        def _cast(p, oc=oc):
            st.w(p, *dl_w[oc])
            if oc >= 2:
                st.w(p, "t", mm_t[(oc - 2, NT - 1)])
            st.tag(p.tensor_copy(out=wbuf[oc % 2][:, :], in_=wstage[oc % 2][:, :]), "p")
            assert st.cnt["p"] == cast_p[oc]
        blk.gpsimd(_cast)
        if oc < 8:
            def _z(v, oc=oc):
                if oc > 0:
                    st.w(v, "s", silu_s[(oc - 1, NT - 1)])
                v.memset(prev3[:, :], 0.0)
            blk.vector(_z)
        for tt in range(NT):
            def _mm(t, oc=oc, tt=tt):
                st.w(t, "p", cast_p[oc])
                bank = (oc * NT + tt) % 4
                prevkey = None
                idx = oc * NT + tt
                if idx >= 4:
                    pidx = idx - 4
                    prevkey = (pidx // NT, pidx % NT)
                if prevkey is not None:
                    st.w(t, "s", evac_s[prevkey])
                for kc in range(NCH):
                    ins = t.matmul(out=ct["PS_A"][:, bank * 512:(bank + 1) * 512],
                                   lhsT=wbuf[oc % 2][:, kc * 128:(kc + 1) * 128],
                                   rhs=xn2[:, kc * T + tt * 512:kc * T + (tt + 1) * 512],
                                   start=(kc == 0), stop=(kc == NCH - 1))
                st.tag(ins, "t")
                assert st.cnt["t"] == mm_t[(oc, tt)]
            blk.tensor(_mm)

            if oc < 8:
                def _evac(s, oc=oc, tt=tt):
                    st.w(s, "t", mm_t[(oc, tt)])
                    if tt > 0:
                        st.w(s, "v", conv_v[(oc, tt - 1)])
                    elif oc > 0:
                        st.w(s, "v", conv_v[(oc - 1, NT - 1)])
                    bank = (oc * NT + tt) % 4
                    st.tag(s.activation(out=upad[:, 3:515],
                                        in_=ct["PS_A"][:, bank * 512:(bank + 1) * 512],
                                        func=AFT.Identity, bias=bin_sb[:, oc:oc + 1]), "s")
                    assert st.cnt["s"] == evac_s[(oc, tt)]
                blk.scalar(_evac)

                def _conv(v, oc=oc, tt=tt):
                    st.w(v, "s", evac_s[(oc, tt)])
                    st.tag(v.tensor_copy(out=upad[:, 0:3], in_=prev3[:, :]), "v")
                    st.w(v, "v")
                    v.tensor_scalar(out=uconv[:, :], in0=upad[:, 0:512],
                                    scalar1=cw_sb[:, oc * 4:oc * 4 + 1],
                                    scalar2=None, op0=ALU.mult)
                    for i in range(1, 4):
                        ins = v.scalar_tensor_tensor(out=uconv[:, :], in0=upad[:, i:i + 512],
                                                     scalar=cw_sb[:, oc * 4 + i:oc * 4 + i + 1],
                                                     in1=uconv[:, :], op0=ALU.mult, op1=ALU.add)
                    st.tag(ins, "v")
                    st.tag(v.tensor_copy(out=prev3[:, :], in_=upad[:, 512:515]), "v")
                    st.w(v, "v")
                    assert st.cnt["v"] == conv_v[(oc, tt)]
                blk.vector(_conv)

                def _silu(s, oc=oc, tt=tt):
                    st.w(s, "v", conv_v[(oc, tt)])
                    idx = oc * NT + tt
                    if idx >= 2:
                        pidx = idx - 2
                        st.w(s, *store_ds[(pidx // NT, pidx % NT, "u")])
                        st.w(s, "v", cast_v[(pidx // NT, pidx % NT)])
                    st.tag(s.activation(out=ustore[idx % 2][:, :], in_=uconv[:, :],
                                        func=AFT.Silu, bias=cb_sb[:, oc:oc + 1]), "s")
                    assert st.cnt["s"] == silu_s[(oc, tt)]
                blk.scalar(_silu)

                def _ucast(v, oc=oc, tt=tt):
                    st.w(v, "s", silu_s[(oc, tt)])
                    idx = oc * NT + tt
                    st.tag(v.tensor_copy(out=ubf[:, oc * T + tt * 512:oc * T + (tt + 1) * 512],
                                         in_=ustore[idx % 2][:, :]), "v")
                    assert st.cnt["v"] == cast_v[(oc, tt)]
                blk.vector(_ucast)

                def _ustore(q, oc=oc, tt=tt):
                    st.w(q, "s", silu_s[(oc, tt)])
                    idx = oc * NT + tt
                    us = "us%d" % (idx % 2)
                    st.tag(q.dma_start(
                        out=AP(u_out.tensor, oc * 128 * T + tt * 512, [[T, 128], [1, 512]]),
                        in_=ustore[idx % 2][:, :]), us, 16)
                    store_ds[(oc, tt, "u")] = (us, st.cnt[us])
                blk.sync(_ustore)
            else:
                def _evac(s, oc=oc, tt=tt):
                    st.w(s, "t", mm_t[(oc, tt)])
                    idx = (oc - 8) * NT + tt
                    if idx >= 2:
                        pidx = idx - 2
                        st.w(s, *store_ds[((pidx // NT) + 8, pidx % NT, "r")])
                    bank = (oc * NT + tt) % 4
                    st.tag(s.activation(out=rstore[idx % 2][:, :],
                                        in_=ct["PS_A"][:, bank * 512:(bank + 1) * 512],
                                        func=AFT.Identity, bias=bin_sb[:, oc:oc + 1]), "s")
                    assert st.cnt["s"] == evac_s[(oc, tt)]
                blk.scalar(_evac)

                def _rstore(q, oc=oc, tt=tt):
                    st.w(q, "s", evac_s[(oc, tt)])
                    idx = (oc - 8) * NT + tt
                    rs = "rs%d" % (idx % 2)
                    st.tag(q.dma_start(
                        out=AP(res_out.tensor, (oc - 8) * 128 * T + tt * 512,
                               [[T, 128], [1, 512]]),
                        in_=rstore[idx % 2][:, :]), rs, 16)
                    store_ds[(oc, tt, "r")] = (rs, st.cnt[rs])
                blk.sync(_rstore)
    fence(st)

    xd_t, xd_s, xd_ds = {}, {}, {}
    tA2, sA2 = st.cnt["t"], st.cnt["s"]
    for tt in range(NT):
        tA2 += 1
        xd_t[tt] = tA2
        sA2 += 1
        xd_s[tt] = sA2

    for tt in range(NT):
        def _xmm(t, tt=tt):
            if tt >= 2:
                st.w(t, "s", xd_s[tt - 2])
            bank = tt % 2
            for kc in range(8):
                ins = t.matmul(out=ct["PS_B"][0:96, bank * 512:bank * 512 + 512],
                               lhsT=wxb[:, kc * 96:(kc + 1) * 96],
                               rhs=ubf[:, kc * T + tt * 512:kc * T + (tt + 1) * 512],
                               start=(kc == 0), stop=(kc == 7))
            st.tag(ins, "t")
        blk.tensor(_xmm)

        def _xevac(s, tt=tt):
            st.w(s, "t", xd_t[tt])
            if tt >= 2:
                st.w(s, *xd_ds[tt - 2])
            bank = tt % 2
            st.tag(s.copy(out=xdst[tt % 2][0:96, :],
                          in_=ct["PS_B"][0:96, bank * 512:bank * 512 + 512]), "s")
        blk.scalar(_xevac)

        def _xstore(q, tt=tt):
            st.w(q, "s", xd_s[tt])
            xs = "xs%d" % (tt % 2)
            st.tag(q.dma_start(
                out=AP(xdbl_out.tensor, tt * 512, [[T, 96], [1, 512]]),
                in_=xdst[tt % 2][0:96, :]), xs, 16)
            xd_ds[tt] = (xs, st.cnt[xs])
        blk.sync(_xstore)
    fence(st)


def emit_phase_C(nc, st, blk, ct, io):
    T = S
    NCK = T // 512
    PS_A, PS_B = ct["PS_A"], ct["PS_B"]
    u_in, res_in = io["u_d"], io["res_d"]
    xdbl_d = io["cc_xd_out"]
    delta_d, dtu_d, a_d = io["delta_d"], io["dtu_d"], io["a_d"]
    part_out = io["cc_part_in"]
    u_ck, res_ck = ct["u_ck"], ct["res_ck"]
    delta, dtu = ct["delta"], ct["dtu"]
    xdsum, wout, wdt, bones, yg = ct["xdsum"], ct["wout"], ct["wdt"], ct["bones"], ct["yg"]
    A_il, asb, Dv, bdt_sb, hstate = ct["A_il"], ct["asb"], ct["Dv"], ct["bdt_sb"], ct["hstate"]
    dlt16, dtu16, B16, C16 = ct["dlt16"], ct["dtu16"], ct["B16"], ct["C16"]
    dAt, dtuB, hbuf, hCr = ct["dAt"], ct["dtuB"], ct["hbuf"], ct["hCr"]
    ydt, sut, outst = ct["ydt"], ct["sut"], ct["outst"]

    def _l0(q):
        st.fence_eng(q)
        st.tag(q.dma_start(out=wdt.ap().rearrange("p (c u) -> p c u", u=128),
                           in_=AP(io["wdt_in"].tensor, 0, [[DIH, 64], [128, 8], [1, 128]])), "dl", 16)
        st.tag(q.dma_start(
            out=wout.ap().rearrange("p (kc oc u) -> p kc oc u", oc=8, u=128),
            in_=AP(io["wout_in"].tensor, 0, [[D, 128], [128 * D, 8], [128, 8], [1, 128]])), "dl", 16)
        st.tag(q.dma_start(out=bones[:, :], in_=io["bones_in"]), "dl", 16)
        st.tag(q.dma_start(out=asb.ap().rearrange("p (c u) -> p c u", u=NST),
                           in_=AP(io["alog_in"].tensor, 0, [[NST, 128], [NST * 128, 8], [1, NST]])), "dl", 16)
        st.tag(q.dma_start(out=Dv[:, 0:8],
                           in_=AP(io["dvec_in"].tensor, 0, [[1, 128], [128, 8]]),
                           allow_slow_non_contiguous=True), "dl", 16)
        st.tag(q.dma_start(out=bdt_sb[:, 0:8],
                           in_=AP(io["bdt_in"].tensor, 0, [[1, 128], [128, 8]]),
                           allow_slow_non_contiguous=True), "dl", 16)
        st.tag(q.dma_start(out=xdsum.ap().bitcast(F32), in_=xdbl_d), "dl", 16)
    blk.sync(_l0)
    fence(st, "vs")

    def _xds(v):
        st.tag(v.memset(hstate[:, :], 0.0), "v")
    blk.vector(_xds)

    def _aprep(s):
        st.fence_eng(s, skip=("s",))
        st.tag(s.activation(out=asb[:, :], in_=asb[:, :], func=AFT.Exp), "s")
    blk.scalar(_aprep)

    def _aneg(v):
        st.fence_eng(v, skip=("v",))
        st.tag(v.tensor_scalar(out=asb[:, :], in0=asb[:, :], scalar1=-1.0,
                               scalar2=None, op0=ALU.mult), "v")
    blk.vector(_aneg)

    def _adma(q):
        st.fence_eng(q)
        st.tag(q.dma_start(out=a_d, in_=asb.ap().rearrange("p (c u) -> p c u", u=NST)), "ds", 16)
        st.w(q, "ds")
        for j in range(128):
            st.tag(q.dma_start(out=A_il[:, j:j + 1],
                               in_=AP(a_d.tensor, 8 * j * NST, [[NST, 8], [1, NST], [0, 1]])),
                   "dl", 16)
    blk.sync(_adma)
    fence(st)

    for ck in range(NCK):
        def _cl(q, ck=ck):
            if ck > 0:
                st.fence_eng(q, skip=("dl", "ds"))
            for src_d, dst in ((u_in, u_ck), (res_in, res_ck)):
                st.tag(q.dma_start(
                    out=dst.ap().rearrange("p (c u) -> p c u", u=512),
                    in_=AP(src_d.tensor, ck * 512, [[T, 128], [128 * T, 8], [1, 512]])),
                    "dl", 16)
        blk.sync(_cl)

        dmm_t, dsp_s, dsp_v = {}, {}, {}
        tA, sA, vA = st.cnt["t"], st.cnt["s"], st.cnt["v"]
        for dc in range(8):
            tA += 1
            dmm_t[dc] = tA
            sA += 4
            dsp_s[dc] = sA
            vA += 1
            dsp_v[dc] = vA
        for dc in range(8):
            def _dmm(t, ck=ck, dc=dc):
                if dc == 0:
                    st.fence_eng(t, skip=("t",))
                if dc >= 2:
                    st.w(t, "s", dsp_s[dc - 2])
                ins = t.matmul(out=PS_A[:, (dc % 2) * 512:(dc % 2 + 1) * 512],
                               lhsT=wdt[:, dc * 128:(dc + 1) * 128],
                               rhs=xdsum[0:64, ck * 512:(ck + 1) * 512],
                               start=True, stop=True)
                st.tag(ins, "t")
                assert st.cnt["t"] == dmm_t[dc]
            blk.tensor(_dmm)

            def _dsp(s, ck=ck, dc=dc):
                st.w(s, "t", dmm_t[dc])
                if dc >= 1:
                    st.w(s, "v", dsp_v[dc - 1])
                pv = PS_A[:, (dc % 2) * 512:(dc % 2 + 1) * 512]
                s.activation(out=sut[:, :], in_=pv, func=AFT.Abs,
                             bias=bdt_sb[:, dc:dc + 1])
                s.activation(out=sut[:, :], in_=sut[:, :], func=AFT.Exp, scale=-1.0)
                s.activation(out=sut[:, :], in_=sut[:, :], func=AFT.Ln, bias=1.0)
                st.tag(s.activation(out=ydt[:, :], in_=pv, func=AFT.Relu,
                                    bias=bdt_sb[:, dc:dc + 1]), "s", 4)
                assert st.cnt["s"] == dsp_s[dc]
            blk.scalar(_dsp)

            def _dv(v, ck=ck, dc=dc):
                st.w(v, "s", dsp_s[dc])
                st.tag(v.tensor_add(out=delta[:, dc * 512:(dc + 1) * 512],
                                    in0=sut[:, :], in1=ydt[:, :]), "v")
                assert st.cnt["v"] == dsp_v[dc]
            blk.vector(_dv)

        def _dtu(v, ck=ck):
            st.fence_eng(v, skip=("v",))
            for kc in range(8):
                ins = v.tensor_mul(out=dtu[:, kc * 512:(kc + 1) * 512],
                                   in0=delta[:, kc * 512:(kc + 1) * 512],
                                   in1=u_ck[:, kc * 512:(kc + 1) * 512])
            st.tag(ins, "v")
        blk.vector(_dtu)

        def _dst(q, ck=ck):
            st.fence_eng(q, skip=("dl", "ds", "sc0", "sc1", "bc", "dd"))
            st.tag(q.dma_start(
                out=AP(delta_d.tensor, ck * 512, [[T, 128], [128 * T, 8], [1, 512]]),
                in_=delta.ap().rearrange("p (c u) -> p c u", u=512)), "dd", 16)
            st.tag(q.dma_start(
                out=AP(dtu_d.tensor, ck * 512, [[T, 128], [128 * T, 8], [1, 512]]),
                in_=dtu.ap().rearrange("p (c u) -> p c u", u=512)), "dd", 16)
            st.w(q, "dd")
            st.tag(q.dma_start(out=B16[:, :],
                               in_=AP(xdbl_d.tensor, 64 * T + ck * 512,
                                      [[0, 8], [T, 16], [1, 512]])), "bc", 16)
            st.tag(q.dma_start(out=C16[:, :],
                               in_=AP(xdbl_d.tensor, 80 * T + ck * 512,
                                      [[0, 8], [T, 16], [1, 512]])), "bc", 16)
        blk.sync(_dst)

        base = dict(s=st.cnt["s"], v=st.cnt["v"], t=st.cnt["t"])
        sdl, ss, sv, stt = {}, {}, {}, {}
        sA2, vA2, tA2 = base["s"], base["v"], base["t"]
        scc = {0: st.cnt["sc0"], 1: st.cnt["sc1"]}
        for j in range(128):
            scc[j % 2] += 32
            sdl[j] = ("sc%d" % (j % 2), scc[j % 2])
            sA2 += 1
            ss[j] = sA2
            vA2 += 2
            sv[j] = vA2
            tA2 += 1
            stt[j] = tA2

        def _sdma(q, ck=ck):
            for j in range(128):
                if j >= 2:
                    st.w(q, "s", ss[j - 2])
                    st.w(q, "v", sv[j - 2])
                scs = "sc%d" % (j % 2)
                st.tag(q.dma_start(out=dlt16[j % 2][:, :],
                                   in_=AP(delta_d.tensor, 8 * j * T + ck * 512,
                                          [[T, 8], [0, 16], [1, 512]])), scs, 16)
                st.tag(q.dma_start(out=dtu16[j % 2][:, :],
                                   in_=AP(dtu_d.tensor, 8 * j * T + ck * 512,
                                          [[T, 8], [0, 16], [1, 512]])), scs, 16)
                assert st.cnt[scs] == sdl[j][1]
        blk.sync(_sdma)

        def _sact(s, ck=ck):
            for j in range(128):
                st.w(s, *sdl[j])
                if j == 0:
                    st.w(s, "bc", st.cnt["bc"])
                if j >= 2:
                    st.w(s, "v", sv[j - 2])
                st.tag(s.activation(out=dAt[j % 2][:, :], in_=dlt16[j % 2][:, :],
                                    func=AFT.Exp, scale=A_il[:, j:j + 1]), "s")
        blk.scalar(_sact)

        def _sv_(v, ck=ck):
            for j in range(128):
                st.w(v, "s", ss[j])
                st.w(v, *sdl[j])
                if j == 0:
                    st.w(v, "bc", st.cnt["bc"])
                if j >= 2:
                    st.w(v, "t", stt[j - 2])
                v.tensor_mul(out=dtuB[j % 2][:, :], in0=dtu16[j % 2][:, :],
                             in1=B16[:, :])
                v.tensor_tensor_scan(out=hbuf[j % 2][:, :], data0=dAt[j % 2][:, :],
                                     data1=dtuB[j % 2][:, :],
                                     initial=hstate[:, j:j + 1],
                                     op0=ALU.mult, op1=ALU.add)
                st.tag(v.tensor_copy(out=hstate[:, j:j + 1],
                                     in_=hbuf[j % 2][:, 511:512]), "v")
                st.tag(v.tensor_mul(out=hCr[j % 2][:, :], in0=hbuf[j % 2][:, :],
                                    in1=C16[:, :]), "v")
        blk.vector(_sv_)

        def _st_(t, ck=ck):
            if ck > 0:
                st.w(t, "s", ct["_prev_wev7"])
            for j in range(128):
                st.w(t, "v", sv[j])
                st.tag(t.matmul(out=(PS_A[:, (j // 16) * 512:(j // 16 + 1) * 512] if j < 64 else
                                     PS_B[:, ((j // 16) - 4) * 512:((j // 16) - 3) * 512]),
                                lhsT=bones[:, (j % 16) * 128:(j % 16 + 1) * 128],
                                rhs=hCr[j % 2][:, :],
                                start=(j % 16 == 0), stop=(j % 16 == 15)), "t")
        blk.tensor(_st_)

        yg_v = {}
        for oc in range(8):
            def _yev(v, ck=ck, oc=oc):
                if oc == 0:
                    st.w(v, "t", stt[127])
                    st.w(v, "s", st.cnt["s"])
                pv = (PS_A[:, oc * 512:(oc + 1) * 512] if oc < 4 else
                      PS_B[:, (oc - 4) * 512:(oc - 3) * 512])
                st.tag(v.scalar_tensor_tensor(out=ydt[:, :],
                                              in0=u_ck[:, oc * 512:(oc + 1) * 512],
                                              scalar=Dv[:, oc:oc + 1], in1=pv,
                                              op0=ALU.mult, op1=ALU.add), "v")
            blk.vector(_yev)

            def _ysu(s, ck=ck, oc=oc):
                st.w(s, "v", st.cnt["v"])
                st.tag(s.activation(out=sut[:, :], in_=res_ck[:, oc * 512:(oc + 1) * 512],
                                    func=AFT.Silu), "s")
            blk.scalar(_ysu)

            def _ymu(v, ck=ck, oc=oc):
                st.w(v, "s", st.cnt["s"])
                st.tag(v.tensor_mul(out=yg[:, oc * 512:(oc + 1) * 512],
                                    in0=ydt[:, :], in1=sut[:, :]), "v")
                yg_v[oc] = st.cnt["v"]
            blk.vector(_ymu)

        wev_s = {}
        for outc in range(8):
            def _wmm(t, ck=ck, outc=outc):
                if outc == 0:
                    st.w(t, "v", yg_v[7])
                if outc >= 4:
                    st.w(t, "s", wev_s[outc - 4])
                for kc in range(8):
                    ins = t.matmul(out=PS_A[:, (outc % 4) * 512:(outc % 4 + 1) * 512],
                                   lhsT=wout[:, (kc * 8 + outc) * 128:(kc * 8 + outc + 1) * 128],
                                   rhs=yg[:, kc * 512:(kc + 1) * 512],
                                   start=(kc == 0), stop=(kc == 7))
                st.tag(ins, "t")
            blk.tensor(_wmm)

            def _wev(s, ck=ck, outc=outc):
                st.w(s, "t", st.cnt["t"])
                if outc >= 2:
                    st.w(s, "ds", st.cnt["ds"])
                st.tag(s.copy(out=outst[outc % 2][:, :],
                              in_=PS_A[:, (outc % 4) * 512:(outc % 4 + 1) * 512]), "s")
                wev_s[outc] = st.cnt["s"]
            blk.scalar(_wev)

            def _wst(q, ck=ck, outc=outc):
                st.w(q, "s", wev_s[outc])
                st.tag(q.dma_start(
                    out=AP(part_out.tensor,
                           (ck // 4) * D * TOK + outc * 128 * TOK + (ck % 4) * 512,
                           [[TOK, 128], [1, 512]]),
                    in_=outst[outc % 2][:, :]), "ds", 16)
            blk.sync(_wst)
        ct["_prev_wev7"] = wev_s[7]
    fence(st)


def emit_phase_D(nc, st, blk, ct, io):
    T = TOK
    NT = T // 512
    PS_A, PS_B = ct["PS_A"], ct["PS_B"]
    x_fm, xnb = ct["x_fm"], ct["xnb"]
    wg_f, rw_f, rb_sb, bout_sb = ct["wg_f"], ct["rw_f"], ct["rb_sb"], ct["bout_sb"]
    pb0 = ct["w1b"].bitcast(F32)
    cos_sb = ct["wbuf"][0].ap().bitcast(F32)
    sin_sb = ct["wbuf"][1].ap().bitcast(F32)
    qst = ct["qst"]
    gt_dram = io["gt_dram"]

    def _l0(q):
        st.fence_eng(q)
        st.tag(q.dma_start(out=x_fm.ap().rearrange("p (c u) -> p c u", u=T),
                           in_=fm_in_ap(io["x1_spill"], NCH, T)), "dl", 16)
        for w_, d_ in ((io["wg_in"], wg_f), (io["rw_in"], rw_f)):
            src = AP(w_.tensor, 0, [[NE, 128], [NE * 128, NCH], [1, NE]])
            st.tag(q.dma_start(out=d_.ap().rearrange("p (c e) -> p c e", e=NE),
                               in_=src), "dl", 16)
        st.tag(q.dma_start(out=rb_sb[0:1, :], in_=io["rb_in"]), "dl", 16)
        st.tag(q.dma_start(out=bout_sb[:, 0:8],
                           in_=AP(io["bout_in"].tensor, 0, [[1, 128], [128, 8]]),
                           allow_slow_non_contiguous=True), "dl", 16)
    blk.sync(_l0)
    fence(st)

    ln_stats_emit(st, ct, x_fm, T)
    fence(st)
    bcast_rows_emit(st, ct, ct["r_mean"], ct["r_rstd"], T)
    fence(st, "q")
    pld, cmb_v = {}, {}

    def _pl2(q):
        for c in range(NCH):
            if c >= 1:
                st.w(q, "v", cmb_v[c - 1])
            st.tag(q.dma_start(out=pb0[:, 0:T],
                               in_=AP(io["cc_part_out"].tensor, c * 128 * T,
                                      [[T, 128], [1, T]])), "pc0", 16)
            pld[c] = ("pc0", st.cnt["pc0"])
    vA = st.cnt["v"]
    for c in range(NCH):
        vA += 1
        cmb_v[c] = vA
    blk.sync(_pl2)

    def _cmb(v):
        st.fence_eng(v, skip=("v", "pc0"))
        for c in range(NCH):
            st.w(v, *pld[c])
            xc = x_fm[:, c * T:(c + 1) * T]
            v.tensor_sub(out=ct["tmpa"][:, 0:T], in0=xc, in1=ct["PS_A"][:, 0:T])
            v.tensor_mul(out=ct["tmpb"][:, 0:T], in0=ct["tmpa"][:, 0:T],
                         in1=ct["PS_B"][:, 0:T])
            v.scalar_tensor_tensor(out=ct["tmpb"][:, 0:T], in0=ct["tmpb"][:, 0:T],
                                   scalar=bout_sb[:, c:c + 1], in1=xc,
                                   op0=ALU.add, op1=ALU.add)
            st.tag(v.tensor_add(out=xc, in0=ct["tmpb"][:, 0:T], in1=pb0[:, 0:T]), "v")
            assert st.cnt["v"] == cmb_v[c]
    blk.vector(_cmb)
    fence(st)
    if "dbg_x2" in io:
        def _dx2(q):
            st.fence_eng(q)
            st.tag(q.dma_start(out=fm_in_ap(io["dbg_x2"], NCH, T),
                               in_=x_fm.ap().rearrange("p (c u) -> p c u", u=T)), "ds", 16)
        blk.sync(_dx2)
        fence(st, "q")

    ln_emit(st, ct, x_fm, T, xnb)
    fence(st)
    gates_emit(st, ct, x_fm, T, wg_f, None, gt_dram)
    fence(st)
    moe_l0_emit(st, ct, x_fm, xnb, T, io["we_in"], io["be_in"], gt_dram)
    fence(st)
    if "dbg_x3" in io:
        def _dx3(q):
            st.fence_eng(q)
            st.tag(q.dma_start(out=fm_in_ap(io["dbg_x3"], NCH, T),
                               in_=x_fm.ap().rearrange("p (c u) -> p c u", u=T)), "ds", 16)
        blk.sync(_dx3)
        fence(st, "q")

    ln_emit(st, ct, x_fm, T, xnb)
    fence(st)
    gates_emit(st, ct, x_fm, T, rw_f, rb_sb, gt_dram)
    fence(st)
    moe_l1_emit(st, ct, x_fm, xnb, T, io["w1_in"], io["w2_in"], io["b1_in"],
                io["b2_in"], gt_dram)
    fence(st)
    if "dbg_x4" in io:
        def _dx4(q):
            st.fence_eng(q)
            st.tag(q.dma_start(out=fm_in_ap(io["dbg_x4"], NCH, T),
                               in_=x_fm.ap().rearrange("p (c u) -> p c u", u=T)), "ds", 16)
        blk.sync(_dx4)
        fence(st, "q")

    ln_emit(st, ct, x_fm, T, xnb)
    fence(st)
    if "dbg_xn5" in io:
        def _dxn5(q):
            st.fence_eng(q)
            st.tag(q.dma_start(out=fm_in_ap(io["dbg_xn5"], NCH, T),
                               in_=xnb.ap().rearrange("p (c u) -> p c u", u=T)), "ds", 16)
        blk.sync(_dxn5)
        fence(st, "q")

    def _lrope(q):
        st.fence_eng(q)
        st.tag(q.dma_start(out=cos_sb[:, 0:T], in_=io["cos_in"]), "dl", 16)
        st.tag(q.dma_start(out=sin_sb[:, 0:T], in_=io["sin_in"]), "dl", 16)
    blk.sync(_lrope)
    fence(st)

    swapmask = []
    for i in range(16):
        swapmask += [2 * i + 1, 2 * i]
    qst_hist = []
    for pj, (wT, dst, dorope) in enumerate(
            [(io["wqT_in"], io["q_d"], True), (io["wkT_in"], io["cc_k_in"], True),
             (io["wvT_in"], io["cc_v_in"], False)]):
        tA, sA, vA, pA = st.cnt["t"], st.cnt["s"], st.cnt["v"], st.cnt["p"]
        qmm_t, qev_s, qrp_v, qst_ds = {}, {}, {}, {}
        wcast_p, wld = {}, {}
        sA_, vA_, tA_, pA_ = sA, vA, tA, pA
        for oc in range(NCH):
            pA_ += 1
            wcast_p[oc] = pA_
            for tt in range(NT):
                tA_ += 1
                qmm_t[(oc, tt)] = tA_
                sA_ += 1
                qev_s[(oc, tt)] = sA_
                if dorope:
                    vA_ += 2
                    qrp_v[(oc, tt)] = vA_

        def _wl(q, wT=wT, pj=pj, wcast_p=wcast_p, wld=wld):
            st.fence_eng(q, skip=("dl", "ds", "wl0", "wl1", "pc0", "pc1",
                                  "ge0", "ge1"))
            for oc in range(NCH):
                ws = "wl%d" % (oc % 2)
                if oc >= 2:
                    st.w(q, "p", wcast_p[oc - 2])
                src = AP(wT.tensor, oc * 128, [[D, 128], [128 * D, 8], [1, 128]])
                st.tag(q.dma_start(out=ct["wstage"][oc % 2].ap().rearrange(
                    "p (kc j) -> p kc j", j=128), in_=src), ws, 16)
                wld[oc] = (ws, st.cnt[ws])
        blk.sync(_wl)

        for oc in range(NCH):
            def _wc(p, oc=oc, wld=wld, wcast_p=wcast_p, qmm_t=qmm_t):
                st.w(p, *wld[oc])
                if oc >= 2:
                    st.w(p, "t", qmm_t[(oc - 2, NT - 1)])
                st.tag(p.tensor_copy(out=ct["xg"][:, (oc % 2) * 1024:(oc % 2) * 1024 + 1024],
                                     in_=ct["wstage"][oc % 2][:, :]), "p")
                assert st.cnt["p"] == wcast_p[oc]
            blk.gpsimd(_wc)
            for tt in range(NT):
                def _qmm(t, oc=oc, tt=tt, wcast_p=wcast_p, qmm_t=qmm_t, qev_s=qev_s):
                    st.w(t, "p", wcast_p[oc])
                    idx = oc * NT + tt
                    if idx >= 2:
                        st.w(t, "s", qev_s[((idx - 2) // NT, (idx - 2) % NT)])
                    for kc in range(NCH):
                        ins = t.matmul(out=ct["PS_A"][:, (idx % 2) * 512:(idx % 2) * 512 + 512],
                                       lhsT=ct["xg"][:, (oc % 2) * 1024 + kc * 128:(oc % 2) * 1024 + (kc + 1) * 128],
                                       rhs=xnb[:, kc * T + tt * 512:kc * T + (tt + 1) * 512],
                                       start=(kc == 0), stop=(kc == NCH - 1))
                    st.tag(ins, "t")
                    assert st.cnt["t"] == qmm_t[(oc, tt)]
                blk.tensor(_qmm)

                if dorope:
                    def _qev(s, oc=oc, tt=tt, qmm_t=qmm_t, qev_s=qev_s, qrp_v=qrp_v):
                        st.w(s, "t", qmm_t[(oc, tt)])
                        idx = oc * NT + tt
                        if idx >= 1:
                            st.w(s, "v", qrp_v[((idx - 1) // NT, (idx - 1) % NT)])
                        st.tag(s.copy(out=ct["tmpa"][:, 0:512],
                                      in_=ct["PS_A"][:, (idx % 2) * 512:(idx % 2) * 512 + 512]), "s")
                        assert st.cnt["s"] == qev_s[(oc, tt)]
                    blk.scalar(_qev)

                    def _qrp(v, oc=oc, tt=tt, pj=pj, qev_s=qev_s, qrp_v=qrp_v, dst=dst):
                        st.w(v, "s", qev_s[(oc, tt)])
                        idx = oc * NT + tt
                        gidx = pj * NCH * NT + idx
                        if gidx >= 2:
                            st.w(v, *qst_hist[gidx - 2])
                        v.stream_shuffle(out=ct["tmpa"][:, 512:1024],
                                         in_=ct["tmpa"][:, 0:512], mask=swapmask)
                        st.tag(v.tensor_mul(out=ct["tmpa"][:, 512:1024],
                                            in0=ct["tmpa"][:, 512:1024],
                                            in1=sin_sb[:, tt * 512:(tt + 1) * 512]), "v")
                        st.w(v, "v")
                        v.tensor_mul(out=ct["tmpa"][:, 0:512], in0=ct["tmpa"][:, 0:512],
                                     in1=cos_sb[:, tt * 512:(tt + 1) * 512])
                        st.tag(v.tensor_add(out=qst[idx % 2][:, :],
                                            in0=ct["tmpa"][:, 0:512],
                                            in1=ct["tmpa"][:, 512:1024]), "v")
                        assert st.cnt["v"] == qrp_v[(oc, tt)]
                    blk.vector(_qrp)

                    def _qst(q, oc=oc, tt=tt, qrp_v=qrp_v, dst=dst):
                        st.w(q, "v", qrp_v[(oc, tt)])
                        idx = oc * NT + tt
                        pc_ = "pc%d" % (idx % 2)
                        st.tag(q.dma_start(
                            out=AP(dst.tensor, oc * 128 * T + tt * 512, [[T, 128], [1, 512]]),
                            in_=qst[idx % 2][:, :]), pc_, 16)
                        qst_hist.append((pc_, st.cnt[pc_]))
                    blk.gpsimd(_qst)
                else:
                    def _qev(s, oc=oc, tt=tt, pj=pj, qmm_t=qmm_t, qev_s=qev_s):
                        st.w(s, "t", qmm_t[(oc, tt)])
                        idx = oc * NT + tt
                        gidx = pj * NCH * NT + idx
                        if gidx >= 2:
                            st.w(s, *qst_hist[gidx - 2])
                        st.tag(s.copy(out=qst[idx % 2][:, :],
                                      in_=ct["PS_A"][:, (idx % 2) * 512:(idx % 2) * 512 + 512]), "s")
                        assert st.cnt["s"] == qev_s[(oc, tt)]
                    blk.scalar(_qev)

                    def _qst(q, oc=oc, tt=tt, qev_s=qev_s, dst=dst):
                        st.w(q, "s", qev_s[(oc, tt)])
                        idx = oc * NT + tt
                        pc_ = "pc%d" % (idx % 2)
                        st.tag(q.dma_start(
                            out=AP(dst.tensor, oc * 128 * T + tt * 512, [[T, 128], [1, 512]]),
                            in_=qst[idx % 2][:, :]), pc_, 16)
                        qst_hist.append((pc_, st.cnt[pc_]))
                    blk.gpsimd(_qst)
    fence(st)


def emit_phase_E(nc, st, blk, ct, io):
    T = TOK
    NT = T // 512
    NQS = T // 512
    PS_A, PS_B = ct["PS_A"], ct["PS_B"]
    x_fm = ct["x_fm"]
    att, big1 = ct["att"], ct["big1"]
    A1, A3, A4, P_sb = ct["A1"], ct["A3"], ct["A4"], ct["P_sb"]
    rw_f, rb_sb = ct["rw_f"], ct["rb_sb"]
    rsum, nmx, rla = ct["rsum"], ct["nmx"], ct["rla"]
    rlrow, accf = ct["rlrow"], ct["accf"]
    PAb = PS_A.ap().bitcast(BF16)
    gt_dram = io["gt_dram"]

    def _l0(q):
        st.fence_eng(q)
        src = AP(io["rw_in"].tensor, 0, [[NE, 128], [NE * 128, NCH], [1, NE]])
        st.tag(q.dma_start(out=rw_f.ap().rearrange("p (c e) -> p c e", e=NE),
                           in_=src), "dl", 16)
        st.tag(q.dma_start(out=rb_sb[0:1, :], in_=io["rb_in"]), "dl", 16)
    blk.sync(_l0)
    fence(st)

    A5e = ct["A5e"]
    ones3 = ct["ones3"]
    accf65 = ct["tmpa"].ap()[0:65, 1024:1536]
    drow = ct["tmpa"].ap()[64:65, 1024:1536]
    rrow0 = ct["tmpa"].ap()[0:1, 1536:2048]
    P2 = ct["P_sb"].ap()

    def _a5ones(v):
        st.fence_eng(v, skip=("v",))
        st.tag(v.memset(A5e[:, :], 1.0), "v")
    blk.vector(_a5ones)

    hload, hdone_t, hdone_v, hdone_s = {}, {}, {}, {}
    att_hist = []
    NQB = T // 512
    for h in range(NH):
        kbuf = A1.ap()[0:64, 0:S]
        vbuf = A3.ap()[0:64, 0:S]
        qbuf = A4.ap()[0:64, 0:T]

        def _hl(q, h=h, kbuf=kbuf, vbuf=vbuf, qbuf=qbuf):
            kv = "kv%d" % (h % 2)
            if h >= 1:
                st.w(q, "t", hdone_t[h - 1])
                st.w(q, "v", hdone_v[h - 1])
                st.w(q, "s", hdone_s[h - 1])
            st.tag(q.dma_start(
                out=kbuf.rearrange("p (hh t) -> p hh t", hh=2, t=TOK),
                in_=AP(io["cc_kgath"].tensor, h * 64 * TOK,
                       [[TOK, 64], [D * TOK, 2], [1, TOK]])), kv, 16)
            st.tag(q.dma_start(
                out=vbuf.rearrange("p (hh t) -> p hh t", hh=2, t=TOK),
                in_=AP(io["cc_vgath"].tensor, h * 64 * TOK,
                       [[TOK, 64], [D * TOK, 2], [1, TOK]])), kv, 16)
            st.tag(q.dma_start(out=qbuf,
                               in_=AP(io["q_d"].tensor, h * 64 * T, [[T, 64], [1, T]])), kv, 16)
            hload[h] = (kv, st.cnt[kv])
        blk.sync(_hl)

        def _vt(t, h=h, vbuf=vbuf):
            st.w(t, *hload[h])
            if h >= 1:
                st.w(t, "s", hdone_s[h - 1])
            for kt in range(32):
                ins = t.matmul(out=PAb[:, kt * 64:(kt + 1) * 64],
                               lhsT=vbuf[:, kt * 128:(kt + 1) * 128],
                               rhs=ct["identb"][0:64, 0:64],
                               is_transpose=True,
                               start=(kt % 8 == 0), stop=(kt % 8 == 7))
            st.tag(ins, "t")
        blk.tensor(_vt)

        def _vte(s, h=h):
            st.fence_eng(s, skip=("s", "dl", "ds", "kv0", "kv1", "wl0", "wl1",
                                  "ge0", "ge1", "pc0", "pc1", "p"))
            st.tag(s.copy(out=A5e.ap().rearrange("p (kt u) -> p kt u", u=65)[:, :, 0:64],
                          in_=PAb[:, 0:2048].rearrange("p (kt u) -> p kt u", u=64)), "s")
        blk.scalar(_vte)
        vte_s = st.cnt["s"]

        for qb in range(NQB):
            gidx = h * NQB + qb
            t_s, t_pv, s_exp = {}, {}, {}
            s_base = st.cnt["s"]
            for kt in range(32):
                s_exp[kt] = s_base + kt + 1

            def _pe(t, h=h, qb=qb, gidx=gidx, kbuf=kbuf, qbuf=qbuf,
                    t_s=t_s, t_pv=t_pv, s_exp=s_exp):
                if qb == 0:
                    st.w(t, "s", vte_s)
                if gidx >= 2:
                    st.w(t, "s", att_hist[gidx - 2][0])
                for kt in range(33):
                    if kt < 32:
                        if kt >= 2:
                            st.w(t, "s", s_exp[kt - 2])
                        st.tag(t.matmul(out=PS_A[:, (kt % 2) * 512:(kt % 2 + 1) * 512],
                                        lhsT=kbuf[:, kt * 128:(kt + 1) * 128],
                                        rhs=qbuf[:, qb * 512:(qb + 1) * 512],
                                        start=True, stop=True), "t")
                        t_s[kt] = st.cnt["t"]
                    if kt >= 1:
                        st.w(t, "s", s_exp[kt - 1])
                        st.tag(t.matmul(out=PS_B[0:65, (qb % 2) * 512:(qb % 2 + 1) * 512],
                                        lhsT=A5e[:, (kt - 1) * 65:kt * 65],
                                        rhs=P2[:, ((kt - 1) % 2) * 512:((kt - 1) % 2 + 1) * 512],
                                        start=(kt == 1), stop=(kt == 32)), "t")
                        t_pv[kt - 1] = st.cnt["t"]
            blk.tensor(_pe)

            def _ex(s, h=h, qb=qb, t_s=t_s, t_pv=t_pv, s_exp=s_exp):
                for kt in range(32):
                    st.w(s, "t", t_s[kt])
                    if kt >= 2:
                        st.w(s, "t", t_pv[kt - 2])
                    st.tag(s.activation(out=P2[:, (kt % 2) * 512:(kt % 2 + 1) * 512],
                                        in_=PS_A[:, (kt % 2) * 512:(kt % 2 + 1) * 512],
                                        func=AFT.Exp, scale=0.125), "s")
                    assert s_exp[kt] == st.cnt["s"]
            blk.scalar(_ex)

            def _evac(s, h=h, qb=qb, gidx=gidx, t_pv=t_pv):
                st.w(s, "t", t_pv[31])
                if gidx >= 2:
                    st.w(s, "v", att_hist[gidx - 2][1])
                st.tag(s.copy(out=accf65,
                              in_=PS_B[0:65, (qb % 2) * 512:(qb % 2 + 1) * 512]), "s")
            blk.scalar(_evac)
            ev_s = st.cnt["s"]
            prev_bc = att_hist[-1][2] if att_hist else 0

            def _dd(q, ev_s=ev_s, prev_bc=prev_bc):
                st.w(q, "s", ev_s)
                if prev_bc:
                    st.w(q, "t", prev_bc)
                st.tag(q.dma_start(out=rrow0, in_=drow), "bc", 16)
            blk.gpsimd(_dd)
            dd_c = st.cnt["bc"]

            def _rc(v, h=h, qb=qb, dd_c=dd_c):
                st.w(v, "bc", dd_c)
                st.tag(v.reciprocal(out=rrow0, in_=rrow0), "v")
            blk.vector(_rc)
            rc_v = st.cnt["v"]

            def _bc(t, h=h, qb=qb, rc_v=rc_v):
                st.w(t, "v", rc_v)
                st.tag(t.matmul(out=PS_B[0:64, 1024:1536], lhsT=ct["ones_row"][0:1, 0:64],
                                rhs=rrow0, start=True, stop=True), "t")
            blk.tensor(_bc)
            bc_t = st.cnt["t"]

            def _nrm(v, h=h, qb=qb, bc_t=bc_t):
                st.w(v, "t", bc_t)
                row0 = (h % 2) * 64
                dview = att.ap().rearrange("p (c u) -> p c u", u=T)[
                    row0:row0 + 64, h // 2, qb * 512:(qb + 1) * 512]
                st.tag(v.tensor_mul(out=dview, in0=ct["tmpa"].ap()[0:64, 1024:1536],
                                    in1=PS_B[0:64, 1024:1536]), "v")
            blk.vector(_nrm)
            att_hist.append((st.cnt["s"], st.cnt["v"], bc_t))
        hdone_t[h] = st.cnt["t"]
        hdone_s[h] = st.cnt["s"]
        hdone_v[h] = st.cnt["v"]
    fence(st)

    wo_t, wo_s, wo_p, wld = {}, {}, {}, {}
    tA, sA, pA = st.cnt["t"], st.cnt["s"], st.cnt["p"]
    for oc in range(NCH):
        pA += 1
        wo_p[oc] = pA
        for tt in range(NT):
            tA += 1
            wo_t[(oc, tt)] = tA

    def _wol(q):
        st.fence_eng(q, skip=("dl", "ds", "wl0", "wl1"))
        for oc in range(NCH):
            ws = "wl%d" % (oc % 2)
            if oc >= 2:
                st.w(q, "p", wo_p[oc - 2])
            src = AP(io["woT_in"].tensor, oc * 128, [[D, 128], [128 * D, 8], [1, 128]])
            st.tag(q.dma_start(out=ct["wstage"][oc % 2].ap().rearrange(
                "p (kc j) -> p kc j", j=128), in_=src), ws, 16)
            wld[oc] = (ws, st.cnt[ws])
    blk.sync(_wol)

    for oc in range(NCH):
        def _woc(p, oc=oc):
            st.w(p, *wld[oc])
            if oc >= 2:
                st.w(p, "t", wo_t[(oc - 2, NT - 1)])
            st.tag(p.tensor_copy(out=A3.ap()[:, (oc % 2) * 1024:(oc % 2 + 1) * 1024],
                                 in_=ct["wstage"][oc % 2][:, :]), "p")
            assert st.cnt["p"] == wo_p[oc]
        blk.gpsimd(_woc)
        for tt in range(NT):
            def _wom(t, oc=oc, tt=tt):
                st.w(t, "p", wo_p[oc])
                idx = oc * NT + tt
                if idx >= 2:
                    st.w(t, "v", st.cnt["v"])
                for kc in range(NCH):
                    ins = t.matmul(out=PS_A[:, (idx % 2) * 512:(idx % 2) * 512 + 512],
                                   lhsT=A3[:, (oc % 2) * 1024 + kc * 128:(oc % 2) * 1024 + (kc + 1) * 128],
                                   rhs=att[:, kc * T + tt * 512:kc * T + (tt + 1) * 512],
                                   start=(kc == 0), stop=(kc == NCH - 1))
                st.tag(ins, "t")
                assert st.cnt["t"] == wo_t[(oc, tt)]
            blk.tensor(_wom)

            def _woe(v, oc=oc, tt=tt):
                st.w(v, "t", wo_t[(oc, tt)])
                idx = oc * NT + tt
                xc = x_fm[:, oc * T + tt * 512:oc * T + (tt + 1) * 512]
                st.tag(v.tensor_add(out=xc,
                                    in0=PS_A[:, (idx % 2) * 512:(idx % 2) * 512 + 512],
                                    in1=xc), "v")
            blk.vector(_woe)
    fence(st)

    xnb = big1.ap()[:, 0:NCH * T]
    ln_emit(st, ct, x_fm, T, xnb)
    fence(st)
    gates_emit(st, ct, x_fm, T, rw_f, rb_sb, gt_dram)
    fence(st)
    moe_l1_emit(st, ct, x_fm, xnb, T, io["w1_in"], io["w2_in"], io["b1_in"],
                io["b2_in"], gt_dram)
    fence(st)

    def _sx6(q):
        st.fence_eng(q)
        st.tag(q.dma_start(out=fm_in_ap(io["x6_out"], NCH, T),
                           in_=x_fm.ap().rearrange("p (c u) -> p c u", u=T)), "ds", 16)
        st.w(q, "ds")
    blk.sync(_sx6)


def build_fused(phases=5, dbg=False):
    from contextlib import ExitStack
    nc = bass.Bass("TRN2", target_bir_lowering=False, debug=False, **BASS_KW)
    T = TOK
    dri = lambda nm, shp, dt: nc.dram_tensor(nm, shp, dt, kind="ExternalInput").ap()
    drs = lambda nm, shp, dt: nc.dram_tensor(nm, shp, dt, kind="Internal").ap()
    io = {}
    io["x_in"] = dri("x_in", [D, T], F32)
    io["wg_in"] = dri("wg_in", [D, NE], F32)
    io["we_in"] = dri("we_in", [NE, D, D], F32)
    io["be_in"] = dri("be_in", [NE, D], F32)
    io["g3_in"] = dri("g3_in", [8, 128], F32)
    io["b3_in"] = dri("b3_in", [8, 128], F32)
    io["win_in"] = dri("win_in", [D, DI], F32)
    io["bin_in"] = dri("bin_in", [16, 128], F32)
    io["convw_in"] = dri("convw_in", [4, DIH], F32)
    io["convb_in"] = dri("convb_in", [8, 128], F32)
    io["wx_in"] = dri("wx_in", [DIH, 96], F32)
    io["wdt_in"] = dri("wdt_in", [DTR, DIH], F32R)
    io["bdt_in"] = dri("bdt_in", [8, 128], F32)
    io["alog_in"] = dri("alog_in", [DIH, NST], F32)
    io["dvec_in"] = dri("dvec_in", [8, 128], F32)
    io["wout_in"] = dri("wout_in", [DIH, D], F32R)
    io["bones_in"] = dri("bones_in", [128, 16 * 128], F32R)
    io["bout_in"] = dri("bout_in", [8, 128], F32)
    io["rw_in"] = dri("rw_in", [D, NE], F32)
    io["rb_in"] = dri("rb_in", [1, NE], F32)
    io["w1_in"] = dri("w1_in", [NE, D, DFFP], F32)
    io["w2_in"] = dri("w2_in", [NE, DFFP, D], F32)
    io["b1_in"] = dri("b1_in", [NE, DFFP], F32)
    io["b2_in"] = dri("b2_in", [NE, D], F32)
    io["wqT_in"] = dri("wqT_in", [D, D], F32)
    io["wkT_in"] = dri("wkT_in", [D, D], F32)
    io["wvT_in"] = dri("wvT_in", [D, D], F32)
    io["woT_in"] = dri("woT_in", [D, D], F32)
    io["cos_in"] = dri("cos_in", [128, T], F32)
    io["sin_in"] = dri("sin_in", [128, T], F32)
    io["x6_out"] = nc.dram_tensor("x6_out", [D, T], F32, kind="ExternalOutput").ap()
    io["x1_spill"] = drs("x1_spill", [D, T], F32)
    io["cc_xn2_in"] = drs("cc_xn2_in", [D, T], BF16)
    io["cc_xn2_out"] = drs("cc_xn2_out", [2 * D, T], BF16)
    io["u_d"] = drs("u_d", [DIH, S], F32)
    io["res_d"] = drs("res_d", [DIH, S], F32)
    io["cc_xd_in"] = drs("cc_xd_in", [96, S], F32)
    io["cc_xd_out"] = drs("cc_xd_out", [96, S], F32)
    io["delta_d"] = drs("delta_d", [DIH, S], F32)
    io["dtu_d"] = drs("dtu_d", [DIH, S], F32)
    io["a_d"] = drs("a_d", [DIH, NST], F32)
    io["cc_part_in"] = drs("cc_part_in", [2 * D, T], F32)
    io["cc_part_out"] = drs("cc_part_out", [D, T], F32)
    io["q_d"] = drs("q_d", [D, T], BF16)
    io["cc_k_in"] = drs("cc_k_in", [D, T], BF16)
    io["cc_kgath"] = drs("cc_kgath", [2 * D, T], BF16)
    io["cc_v_in"] = drs("cc_v_in", [D, T], BF16)
    io["cc_vgath"] = drs("cc_vgath", [2 * D, T], BF16)
    io["gt_dram"] = drs("gt_scr", [NE, T], BF16)
    dumps = []
    if dbg:
        def dbgt(nm, src_ap, shp, dt):
            d_ = nc.dram_tensor("dbg_" + nm, shp, dt, kind="ExternalOutput").ap()
            dumps.append((d_, src_ap))
        if phases >= 1:
            dbgt("x1", io["x1_spill"], [D, T], F32)
            dbgt("xn2g", io["cc_xn2_out"], [2 * D, T], BF16)
        if phases >= 2:
            dbgt("u", io["u_d"], [DIH, S], F32)
            dbgt("res", io["res_d"], [DIH, S], F32)
            dbgt("xdg", io["cc_xd_out"], [96, S], F32)
        if phases >= 3:
            dbgt("partg", io["cc_part_out"], [D, T], F32)
        if phases >= 4:
            io["dbg_x2"] = nc.dram_tensor("dbg_x2", [D, T], F32, kind="ExternalOutput").ap()
            io["dbg_x3"] = nc.dram_tensor("dbg_x3", [D, T], F32, kind="ExternalOutput").ap()
            io["dbg_x4"] = nc.dram_tensor("dbg_x4", [D, T], F32, kind="ExternalOutput").ap()
            io["dbg_xn5"] = nc.dram_tensor("dbg_xn5", [D, T], BF16, kind="ExternalOutput").ap()
            dbgt("q", io["q_d"], [D, T], BF16)
            dbgt("kg", io["cc_kgath"], [2 * D, T], BF16)
            dbgt("vg", io["cc_vgath"], [2 * D, T], BF16)

    ct = {}
    ct["PS_A"] = nc.alloc_psum_tensor("PS_A", [128, 2048], F32)
    ct["PS_B"] = nc.alloc_psum_tensor("PS_B", [128, 2048], F32)
    for nm, shp, dt in [("tokg", [128, (T // 128) * 8], F32),
                        ("e8", [128, (T // 128) * 8], F32),
                        ("nm1s", [128, T // 128], F32),
                        ("w1", [128, 1], F32), ("w2", [128, 1], F32),
                        ("mtok", [128, 16], F32), ("rtok", [128, 16], F32),
                        ("sbc", [128, 8], F32), ("rbc", [128, 8], F32),
                        ("w3", [128, 1], F32), ("w8a", [128, 8], F32),
                        ("w8b", [128, 8], F32), ("w8c", [128, 8], F32),
                        ("wg_f", [128, NCH * 8], F32), ("rw_f", [128, NCH * 8], F32),
                        ("g3_sb", [128, 8], F32), ("b3_sb", [128, 8], F32),
                        ("bout_sb", [128, 8], F32),
                        ("rsum", [128, 8], F32), ("nmx", [128, 4], F32),
                        ("rla", [128, 4], F32)]:
        ct[nm] = nc.alloc_sbuf_tensor(nm, shp, dt)
    rb_sb = nc.alloc_sbuf_tensor("rb_sb", [1, 8], F32)
    ct["rb_sb"] = rb_sb
    b1sb = [nc.alloc_sbuf_tensor("b1sb%d" % i, [128, 4], F32) for i in range(2)]
    ct["b1sb"] = b1sb

    extra = ["wl0", "wl1", "ge0", "ge1", "pc0", "pc1", "kv0", "kv1",
             "us0", "us1", "rs0", "rs1", "xs0", "xs1", "sc0", "sc1",
             "dd", "bc"] + CC_SEMS
    with (nc.semaphore("sem_t") as s_t, nc.semaphore("sem_v") as s_v,
          nc.semaphore("sem_s") as s_s, nc.semaphore("sem_p") as s_p,
          nc.semaphore("sem_dl") as s_dl, nc.semaphore("sem_ds") as s_ds,
          nc.Block() as blk):
        sems = dict(t=s_t, v=s_v, s=s_s, p=s_p, dl=s_dl, ds=s_ds)
        for nm in extra:
            sems[nm] = nc.alloc_semaphore(name="sem_" + nm)
        st = St(nc, blk, sems)
        cins = declare_common(nc, ct)

        def _lconsts(q):
            for k, (ap_, tname) in cins.items():
                st.tag(q.dma_start(out=ct[tname][:], in_=ap_), "dl", 16)
        blk.sync(_lconsts)

        def small_rows(es, pfx, nbebf=2):
            for nm in ("r_mean", "r_tmp", "r_rstd"):
                ct[nm] = es.enter_context(nc.sbuf_tensor(pfx + nm, [1, T], F32))
            gtrb = es.enter_context(nc.sbuf_tensor(pfx + "gtrb", [1, T], BF16))
            ct["gtrb"] = [gtrb, gtrb]
            beb = es.enter_context(nc.sbuf_tensor(pfx + "beb", [1, D], F32))
            ct["beb"] = [beb, beb]
            if nbebf == 2:
                ct["bebf"] = [es.enter_context(nc.sbuf_tensor(pfx + "bebf%d" % i, [1, D], BF16))
                              for i in range(2)]
            else:
                bb = es.enter_context(nc.sbuf_tensor(pfx + "bebf", [1, D], BF16))
                ct["bebf"] = [bb, bb]

        if phases >= 1:
         with ExitStack() as es:
            x_fm = es.enter_context(nc.sbuf_tensor("a_x_fm", [128, NCH * T], F32))
            xn1 = es.enter_context(nc.sbuf_tensor("a_xn1", [128, NCH * T], BF16))
            ct["x_fm"], ct["xn1"] = x_fm, xn1
            ct["wstage"] = [es.enter_context(nc.sbuf_tensor("a_wstage%d" % i, [128, 1024], F32)) for i in range(2)]
            ct["wbuf"] = [es.enter_context(nc.sbuf_tensor("a_wbuf%d" % i, [128, 4096], BF16)) for i in range(2)]
            ct["xg"] = es.enter_context(nc.sbuf_tensor("a_xg", [128, NCH * 512], BF16))
            ct["gbb"] = es.enter_context(nc.sbuf_tensor("a_gbb", [128, T], BF16))
            tmpa = es.enter_context(nc.sbuf_tensor("a_tmpa", [128, T], F32))
            tmpb = es.enter_context(nc.sbuf_tensor("a_tmpb", [128, T], F32))
            ct["tmpa"], ct["tmpb"] = tmpa, tmpb
            ct["sq0"], ct["sq1"] = tmpa.ap(), tmpb.ap()
            ct["lsb"] = tmpa.ap()
            ct["lsb_bf"] = tmpb.ap().bitcast(BF16)
            ct["sq_single"] = False
            small_rows(es, "a_")
            emit_phase_A(nc, st, blk, ct, io)
        if phases >= 1:
         emit_cc(st, blk, "AllGather", ALU.bypass,
                 io["cc_xn2_in"], io["cc_xn2_out"], "cc0")

        if phases >= 2:
         with ExitStack() as es:
            ct["xn2"] = es.enter_context(nc.sbuf_tensor("b_xn2", [128, NCH * S], BF16))
            ct["ubf"] = es.enter_context(nc.sbuf_tensor("b_ubf", [128, 8 * S], BF16))
            ct["wstage"] = [es.enter_context(nc.sbuf_tensor("b_wstg%d" % i, [128, 1024], F32)) for i in range(2)]
            ct["wbufB"] = [es.enter_context(nc.sbuf_tensor("b_wbufb%d" % i, [128, 1024], BF16)) for i in range(2)]
            ct["wxf"] = es.enter_context(nc.sbuf_tensor("b_wxf", [128, 8 * 96], F32))
            ct["wxb"] = es.enter_context(nc.sbuf_tensor("b_wxb", [128, 8 * 96], BF16))
            ct["upad"] = es.enter_context(nc.sbuf_tensor("b_upad", [128, 515], F32))
            ct["uconv"] = es.enter_context(nc.sbuf_tensor("b_uconv", [128, 512], F32))
            ct["ustore"] = [es.enter_context(nc.sbuf_tensor("b_ustore%d" % i, [128, 512], F32)) for i in range(2)]
            ct["rstore"] = [es.enter_context(nc.sbuf_tensor("b_rstore%d" % i, [128, 512], F32)) for i in range(2)]
            ct["xdst"] = [es.enter_context(nc.sbuf_tensor("b_xdst%d" % i, [96, 512], F32)) for i in range(2)]
            ct["prev3"] = es.enter_context(nc.sbuf_tensor("b_prev3", [128, 3], F32))
            ct["cw_sb"] = es.enter_context(nc.sbuf_tensor("b_cw_sb", [128, 8 * 4], F32))
            ct["cb_sb"] = es.enter_context(nc.sbuf_tensor("b_cb_sb", [128, 8], F32))
            ct["bin_sb"] = es.enter_context(nc.sbuf_tensor("b_bin_sb", [128, 16], F32))
            emit_phase_B(nc, st, blk, ct, io)
        if phases >= 2:
         emit_cc(st, blk, "AllReduce", ALU.add,
                 io["cc_xd_in"], io["cc_xd_out"], "cc1")

        if phases >= 3:
         with ExitStack() as es:
            for nm, shp, dt in [("u_ck", [128, 8 * 512], F32), ("res_ck", [128, 8 * 512], F32),
                                ("delta", [128, 8 * 512], F32), ("dtu", [128, 8 * 512], F32),
                                ("xdsum", [96, S], F32R), ("wout", [128, 64 * 128], F32R),
                                ("wdt", [64, 8 * 128], F32R), ("bones", [128, 16 * 128], F32R),
                                ("yg", [128, 8 * 512], F32R), ("A_il", [128, 128], F32),
                                ("asb", [128, 8 * 16], F32), ("Dv", [128, 8], F32),
                                ("bdt_sb", [128, 8], F32), ("hstate", [128, 128], F32),
                                ("B16", [128, 512], F32), ("C16", [128, 512], F32),
                                ("ydt", [128, 512], F32), ("sut", [128, 512], F32)]:
                ct[nm] = es.enter_context(nc.sbuf_tensor("c_" + nm, shp, dt))
            for nm, dt in [("dlt16", F32), ("dtu16", F32), ("dAt", F32),
                           ("dtuB", F32), ("hbuf", F32), ("hCr", F32R),
                           ("outst", F32)]:
                ct[nm] = [es.enter_context(nc.sbuf_tensor("c_%s%d" % (nm, i), [128, 512], dt))
                          for i in range(2)]
            emit_phase_C(nc, st, blk, ct, io)
        if phases >= 3:
         emit_cc(st, blk, "ReduceScatter", ALU.add,
                 io["cc_part_in"], io["cc_part_out"], "cc2")

        if phases >= 4:
         with ExitStack() as esx:
            x_fm = esx.enter_context(nc.sbuf_tensor("de_x_fm", [128, NCH * T], F32))
            ct["x_fm"] = x_fm
            with ExitStack() as es:
                ct["xnb"] = es.enter_context(nc.sbuf_tensor("d_xnb", [128, NCH * T], BF16))
                ct["wstage"] = [es.enter_context(nc.sbuf_tensor("d_wstage%d" % i, [128, 1024], F32)) for i in range(2)]
                ct["wbuf"] = [es.enter_context(nc.sbuf_tensor("d_wbuf%d" % i, [128, 4096], BF16)) for i in range(2)]
                ct["w1b"] = ct["wbuf"][0].ap()
                ct["w2b"] = ct["wbuf"][1].ap()
                ct["xg"] = es.enter_context(nc.sbuf_tensor("d_xg", [128, NCH * 512], BF16))
                ct["h1g"] = ct["xg"].ap()[:, 0:2048].bitcast(BF16)
                ct["gbb"] = es.enter_context(nc.sbuf_tensor("d_gbb", [128, T], BF16))
                ct["h1f"] = es.enter_context(nc.sbuf_tensor("d_h1f", [128, 512], F32))
                ct["qst"] = [es.enter_context(nc.sbuf_tensor("d_qst%d" % i, [128, 512], BF16)) for i in range(2)]
                tmpa = es.enter_context(nc.sbuf_tensor("d_tmpa", [128, T], F32))
                ct["tmpa"] = tmpa
                ct["tmpb"] = tmpa
                ct["sq0"] = tmpa.ap()
                ct["sq1"] = tmpa.ap()
                ct["lsb"] = tmpa.ap()
                ct["lsb_bf"] = ct["gbb"].ap()
                ct["sq_single"] = True
                small_rows(es, "d_")
                emit_phase_D(nc, st, blk, ct, io)
            emit_cc(st, blk, "AllGather", ALU.bypass,
                    io["cc_k_in"], io["cc_kgath"], "cc3")
            emit_cc(st, blk, "AllGather", ALU.bypass,
                    io["cc_v_in"], io["cc_vgath"], "cc4")
            if phases >= 5:
             with ExitStack() as es:
                att = es.enter_context(nc.sbuf_tensor("e_att", [128, NCH * T], BF16))
                big1 = es.enter_context(nc.sbuf_tensor("e_big1", [128, 16384], BF16))
                A1 = es.enter_context(nc.sbuf_tensor("e_A1", [128, 4096], BF16))
                A3 = es.enter_context(nc.sbuf_tensor("e_A3", [128, 4096], BF16))
                A4 = es.enter_context(nc.sbuf_tensor("e_A4", [128, 2048], BF16))
                A5e = es.enter_context(nc.sbuf_tensor("e_A5e", [128, 32 * 65], BF16))
                ones3 = es.enter_context(nc.sbuf_tensor("e_ones3", [96, 128], F32))
                P_sb = es.enter_context(nc.sbuf_tensor("e_P_sb", [128, 4096], BF16))
                ct["att"], ct["big1"] = att, big1
                ct["A1"], ct["A3"], ct["A4"], ct["P_sb"] = A1, A3, A4, P_sb
                ct["A5e"], ct["ones3"] = A5e, ones3
                ct["wstage"] = [es.enter_context(nc.sbuf_tensor("e_wstage%d" % i, [128, 1024], F32)) for i in range(2)]
                ct["w1b"] = A1.ap()
                ct["w2b"] = P_sb.ap()
                ct["h1g"] = A3.ap()[:, 0:2048]
                ct["gbb"] = A3.ap()[:, 2048:4096]
                ct["h1f"] = es.enter_context(nc.sbuf_tensor("e_h1f", [128, 512], F32))
                tmpa = es.enter_context(nc.sbuf_tensor("e_tmpa", [128, T], F32))
                ct["tmpa"] = tmpa
                ct["sq0"] = tmpa.ap()
                ct["sq1"] = tmpa.ap()
                ct["sq_single"] = True
                ct["lsb"] = tmpa.ap()
                ct["lsb_bf"] = P_sb.ap()[:, 0:T]
                ct["rlrow"] = tmpa.ap()[0:1, 512:1024]
                ct["accf"] = tmpa.ap()[0:64, 1024:1536]
                attf = att.ap().bitcast(F32)
                ct["r_mean"] = attf[0:1, 0:2048]
                ct["r_tmp"] = attf[0:1, 2048:4096]
                ct["r_rstd"] = attf[0:1, 4096:6144]
                gtrb = es.enter_context(nc.sbuf_tensor("e_gtrb", [1, T], BF16))
                ct["gtrb"] = [gtrb, gtrb]
                beb = es.enter_context(nc.sbuf_tensor("e_beb", [1, D], F32))
                ct["beb"] = [beb, beb]
                bebf = es.enter_context(nc.sbuf_tensor("e_bebf", [1, D], BF16))
                ct["bebf"] = [bebf, bebf]
                emit_phase_E(nc, st, blk, ct, io)

        if dumps:
            def _dump(q):
                st.fence_eng(q)
                for d_, s_ in dumps:
                    st.tag(q.dma_start(out=d_, in_=s_), "ds", 16)
                st.w(q, "ds")
            blk.sync(_dump)
    return nc


def inmaps_fused(inputs, consts):
    x = np.asarray(inputs["L0_attn_x"])
    wg = np.ascontiguousarray(inputs["L0_ffn_Wg"])
    we = np.ascontiguousarray(inputs["L0_ffn_We"])
    be = np.ascontiguousarray(inputs["L0_ffn_be"])
    g3 = np.ascontiguousarray(inputs["L0_attn_ln_gamma"].reshape(8, 128))
    b3 = np.ascontiguousarray(inputs["L0_attn_ln_beta"].reshape(8, 128))
    win = np.asarray(inputs["L0_attn_W_in"])
    bin_ = np.asarray(inputs["L0_attn_b_in"])
    cw = np.asarray(inputs["L0_attn_conv_w"])
    cb = np.asarray(inputs["L0_attn_conv_b"])
    wx = np.asarray(inputs["L0_attn_W_x"])
    wdt = np.asarray(inputs["L0_attn_W_dt"])
    bdt = np.asarray(inputs["L0_attn_b_dt"])
    alog = np.asarray(inputs["L0_attn_A_log"])
    dvec = np.asarray(inputs["L0_attn_D"])
    wo = np.asarray(inputs["L0_attn_W_out"])
    bout = np.ascontiguousarray(inputs["L0_attn_b_out"].reshape(8, 128))
    rw = np.ascontiguousarray(inputs["L1_ffn_router_w"])
    rb = np.ascontiguousarray(inputs["L1_ffn_router_b"].reshape(1, 8))
    w1 = np.ascontiguousarray(inputs["L1_ffn_W1"])
    w2 = np.ascontiguousarray(inputs["L1_ffn_W2"])
    b1 = np.ascontiguousarray(inputs["L1_ffn_b1"])
    b2 = np.ascontiguousarray(inputs["L1_ffn_b2"])
    wqT = np.ascontiguousarray(inputs["L1_attn_W_q"].T)
    wkT = np.ascontiguousarray(inputs["L1_attn_W_k"].T)
    wvT = np.ascontiguousarray(inputs["L1_attn_W_v"].T)
    woT = np.ascontiguousarray(inputs["L1_attn_W_o"].T)
    sin_signed = consts["sin128"].copy()
    sin_signed[0::2, :] *= -1.0
    maps = []
    for c in range(8):
        b, h = c // 2, c % 2
        dh = h
        t0 = h * TOK
        m = common_inmap(consts)
        m["x_in"] = np.ascontiguousarray(x[b, t0:t0 + TOK, :].T)
        m["wg_in"] = wg
        m["we_in"] = we
        m["be_in"] = be
        m["g3_in"] = g3
        m["b3_in"] = b3
        ucols = slice(dh * DIH, (dh + 1) * DIH)
        rcols = slice(DI + dh * DIH, DI + (dh + 1) * DIH)
        m["win_in"] = np.ascontiguousarray(
            np.concatenate([win[:, ucols], win[:, rcols]], axis=1))
        m["bin_in"] = np.ascontiguousarray(
            np.concatenate([bin_[ucols], bin_[rcols]]).reshape(16, 128))
        m["convw_in"] = np.ascontiguousarray(cw[:, dh * DIH:(dh + 1) * DIH])
        m["convb_in"] = np.ascontiguousarray(cb[dh * DIH:(dh + 1) * DIH].reshape(8, 128))
        m["wx_in"] = np.ascontiguousarray(wx[dh * DIH:(dh + 1) * DIH, :])
        sl = slice(dh * DIH, (dh + 1) * DIH)
        m["wdt_in"] = np.ascontiguousarray(wdt[:, sl])
        m["bdt_in"] = np.ascontiguousarray(bdt[sl].reshape(8, 128))
        m["alog_in"] = np.ascontiguousarray(alog[sl])
        m["dvec_in"] = np.ascontiguousarray(dvec[sl].reshape(8, 128))
        m["wout_in"] = np.ascontiguousarray(wo[sl])
        m["bones_in"] = consts["bones"]
        m["bout_in"] = bout
        m["rw_in"] = rw
        m["rb_in"] = rb
        m["w1_in"] = w1
        m["w2_in"] = w2
        m["b1_in"] = b1
        m["b2_in"] = b2
        m["wqT_in"] = wqT
        m["wkT_in"] = wkT
        m["wvT_in"] = wvT
        m["woT_in"] = woT
        m["cos_in"] = np.ascontiguousarray(consts["cos128"][:, t0:t0 + TOK])
        m["sin_in"] = np.ascontiguousarray(sin_signed[:, t0:t0 + TOK])
        maps.append(m)
    return maps


def kernel(**inputs):
    consts = host_consts()
    nc = build_fused()
    r = run_bass_kernel_spmd(nc, inmaps_fused(inputs, consts),
                             core_ids=list(range(8)))
    out = np.empty((B, S, D), np.float32)
    for c in range(8):
        b, h = c // 2, c % 2
        out[b, h * TOK:(h + 1) * TOK, :] = np.asarray(r.results[c]["x6_out"]).T
    return out
